# version 02
# baseline (speedup 1.0000x reference)
import math
import numpy as np
from contextlib import ExitStack

import concourse.bass as bass
import concourse.mybir as mybir
from concourse.bass_utils import run_bass_kernel_spmd

F32 = mybir.dt.float32
BF16 = mybir.dt.bfloat16
F16 = mybir.dt.float16
F8 = mybir.dt.float8e4
I32 = mybir.dt.int32
AF = mybir.ActivationFunctionType
ALU = mybir.AluOpType
AX = mybir.AxisListType

ENGS = ("tensor", "vector", "scalar", "gpsimd", "sync")
NCORES = 8
D = 1024
EPS = 1e-5


class PB:

    def __init__(self):
        self.nc = bass.Bass("TRN2", target_bir_lowering=False)
        self.es = ExitStack()
        self.ops = {k: [] for k in ENGS}
        self.sem = {}
        self.cnt = {}
        self._uid = 0
        self.tag = {}

    def uid(self):
        self._uid += 1
        return self._uid

    def semaphore(self, name):
        if name not in self.sem:
            self.sem[name] = self.es.enter_context(self.nc.semaphore(name))
            self.cnt[name] = 0
        return self.sem[name]

    def sb(self, name, shape, dtype=F32):
        es = getattr(self, "_scope", None) or self.es
        return es.enter_context(self.nc.sbuf_tensor(name, list(shape), dtype))

    def push_scope(self):
        self._scope = ExitStack()

    def pop_scope(self, gate=()):
        self._scope.close()
        self._scope = None
        g = [x for x in (gate if isinstance(gate, (list, tuple)) else [gate]) if x is not None]
        self.mem_gate = g

    def ps(self, name, shape, dtype=F32):
        es = getattr(self, "_scope", None) or self.es
        return es.enter_context(self.nc.psum_tensor(name, list(shape), dtype))

    def dram(self, name, shape, dtype=F32, kind="Internal"):
        return self.nc.dram_tensor(name, list(shape), dtype, kind=kind).ap()

    def const(self, arr, name=None):
        return self.nc.inline_tensor(np.ascontiguousarray(arr), name or f"c{self.uid()}").ap()

    def op(self, eng, fn, waits=(), inc=None, inc_by=1):
        w = [x for x in waits if x is not None] + list(getattr(self, "mem_gate", ()))
        tag = None
        if inc is not None:
            self.semaphore(inc)
            if inc_by == 16 and self.cnt[inc] > 0:
                w.append((inc, self.cnt[inc]))
            self.cnt[inc] += inc_by
            tag = (inc, self.cnt[inc])
        for s, _ in w:
            self.semaphore(s)
        sems = self.sem

        def run(e, w=tuple(w), fn=fn, tag=tag, inc_by=inc_by):
            for (s, v) in w:
                e.wait_ge(sems[s], v)
            inst = fn(e)
            if tag is not None:
                assert inst is not None
                inst.then_inc(sems[tag[0]], inc_by)

        self.ops[eng].append(run)
        return tag

    def wait(self, eng, *tags):
        for t in tags:
            if t is None:
                continue
            s, v = t
            self.semaphore(s)
            self.ops[eng].append(lambda e, s=s, v=v: e.wait_ge(self.sem[s], v))

    def finalize(self):
        with self.es, self.nc.Block() as block:
            for name in ENGS:
                getattr(block, name)(lambda e, name=name: [f(e) for f in self.ops[name]])
        return self.nc



def emit_identity(pb):
    eye = np.eye(128, dtype=np.float32)
    h = pb.const(eye, "ident128")
    idf = pb.sb("ident_f32", [128, 128], F32)
    idb = pb.sb("ident_bf16", [128, 128], BF16)
    t1 = pb.op("sync", lambda e: e.dma_start(out=idf[:], in_=h), inc="Wq", inc_by=16)
    t2 = pb.op("vector", lambda e: e.tensor_copy(idb[:], idf[:]), waits=[t1], inc="V")
    pb.tag["id"] = [t1, t2]
    return idf, idb


def alloc_wstage(pb, elems=8192, f8=True, tst=4096, sfx=""):
    pb.w_elems = elems
    pb.w_st32 = pb.sb("w_st32" + sfx, [128, elems], F32)
    pb.w_stq = pb.sb("w_stq" + sfx, [128, elems], F8) if f8 else None
    pb.w_stbf = pb.sb("w_stbf" + sfx, [128, elems], BF16)
    pb.w_tst = pb.sb("w_tst" + sfx, [128, tst], BF16) if tst else None
    pb.tag["wstage"] = None


def emit_weight_T(pb, w_ap, O, I, name, quant=None, fold=None, dst_dtype=BF16,
                  fold_first_rows=None, wT_pre=None, fold_cols=None):
    KT = I // 128
    wT = wT_pre if wT_pre is not None else pb.sb(f"{name}T", [128, KT * O], dst_dtype)
    pre = pb.dram(f"{name}_pre", [O, I], BF16)
    ot = min(max(1, pb.w_elems // I), O // 128)
    oc = ot * 128
    n_oc = O // oc
    assert O % oc == 0
    st32, stq, stbf = pb.w_st32, pb.w_stq, pb.w_stbf
    wv = w_ap.rearrange("(a p) i -> a p i", p=128)
    prev = pb.tag.get("wstage")
    last = None
    for c in range(n_oc):
        tags = []
        for r in range(ot):
            tl = pb.op("sync", lambda e, c=c, r=r: e.dma_start(
                out=st32[:, r * I:(r + 1) * I], in_=wv[c * ot + r]),
                waits=[last, prev], inc="Wq", inc_by=16)
            tags.append(tl)
        if quant == 'e4m3':
            tq = pb.op("vector", lambda e: e.tensor_copy(
                stq[:, :ot * I], st32[:, :ot * I]), waits=tags + [prev], inc="V")
            tb = pb.op("vector", lambda e: e.tensor_copy(
                stbf[:, :ot * I], stq[:, :ot * I]), waits=[tq], inc="V")
        elif callable(quant):
            tb = quant(pb, st32[:, :ot * I], stbf[:, :ot * I], ot * I, tags + [prev])
        else:
            tb = pb.op("vector", lambda e: e.tensor_copy(
                stbf[:, :ot * I], st32[:, :ot * I]), waits=tags + [prev], inc="V")
        if fold is not None:
            tb = pb.op("vector", lambda e: e.tensor_scalar(
                stbf[:, :ot * I], stbf[:, :ot * I], float(fold), None, ALU.mult, ALU.bypass),
                waits=[tb], inc="V")
        if fold_cols is not None:
            for r in range(ot):
                tb = pb.op("vector", lambda e, r=r, c=c: e.tensor_scalar(
                    stbf[:, r * I:(r + 1) * I], stbf[:, r * I:(r + 1) * I],
                    fold_cols[0][:, c * ot + r:c * ot + r + 1], None, ALU.mult, ALU.bypass),
                    waits=[tb, fold_cols[1]], inc="V")
        if fold_first_rows is not None and c * oc < fold_first_rows[0]:
            nr = min(ot, (fold_first_rows[0] - c * oc) // 128)
            tb = pb.op("vector", lambda e, nr=nr: e.tensor_scalar(
                stbf[:, :nr * I], stbf[:, :nr * I], float(fold_first_rows[1]), None,
                ALU.mult, ALU.bypass), waits=[tb], inc="V")
        prev_w = [tb]
        for r in range(ot):
            last = pb.op("gpsimd", lambda e, c=c, r=r: e.dma_start(
                out=pre.rearrange("(a p) i -> a p i", p=128)[c * ot + r],
                in_=stbf[:, r * I:(r + 1) * I]),
                waits=prev_w, inc="Ws", inc_by=16)
        
    rdy = []
    tprev = None
    for k in range(KT):
        if dst_dtype == BF16:
            t = pb.op("sync", lambda e, k=k: e.dma_start(
                wT[:, k * O:(k + 1) * O], pre[:, 128 * k:128 * (k + 1)], transpose=True),
                waits=[last], inc="Wq", inc_by=16)
        else:
            tt = pb.op("sync", lambda e, k=k: e.dma_start(
                pb.w_tst[:, :O], pre[:, 128 * k:128 * (k + 1)], transpose=True),
                waits=[last, tprev], inc="Wq", inc_by=16)
            t = pb.op("vector", lambda e, k=k: e.tensor_copy(
                wT[:, k * O:(k + 1) * O], pb.w_tst[:, :O]), waits=[tt], inc="V")
            tprev = t
        rdy.append(t)
    pb.tag["wstage"] = rdy[-1]
    return wT, rdy[-1]


def emit_bias_cols(pb, b_ap, n, name, idf, quant=None):
    m = n // 128
    stage = pb.sb(f"{name}_stg", [m, 128], F32)
    out = pb.sb(f"{name}c", [128, m], F32)
    pst = pb.ps(f"{name}_ps", [128, m], F32)
    t1 = pb.op("sync", lambda e: e.dma_start(
        out=stage[:], in_=b_ap.rearrange("(a p) -> a p", p=128)), inc="Wq", inc_by=16)
    t2 = pb.op("tensor", lambda e: e.transpose(pst[:], stage[:], idf[0:m, 0:m]),
               waits=[t1] + pb.tag["id"], inc="T")
    if quant == 'e4m3':
        q8 = pb.sb(f"{name}q8", [128, m], F8)
        t3 = pb.op("vector", lambda e: e.tensor_copy(q8[:], pst[:]), waits=[t2], inc="V")
        t4 = pb.op("vector", lambda e: e.tensor_copy(out[:], q8[:]), waits=[t3], inc="V")
        return out, t4
    t3 = pb.op("vector", lambda e: e.tensor_copy(out[:], pst[:]), waits=[t2], inc="V")
    return out, t3


def alloc_ln(pb):
    pb.ln_bufs = (pb.sb("ln_bn6", [128, 12], F32), pb.sb("ln_mv", [128, 2], F32),
                  pb.sb("ln_rstd", [128, 1], F32), pb.sb("ln_negmr", [128, 1], F32))
    pb.tag["ln_prev"] = None


def emit_ln(pb, xa, xn_out, waits):
    bn6, mv, rstd, negmr = pb.ln_bufs
    prev = pb.tag.get("ln_prev")
    t0 = pb.op("vector", lambda e: e.bn_stats(bn6[:, 0:6], xa[:, 0:512]),
               waits=list(waits) + [prev], inc="V")
    t1 = pb.op("vector", lambda e: e.bn_stats(bn6[:, 6:12], xa[:, 512:1024]),
               waits=[t0], inc="V")
    t2 = pb.op("vector", lambda e: e.bn_aggr(mv[:], bn6[:]), waits=[t1], inc="V")
    t3 = pb.op("scalar", lambda e: e.activation(rstd[:], mv[:, 1:2], AF.Ln, bias=np.float32(EPS)),
               waits=[t2], inc="S")
    t4 = pb.op("scalar", lambda e: e.activation(rstd[:], rstd[:], AF.Exp, scale=np.float32(-0.5)),
               waits=[t3], inc="S")
    t5 = pb.op("vector", lambda e: e.tensor_scalar(
        negmr[:], mv[:, 0:1], rstd[:, 0:1], -1.0, ALU.mult, ALU.mult), waits=[t4], inc="V")
    t6 = pb.op("scalar", lambda e: e.activation(
        xn_out[:], xa[:], AF.Identity, bias=negmr[:, 0:1], scale=rstd[:, 0:1]),
        waits=[t5], inc="S")
    pb.tag["ln_prev"] = t6
    return t6



def build_l0ffn(R=2048):
    pb = PB()
    xin = pb.dram("xin", [R, D], F32, kind="ExternalInput")
    w1 = pb.dram("w1", [4096, D], F32, kind="ExternalInput")
    b1 = pb.dram("b1", [4096], F32, kind="ExternalInput")
    w2 = pb.dram("w2", [D, 4096], F32, kind="ExternalInput")
    b2 = pb.dram("b2", [D], F32, kind="ExternalInput")
    mask = pb.dram("mask", [R, D], F32, kind="ExternalInput")
    xout = pb.dram("xout", [R, D], F32, kind="ExternalOutput")

    idf, idb = emit_identity(pb)
    alloc_wstage(pb)
    alloc_ln(pb)
    w1T, w1rdy = emit_weight_T(pb, w1, 4096, D, "w1", quant='e4m3', dst_dtype=F8)
    w2T, w2rdy = emit_weight_T(pb, w2, D, 4096, "w2", quant='e4m3', dst_dtype=F8)
    b1c, b1rdy = emit_bias_cols(pb, b1, 4096, "b1", idf, quant='e4m3')
    b2c, b2rdy = emit_bias_cols(pb, b2, D, "b2", idf, quant='e4m3')

    xa = [pb.sb(f"xa{i}", [128, D], F32) for i in range(2)]
    xn = pb.sb("xn", [128, D], F32)
    xq8 = pb.sb("xq8", [128, D], F8)
    xqb = pb.sb("xqb", [128, D], BF16)
    xfm = pb.sb("xfm", [128, 8 * 512], BF16)
    psT = [pb.ps(f"psT{i}", [128, 128], BF16) for i in range(2)]
    psM = [pb.ps(f"psM{i}", [128, 512], F32) for i in range(2)]
    h16 = pb.sb("h16", [128, 512], F16)
    hfull = pb.sb("hfull", [128, 32 * 512], F8)
    y16 = pb.sb("y16", [128, 512], F16)
    yfm = pb.sb("yfm", [128, 8 * 512], F32)
    psY = [pb.ps(f"psY{i}", [128, 128], F32) for i in range(2)]
    ytk = pb.sb("ytk", [128, D], F32)
    mk = pb.sb("mk", [128, D], F32)

    xv = xin.rearrange("(t p) d -> t p d", p=128)
    ov = xout.rearrange("(t p) d -> t p d", p=128)
    mkv = mask.rearrange("(t p) d -> t p d", p=128)
    T = pb.tag

    for n in range(R // 512):
        tr_tags = []
        for ti in range(4):
            t = 4 * n + ti
            a = xa[ti % 2]
            tl = pb.op("sync", lambda e, t=t, a=a: e.dma_start(out=a[:], in_=xv[t]),
                       waits=[T.get(("xa", ti % 2))], inc="Xq", inc_by=16)
            tln = emit_ln(pb, a, xn, [tl, T.get("xn_rd")])
            T[("xa", ti % 2)] = tln
            t8 = pb.op("vector", lambda e: e.tensor_copy(xq8[:], xn[:]),
                       waits=[tln, T.get("xq8_rd")], inc="V")
            T["xn_rd"] = t8
            tb = pb.op("vector", lambda e: e.tensor_copy(xqb[:], xq8[:]),
                       waits=[t8, T.get("xqb_rd")], inc="V")
            T["xq8_rd"] = tb
            for f in range(8):
                ps = psT[f % 2]
                tt = pb.op("tensor", lambda e, f=f, ps=ps: e.transpose(
                    ps[:], xqb[:, 128 * f:128 * (f + 1)], idb[:]),
                    waits=[tb, T.get(("psT", f % 2))] + T["id"], inc="T")
                ev = pb.op("vector", lambda e, f=f, ti=ti, ps=ps: e.tensor_copy(
                    xfm[:, f * 512 + ti * 128: f * 512 + ti * 128 + 128], ps[:]),
                    waits=[tt], inc="V")
                T[("psT", f % 2)] = ev
                tr_tags.append(ev)
                if f == 7:
                    T["xqb_rd"] = tt
        for m in range(32):
            ps = psM[m % 2]
            def mm(e, m=m, ps=ps):
                inst = None
                for k in range(8):
                    inst = e.matmul(ps[:], w1T[:, k * 4096 + 128 * m: k * 4096 + 128 * (m + 1)],
                                    xfm[:, k * 512:(k + 1) * 512],
                                    start=(k == 0), stop=(k == 7))
                return inst
            tm = pb.op("tensor", mm, waits=tr_tags + [w1rdy, T.get(("psM", m % 2))], inc="T")
            te = pb.op("scalar", lambda e, m=m, ps=ps: e.activation(
                h16[:], ps[:], AF.Identity, bias=b1c[:, m:m + 1]),
                waits=[tm, b1rdy, T.get("h16_rd")], inc="S")
            T[("psM", m % 2)] = te
            tr8 = pb.op("vector", lambda e, m=m: e.tensor_relu(
                hfull[:, m * 512:(m + 1) * 512], h16[:]), waits=[te], inc="V")
            T["h16_rd"] = tr8
        hread = []
        for m in range(8):
            ps = psM[m % 2]
            def mmy(e, m=m, ps=ps):
                inst = None
                for k in range(32):
                    inst = e.matmul(ps[:], w2T[:, k * D + 128 * m: k * D + 128 * (m + 1)],
                                    hfull[:, k * 512:(k + 1) * 512],
                                    start=(k == 0), stop=(k == 31))
                return inst
            tm = pb.op("tensor", mmy,
                       waits=[T["h16_rd"], w2rdy, T.get(("psM", m % 2))], inc="T")
            hread.append(tm)
            te = pb.op("scalar", lambda e, m=m, ps=ps: e.activation(
                y16[:], ps[:], AF.Identity, bias=b2c[:, m:m + 1]),
                waits=[tm, b2rdy, T.get("y16_rd")], inc="S")
            T[("psM", m % 2)] = te
            tc = pb.op("vector", lambda e, m=m: e.tensor_copy(
                yfm[:, m * 512:(m + 1) * 512], y16[:]), waits=[te, T.get(("yfm_rd", m))], inc="V")
            T["y16_rd"] = tc
            T[("yfm", m)] = tc
        for ti in range(4):
            t = 4 * n + ti
            trb = []
            for m in range(8):
                ps = psY[m % 2]
                tt = pb.op("tensor", lambda e, m=m, ti=ti, ps=ps: e.transpose(
                    ps[:], yfm[:, m * 512 + ti * 128: m * 512 + ti * 128 + 128], idf[:]),
                    waits=[T[("yfm", m)], T.get(("psY", m % 2))] + T["id"], inc="T")
                if ti == 3:
                    T[("yfm_rd", m)] = tt
                ev = pb.op("vector", lambda e, m=m, ps=ps: e.tensor_copy(
                    ytk[:, 128 * m:128 * (m + 1)], ps[:]),
                    waits=[tt, T.get("ytk_rd")], inc="V")
                T[("psY", m % 2)] = ev
                trb.append(ev)
            tlm = pb.op("sync", lambda e, t=t: e.dma_start(out=mk[:], in_=mkv[t]),
                        waits=[T.get("mk_rd")], inc="Xq", inc_by=16)
            a = xa[ti % 2]
            tlx = pb.op("sync", lambda e, t=t, a=a: e.dma_start(out=a[:], in_=xv[t]),
                        waits=[T.get(("xa", ti % 2))], inc="Xq", inc_by=16)
            tmu = pb.op("vector", lambda e: e.tensor_mul(ytk[:], ytk[:], mk[:]),
                        waits=trb + [tlm], inc="V")
            T["mk_rd"] = tmu
            tad = pb.op("vector", lambda e, a=a: e.tensor_add(a[:], a[:], ytk[:]),
                        waits=[tmu, tlx], inc="V")
            T["ytk_rd"] = tad
            tst = pb.op("scalar", lambda e, t=t, a=a: e.dma_start(out=ov[t], in_=a[:]),
                        waits=[tad], inc="St", inc_by=16)
            T[("xa", ti % 2)] = tst
    pb.wait("gpsimd", ("St", pb.cnt["St"]))
    return pb.finalize()



def emit_l1_compose(pb, Wexp, bexp, upw, upb, dnw, Wcon, idf):
    T = pb.tag
    DH = 4096
    beff = pb.sb("beff", [128, 32], F32)

    WexpT_dram = pb.dram("WexpT", [DH, D], F32)
    R1p_dram = pb.dram("R1p", [DH, 1025], F32)
    S1_dram = pb.dram("S1d", [DH, D], F32)
    A_dram = pb.dram("A_dram", [DH, 1025], F32)
    Wc_dram = pb.dram("Wc_dram", [DH, D], F32)

    pb.push_scope()
    wstg = pb.sb("wstg", [128, 1024], F32)
    tstg = pb.sb("tstg", [128, 1024], F32)
    psW = [pb.ps(f"psW{i}", [128, 128], F32) for i in range(2)]
    wev = Wexp.rearrange("(r p) c -> r p c", p=128)
    for cc in range(32):
        ld = []
        for r in range(8):
            tl = pb.op("sync", lambda e, cc=cc, r=r: e.dma_start(
                out=wstg[:, r * 128:(r + 1) * 128], in_=wev[r, :, 128 * cc:128 * (cc + 1)]),
                waits=[T.get("wstg_rd")], inc="Wq", inc_by=16)
            ld.append(tl)
        evs = []
        for r in range(8):
            ps = psW[r % 2]
            tt = pb.op("tensor", lambda e, r=r, ps=ps: e.transpose(
                ps[:], wstg[:, r * 128:(r + 1) * 128], idf[:]),
                waits=ld + [T.get(("psW", r % 2))] + T["id"], inc="T")
            ev = pb.op("vector", lambda e, r=r, ps=ps: e.tensor_copy(
                tstg[:, r * 128:(r + 1) * 128], ps[:]), waits=[tt, T.get("tstg_rd")], inc="V")
            T[("psW", r % 2)] = ev
            evs.append(ev)
            if r == 7:
                T["wstg_rd"] = tt
        tw = pb.op("gpsimd", lambda e, cc=cc: e.dma_start(
            out=WexpT_dram[128 * cc:128 * (cc + 1), :], in_=tstg[:]),
            waits=evs, inc="Ws", inc_by=16)
        T["tstg_rd"] = tw
        T["wexpT_done"] = tw

    C0 = np.zeros((64, DH), np.float32)
    for g in range(64):
        C0[:, g * 64:(g + 1) * 64] = np.eye(64, dtype=np.float32)
    D0 = np.zeros((64, DH), np.float32)
    for r in range(64):
        D0[r, r * 64:(r + 1) * 64] = 1.0
    hC0 = pb.const(C0, "C0"); hD0 = pb.const(D0, "D0")
    X = pb.sb("bfX", [64, DH], F32)
    X2 = pb.sb("bfX2", [64, DH], F32)
    TMP = pb.sb("bfTMP", [64, DH // 2], F32)
    coef = [pb.sb(f"coef{i}", [64, DH // 2], F32) for i in range(4)]
    srow = pb.sb("srow", [1, 8192], F32)
    ones1 = pb.sb("ones1", [1, 64], F32)
    psR = pb.ps("psR", [64, 512], F32)
    t_on = pb.op("vector", lambda e: e.memset(ones1[:], 1.0), inc="V")

    def load_coefs(w_ap, stage):
        tl = pb.op("sync", lambda e: e.dma_start(out=srow[:], in_=w_ap[stage].rearrange(
            "k a b -> (k a b)").rearrange("f -> 1 f") if False else
            w_ap.rearrange("s k a b -> s (k a b)")[stage:stage + 1, :]),
            waits=[T.get("srow_rd")], inc="Wq", inc_by=16)
        tags = []
        for ij in range(4):
            for ch in range(4):
                tm = pb.op("tensor", lambda e, ij=ij, ch=ch: e.matmul(
                    psR[:], ones1[:], srow.rearrange("p (k four) -> p k four", four=4)
                    [:, 512 * ch:512 * (ch + 1), ij], start=True, stop=True),
                    waits=[tl, t_on, T.get("psR_rd")], inc="T")
                ev = pb.op("vector", lambda e, ij=ij, ch=ch: e.tensor_copy(
                    coef[ij][:, 512 * ch:512 * (ch + 1)], psR[:]),
                    waits=[tm, T.get(("coef_rd", ij))], inc="V")
                T["psR_rd"] = ev
                tags.append(ev)
        T["srow_rd"] = tags[-1]
        return tags

    def apply_stage(src, dst, stride, transposed, ctags):
        nb = DH // (2 * stride)
        sA = src.rearrange("p (nb two ss) -> p nb two ss", two=2, ss=stride)
        dA = dst.rearrange("p (nb two ss) -> p nb two ss", two=2, ss=stride)
        cw = [c.rearrange("p (nb ss) -> p nb ss", ss=stride) for c in coef]
        if not transposed:
            iaa, iab, iba, ibb = 0, 1, 2, 3
        else:
            iaa, iab, iba, ibb = 0, 2, 1, 3
        ts_ = []
        w0 = list(ctags) + [T.get("bfly_prev")]
        t1 = pb.op("vector", lambda e: e.tensor_mul(dA[:, :, 0, :], sA[:, :, 0, :],
                   cw[iaa][:]), waits=w0, inc="V")
        t2 = pb.op("vector", lambda e: e.tensor_mul(TMP.rearrange(
            "p (nb ss) -> p nb ss", ss=stride)[:], sA[:, :, 1, :], cw[iab][:]),
            waits=w0 + [T.get("tmp_rd")], inc="V")
        t3 = pb.op("vector", lambda e: e.tensor_add(dA[:, :, 0, :], dA[:, :, 0, :],
                   TMP.rearrange("p (nb ss) -> p nb ss", ss=stride)[:]), waits=[t1, t2], inc="V")
        t4 = pb.op("vector", lambda e: e.tensor_mul(dA[:, :, 1, :], sA[:, :, 0, :],
                   cw[iba][:]), waits=w0, inc="V")
        t5 = pb.op("vector", lambda e: e.tensor_mul(TMP.rearrange(
            "p (nb ss) -> p nb ss", ss=stride)[:], sA[:, :, 1, :], cw[ibb][:]),
            waits=[t3, t4], inc="V")
        T["tmp_rd"] = t5
        t6 = pb.op("vector", lambda e: e.tensor_add(dA[:, :, 1, :], dA[:, :, 1, :],
                   TMP.rearrange("p (nb ss) -> p nb ss", ss=stride)[:]), waits=[t5], inc="V")
        T["bfly_prev"] = t6
        return t6

    def build_compact(dst_name, init_h, w_ap, stages, transposed):
        tl = pb.op("sync", lambda e: e.dma_start(out=X[:], in_=init_h),
                   waits=[T.get("bfly_prev")], inc="Wq", inc_by=16)
        T["bfly_prev"] = tl
        cur, nxt = X, X2
        for st in stages:
            ctags = load_coefs(w_ap, st)
            apply_stage(cur, nxt, 2 ** st, transposed, ctags)
            cur, nxt = nxt, cur
        out = pb.sb(dst_name, [64, DH], F32)
        tc = pb.op("vector", lambda e: e.tensor_copy(out[:], cur[:]),
                   waits=[T["bfly_prev"]], inc="V")
        T["bfly_prev"] = tc
        return out, tc

    C_u, tCu = build_compact("C_u", hC0, upw, range(6), False)
    D_u, tDu = build_compact("D_u", hD0, upw, range(6, 12), False)
    DtT, tDt = build_compact("DtT", hD0, dnw, range(11, 5, -1), True)
    CTd, tCt = build_compact("CTd", hC0, dnw, range(5, -1, -1), True)
    cdone = [tCu, tDu, tDt, tCt]

    pack = [pb.sb(f"pack{i}", [128, 128], F32) for i in range(2)]
    movb = [pb.sb(f"movb{i}", [128, 1025], F32) for i in range(2)]
    evb = [pb.sb(f"evb{i}", [128, 1025], F32) for i in range(2)]
    upcol = pb.sb("upcol", [128, 1], F32)
    psC = [pb.ps(f"psC{i}", [128, 512], F32) for i in range(2)]
    psB = pb.ps("psB", [128, 1], F32)
    tz = [pb.op("vector", lambda e, i=i: e.memset(pack[i][:], 0.0), inc="V") for i in range(2)]

    def pair_loop(nm, stat_src, stat_strided, mov_fn, nmov, evac_fn):
        for p2 in range(32):
            pk = pack[p2 % 2]
            def cpy(e, p2=p2, pk=pk):
                if stat_strided:
                    e.tensor_copy(pk[0:64, 0:64], stat_src[:, (2 * p2)::64])
                    return e.tensor_copy(pk[64:128, 64:128], stat_src[:, (2 * p2 + 1)::64])
                else:
                    e.tensor_copy(pk[0:64, 0:64], stat_src[:, (2 * p2) * 64:(2 * p2 + 1) * 64])
                    return e.tensor_copy(pk[64:128, 64:128],
                                         stat_src[:, (2 * p2 + 1) * 64:(2 * p2 + 2) * 64])
            tp = pb.op("vector", cpy, waits=cdone + tz + [T.get((nm, "pk", p2 % 2))], inc="V")
            mb = movb[p2 % 2]
            tm_ld = mov_fn(p2, mb)
            mms = []
            for ch in range((nmov + 511) // 512):
                n0 = 512 * ch
                n1 = min(nmov, n0 + 512)
                ps = psC[ch % 2] if n1 - n0 > 1 else psB
                tmm = pb.op("tensor", lambda e, ch=ch, n0=n0, n1=n1, ps=ps, pk=pk, mb=mb:
                            e.matmul(ps[:, 0:n1 - n0], pk[:], mb[:, n0:n1],
                                     start=True, stop=True),
                            waits=[tp, tm_ld, T.get((nm, "ps", ch))], inc="T")
                mms.append((tmm, ch, n0, n1))
            T[(nm, "pk", p2 % 2)] = mms[-1][0]
            eb = evb[p2 % 2]
            evs = []
            for (tmm, ch, n0, n1) in mms:
                ps = psC[ch % 2] if n1 - n0 > 1 else psB
                ev = pb.op("vector", lambda e, ch=ch, n0=n0, n1=n1, ps=ps, eb=eb:
                           e.tensor_copy(eb[:, n0:n1], ps[:, 0:n1 - n0]),
                           waits=[tmm, T.get((nm, "eb", p2 % 2))], inc="V")
                T[(nm, "ps", ch)] = ev
                evs.append(ev)
            tout = evac_fn(p2, eb, evs)
            T[(nm, "eb", p2 % 2)] = tout
    bev = bexp.rearrange("(a p) -> a p", p=128)
    def movR1(p2, mb):
        t1 = pb.op("sync", lambda e, p2=p2: e.dma_start(
            out=mb[:, 0:1024], in_=WexpT_dram[128 * p2:128 * (p2 + 1), :]),
            waits=[T["wexpT_done"], T.get(("R1", "mb_rd", p2 % 2))], inc="Wq", inc_by=16)
        t2 = pb.op("sync", lambda e, p2=p2: e.dma_start(
            out=mb[:, 1024:1025].rearrange("p c -> (p c)") if False else mb[:, 1024:1025],
            in_=bev[p2].unsqueeze(1)), waits=[t1], inc="Wq", inc_by=16)
        return t2
    def evR1(p2, eb, evs):
        r1v = R1p_dram.rearrange("(j g) c -> g j c", g=64)
        t1 = pb.op("gpsimd", lambda e, p2=p2, eb=eb: e.dma_start(
            out=r1v[2 * p2], in_=eb[0:64, :]), waits=evs, inc="Ws", inc_by=16)
        t2 = pb.op("gpsimd", lambda e, p2=p2, eb=eb: e.dma_start(
            out=r1v[2 * p2 + 1], in_=eb[64:128, :]), waits=evs, inc="Ws", inc_by=16)
        T[("R1", "mb_rd", p2 % 2)] = t2
        T["r1_done"] = t2
        return t2
    pair_loop("R1", C_u, False, movR1, 1025, evR1)
    ubv = upb.rearrange("(hi lo) -> lo hi", lo=64)
    def movR2(p2, mb):
        return pb.op("sync", lambda e, p2=p2, mb=mb: e.dma_start(
            out=mb[:], in_=R1p_dram[128 * p2:128 * (p2 + 1), :]),
            waits=[T["r1_done"], T.get(("R2", "mb_rd", p2 % 2))], inc="Wq", inc_by=16)
    def evR2(p2, eb, evs):
        tl0 = pb.op("sync", lambda e, p2=p2: e.dma_start(
            out=upcol[0:64, 0:1], in_=ubv[2 * p2].unsqueeze(1),
            allow_slow_non_contiguous=True),
            waits=[T.get("upcol_rd")], inc="Wq", inc_by=16)
        tl = pb.op("sync", lambda e, p2=p2: e.dma_start(
            out=upcol[64:128, 0:1], in_=ubv[2 * p2 + 1].unsqueeze(1),
            allow_slow_non_contiguous=True),
            waits=[tl0], inc="Wq", inc_by=16)
        tb = pb.op("vector", lambda e, p2=p2, eb=eb: e.tensor_add(
            beff[:, p2:p2 + 1], eb[:, 1024:1025], upcol[:]), waits=evs + [tl], inc="V")
        T["upcol_rd"] = tb
        t1 = pb.op("gpsimd", lambda e, p2=p2, eb=eb: e.dma_start(
            out=A_dram[128 * p2:128 * (p2 + 1), :], in_=eb[:]), waits=evs, inc="Ws", inc_by=16)
        T[("R2", "mb_rd", p2 % 2)] = t1
        T["a_done"] = t1
        pb.tag["beff_done"] = tb
        return t1
    pair_loop("R2", D_u, True, movR2, 1025, evR2)
    wcv = Wcon.rearrange("(hi lo) d -> lo hi d", lo=64)
    def movS1(p2, mb):
        ta = pb.op("sync", lambda e, p2=p2, mb=mb: e.dma_start(
            out=mb[0:64, 0:1024], in_=wcv[2 * p2]),
            waits=[T.get(("S1", "mb_rd", p2 % 2))], inc="Wq", inc_by=16)
        return pb.op("sync", lambda e, p2=p2, mb=mb: e.dma_start(
            out=mb[64:128, 0:1024], in_=wcv[2 * p2 + 1]),
            waits=[ta], inc="Wq", inc_by=16)
    def evS1(p2, eb, evs):
        s1v = S1_dram.rearrange("(hi lo) d -> lo hi d", lo=64)
        t1 = pb.op("gpsimd", lambda e, p2=p2, eb=eb: e.dma_start(
            out=s1v[2 * p2], in_=eb[0:64, 0:1024]), waits=evs, inc="Ws", inc_by=16)
        t2 = pb.op("gpsimd", lambda e, p2=p2, eb=eb: e.dma_start(
            out=s1v[2 * p2 + 1], in_=eb[64:128, 0:1024]), waits=evs, inc="Ws", inc_by=16)
        T[("S1", "mb_rd", p2 % 2)] = t2
        T["s1_done"] = t2
        return t2
    pair_loop("S1", DtT, True, movS1, 1024, evS1)
    def movS2(p2, mb):
        return pb.op("sync", lambda e, p2=p2, mb=mb: e.dma_start(
            out=mb[:, 0:1024], in_=S1_dram[128 * p2:128 * (p2 + 1), :]),
            waits=[T["s1_done"], T.get(("S2", "mb_rd", p2 % 2))], inc="Wq", inc_by=16)
    def evS2(p2, eb, evs):
        wv2 = Wc_dram.rearrange("(j hi) d -> hi j d", hi=64)
        t1 = pb.op("gpsimd", lambda e, p2=p2, eb=eb: e.dma_start(
            out=wv2[2 * p2], in_=eb[0:64, 0:1024]), waits=evs, inc="Ws", inc_by=16)
        t2 = pb.op("gpsimd", lambda e, p2=p2, eb=eb: e.dma_start(
            out=wv2[2 * p2 + 1], in_=eb[64:128, 0:1024]), waits=evs, inc="Ws", inc_by=16)
        T[("S2", "mb_rd", p2 % 2)] = t2
        T["wc_done"] = t2
        return t2
    pair_loop("S2", CTd, False, movS2, 1024, evS2)
    done = [T["a_done"], T["wc_done"], T["beff_done"]]
    pb.pop_scope(gate=done)
    return A_dram, Wc_dram, beff, done


def build_l1ffn(R=2048):
    pb = PB()
    xin = pb.dram("xin", [R, D], F32, kind="ExternalInput")
    Wexp = pb.dram("W_exp", [D, 4096], F32, kind="ExternalInput")
    bexp = pb.dram("b_exp", [4096], F32, kind="ExternalInput")
    upw = pb.dram("up_w", [12, 2048, 2, 2], F32, kind="ExternalInput")
    upb = pb.dram("up_b", [4096], F32, kind="ExternalInput")
    dnw = pb.dram("dn_w", [12, 2048, 2, 2], F32, kind="ExternalInput")
    Wcon = pb.dram("W_con", [4096, D], F32, kind="ExternalInput")
    bcon = pb.dram("b_con", [D], F32, kind="ExternalInput")
    dnb = pb.dram("dn_b", [D], F32, kind="ExternalInput")
    xout = pb.dram("xout", [R, D], F32, kind="ExternalOutput")

    idf, idb = emit_identity(pb)
    alloc_ln(pb)
    A_dram, Wc_dram, beff, cps_done = emit_l1_compose(pb, Wexp, bexp, upw, upb, dnw, Wcon, idf)

    T = pb.tag
    AT = pb.sb("AT", [128, 8 * 4096], BF16)
    WcT = pb.sb("WcT", [128, 32 * 1024], BF16)
    psT = [pb.ps(f"psT{i}", [128, 128], BF16) for i in range(2)]
    psM = [pb.ps(f"psM{i}", [128, 512], F32) for i in range(2)]
    psY = [pb.ps(f"psY{i}", [128, 128], F32) for i in range(2)]
    psYb = [pb.ps(f"psYb{i}", [128, 128], BF16) for i in range(2)]
    psA = psY
    bout = pb.sb("bout", [128, 8], F32)
    pb.push_scope()
    astg = pb.sb("astg", [128, 1024], F32)
    for mblk in range(32):
        tl = pb.op("sync", lambda e, mblk=mblk: e.dma_start(
            out=astg[:], in_=A_dram[128 * mblk:128 * (mblk + 1), 0:1024]),
            waits=cps_done + [T.get("astg_rd")], inc="Wq", inc_by=16)
        for rr in range(8):
            ps = psA[rr % 2]
            tt = pb.op("tensor", lambda e, rr=rr, ps=ps: e.transpose(
                ps[:], astg[:, 128 * rr:128 * (rr + 1)], idf[:]),
                waits=[tl, T.get(("psA", rr % 2)), T.get(("psY", rr % 2))] + T["id"], inc="T")
            ev = pb.op("vector", lambda e, rr=rr, mblk=mblk, ps=ps: e.tensor_copy(
                AT[:, rr * 4096 + 128 * mblk: rr * 4096 + 128 * (mblk + 1)], ps[:]),
                waits=[tt], inc="V")
            T[("psA", rr % 2)] = ev
            if rr == 7:
                T["astg_rd"] = tt
            T["at_done"] = ev
    wcstg = pb.sb("wcstg", [128, 1024], F32)
    for k in range(32):
        tl = pb.op("sync", lambda e, k=k: e.dma_start(
            out=wcstg[:], in_=Wc_dram[128 * k:128 * (k + 1), :]),
            waits=cps_done + [T.get("wcstg_rd")], inc="Wq", inc_by=16)
        tc = pb.op("vector", lambda e, k=k: e.tensor_scalar(
            WcT[:, k * 1024:(k + 1) * 1024], wcstg[:], 0.5, None, ALU.mult, ALU.bypass),
            waits=[tl], inc="V")
        T["wcstg_rd"] = tc
        T["wc_res"] = tc
    pb.pop_scope() if False else None
    bstg1 = pb.sb("bstg1", [8, 128], F32)
    bstg2 = pb.sb("bstg2", [8, 128], F32)

    t1 = pb.op("sync", lambda e: e.dma_start(out=bstg1[:], in_=bcon.rearrange(
        "(a p) -> a p", p=128)), inc="Wq", inc_by=16)
    t2 = pb.op("sync", lambda e: e.dma_start(out=bstg2[:], in_=dnb.rearrange(
        "(a p) -> a p", p=128)), inc="Wq", inc_by=16)
    t3 = pb.op("vector", lambda e: e.tensor_add(bstg1[:], bstg1[:], bstg2[:]),
               waits=[t1, t2], inc="V")
    t4 = pb.op("tensor", lambda e: e.transpose(psY[0][:, 0:8], bstg1[:], idf[0:8, 0:8]),
               waits=[t3, T.get(("psA", 0))] + T["id"], inc="T")
    tbo = pb.op("vector", lambda e: e.tensor_scalar(
        bout[:], psY[0][:, 0:8], 0.5, None, ALU.mult, ALU.bypass), waits=[t4], inc="V")
    T[("psY", 0)] = tbo
    pb.pop_scope()

    xa = [pb.sb(f"xa{i}", [128, D], F32) for i in range(2)]
    xqb = pb.sb("xqb", [128, D], BF16)
    xfm = pb.sb("xfm", [128, 8 * 512], BF16)
    hfull = pb.sb("hfull", [128, 32 * 512], BF16)
    yfm = pb.sb("yfm", [128, 8 * 512], BF16)
    ytk = pb.sb("ytk", [128, D], BF16)

    xv = xin.rearrange("(t p) d -> t p d", p=128)
    ov = xout.rearrange("(t p) d -> t p d", p=128)

    for n in range(R // 512):
        tr_tags = []
        for ti in range(4):
            t = 4 * n + ti
            a = xa[ti % 2]
            tl = pb.op("sync", lambda e, t=t, a=a: e.dma_start(out=a[:], in_=xv[t]),
                       waits=[T.get(("xa", ti % 2))], inc="Xq", inc_by=16)
            tln = emit_ln_bf16(pb, a, xqb, [tl, T.get("xqb_rd")])
            T[("xa", ti % 2)] = tln
            for f in range(8):
                ps = psT[f % 2]
                tt = pb.op("tensor", lambda e, f=f, ps=ps: e.transpose(
                    ps[:], xqb[:, 128 * f:128 * (f + 1)], idb[:]),
                    waits=[tln, T.get(("psT", f % 2))] + T["id"], inc="T")
                ev = pb.op("vector", lambda e, f=f, ti=ti, ps=ps: e.tensor_copy(
                    xfm[:, f * 512 + ti * 128: f * 512 + ti * 128 + 128], ps[:]),
                    waits=[tt], inc="V")
                T[("psT", f % 2)] = ev
                tr_tags.append(ev)
                if f == 7:
                    T["xqb_rd"] = tt
        for m in range(32):
            ps = psM[m % 2]
            def mm(e, m=m, ps=ps):
                inst = None
                for k in range(8):
                    inst = e.matmul(ps[:], AT[:, k * 4096 + 128 * m: k * 4096 + 128 * (m + 1)],
                                    xfm[:, k * 512:(k + 1) * 512],
                                    start=(k == 0), stop=(k == 7))
                return inst
            tm = pb.op("tensor", mm, waits=tr_tags + [T["at_done"], T.get(("psM", m % 2))],
                       inc="T")
            te = pb.op("scalar", lambda e, m=m, ps=ps: e.activation(
                hfull[:, m * 512:(m + 1) * 512], ps[:], AF.Gelu, bias=beff[:, m:m + 1]),
                waits=[tm, cps_done[2], T.get("hf_guard")], inc="S")
            T[("psM", m % 2)] = te
            T["h_done"] = te
        for m in range(8):
            ps = psM[m % 2]
            def mmy(e, m=m, ps=ps):
                inst = None
                for k in range(32):
                    inst = e.matmul(ps[:], WcT[:, k * 1024 + 128 * m: k * 1024 + 128 * (m + 1)],
                                    hfull[:, k * 512:(k + 1) * 512],
                                    start=(k == 0), stop=(k == 31))
                return inst
            tm = pb.op("tensor", mmy, waits=[T["h_done"], T["wc_res"], T.get(("psM", m % 2))],
                       inc="T")
            if m == 7:
                T["hf_guard"] = tm
            te = pb.op("scalar", lambda e, m=m, ps=ps: e.activation(
                yfm[:, m * 512:(m + 1) * 512], ps[:], AF.Identity, bias=bout[:, m:m + 1]),
                waits=[tm, tbo, T.get(("yfm_rd", m))], inc="S")
            T[("psM", m % 2)] = te
            T[("yfm", m)] = te
        for ti in range(4):
            t = 4 * n + ti
            trb = []
            for m in range(8):
                ps = psYb[m % 2]
                tt = pb.op("tensor", lambda e, m=m, ti=ti, ps=ps: e.transpose(
                    ps[:], yfm[:, m * 512 + ti * 128: m * 512 + ti * 128 + 128], idb[:]),
                    waits=[T[("yfm", m)], T.get(("psYb", m % 2))] + T["id"], inc="T")
                if ti == 3:
                    T[("yfm_rd", m)] = tt
                ev = pb.op("vector", lambda e, m=m, ps=ps: e.tensor_copy(
                    ytk[:, 128 * m:128 * (m + 1)], ps[:]),
                    waits=[tt, T.get("ytk_rd")], inc="V")
                T[("psYb", m % 2)] = ev
                trb.append(ev)
            a = xa[ti % 2]
            tlx = pb.op("sync", lambda e, t=t, a=a: e.dma_start(out=a[:], in_=xv[t]),
                        waits=[T.get(("xa", ti % 2))], inc="Xq", inc_by=16)
            tad = pb.op("vector", lambda e, a=a: e.tensor_add(a[:], a[:], ytk[:]),
                        waits=trb + [tlx], inc="V")
            T["ytk_rd"] = tad
            tst = pb.op("scalar", lambda e, t=t, a=a: e.dma_start(out=ov[t], in_=a[:]),
                        waits=[tad], inc="St", inc_by=16)
            T[("xa", ti % 2)] = tst
    pb.wait("gpsimd", ("St", pb.cnt["St"]))
    return pb.finalize()


def emit_ln_bf16(pb, xa, out_bf, waits):
    return emit_ln(pb, xa, out_bf, waits)



def alloc_fp4(pb, W=4096, need_p=True):
    pb.q_W = W
    pb.q_i32 = pb.sb("q_i32", [128, W], I32)
    pb.q_f32 = pb.sb("q_f32", [128, W], F32)
    pb.q_p = pb.sb("q_p", [128, W], F32) if need_p else None
    pb.q_pi = pb.sb("q_pi", [128, W], F32) if need_p else None
    pb.q_bm = pb.sb("q_bm", [128, W // 16], F32)
    pb.q_sc = pb.sb("q_sc", [128, W // 16], F32)
    pb.q_si = pb.sb("q_si", [128, W // 16], F32)
    pb.q_sh = pb.sb("q_sh", [128, W // 16], F32)
    pb.tag["q_busy"] = None


def emit_fp4_rows(pb, src, dst, W, waits, out_factor=0.5):
    nb = W // 16
    i32, f32a, fp_, fpi = pb.q_i32, pb.q_f32, pb.q_p, pb.q_pi
    bm, sc, si, sh = pb.q_bm, pb.q_sc, pb.q_si, pb.q_sh
    v3 = lambda t: t[:, 0:W].rearrange("p (nb b) -> p nb b", b=16)
    vb = lambda t: t[:, 0:nb].unsqueeze(2).broadcast_to([128, nb, 16])
    w0 = [x for x in waits] + [pb.tag.get("q_busy")]
    t1 = pb.op("vector", lambda e: e.tensor_reduce(
        bm[:, 0:nb].unsqueeze(2), v3(src), AX.X, ALU.max, apply_absolute_value=True),
        waits=w0, inc="V")
    t2 = pb.op("vector", lambda e: e.tensor_scalar(
        sc[:, 0:nb], bm[:, 0:nb], float(1 / 6.0), float(1e-12), ALU.mult, ALU.max),
        waits=[t1], inc="V")
    t3 = pb.op("vector", lambda e: e.reciprocal(si[:, 0:nb], sc[:, 0:nb]), waits=[t2], inc="V")
    t4 = pb.op("vector", lambda e: e.tensor_mul(v3(f32a), v3(src), vb(si)), waits=[t3] + w0, inc="V")
    t5 = pb.op("vector", lambda e: e.tensor_scalar(
        i32[:, 0:W], f32a[:, 0:W].bitcast(I32), 0x7F800000, None, ALU.bitwise_and, ALU.bypass),
        waits=[t4], inc="V")
    t6 = pb.op("vector", lambda e: e.tensor_scalar(
        fp_[:, 0:W], i32[:, 0:W].bitcast(F32), 1.0, 4.0,
        ALU.max, ALU.min), waits=[t5], inc="V")
    t7 = pb.op("vector", lambda e: e.reciprocal(fpi[:, 0:W], fp_[:, 0:W]), waits=[t6], inc="V")
    t8 = pb.op("vector", lambda e: e.scalar_tensor_tensor(
        f32a[:, 0:W], f32a[:, 0:W], 2.0, fpi[:, 0:W], ALU.mult, ALU.mult),
        waits=[t7], inc="V")
    t9 = pb.op("vector", lambda e: e.tensor_copy(i32[:, 0:W], f32a[:, 0:W]), waits=[t8], inc="V")
    t10 = pb.op("vector", lambda e: e.tensor_copy(f32a[:, 0:W], i32[:, 0:W]), waits=[t9], inc="V")
    t11 = pb.op("vector", lambda e: e.tensor_mul(f32a[:, 0:W], f32a[:, 0:W], fp_[:, 0:W]),
                waits=[t10], inc="V")
    t12 = pb.op("vector", lambda e: e.tensor_scalar(
        sh[:, 0:nb], sc[:, 0:nb], float(out_factor), None, ALU.mult, ALU.bypass),
        waits=[t2], inc="V")
    t13 = pb.op("vector", lambda e: e.tensor_mul(v3(dst), v3(f32a), vb(sh)),
                waits=[t11, t12], inc="V")
    pb.tag["q_busy"] = t13
    return t13


def build_l3ffn(R=2048):
    pb = PB()
    NI = 2816
    NKT = NI // 128
    xin = pb.dram("xin", [R, D], F32, kind="ExternalInput")
    wg = pb.dram("w_gate", [NI, D], F32, kind="ExternalInput")
    wu = pb.dram("w_up", [NI, D], F32, kind="ExternalInput")
    wd = pb.dram("w_down", [D, NI], F32, kind="ExternalInput")
    xout = pb.dram("xout", [R, D], F32, kind="ExternalOutput")
    hq_dram = pb.dram("hq_dram", [NI, R], BF16)

    idf, idb = emit_identity(pb)
    alloc_ln(pb)
    alloc_fp4(pb, 2816)
    alloc_wstage(pb, 2816, f8=False, tst=None)
    T = pb.tag
    qfn = lambda pb, s32, sbf, W, tags: emit_fp4_rows(pb, s32, sbf, W, tags, out_factor=0.5)
    qfn_half = lambda pb, s32, sbf, W, tags: emit_fp4_rows(pb, s32, sbf, W, tags, out_factor=0.25)

    chunks = [(0, 512), (512, 1024), (1024, 1536), (1536, 2048), (2048, 2560), (2560, 2816)]

    pb.push_scope()
    wgT, wgrdy = emit_weight_T(pb, wg, NI, D, "wg", quant=qfn, dst_dtype=BF16)
    wuT, wurdy = emit_weight_T(pb, wu, NI, D, "wu", quant=qfn, dst_dtype=BF16)
    xa = [pb.sb(f"xa{i}", [128, D], F32) for i in range(2)]
    xn = pb.sb("xn", [128, D], F32)
    xqb = pb.sb("xqb", [128, D], BF16)
    xfm = pb.sb("xfm", [128, 8 * 128], BF16)
    psT = [pb.ps(f"psT{i}", [128, 128], BF16) for i in range(2)]
    psM = [pb.ps(f"psM{i}", [128, 512], F32) for i in range(2)]
    psB = [pb.ps(f"psB{i}", [128, 128], BF16) for i in range(2)]
    sgtk = pb.sb("sgtk", [128, NI], F32)
    utk = pb.sb("utk", [128, NI], F32)
    hqtk = pb.sb("hqtk", [128, NI], BF16)
    hqf = pb.sb("hqf", [128, NKT * 128], BF16)
    xv = xin.rearrange("(t p) d -> t p d", p=128)
    ov = xout.rearrange("(t p) d -> t p d", p=128)
    for t in range(R // 128):
        a = xa[t % 2]
        tl = pb.op("sync", lambda e, t=t, a=a: e.dma_start(out=a[:], in_=xv[t]),
                   waits=[T.get(("xa", t % 2))], inc="Xq", inc_by=16)
        tln = emit_ln(pb, a, xn, [tl, T.get("xn_rd")])
        T[("xa", t % 2)] = tln
        tq = emit_fp4_rows(pb, xn, xqb, D, [tln, T.get("xqb_rd")], out_factor=0.5)
        T["xn_rd"] = tq
        tr_tags = []
        for f in range(8):
            ps = psT[f % 2]
            tt = pb.op("tensor", lambda e, f=f, ps=ps: e.transpose(
                ps[:], xqb[:, 128 * f:128 * (f + 1)], idb[:]),
                waits=[tq, T.get(("psT", f % 2))] + T["id"], inc="T")
            ev = pb.op("vector", lambda e, f=f, ps=ps: e.tensor_copy(
                xfm[:, f * 128:(f + 1) * 128], ps[:]),
                waits=[tt, T.get("xfm_rd")], inc="V")
            T[("psT", f % 2)] = ev
            tr_tags.append(ev)
            if f == 7:
                T["xqb_rd"] = tt
        ev_tags = []
        for ci, (c0, c1) in enumerate(chunks):
            for which, wT, wrdy, dstt in (("g", wgT, wgrdy, sgtk), ("u", wuT, wurdy, utk)):
                ps = psM[0] if which == "g" else psM[1]
                def mmg(e, c0=c0, c1=c1, ps=ps, wT=wT):
                    inst = None
                    for k in range(8):
                        inst = e.matmul(ps[:, 0:c1 - c0], xfm[:, k * 128:(k + 1) * 128],
                                        wT[:, k * NI + c0:k * NI + c1],
                                        start=(k == 0), stop=(k == 7))
                    return inst
                tm = pb.op("tensor", mmg,
                           waits=tr_tags + [wrdy, T.get(("psM", which, ci % 1))], inc="T")
                if which == "g":
                    te = pb.op("scalar", lambda e, c0=c0, c1=c1, ps=ps: e.activation(
                        sgtk[:, c0:c1], ps[:, 0:c1 - c0], AF.Silu),
                        waits=[tm, T.get("sg_rd")], inc="S")
                else:
                    te = pb.op("scalar", lambda e, c0=c0, c1=c1, ps=ps: e.copy(
                        utk[:, c0:c1], ps[:, 0:c1 - c0]),
                        waits=[tm, T.get("sg_rd")], inc="S")
                T[("psM", which, 0)] = te
                ev_tags.append(te)
        th = pb.op("vector", lambda e: e.tensor_mul(sgtk[:], sgtk[:], utk[:]),
                   waits=ev_tags, inc="V")
        thq = emit_fp4_rows(pb, sgtk, hqtk, NI, [th], out_factor=0.5)
        T["sg_rd"] = thq
        sp_tags = []
        for f in range(NKT):
            ps = psB[f % 2]
            tt = pb.op("tensor", lambda e, f=f, ps=ps: e.transpose(
                ps[:], hqtk[:, 128 * f:128 * (f + 1)], idb[:]),
                waits=[thq, T.get(("psB", f % 2))] + T["id"], inc="T")
            ev = pb.op("vector", lambda e, f=f, t=t, ps=ps: e.tensor_copy(
                hqf[:, f * 128:(f + 1) * 128], ps[:]),
                waits=[tt, T.get("hqf_rd")], inc="V")
            T[("psB", f % 2)] = ev
            sp_tags.append(ev)
        tsp = None
        for f in range(NKT):
            tsp = pb.op("gpsimd", lambda e, t=t, f=f: e.dma_start(
                out=hq_dram[128 * f:128 * (f + 1), 128 * t:128 * (t + 1)],
                in_=hqf[:, f * 128:(f + 1) * 128]),
                waits=sp_tags, inc="Hs", inc_by=16)
        T["hqf_rd"] = tsp
        T["hq_done"] = tsp
    pb.pop_scope(gate=[T["hq_done"]])

    wdT, wdrdy = emit_weight_T(pb, wd, D, NI, "wd", quant=qfn_half, dst_dtype=BF16)
    xa2 = [pb.sb(f"xb{i}", [128, D], F32) for i in range(2)]
    hld = [pb.sb(f"hld{i}", [128, 512], BF16) for i in range(2)]
    psM2 = [pb.ps(f"ps2{i}", [128, 512], F32) for i in range(2)]
    psY = [pb.ps(f"psY{i}", [128, 128], BF16) for i in range(2)]
    yfm = pb.sb("yfm", [128, 8 * 512], BF16)
    ytk = pb.sb("ytk", [128, D], BF16)
    for n in range(R // 512):
        ld_tags = []
        for m in range(8):
            ps = psM2[m % 2]
            mm_tag = None
            for k in range(NKT):
                hb = hld[k % 2]
                tl = pb.op("sync", lambda e, k=k, n=n, hb=hb: e.dma_start(
                    out=hb[:], in_=hq_dram[128 * k:128 * (k + 1), 512 * n:512 * (n + 1)]),
                    waits=[T["hq_done"], T.get(("hld", k % 2))], inc="Xq", inc_by=16)
                mm_tag = pb.op("tensor", lambda e, k=k, m=m, ps=ps, hb=hb: e.matmul(
                    ps[:], wdT[:, k * D + 128 * m:k * D + 128 * (m + 1)], hb[:],
                    start=(k == 0), stop=(k == NKT - 1)),
                    waits=[tl, wdrdy, T.get(("psM2", m % 2))], inc="T")
                T[("hld", k % 2)] = mm_tag
            te = pb.op("scalar", lambda e, m=m, ps=ps: e.copy(
                yfm[:, m * 512:(m + 1) * 512], ps[:]),
                waits=[mm_tag, T.get(("yfm_rd", m))], inc="S")
            T[("psM2", m % 2)] = te
            T[("yfm", m)] = te
        for ti in range(4):
            t = 4 * n + ti
            trb = []
            for m in range(8):
                ps = psY[m % 2]
                tt = pb.op("tensor", lambda e, m=m, ti=ti, ps=ps: e.transpose(
                    ps[:], yfm[:, m * 512 + ti * 128: m * 512 + ti * 128 + 128], idb[:]),
                    waits=[T[("yfm", m)], T.get(("psY", m % 2))] + T["id"], inc="T")
                if ti == 3:
                    T[("yfm_rd", m)] = tt
                ev = pb.op("vector", lambda e, m=m, ps=ps: e.tensor_copy(
                    ytk[:, 128 * m:128 * (m + 1)], ps[:]),
                    waits=[tt, T.get("ytk_rd")], inc="V")
                T[("psY", m % 2)] = ev
                trb.append(ev)
            a = xa2[ti % 2]
            tlx = pb.op("sync", lambda e, t=t, a=a: e.dma_start(out=a[:], in_=xv[t]),
                        waits=[T.get(("xa2", ti % 2))], inc="Xq", inc_by=16)
            tad = pb.op("vector", lambda e, a=a: e.tensor_add(a[:], a[:], ytk[:]),
                        waits=trb + [tlx], inc="V")
            T["ytk_rd"] = tad
            tst = pb.op("scalar", lambda e, t=t, a=a: e.dma_start(out=ov[t], in_=a[:]),
                        waits=[tad], inc="St", inc_by=16)
            T[("xa2", ti % 2)] = tst
    pb.wait("gpsimd", ("St", pb.cnt["St"]))
    return pb.finalize()



def emit_lpbq_rows(pb, src, dst, W, waits, fold=None):
    nb = W // 32
    i32, f32a = pb.q_i32, pb.q_f32
    bm, sc, si, sh = pb.q_bm, pb.q_sc, pb.q_si, pb.q_sh
    v3 = lambda t: t[:, 0:W].rearrange("p (nb b) -> p nb b", b=32)
    vb = lambda t: t[:, 0:nb].unsqueeze(2).broadcast_to([128, nb, 32])
    w0 = [x for x in waits] + [pb.tag.get("q_busy")]
    t1 = pb.op("vector", lambda e: e.tensor_reduce(
        bm[:, 0:nb].unsqueeze(2), v3(src), AX.X, ALU.max, apply_absolute_value=True),
        waits=w0, inc="V")
    t2 = pb.op("vector", lambda e: e.tensor_scalar(
        sc[:, 0:nb], bm[:, 0:nb], float(1 / 7.0), 1e-12, ALU.mult, ALU.max),
        waits=[t1], inc="V")
    t3 = pb.op("vector", lambda e: e.reciprocal(si[:, 0:nb], sc[:, 0:nb]), waits=[t2], inc="V")
    t4 = pb.op("vector", lambda e: e.tensor_mul(v3(f32a), v3(src), vb(si)), waits=[t3] + w0, inc="V")
    t5 = pb.op("vector", lambda e: e.tensor_copy(i32[:, 0:W], f32a[:, 0:W]), waits=[t4], inc="V")
    t6 = pb.op("vector", lambda e: e.tensor_copy(f32a[:, 0:W], i32[:, 0:W]), waits=[t5], inc="V")
    if fold is not None:
        t6b = pb.op("vector", lambda e: e.tensor_scalar(
            sh[:, 0:nb], sc[:, 0:nb], float(fold), None, ALU.mult, ALU.bypass),
            waits=[t2], inc="V")
        svec = sh
    else:
        t6b = t2
        svec = sc
    t7 = pb.op("vector", lambda e: e.tensor_mul(v3(dst), v3(f32a), vb(svec)),
               waits=[t6, t6b], inc="V")
    pb.tag["q_busy"] = t7
    return t7


def emit_global_absmax(pb, col, name, idf, gmax_out):
    T = pb.tag
    if not hasattr(pb, "gm_ps"):
        pb.gm_ps = pb.ps("gm_ps", [1, 128], F32)
        pb.gm_row = pb.sb("gm_row", [1, 128], F32)
        pb.gm_st = pb.sb("gm_st", [1, 16], F32)
    psx = pb.gm_ps
    row = pb.gm_row
    cin = pb.dram(f"gm_in_{name}", [1, 16], F32)
    cout = pb.dram(f"gm_out_{name}", [1, 16], F32)
    t1 = pb.op("tensor", lambda e: e.transpose(psx[:], col[:], idf[:]),
               waits=[T[("gmcol", name)], T.get("gm_busy")] + T["id"], inc="T")
    t2 = pb.op("vector", lambda e: e.tensor_copy(row[:], psx[:]), waits=[t1], inc="V")
    t3 = pb.op("vector", lambda e: e.tensor_reduce(
        gmax_out[:], row[:], AX.X, ALU.max), waits=[t2], inc="V")
    t3b = pb.op("vector", lambda e: e.tensor_copy(
        pb.gm_st[:], gmax_out[0:1, 0:1].broadcast_to([1, 16])),
        waits=[t3, T.get("gm_st_rd")], inc="V")
    t4 = pb.op("gpsimd", lambda e: e.dma_start(out=cin, in_=pb.gm_st[:]),
               waits=[t3b], inc="Hs", inc_by=16)
    T["gm_st_rd"] = t4
    t5 = pb.op("gpsimd", lambda e: e.collective_compute(
        "AllReduce", ALU.max, [[0, 1, 2, 3, 4, 5, 6, 7]], ins=[cin.opt()], outs=[cout.opt()]),
        waits=[t4], inc="CC")
    t6 = pb.op("gpsimd", lambda e: e.dma_start(out=gmax_out[:], in_=cout[:, 0:1]),
               waits=[t5], inc="Hs", inc_by=16)
    T["gm_busy"] = t2
    return t6


def emit_scale_cols(pb, gmax, s_col, si_col, ones1, waits, psrep):
    t1 = pb.op("vector", lambda e: e.tensor_scalar(
        gmax[0:1, 0:1], gmax[0:1, 0:1], float(1 / 32767.0), 1e-12, ALU.mult, ALU.max),
        waits=waits, inc="V")
    tsi = pb.op("vector", lambda e: e.reciprocal(gmax[0:1, 1:2], gmax[0:1, 0:1]),
                waits=[t1], inc="V")
    tm = pb.op("tensor", lambda e: e.matmul(psrep[:, 0:2], ones1[:], gmax[0:1, 0:2],
               start=True, stop=True), waits=[tsi, pb.tag.get("psrep_rd")], inc="T")
    t2 = pb.op("vector", lambda e: e.tensor_copy(s_col[:], psrep[:, 0:1]), waits=[tm], inc="V")
    t3 = pb.op("vector", lambda e: e.tensor_copy(si_col[:], psrep[:, 1:2]), waits=[tm], inc="V")
    pb.tag["psrep_rd"] = t3
    return t3


def emit_qdq4(pb, src, dst, si_col, s_col, tmp32, i32t, waits, n=512):
    waits = list(waits) + [pb.tag.get(("qdq_busy", id(tmp32)))]
    t1 = pb.op("vector", lambda e: e.tensor_scalar(
        tmp32[:, 0:n], src, si_col[:, 0:1], None, ALU.mult, ALU.bypass), waits=waits, inc="V")
    t2 = pb.op("vector", lambda e: e.tensor_copy(i32t[:, 0:n], tmp32[:, 0:n]), waits=[t1], inc="V")
    t3 = pb.op("vector", lambda e: e.tensor_copy(tmp32[:, 0:n], i32t[:, 0:n]), waits=[t2], inc="V")
    t4 = pb.op("vector", lambda e: e.tensor_scalar(
        dst, tmp32[:, 0:n], s_col[:, 0:1], None, ALU.mult, ALU.bypass), waits=[t3], inc="V")
    pb.tag[("qdq_busy", id(tmp32))] = t4
    return t4


def build_l2ffn(R=2048, stop_phase=99):
    pb = PB()
    NI = 3072
    NKT = NI // 128
    xin = pb.dram("xin", [R, D], F32, kind="ExternalInput")
    wg = pb.dram("w_gate", [NI, D], F32, kind="ExternalInput")
    wu = pb.dram("w_up", [NI, D], F32, kind="ExternalInput")
    wd = pb.dram("w_down", [D, NI], F32, kind="ExternalInput")
    xout = pb.dram("xout", [R, D], F32, kind="ExternalOutput")
    xn_dram = pb.dram("xn_dram", [R, D], F32)
    xq_dram = pb.dram("xq_dram", [D, R], BF16)
    g_dram = pb.dram("g_dram", [NI, R], BF16)
    u_dram = pb.dram("u_dram", [NI, R], BF16)
    g2_dram = pb.dram("g2_dram", [NI, R], BF16)
    o_dram = pb.dram("o_dram", [NI, R], BF16)

    idf, idb = emit_identity(pb)
    alloc_ln(pb)
    alloc_fp4(pb, 3072, need_p=False)
    T = pb.tag
    ones1 = pb.sb("ones1r", [1, 128], F32)
    t_on = pb.op("vector", lambda e: e.memset(ones1[:], 1.0), inc="V")
    T["id"] = T["id"] + [t_on]
    psrep = pb.ps("psrep", [128, 2], F32)
    gx = pb.sb("gx", [1, 2], F32)
    gg = pb.sb("gg", [1, 2], F32)
    gu = pb.sb("gu", [1, 2], F32)
    g2 = pb.sb("g2", [1, 2], F32)
    go = pb.sb("go", [1, 2], F32)
    scx = pb.sb("scx", [128, 2], F32)
    scg = pb.sb("scg", [128, 2], F32)
    scu = pb.sb("scu", [128, 2], F32)
    sc2 = pb.sb("sc2", [128, 2], F32)
    sco = pb.sb("sco", [128, 2], F32)
    mxc = pb.sb("mxc", [128, 5], F32)
    mtmp = pb.sb("mtmp", [128, 1], F32)
    tmz = pb.op("vector", lambda e: e.memset(mxc[:], 0.0), inc="V")
    T["mxc_z"] = tmz
    tmp32 = pb.sb("tmp32", [128, 512], F32)
    tmp32b = pb.sb("tmp32b", [128, 512], F32)
    i32t = pb.sb("i32t", [128, 512], I32)
    xv = xin.rearrange("(t p) d -> t p d", p=128)
    xnv = xn_dram.rearrange("(t p) d -> t p d", p=128)
    ov = xout.rearrange("(t p) d -> t p d", p=128)

    pb.push_scope()
    alloc_wstage(pb, 3072, f8=False)
    wgT, wgrdy = emit_weight_T(pb, wg, NI, D, "wg",
                               quant=lambda p_, a, b, W, tg: emit_lpbq_rows(p_, a, b, W, tg))
    wuT, wurdy = emit_weight_T(pb, wu, NI, D, "wu",
                               quant=lambda p_, a, b, W, tg: emit_lpbq_rows(p_, a, b, W, tg))
    xa = [pb.sb(f"xa{i}", [128, D], F32) for i in range(2)]
    xn = pb.sb("xn", [128, D], F32)
    xqb = pb.sb("xqb", [128, D], BF16)
    xfm = pb.sb("xfm", [128, 8 * 512], BF16)
    psT = [pb.ps(f"psT{i}", [128, 128], BF16) for i in range(2)]
    psM = [pb.ps(f"psM{i}", [128, 512], F32) for i in range(2)]
    gbf = pb.sb("gbf", [128, 512], BF16)
    ubf = pb.sb("ubf", [128, 512], BF16)
    for t in range(R // 128):
        a = xa[t % 2]
        tl = pb.op("sync", lambda e, t=t, a=a: e.dma_start(out=a[:], in_=xv[t]),
                   waits=[T.get(("xa", t % 2))], inc="Xq", inc_by=16)
        tln = emit_ln(pb, a, xn, [tl, T.get("xn_rd")])
        T[("xa", t % 2)] = tln
        tsp = pb.op("gpsimd", lambda e, t=t: e.dma_start(out=xnv[t], in_=xn[:]),
                    waits=[tln], inc="Hs", inc_by=16)
        tmx = pb.op("vector", lambda e: e.tensor_reduce(
            mtmp[:], xn[:], AX.X, ALU.max, apply_absolute_value=True),
            waits=[tln, T.get("mtmp_rd")], inc="V")
        tac = pb.op("vector", lambda e: e.tensor_max(
            mxc[:, 0:1], mxc[:, 0:1], mtmp[:]), waits=[tmx, T["mxc_z"], T.get("mx0")], inc="V")
        T["mtmp_rd"] = tac
        T["mx0"] = tac
        T["xn_rd"] = tsp
    T[("gmcol", "x")] = T["mx0"]
    if stop_phase < 1:
        pb.wait("gpsimd", pb.tag["xn_rd"]); pb.pop_scope(); return pb.finalize()
    tgx = emit_global_absmax(pb, mxc[:, 0:1], "x", idf, gx[0:1, 0:1])
    tscx = emit_scale_cols(pb, gx, scx[:, 0:1], scx[:, 1:2], ones1, [tgx], psrep)
    if stop_phase < 2:
        pb.wait("gpsimd", tscx); pb.pop_scope(); return pb.finalize()
    for t in range(R // 128):
        tl = pb.op("sync", lambda e, t=t: e.dma_start(out=xn[:], in_=xnv[t]),
                   waits=[T.get("xn_rd2"), T["xn_rd"], T["mx0"]], inc="Xq", inc_by=16)
        tq = emit_qdq4(pb, xn[:], xqb[:], scx[:, 1:2], scx[:, 0:1], pb.q_f32, pb.q_i32,
                       [tl, tscx, T.get("xqb_rd")], n=1024)
        T["xn_rd2"] = tq
        for f in range(8):
            ps = psT[f % 2]
            tt = pb.op("tensor", lambda e, f=f, ps=ps: e.transpose(
                ps[:], xqb[:, 128 * f:128 * (f + 1)], idb[:]),
                waits=[tq, T.get(("psT", f % 2))] + T["id"], inc="T")
            ev = pb.op("vector", lambda e, f=f, ps=ps: e.tensor_copy(
                xfm[:, f * 128:(f + 1) * 128] if False else xfm[:, f * 512:f * 512 + 128], ps[:]),
                waits=[tt], inc="V")
            tw = pb.op("gpsimd", lambda e, f=f, t=t: e.dma_start(
                out=xq_dram[128 * f:128 * (f + 1), 128 * t:128 * (t + 1)],
                in_=xfm[:, f * 512:f * 512 + 128]), waits=[ev], inc="Hs", inc_by=16)
            T[("psT", f % 2)] = tw
            if f == 7:
                T["xqb_rd"] = tt
                T["xq_done"] = tw
    if stop_phase < 3:
        pb.wait("gpsimd", pb.tag["xq_done"]); pb.pop_scope(); return pb.finalize()
    for n in range(R // 512):
        lds = []
        for f in range(8):
            tl = pb.op("sync", lambda e, f=f, n=n: e.dma_start(
                out=xfm[:, f * 512:(f + 1) * 512],
                in_=xq_dram[128 * f:128 * (f + 1), 512 * n:512 * (n + 1)]),
                waits=[T["xq_done"], T.get("xfm_rd")], inc="Xq", inc_by=16)
            lds.append(tl)
        for m in range(NKT):
            for wi, (wT, wrdy, obf, od, mcol) in enumerate(
                    ((wgT, wgrdy, gbf, g_dram, 1), (wuT, wurdy, ubf, u_dram, 2))):
                ps = psM[wi]
                def mmg(e, m=m, ps=ps, wT=wT):
                    inst = None
                    for k in range(8):
                        inst = e.matmul(ps[:], wT[:, k * NI + 128 * m:k * NI + 128 * (m + 1)],
                                        xfm[:, k * 512:(k + 1) * 512],
                                        start=(k == 0), stop=(k == 7))
                    return inst
                tm = pb.op("tensor", mmg, waits=lds + [wrdy, T.get(("psMl", wi)),
                           T.get(("psMlr", wi))], inc="T")
                if m == NKT - 1:
                    T["xfm_rd"] = tm
                te = pb.op("scalar", lambda e, ps=ps, obf=obf: e.copy(obf[:], ps[:]),
                           waits=[tm, T.get(("obf", wi))], inc="S")
                tr = pb.op("vector", lambda e, ps=ps: e.tensor_reduce(
                    mtmp[:], ps[:], AX.X, ALU.max, apply_absolute_value=True),
                    waits=[tm, te, T.get("mtmp_rd")], inc="V")
                tr2 = pb.op("vector", lambda e, mcol=mcol: e.tensor_max(
                    mxc[:, mcol:mcol + 1], mxc[:, mcol:mcol + 1], mtmp[:]),
                    waits=[tr, T["mxc_z"], T.get(("mxa", mcol))], inc="V")
                T["mtmp_rd"] = tr2
                T[("mxa", mcol)] = tr2
                T[("psMlr", wi)] = tr
                tw = pb.op("gpsimd", lambda e, m=m, n=n, obf=obf, od=od: e.dma_start(
                    out=od[128 * m:128 * (m + 1), 512 * n:512 * (n + 1)], in_=obf[:]),
                    waits=[te, tr2], inc="Hs", inc_by=16)
                T[("obf", wi)] = tw
                T[("psMl", wi)] = te
                T["gu_done"] = tw
    T[("gmcol", "g")] = T[("mxa", 1)]
    T[("gmcol", "u")] = T[("mxa", 2)]
    if stop_phase < 4:
        pb.wait("gpsimd", pb.tag["gu_done"]); pb.pop_scope(); return pb.finalize()
    tgg = emit_global_absmax(pb, mxc[:, 1:2], "g", idf, gg[0:1, 0:1])
    tgu = emit_global_absmax(pb, mxc[:, 2:3], "u", idf, gu[0:1, 0:1])
    tscg = emit_scale_cols(pb, gg, scg[:, 0:1], scg[:, 1:2], ones1, [tgg], psrep)
    tscu = emit_scale_cols(pb, gu, scu[:, 0:1], scu[:, 1:2], ones1, [tgu], psrep)
    if stop_phase < 5:
        pb.wait("gpsimd", tscu); pb.pop_scope(); return pb.finalize()
    sigt = pb.sb("sigt", [128, 512], F32)
    for n in range(R // 512):
        for m in range(NKT):
            tl = pb.op("sync", lambda e, m=m, n=n: e.dma_start(
                out=gbf[:], in_=g_dram[128 * m:128 * (m + 1), 512 * n:512 * (n + 1)]),
                waits=[T["gu_done"], T.get("gbf_rd")], inc="Xq", inc_by=16)
            tq = emit_qdq4(pb, gbf[:], tmp32b[:], scg[:, 1:2], scg[:, 0:1], pb.q_f32,
                           pb.q_i32, [tl, tscg, T.get("t32b_rd")])
            T["gbf_rd"] = tq
            tsg = pb.op("scalar", lambda e: e.activation(sigt[:], tmp32b[:], AF.Sigmoid),
                        waits=[tq, T.get("sig_rd")], inc="S")
            t1 = pb.op("vector", lambda e: e.tensor_scalar(
                pb.q_f32[:, 0:512], sigt[:], 65536.0, None, ALU.mult, ALU.bypass),
                waits=[tsg, pb.tag.get("q_busy"), pb.tag.get(("qdq_busy", id(pb.q_f32)))], inc="V")
            t2 = pb.op("vector", lambda e: e.tensor_copy(pb.q_i32[:, 0:512], pb.q_f32[:, 0:512]),
                       waits=[t1], inc="V")
            t3 = pb.op("vector", lambda e: e.tensor_copy(pb.q_f32[:, 0:512], pb.q_i32[:, 0:512]),
                       waits=[t2], inc="V")
            t4 = pb.op("vector", lambda e: e.tensor_scalar(
                sigt[:], pb.q_f32[:, 0:512], 65535.0, float(1 / 65536.0), ALU.min, ALU.mult),
                waits=[t3], inc="V")
            pb.tag["q_busy"] = t4
            pb.tag[("qdq_busy", id(pb.q_f32))] = t4
            T["sig_rd"] = t4
            tg2 = pb.op("vector", lambda e: e.tensor_mul(tmp32b[:], tmp32b[:], sigt[:]),
                        waits=[t4], inc="V")
            tmr = pb.op("vector", lambda e: e.tensor_reduce(
                mtmp[:], tmp32b[:], AX.X, ALU.max, apply_absolute_value=True),
                waits=[tg2, T.get("mtmp_rd")], inc="V")
            tmx = pb.op("vector", lambda e: e.tensor_max(
                mxc[:, 3:4], mxc[:, 3:4], mtmp[:]), waits=[tmr, T.get(("mxa", 3))], inc="V")
            T["mtmp_rd"] = tmx
            T[("mxa", 3)] = tmx
            tcb = pb.op("vector", lambda e: e.tensor_copy(gbf[:], tmp32b[:]),
                        waits=[tg2, tmx if False else tmx], inc="V")
            T["t32b_rd"] = tcb
            tw = pb.op("gpsimd", lambda e, m=m, n=n: e.dma_start(
                out=g2_dram[128 * m:128 * (m + 1), 512 * n:512 * (n + 1)], in_=gbf[:]),
                waits=[tcb], inc="Hs", inc_by=16)
            T["gbf_rd"] = tw
            T["g2_done"] = tw
    T[("gmcol", "g2")] = T[("mxa", 3)]
    tg2g = emit_global_absmax(pb, mxc[:, 3:4], "g2", idf, g2[0:1, 0:1])
    tsc2 = emit_scale_cols(pb, g2, sc2[:, 0:1], sc2[:, 1:2], ones1, [tg2g], psrep)
    if stop_phase < 6:
        pb.wait("gpsimd", tsc2); pb.pop_scope(); return pb.finalize()
    for n in range(R // 512):
        for m in range(NKT):
            tl1 = pb.op("sync", lambda e, m=m, n=n: e.dma_start(
                out=gbf[:], in_=g2_dram[128 * m:128 * (m + 1), 512 * n:512 * (n + 1)]),
                waits=[T["g2_done"], T.get("gbf_rd")], inc="Xq", inc_by=16)
            tl2 = pb.op("sync", lambda e, m=m, n=n: e.dma_start(
                out=ubf[:], in_=u_dram[128 * m:128 * (m + 1), 512 * n:512 * (n + 1)]),
                waits=[T["gu_done"], T.get("ubf_rd")], inc="Xq", inc_by=16)
            tq1 = emit_qdq4(pb, gbf[:], tmp32b[:], sc2[:, 1:2], sc2[:, 0:1], pb.q_f32,
                            pb.q_i32, [tl1, tsc2, T.get("t32b_rd")])
            T["gbf_rd"] = tq1
            tq2 = emit_qdq4(pb, ubf[:], tmp32[:], scu[:, 1:2], scu[:, 0:1], pb.q_f32,
                            pb.q_i32, [tl2, tscu, T.get("t32_rd")])
            T["ubf_rd"] = tq2
            to = pb.op("vector", lambda e: e.tensor_mul(tmp32b[:], tmp32b[:], tmp32[:]),
                       waits=[tq1, tq2], inc="V")
            T["t32_rd"] = to
            tmr = pb.op("vector", lambda e: e.tensor_reduce(
                mtmp[:], tmp32b[:], AX.X, ALU.max, apply_absolute_value=True),
                waits=[to, T.get("mtmp_rd")], inc="V")
            tmx = pb.op("vector", lambda e: e.tensor_max(
                mxc[:, 4:5], mxc[:, 4:5], mtmp[:]), waits=[tmr, T.get(("mxa", 4))], inc="V")
            T["mtmp_rd"] = tmx
            T[("mxa", 4)] = tmx
            tcb = pb.op("vector", lambda e: e.tensor_copy(gbf[:], tmp32b[:]),
                        waits=[to, tmx], inc="V")
            T["t32b_rd"] = tcb
            tw = pb.op("gpsimd", lambda e, m=m, n=n: e.dma_start(
                out=o_dram[128 * m:128 * (m + 1), 512 * n:512 * (n + 1)], in_=gbf[:]),
                waits=[tcb], inc="Hs", inc_by=16)
            T["gbf_rd"] = tw
            T["o_done"] = tw
    T[("gmcol", "o")] = T[("mxa", 4)]
    tgo = emit_global_absmax(pb, mxc[:, 4:5], "o", idf, go[0:1, 0:1])
    tsco = emit_scale_cols(pb, go, sco[:, 0:1], sco[:, 1:2], ones1, [tgo], psrep)
    if stop_phase < 7:
        pb.wait("gpsimd", tsco); pb.pop_scope(); return pb.finalize()
    pb.pop_scope()
    wdT, wdrdy = None, None
    pb.push_scope()
    alloc_wstage(pb, 3072, f8=False, sfx="d")
    wdT, wdrdy = emit_weight_T(pb, wd, D, NI, "wd",
                               quant=lambda p_, a, b, W, tg: emit_lpbq_rows(p_, a, b, W, tg, fold=0.5))
    oqf = pb.sb("oqf", [128, NKT * 512], BF16)
    obf2 = pb.sb("obf2", [128, 512], BF16)
    psM2 = [pb.ps(f"ps2{i}", [128, 512], F32) for i in range(2)]
    psY = [pb.ps(f"psY{i}", [128, 128], BF16) for i in range(2)]
    yfm = pb.sb("yfm", [128, 8 * 512], BF16)
    ytk = pb.sb("ytk", [128, D], BF16)
    xa2 = [pb.sb(f"xb{i}", [128, D], F32) for i in range(2)]
    for n in range(R // 512):
        qt = []
        for k in range(NKT):
            tl = pb.op("sync", lambda e, k=k, n=n: e.dma_start(
                out=obf2[:], in_=o_dram[128 * k:128 * (k + 1), 512 * n:512 * (n + 1)]),
                waits=[T["o_done"], T.get("obf2_rd")], inc="Xq", inc_by=16)
            tq = emit_qdq4(pb, obf2[:], oqf[:, k * 512:(k + 1) * 512], sco[:, 1:2],
                           sco[:, 0:1], tmp32, i32t, [tl, tsco, T.get(("oqf_rd", k))])
            T["obf2_rd"] = tq
            qt.append(tq)
        for m in range(8):
            ps = psM2[m % 2]
            def mmy(e, m=m, ps=ps):
                inst = None
                for k in range(NKT):
                    inst = e.matmul(ps[:], wdT[:, k * D + 128 * m:k * D + 128 * (m + 1)],
                                    oqf[:, k * 512:(k + 1) * 512],
                                    start=(k == 0), stop=(k == NKT - 1))
                return inst
            tm = pb.op("tensor", mmy, waits=qt + [wdrdy, T.get(("psM2", m % 2))], inc="T")
            if m == 7:
                for k in range(NKT):
                    T[("oqf_rd", k)] = tm
            te = pb.op("scalar", lambda e, m=m, ps=ps: e.copy(
                yfm[:, m * 512:(m + 1) * 512], ps[:]),
                waits=[tm, T.get(("yfm_rd", m))], inc="S")
            T[("psM2", m % 2)] = te
            T[("yfm", m)] = te
        for ti in range(4):
            t = 4 * n + ti
            trb = []
            for m in range(8):
                ps = psY[m % 2]
                tt = pb.op("tensor", lambda e, m=m, ti=ti, ps=ps: e.transpose(
                    ps[:], yfm[:, m * 512 + ti * 128: m * 512 + ti * 128 + 128], idb[:]),
                    waits=[T[("yfm", m)], T.get(("psY", m % 2))] + T["id"], inc="T")
                if ti == 3:
                    T[("yfm_rd", m)] = tt
                ev = pb.op("vector", lambda e, m=m, ps=ps: e.tensor_copy(
                    ytk[:, 128 * m:128 * (m + 1)], ps[:]),
                    waits=[tt, T.get("ytk_rd")], inc="V")
                T[("psY", m % 2)] = ev
                trb.append(ev)
            a = xa2[ti % 2]
            tlx = pb.op("sync", lambda e, t=t, a=a: e.dma_start(out=a[:], in_=xv[t]),
                        waits=[T.get(("xa2", ti % 2))], inc="Xq", inc_by=16)
            tad = pb.op("vector", lambda e, a=a: e.tensor_add(a[:], a[:], ytk[:]),
                        waits=trb + [tlx], inc="V")
            T["ytk_rd"] = tad
            tst = pb.op("scalar", lambda e, t=t, a=a: e.dma_start(out=ov[t], in_=a[:]),
                        waits=[tad], inc="St", inc_by=16)
            T[("xa2", ti % 2)] = tst
    pb.pop_scope()
    pb.wait("gpsimd", ("St", pb.cnt["St"]))
    return pb.finalize()



def _causal_mask_table():
    m = np.zeros((4, 128, 512), np.float32)
    for d in range(4):
        for j in range(4):
            if d < j:
                m[d][:, j * 128:(j + 1) * 128] = 1.0
            elif d == j:
                tri = (np.arange(128)[:, None] <= np.arange(128)[None, :])
                m[d][:, j * 128:(j + 1) * 128] = tri.astype(np.float32)
    return m


def build_causal_attn(kind, R=4096):
    pb = PB()
    xin = pb.dram("xin", [R, D], F32, kind="ExternalInput")
    if kind == "l0":
        Wq = pb.dram("Wq", [64, 64], F32, kind="ExternalInput")
        Wk = pb.dram("Wk", [64, 64], F32, kind="ExternalInput")
        Wv = pb.dram("Wv", [64, 64], F32, kind="ExternalInput")
        Wu = pb.dram("Wu", [D, D], F32, kind="ExternalInput")
        bu = pb.dram("bu", [D], F32, kind="ExternalInput")
    else:
        w_in = pb.dram("w_in", [3 * D, D], F32, kind="ExternalInput")
        b_in = pb.dram("b_in", [3 * D], F32, kind="ExternalInput")
        w_out = pb.dram("w_out", [D, D], F32, kind="ExternalInput")
        b_out = pb.dram("b_out", [D], F32, kind="ExternalInput")
    xout = pb.dram("xout", [R, D], F32, kind="ExternalOutput")
    qd = pb.dram("q_dram", [D, R], BF16)
    kd = pb.dram("k_dram", [D, R], BF16)
    vd = pb.dram("v_dram", [D, R], BF16)
    yd = pb.dram("y_dram", [D, R], BF16)

    idf, idb = emit_identity(pb)
    alloc_ln(pb)
    T = pb.tag
    NT = R // 128
    NC = R // 512
    xv = xin.rearrange("(t p) d -> t p d", p=128)
    ov = xout.rearrange("(t p) d -> t p d", p=128)

    woT = pb.sb("woT", [128, 8 * D], BF16)
    if kind == "l0":
        boc, bordy = emit_bias_cols(pb, bu, D, "bo", idf)
    else:
        boc, bordy = emit_bias_cols(pb, b_out, D, "bo", idf)

    pb.push_scope()
    alloc_wstage(pb, 4096, f8=False, tst=None)
    wo_src = Wu if kind == "l0" else w_out
    _, wordy = emit_weight_T(pb, wo_src, D, D, "wo", wT_pre=woT)
    if kind == "l3":
        wiT, wirdy = emit_weight_T(pb, w_in, 3 * D, D, "wi", fold_first_rows=(1024, 0.125))
        bic, birdy = emit_bias_cols(pb, b_in, 3 * D, "bi", idf)
        bis = pb.op("vector", lambda e: e.tensor_scalar(
            bic[:, 0:8], bic[:, 0:8], 0.125, None, ALU.mult, ALU.bypass),
            waits=[birdy], inc="V")
    else:
        s4 = float(D) ** -0.25
        packs = []
        wsm = pb.sb("wsm", [64, 64], F32)
        psw = pb.ps("psw", [64, 64], F32)
        wsc = pb.sb("wsc", [128, 64], F32)
        for nm, wap, scl in (("q", Wq, s4), ("k", Wk, s4), ("v", Wv, 1.0)):
            pk = pb.sb(f"pk{nm}", [128, 128], BF16)
            tz = pb.op("vector", lambda e, pk=pk: e.memset(pk[:], 0.0), inc="V")
            tl = pb.op("sync", lambda e, wap=wap: e.dma_start(out=wsm[:], in_=wap),
                       waits=[T.get("wsm_rd")], inc="Wq", inc_by=16)
            tt = pb.op("tensor", lambda e: e.transpose(psw[:], wsm[:], idf[0:64, 0:64]),
                       waits=[tl, T.get("psw_rd")] + T["id"], inc="T")
            T["wsm_rd"] = tt
            tsc = pb.op("vector", lambda e, scl=scl: e.tensor_scalar(
                wsc[0:64, :], psw[:], float(scl), None, ALU.mult, ALU.bypass),
                waits=[tt, T.get("wsc_rd")], inc="V")
            T["psw_rd"] = tsc
            c1 = pb.op("vector", lambda e, pk=pk: e.tensor_copy(pk[0:64, 0:64], wsc[0:64, :]),
                       waits=[tsc, tz], inc="V")
            c2 = pb.op("vector", lambda e, pk=pk: e.tensor_copy(pk[64:128, 64:128], wsc[0:64, :]),
                       waits=[tsc, tz], inc="V")
            T["wsc_rd"] = c2
            packs.append((pk, [c1, c2]))
        (wqp, wqt), (wkp, wkt), (wvp, wvt) = packs

    xa = [pb.sb(f"xa{i}", [128, D], F32) for i in range(2)]
    xqb = pb.sb("xqb", [128, D], BF16)
    xnfm = pb.sb("xnfm", [128, 8 * R], BF16)
    psT = [pb.ps(f"psT{i}", [128, 128], BF16) for i in range(2)]
    psM = [pb.ps(f"psM{i}", [128, 512], F32) for i in range(2)]
    qkvb = pb.sb("qkvb", [128, 512], BF16)
    for t in range(NT):
        a = xa[t % 2]
        tl = pb.op("sync", lambda e, t=t, a=a: e.dma_start(out=a[:], in_=xv[t]),
                   waits=[T.get(("xa", t % 2))], inc="Xq", inc_by=16)
        tln = emit_ln(pb, a, xqb, [tl, T.get("xqb_rd")])
        T[("xa", t % 2)] = tln
        for f in range(8):
            ps = psT[f % 2]
            tt = pb.op("tensor", lambda e, f=f, ps=ps: e.transpose(
                ps[:], xqb[:, 128 * f:128 * (f + 1)], idb[:]),
                waits=[tln, T.get(("psT", f % 2))] + T["id"], inc="T")
            ev = pb.op("vector", lambda e, f=f, t=t, ps=ps: e.tensor_copy(
                xnfm[:, f * R + t * 128: f * R + t * 128 + 128], ps[:]),
                waits=[tt], inc="V")
            T[("psT", f % 2)] = ev
            T["xnfm"] = ev
            if f == 7:
                T["xqb_rd"] = tt
    for hp in range(8):
        for pi_, (nm, dram_) in enumerate((("q", qd), ("k", kd), ("v", vd))):
            for ch in range(NC):
                ps = psM[ch % 2]
                if kind == "l3":
                    def mmp(e, hp=hp, pi_=pi_, ch=ch, ps=ps):
                        inst = None
                        for k in range(8):
                            col = k * 3072 + pi_ * 1024 + 128 * hp
                            inst = e.matmul(ps[:], wiT[:, col:col + 128],
                                            xnfm[:, k * R + ch * 512:k * R + ch * 512 + 512],
                                            start=(k == 0), stop=(k == 7))
                        return inst
                    tm = pb.op("tensor", mmp, waits=[T["xnfm"], wirdy,
                               T.get(("psMp", ch % 2))], inc="T")
                    te = pb.op("scalar", lambda e, hp=hp, pi_=pi_, ps=ps: e.activation(
                        qkvb[:], ps[:], AF.Identity, bias=bic[:, pi_ * 8 + hp:pi_ * 8 + hp + 1]),
                        waits=[tm, bis, T.get("qkvb_rd")], inc="S")
                else:
                    pk = (wqp, wkp, wvp)[pi_]
                    pkt = (wqt, wkt, wvt)[pi_]
                    def mmp(e, hp=hp, ch=ch, ps=ps, pk=pk):
                        return e.matmul(ps[:], pk[:],
                                        xnfm[:, hp * R + ch * 512:hp * R + ch * 512 + 512],
                                        start=True, stop=True)
                    tm = pb.op("tensor", mmp, waits=[T["xnfm"]] + pkt +
                               [T.get(("psMp", ch % 2))], inc="T")
                    te = pb.op("scalar", lambda e, ps=ps: e.copy(qkvb[:], ps[:]),
                               waits=[tm, T.get("qkvb_rd")], inc="S")
                T[("psMp", ch % 2)] = te
                tw = pb.op("gpsimd", lambda e, hp=hp, ch=ch, dram_=dram_: e.dma_start(
                    out=dram_[128 * hp:128 * (hp + 1), 512 * ch:512 * (ch + 1)], in_=qkvb[:]),
                    waits=[te], inc="Hs", inc_by=16)
                T["qkvb_rd"] = tw
                T["qkv_done"] = tw
    pb.pop_scope(gate=[T["qkv_done"]])

    pb.push_scope()
    maskh = pb.const(_causal_mask_table(), "cmask")
    mk32 = pb.sb("mk32", [128, 4 * 512], F32)
    mkb = pb.sb("mkb", [128, 4 * 512], BF16)
    tm1 = pb.op("sync", lambda e: e.dma_start(
        out=mk32[:], in_=maskh.rearrange("d p c -> p d c")), inc="Wq", inc_by=16)
    tm2 = pb.op("vector", lambda e: e.tensor_copy(mkb[:], mk32[:]), waits=[tm1], inc="V")
    ones1 = pb.sb("ones1", [1, 64], F32)
    tON = pb.op("vector", lambda e: e.memset(ones1[:], 1.0), inc="V")
    qf = pb.sb("qf", [128, R], BF16)
    kf = pb.sb("kf", [128, R], BF16)
    vf = pb.sb("vf", [128, R], BF16)
    V2 = [pb.sb(f"V2_{h}", [128, 32 * 65], BF16) for h in range(2)]
    expT = [pb.sb(f"expT{h}", [128, 512], BF16) for h in range(2)]
    yun = [pb.sb(f"yun{h}", [64, 512], F32) for h in range(2)]
    rep_sb = pb.sb("rep_sb", [64, 512], F32)
    rcol = pb.sb("rcol", [1, 512], F32)
    ybf = pb.sb("ybf", [64, 512], BF16)
    psS = [pb.ps(f"psS{h}", [128, 512], F32) for h in range(2)]
    psA = [pb.ps(f"psA{h}", [65, 512], F32) for h in range(2)]
    psV = pb.ps("psV", [128, 64], BF16)
    psR = pb.ps("psR", [64, 512], F32)
    for hp in range(8):
        tq = pb.op("sync", lambda e, hp=hp: e.dma_start(
            out=qf[:], in_=qd[128 * hp:128 * (hp + 1), :]),
            waits=[T["qkv_done"], T.get("qf_rd")], inc="Xq", inc_by=16)
        tk = pb.op("sync", lambda e, hp=hp: e.dma_start(
            out=kf[:], in_=kd[128 * hp:128 * (hp + 1), :]),
            waits=[T["qkv_done"], T.get("kf_rd")], inc="Xq", inc_by=16)
        tv = pb.op("sync", lambda e, hp=hp: e.dma_start(
            out=vf[:], in_=vd[128 * hp:128 * (hp + 1), :]),
            waits=[T["qkv_done"], T.get("vf_rd")], inc="Xq", inc_by=16)
        vrd = []
        for h in range(2):
            tz = pb.op("vector", lambda e, h=h: e.memset(V2[h][:], 1.0),
                       waits=[T.get(("V2", h))], inc="V")
            for ka in range(32):
                ttr = pb.op("tensor", lambda e, h=h, ka=ka: e.transpose(
                    psV[:], vf[64 * h:64 * h + 64, 128 * ka:128 * (ka + 1)],
                    idb[64 * h:64 * h + 64, 64 * h:64 * h + 64]),
                    waits=[tv, T.get("psV_rd")] + T["id"], inc="T")
                ev = pb.op("vector", lambda e, h=h, ka=ka: e.tensor_copy(
                    V2[h][:, 65 * ka:65 * ka + 64], psV[:]), waits=[ttr, tz], inc="V")
                T["psV_rd"] = ev
                vrd.append(ev)
                if h == 1 and ka == 31:
                    T["vf_rd"] = ttr
        for jc in range(NC):
            kmax = 4 * jc + 4
            avt = [None, None]
            for ka in range(kmax):
                for h in range(2):
                    tmm = pb.op("tensor", lambda e, h=h, ka=ka, jc=jc: e.matmul(
                        psS[h][:], kf[64 * h:64 * h + 64, 128 * ka:128 * (ka + 1)],
                        qf[64 * h:64 * h + 64, jc * 512:(jc + 1) * 512],
                        tile_position=(64 * h, 0), start=True, stop=True),
                        waits=[tq, tk, T.get(("psS", h))], inc="T")
                    tex = pb.op("scalar", lambda e, h=h: e.activation(
                        expT[h][:], psS[h][:], AF.Exp),
                        waits=[tmm, T.get(("expT", h))], inc="S")
                    T[("psS", h)] = tex
                    tready = tex
                    if ka >= 4 * jc:
                        d = ka - 4 * jc
                        tready = pb.op("vector", lambda e, h=h, d=d: e.tensor_mul(
                            expT[h][:], expT[h][:], mkb[:, d * 512:(d + 1) * 512]),
                            waits=[tex, tm2], inc="V")
                    tav = pb.op("tensor", lambda e, h=h, ka=ka, kmax=kmax: e.matmul(
                        psA[h][:], V2[h][:, 65 * ka:65 * (ka + 1)], expT[h][:],
                        start=(ka == 0), stop=(ka == kmax - 1)),
                        waits=[tready, vrd[-1]] + ([T.get(("psA", h))] if ka == 0 else []),
                        inc="T")
                    T[("expT", h)] = tav
                    avt[h] = tav
            for h in range(2):
                trc = pb.op("vector", lambda e, h=h: e.reciprocal(
                    rcol[:], psA[h][64:65, :]), waits=[avt[h], T.get("rcol_rd")], inc="V")
                trp = pb.op("tensor", lambda e: e.matmul(
                    psR[:], ones1[:], rcol[:], start=True, stop=True),
                    waits=[trc, tON, T.get("psR_rd")], inc="T")
                tyu = pb.op("scalar", lambda e, h=h: e.copy(yun[h][:], psA[h][0:64, :]),
                            waits=[trc, avt[h], T.get(("yun_rd", h))], inc="S")
                trs = pb.op("vector", lambda e: e.tensor_copy(rep_sb[:], psR[:]),
                            waits=[trp, T.get("rep_rd")], inc="V")
                T["psR_rd"] = trs
                T["rcol_rd"] = trp
                T[("psA", h)] = tyu
                tyn = pb.op("vector", lambda e, h=h: e.tensor_mul(
                    ybf[:], yun[h][:], rep_sb[:]), waits=[tyu, trs, T.get("ybf_rd")], inc="V")
                T[("yun_rd", h)] = tyn
                T["rep_rd"] = tyn
                tw = pb.op("gpsimd", lambda e, hp=hp, h=h, jc=jc: e.dma_start(
                    out=yd[128 * hp + 64 * h:128 * hp + 64 * h + 64, jc * 512:(jc + 1) * 512],
                    in_=ybf[:]), waits=[tyn], inc="Hs", inc_by=16)
                T["ybf_rd"] = tw
                T["y_done"] = tw
        T["qf_rd"] = avt[0]
        T["kf_rd"] = avt[0]
        T[("V2", 0)] = avt[0]
        T[("V2", 1)] = avt[1]
    pb.pop_scope(gate=[T["y_done"]])

    ymv = [pb.sb(f"ymv{i}", [128, 512], BF16) for i in range(8)]
    psM2 = [pb.ps(f"ps2{i}", [128, 512], F32) for i in range(2)]
    psY = [pb.ps(f"psYo{i}", [128, 128], BF16) for i in range(2)]
    yofm = pb.sb("yofm", [128, 8 * 512], BF16)
    ytk = pb.sb("ytk", [128, D], BF16)
    ypos = pb.sb("ypos", [128, D], F32)
    yneg = pb.sb("yneg", [128, D], F32)
    xar = [pb.sb(f"xar{i}", [128, D], F32) for i in range(2)]
    for ch in range(NC):
        lds = []
        for k in range(8):
            tl = pb.op("sync", lambda e, k=k, ch=ch: e.dma_start(
                out=ymv[k][:], in_=yd[128 * k:128 * (k + 1), 512 * ch:512 * (ch + 1)]),
                waits=[T["y_done"], T.get(("ymv", k))], inc="Xq", inc_by=16)
            lds.append(tl)
        for m in range(8):
            ps = psM2[m % 2]
            def mmo(e, m=m, ps=ps):
                inst = None
                for k in range(8):
                    inst = e.matmul(ps[:], woT[:, k * D + 128 * m:k * D + 128 * (m + 1)],
                                    ymv[k][:], start=(k == 0), stop=(k == 7))
                return inst
            tm = pb.op("tensor", mmo, waits=lds + [wordy, T.get(("psM2", m % 2))], inc="T")
            if m == 7:
                for k in range(8):
                    T[("ymv", k)] = tm
            te = pb.op("scalar", lambda e, m=m, ps=ps: e.activation(
                yofm[:, m * 512:(m + 1) * 512], ps[:], AF.Identity, bias=boc[:, m:m + 1]),
                waits=[tm, bordy, T.get(("yofm_rd", m))], inc="S")
            T[("psM2", m % 2)] = te
            T[("yofm", m)] = te
        for ti in range(4):
            t = 4 * ch + ti
            trb = []
            for m in range(8):
                ps = psY[m % 2]
                tt = pb.op("tensor", lambda e, m=m, ti=ti, ps=ps: e.transpose(
                    ps[:], yofm[:, m * 512 + ti * 128: m * 512 + ti * 128 + 128], idb[:]),
                    waits=[T[("yofm", m)], T.get(("psYo", m % 2))] + T["id"], inc="T")
                if ti == 3:
                    T[("yofm_rd", m)] = tt
                ev = pb.op("vector", lambda e, m=m, ps=ps: e.tensor_copy(
                    ytk[:, 128 * m:128 * (m + 1)], ps[:]),
                    waits=[tt, T.get("ytk_rd")], inc="V")
                T[("psYo", m % 2)] = ev
                trb.append(ev)
            a = xar[ti % 2]
            tlx = pb.op("sync", lambda e, t=t, a=a: e.dma_start(out=a[:], in_=xv[t]),
                        waits=[T.get(("xar", ti % 2))], inc="Xq", inc_by=16)
            if kind == "l3":
                tp = pb.op("vector", lambda e: e.tensor_scalar_max(ypos[:], ytk[:], 0.0),
                           waits=trb + [T.get("ypos_rd")], inc="V")
                tn = pb.op("vector", lambda e: e.tensor_scalar_min(yneg[:], ytk[:], 0.0),
                           waits=trb + [T.get("yneg_rd")], inc="V")
                T["ytk_rd"] = tn
                ta1 = pb.op("vector", lambda e, a=a: e.tensor_add(a[:], a[:], ypos[:]),
                            waits=[tp, tlx], inc="V")
                tad = pb.op("vector", lambda e, a=a: e.tensor_add(a[:], a[:], yneg[:]),
                            waits=[tn, ta1], inc="V")
                T["ypos_rd"] = ta1
                T["yneg_rd"] = tad
            else:
                tad = pb.op("vector", lambda e, a=a: e.tensor_add(a[:], a[:], ytk[:]),
                            waits=trb + [tlx], inc="V")
                T["ytk_rd"] = tad
            tst = pb.op("scalar", lambda e, t=t, a=a: e.dma_start(out=ov[t], in_=a[:]),
                        waits=[tad], inc="St", inc_by=16)
            T[("xar", ti % 2)] = tst
    pb.wait("gpsimd", ("St", pb.cnt["St"]))
    return pb.finalize()






def emit_ln_transpose_all(pb, xv, NT, xnfm, R, idb, xa, xqb, psT):
    T = pb.tag
    for t in range(NT):
        a = xa[t % 2]
        tl = pb.op("sync", lambda e, t=t, a=a: e.dma_start(out=a[:], in_=xv[t]),
                   waits=[T.get(("xa", t % 2))], inc="Xq", inc_by=16)
        tln = emit_ln(pb, a, xqb, [tl, T.get("xqb_rd")])
        T[("xa", t % 2)] = tln
        for f in range(8):
            ps = psT[f % 2]
            tt = pb.op("tensor", lambda e, f=f, ps=ps: e.transpose(
                ps[:], xqb[:, 128 * f:128 * (f + 1)], idb[:]),
                waits=[tln, T.get(("psT", f % 2))] + T["id"], inc="T")
            ev = pb.op("vector", lambda e, f=f, t=t, ps=ps: e.tensor_copy(
                xnfm[:, f * R + t * 128: f * R + t * 128 + 128], ps[:]),
                waits=[tt], inc="V")
            T[("psT", f % 2)] = ev
            T["xnfm"] = ev
            if f == 7:
                T["xqb_rd"] = tt
    return T["xnfm"]


def emit_outproj_residual(pb, yd, woT, wordy, boc, bordy, xv, ov, NC, idb, idf, nan_fix,
                          ydone_key="y_done"):
    T = pb.tag
    ymv = [pb.sb(f"ymv{i}", [128, 512], BF16) for i in range(8)]
    psM2 = [pb.ps(f"ps2o{i}", [128, 512], F32) for i in range(2)]
    psY = [pb.ps(f"psYo{i}", [128, 128], BF16) for i in range(2)]
    yofm = pb.sb("yofm", [128, 8 * 512], BF16)
    ytk = pb.sb("ytk_o", [128, D], BF16)
    if nan_fix:
        ypos = pb.sb("ypos", [128, D], F32)
        yneg = pb.sb("yneg", [128, D], F32)
    xar = [pb.sb(f"xar{i}", [128, D], F32) for i in range(2)]
    for ch in range(NC):
        lds = []
        for k in range(8):
            tl = pb.op("sync", lambda e, k=k, ch=ch: e.dma_start(
                out=ymv[k][:], in_=yd[128 * k:128 * (k + 1), 512 * ch:512 * (ch + 1)]),
                waits=[T[ydone_key], T.get(("ymv", k))], inc="Xq", inc_by=16)
            lds.append(tl)
        for m in range(8):
            ps = psM2[m % 2]
            def mmo(e, m=m, ps=ps):
                inst = None
                for k in range(8):
                    inst = e.matmul(ps[:], woT[:, k * D + 128 * m:k * D + 128 * (m + 1)],
                                    ymv[k][:], start=(k == 0), stop=(k == 7))
                return inst
            tm = pb.op("tensor", mmo, waits=[lds[-1], wordy, T.get(("psM2o", m % 2))], inc="T")
            if m == 7:
                for k in range(8):
                    T[("ymv", k)] = tm
            te = pb.op("scalar", lambda e, m=m, ps=ps: e.activation(
                yofm[:, m * 512:(m + 1) * 512], ps[:], AF.Identity, bias=boc[:, m:m + 1]),
                waits=[tm, bordy, T.get(("yofmo_rd", m))], inc="S")
            T[("psM2o", m % 2)] = te
            T[("yofmo", m)] = te
        for ti in range(4):
            t = 4 * ch + ti
            trb = []
            for m in range(8):
                ps = psY[m % 2]
                tt = pb.op("tensor", lambda e, m=m, ti=ti, ps=ps: e.transpose(
                    ps[:], yofm[:, m * 512 + ti * 128: m * 512 + ti * 128 + 128], idb[:]),
                    waits=[T[("yofmo", m)], T.get(("psYoo", m % 2))] + T["id"], inc="T")
                if ti == 3:
                    T[("yofmo_rd", m)] = tt
                ev = pb.op("vector", lambda e, m=m, ps=ps: e.tensor_copy(
                    ytk[:, 128 * m:128 * (m + 1)], ps[:]),
                    waits=[tt, T.get("ytko_rd")], inc="V")
                T[("psYoo", m % 2)] = ev
                trb.append(ev)
            a = xar[ti % 2]
            tlx = pb.op("sync", lambda e, t=t, a=a: e.dma_start(out=a[:], in_=xv[t]),
                        waits=[T.get(("xaro", ti % 2))], inc="Xq", inc_by=16)
            if nan_fix:
                tp = pb.op("vector", lambda e: e.tensor_scalar_max(ypos[:], ytk[:], 0.0),
                           waits=trb + [T.get("ypos_rd")], inc="V")
                tn = pb.op("vector", lambda e: e.tensor_scalar_min(yneg[:], ytk[:], 0.0),
                           waits=trb + [T.get("yneg_rd")], inc="V")
                T["ytko_rd"] = tn
                ta1 = pb.op("vector", lambda e, a=a: e.tensor_add(a[:], a[:], ypos[:]),
                            waits=[tp, tlx], inc="V")
                tad = pb.op("vector", lambda e, a=a: e.tensor_add(a[:], a[:], yneg[:]),
                            waits=[tn, ta1], inc="V")
                T["ypos_rd"] = ta1
                T["yneg_rd"] = tad
            else:
                tad = pb.op("vector", lambda e, a=a: e.tensor_add(a[:], a[:], ytk[:]),
                            waits=trb + [tlx], inc="V")
                T["ytko_rd"] = tad
            tst = pb.op("scalar", lambda e, t=t, a=a: e.dma_start(out=ov[t], in_=a[:]),
                        waits=[tad], inc="St", inc_by=16)
            T[("xaro", ti % 2)] = tst


def emit_weight_plain(pb, w_ap, O, I, name, bias_like=False):
    KT = I // 128
    wT = pb.sb(f"{name}P", [128, KT * O], BF16)
    st = pb.sb(f"{name}_pst", [128, O], F32)
    wv = w_ap.rearrange("(a p) i -> a p i", p=128) if w_ap.ndim == 2 else w_ap
    last = None
    rdy = None
    for k in range(KT):
        tl = pb.op("sync", lambda e, k=k: e.dma_start(out=st[:], in_=wv[k]),
                   waits=[last], inc="Wq", inc_by=16)
        rdy = pb.op("vector", lambda e, k=k: e.tensor_copy(wT[:, k * O:(k + 1) * O], st[:]),
                    waits=[tl], inc="V")
        last = rdy
    return wT, rdy



def build_l1attn(R=4096):
    pb = PB()
    xin = pb.dram("xin", [R, D], F32, kind="ExternalInput")
    Wq = pb.dram("Wq", [D, 16, 64], F32, kind="ExternalInput")
    bq = pb.dram("bq", [16, 64], F32, kind="ExternalInput")
    Wk = pb.dram("Wk", [D, 16, 64], F32, kind="ExternalInput")
    bk = pb.dram("bk", [16, 64], F32, kind="ExternalInput")
    Wv = pb.dram("Wv", [D, 16, 64], F32, kind="ExternalInput")
    bv = pb.dram("bv", [16, 64], F32, kind="ExternalInput")
    Wo = pb.dram("Wo", [16, 64, D], F32, kind="ExternalInput")
    bo = pb.dram("bo", [D], F32, kind="ExternalInput")
    xout = pb.dram("xout", [R, D], F32, kind="ExternalOutput")
    qd = pb.dram("qf_dram", [D, R], BF16)
    kd = pb.dram("kf_dram", [D, R], BF16)
    vd = pb.dram("vf_dram", [D, R], BF16)
    yd = pb.dram("y_dram", [D, R], BF16)
    idf, idb = emit_identity(pb)
    alloc_ln(pb)
    T = pb.tag
    NT, NC = R // 128, R // 512
    xv = xin.rearrange("(t p) d -> t p d", p=128)
    ov = xout.rearrange("(t p) d -> t p d", p=128)
    woP, wordy = emit_weight_plain(pb, Wo.rearrange("h k d -> (h k) d"), D, D, "wo")
    boc, bordy = emit_bias_cols(pb, bo, D, "bo", idf)
    pb.push_scope()
    wqP, t1r = emit_weight_plain(pb, Wq.rearrange("d h k -> d (h k)"), D, D, "wq")
    wkP, t2r = emit_weight_plain(pb, Wk.rearrange("d h k -> d (h k)"), D, D, "wk")
    wvP, t3r = emit_weight_plain(pb, Wv.rearrange("d h k -> d (h k)"), D, D, "wv")
    bqc, tb1 = emit_bias_cols(pb, bq.rearrange("h k -> (h k)"), D, "bq", idf)
    bkc, tb2 = emit_bias_cols(pb, bk.rearrange("h k -> (h k)"), D, "bk", idf)
    bvc, tb3 = emit_bias_cols(pb, bv.rearrange("h k -> (h k)"), D, "bv", idf)
    xa = [pb.sb(f"xa{i}", [128, D], F32) for i in range(2)]
    xqb = pb.sb("xqb", [128, D], BF16)
    xnfm = pb.sb("xnfm", [128, 8 * R], BF16)
    psT = [pb.ps(f"psT{i}", [128, 128], BF16) for i in range(2)]
    psM = [pb.ps(f"psM{i}", [128, 512], F32) for i in range(2)]
    tmpa = pb.sb("tmpa", [128, 512], F32)
    tmpb = pb.sb("tmpb", [128, 512], F32)
    tmpc = pb.sb("tmpc", [128, 512], F32)
    qkvb = pb.sb("qkvb", [128, 512], BF16)
    emit_ln_transpose_all(pb, xv, NT, xnfm, R, idb, xa, xqb, psT)
    for hp in range(8):
        for (wP, wrdy, bcol, brdy, dram_, elu) in ((wqP, t1r, bqc, tb1, qd, True),
                                                   (wkP, t2r, bkc, tb2, kd, True),
                                                   (wvP, t3r, bvc, tb3, vd, False)):
            for ch in range(NC):
                ps = psM[ch % 2]
                def mmp(e, hp=hp, ch=ch, ps=ps, wP=wP):
                    inst = None
                    for k in range(8):
                        inst = e.matmul(ps[:], wP[:, k * D + 128 * hp:k * D + 128 * (hp + 1)],
                                        xnfm[:, k * R + ch * 512:k * R + ch * 512 + 512],
                                        start=(k == 0), stop=(k == 7))
                    return inst
                tm = pb.op("tensor", mmp, waits=[T["xnfm"], wrdy, T.get(("psMp", ch % 2))],
                           inc="T")
                te = pb.op("scalar", lambda e, hp=hp, ps=ps, bcol=bcol: e.activation(
                    tmpa[:], ps[:], AF.Identity, bias=bcol[:, hp:hp + 1]),
                    waits=[tm, brdy, T.get("tmpa_rd")], inc="S")
                T[("psMp", ch % 2)] = te
                if elu:
                    t1 = pb.op("vector", lambda e: e.tensor_scalar_min(tmpb[:], tmpa[:], 0.0),
                               waits=[te, T.get("tmpb_rd")], inc="V")
                    t2 = pb.op("scalar", lambda e: e.activation(tmpb[:], tmpb[:], AF.Exp),
                               waits=[t1], inc="S")
                    t3 = pb.op("vector", lambda e: e.tensor_relu(tmpc[:], tmpa[:]),
                               waits=[te, T.get("tmpc_rd")], inc="V")
                    T["tmpa_rd"] = t3
                    t4 = pb.op("vector", lambda e: e.tensor_add(qkvb[:], tmpb[:], tmpc[:]),
                               waits=[t2, t3, T.get("qkvb_rd")], inc="V")
                    T["tmpb_rd"] = t4
                    T["tmpc_rd"] = t4
                    tev = t4
                else:
                    tev = pb.op("vector", lambda e: e.tensor_copy(qkvb[:], tmpa[:]),
                                waits=[te, T.get("qkvb_rd")], inc="V")
                    T["tmpa_rd"] = tev
                tw = pb.op("gpsimd", lambda e, hp=hp, ch=ch, dram_=dram_: e.dma_start(
                    out=dram_[128 * hp:128 * (hp + 1), 512 * ch:512 * (ch + 1)], in_=qkvb[:]),
                    waits=[tev], inc="Hs", inc_by=16)
                T["qkvb_rd"] = tw
                T["qkv_done"] = tw
    pb.pop_scope(gate=[T["qkv_done"]])
    pb.push_scope()
    ones1 = pb.sb("ones1a", [1, 64], F32)
    tON = pb.op("vector", lambda e: e.memset(ones1[:], 1.0), inc="V")
    qf = pb.sb("qf", [128, R], BF16)
    kf = pb.sb("kf", [128, R], BF16)
    vf = pb.sb("vf", [128, R], BF16)
    kq_tk = [pb.sb(f"kqtk{h}", [128, 32 * 64], BF16) for h in range(2)]
    v2tk = [pb.sb(f"v2tk{h}", [128, 32 * 65], BF16) for h in range(2)]
    kvs = pb.sb("kvs2", [128, 65], BF16)
    rcol = pb.sb("rcol", [1, 512], F32)
    rce = pb.sb("rce", [1, 512], F32)
    rep_sb = pb.sb("rep_sb", [64, 512], F32)
    yun = pb.sb("yun", [64, 512], F32)
    ybf = pb.sb("ybf", [64, 512], BF16)
    psV = pb.ps("psV2", [128, 128], BF16)
    psKV = [pb.ps(f"psKV{h}", [64, 65], F32) for h in range(2)]
    psY2 = [pb.ps(f"psY2{h}", [65, 512], F32) for h in range(2)]
    psR = pb.ps("psR2", [64, 512], F32)
    for hp in range(8):
        tq = pb.op("sync", lambda e, hp=hp: e.dma_start(
            out=qf[:], in_=qd[128 * hp:128 * (hp + 1), :]),
            waits=[T["qkv_done"], T.get("qf_rd")], inc="Xq", inc_by=16)
        tk = pb.op("sync", lambda e, hp=hp: e.dma_start(
            out=kf[:], in_=kd[128 * hp:128 * (hp + 1), :]),
            waits=[T["qkv_done"], T.get("kf_rd")], inc="Xq", inc_by=16)
        tv = pb.op("sync", lambda e, hp=hp: e.dma_start(
            out=vf[:], in_=vd[128 * hp:128 * (hp + 1), :]),
            waits=[T["qkv_done"], T.get("vf_rd")], inc="Xq", inc_by=16)
        trd = []
        for h in range(2):
            tz = pb.op("vector", lambda e, h=h: e.memset(v2tk[h][:], 1.0),
                       waits=[T.get(("v2g", h))], inc="V")
            for ka in range(32):
                tt1 = pb.op("tensor", lambda e, h=h, ka=ka: e.transpose(
                    psV[:, 0:64], kf[64 * h:64 * h + 64, 128 * ka:128 * (ka + 1)],
                    idb[64 * h:64 * h + 64, 64 * h:64 * h + 64]),
                    waits=[tk, T.get("psV_rd")] + T["id"], inc="T")
                ev1 = pb.op("vector", lambda e, h=h, ka=ka: e.tensor_copy(
                    kq_tk[h][:, 64 * ka:64 * (ka + 1)], psV[:, 0:64]),
                    waits=[tt1, T.get(("kqg", h))], inc="V")
                T["psV_rd"] = ev1
                tt2 = pb.op("tensor", lambda e, h=h, ka=ka: e.transpose(
                    psV[:, 0:64], vf[64 * h:64 * h + 64, 128 * ka:128 * (ka + 1)],
                    idb[64 * h:64 * h + 64, 64 * h:64 * h + 64]),
                    waits=[tv, ev1] + T["id"], inc="T")
                ev2 = pb.op("vector", lambda e, h=h, ka=ka: e.tensor_copy(
                    v2tk[h][:, 65 * ka:65 * ka + 64], psV[:, 0:64]),
                    waits=[tt2, tz], inc="V")
                T["psV_rd"] = ev2
                trd.append(ev2)
            if h == 1:
                T["kf_rd"] = tt2
                T["vf_rd"] = tt2
        for h in range(2):
            def mkv(e, h=h):
                inst = None
                for ka in range(32):
                    inst = e.matmul(psKV[h][:], kq_tk[h][:, 64 * ka:64 * (ka + 1)],
                                    v2tk[h][:, 65 * ka:65 * (ka + 1)],
                                    start=(ka == 0), stop=(ka == 31))
                return inst
            tkv = pb.op("tensor", mkv, waits=[trd[-1], T.get(("psKVg", h))], inc="T")
            tkc = pb.op("vector", lambda e, h=h: e.tensor_copy(
                kvs[64 * h:64 * h + 64, :], psKV[h][:]),
                waits=[tkv, T.get(("kvs_rd", h))], inc="V")
            T[("psKVg", h)] = tkc
            T[("kvsrdy", h)] = tkc
        for jc in range(NC):
            for h in range(2):
                tmy = pb.op("tensor", lambda e, h=h, jc=jc: e.matmul(
                    psY2[h][:], kvs[64 * h:64 * h + 64, :],
                    qf[64 * h:64 * h + 64, jc * 512:(jc + 1) * 512],
                    tile_position=(64 * h, 0), start=True, stop=True),
                    waits=[tq, T[("kvsrdy", h)], T.get(("psY2g", h))], inc="T")
                teps = pb.op("vector", lambda e, h=h: e.tensor_scalar(
                    rce[:], psY2[h][64:65, :], 1e-6, None, ALU.add, ALU.bypass),
                    waits=[tmy, T.get("rce_rd")], inc="V")
                trc = pb.op("vector", lambda e: e.reciprocal(rcol[:], rce[:]),
                            waits=[teps, T.get("rcol_rd")], inc="V")
                T["rce_rd"] = trc
                trp = pb.op("tensor", lambda e: e.matmul(
                    psR[:], ones1[:], rcol[:], start=True, stop=True),
                    waits=[trc, tON, T.get("psRg")], inc="T")
                tyu = pb.op("scalar", lambda e, h=h: e.copy(yun[:], psY2[h][0:64, :]),
                            waits=[teps, T.get("yun_rd")], inc="S")
                T[("psY2g", h)] = tyu
                trs = pb.op("vector", lambda e: e.tensor_copy(rep_sb[:], psR[:]),
                            waits=[trp, T.get("rep_rd")], inc="V")
                T["psRg"] = trs
                T["rcol_rd"] = trp
                tyn = pb.op("vector", lambda e: e.tensor_mul(ybf[:], yun[:], rep_sb[:]),
                            waits=[tyu, trs, T.get("ybf_rd")], inc="V")
                T["yun_rd"] = tyn
                T["rep_rd"] = tyn
                tw = pb.op("gpsimd", lambda e, hp=hp, h=h, jc=jc: e.dma_start(
                    out=yd[128 * hp + 64 * h:128 * hp + 64 * h + 64, jc * 512:(jc + 1) * 512],
                    in_=ybf[:]), waits=[tyn], inc="Hs", inc_by=16)
                T["ybf_rd"] = tw
                T["y_done"] = tw
        T["qf_rd"] = tmy
        for h in range(2):
            T[("v2g", h)] = tmy
            T[("kqg", h)] = T[("psKVg", h)]
            T[("kvs_rd", h)] = tmy
    pb.pop_scope(gate=[T["y_done"]])
    emit_outproj_residual(pb, yd, woP, wordy, boc, bordy, xv, ov, NC, idb, idf, False)
    pb.wait("gpsimd", ("St", pb.cnt["St"]))
    return pb.finalize()



def build_l2attn(R=4096):
    pb = PB()
    NN = 512
    xin = pb.dram("xin", [R, D], F32, kind="ExternalInput")
    nu = pb.dram("nu_log", [NN], F32, kind="ExternalInput")
    th = pb.dram("theta_log", [NN], F32, kind="ExternalInput")
    ga = pb.dram("gamma_log", [NN], F32, kind="ExternalInput")
    Bre = pb.dram("B_re", [NN, D], F32, kind="ExternalInput")
    Bim = pb.dram("B_im", [NN, D], F32, kind="ExternalInput")
    Cre = pb.dram("C_re", [D, NN], F32, kind="ExternalInput")
    Cim = pb.dram("C_im", [D, NN], F32, kind="ExternalInput")
    Dm = pb.dram("Dmat", [D, D], F32, kind="ExternalInput")
    xout = pb.dram("xout", [R, D], F32, kind="ExternalOutput")
    xsr_d = pb.dram("xsr_d", [NN, R], BF16)
    xsi_d = pb.dram("xsi_d", [NN, R], BF16)
    idf, idb = emit_identity(pb)
    alloc_ln(pb)
    T = pb.tag
    NT, NC = R // 128, R // 512
    xv = xin.rearrange("(t p) d -> t p d", p=128)
    ov = xout.rearrange("(t p) d -> t p d", p=128)
    nuc, tnu = emit_bias_cols(pb, nu, NN, "nu", idf)
    thc, tth = emit_bias_cols(pb, th, NN, "th", idf)
    gac, tga = emit_bias_cols(pb, ga, NN, "ga", idf)
    rho = pb.sb("rho", [128, 4], F32)
    the = pb.sb("the", [128, 4], F32)
    gam = pb.sb("gam", [128, 4], F32)
    t1 = pb.op("scalar", lambda e: e.activation(rho[:], nuc[:], AF.Exp), waits=[tnu], inc="S")
    t2 = pb.op("scalar", lambda e: e.activation(rho[:], rho[:], AF.Exp, scale=np.float32(-1.0)),
               waits=[t1], inc="S")
    t3 = pb.op("scalar", lambda e: e.activation(the[:], thc[:], AF.Exp), waits=[tth], inc="S")
    t4 = pb.op("scalar", lambda e: e.activation(gam[:], gac[:], AF.Exp), waits=[tga], inc="S")
    xnfm = pb.sb("xnfm", [128, 8 * R], BF16)
    pb.push_scope()
    alloc_wstage(pb, 2048, f8=False, tst=None)
    Bgr, tbr = emit_weight_T(pb, Bre, NN, D, "bgr", fold_cols=(gam, t4))
    Bgi, tbi = emit_weight_T(pb, Bim, NN, D, "bgi", fold_cols=(gam, t4))
    xa = [pb.sb(f"xa{i}", [128, D], F32) for i in range(2)]
    xqb = pb.sb("xqb", [128, D], BF16)
    psT = [pb.ps(f"psT{i}", [128, 128], BF16) for i in range(2)]
    psM = [pb.ps(f"psM{i}", [128, 512], F32) for i in range(2)]
    emit_ln_transpose_all(pb, xv, NT, xnfm, R, idb, xa, xqb, psT)
    br = pb.sb("br", [128, R], F32)
    bi = pb.sb("bi", [128, R], F32)
    s1 = pb.sb("s1", [128, R], F32)
    s2 = pb.sb("s2", [128, R], F32)
    cosT = pb.sb("cosT", [128, R], F32)
    sinT = pb.sb("sinT", [128, R], F32)

    for nt in range(4):
        evs = []
        for ch in range(NC):
            for ri, (wT, wrdy, dst) in enumerate(((Bgr, tbr, br), (Bgi, tbi, bi))):
                ps = psM[ri]
                def mmb(e, nt=nt, ch=ch, ps=ps, wT=wT):
                    inst = None
                    for k in range(8):
                        inst = e.matmul(ps[:], wT[:, k * NN + 128 * nt:k * NN + 128 * (nt + 1)],
                                        xnfm[:, k * R + ch * 512:k * R + ch * 512 + 512],
                                        start=(k == 0), stop=(k == 7))
                    return inst
                tm = pb.op("tensor", mmb, waits=[T["xnfm"], wrdy, T.get(("psMb", ri))], inc="T")
                te = pb.op("scalar", lambda e, ch=ch, ps=ps, dst=dst: e.copy(
                    dst[:, ch * 512:(ch + 1) * 512], ps[:]),
                    waits=[tm, T.get(("bu_rd", id(dst)))], inc="S")
                T[("psMb", ri)] = te
                evs.append(te)
        tio = pb.op("gpsimd", lambda e: e.iota(
            s1[:], [[1, R]], base=0, channel_multiplier=0,
            allow_small_or_imprecise_dtypes=True), waits=[T.get("s1_rd")], inc="G")
        tan = pb.op("vector", lambda e, nt=nt: e.tensor_scalar(
            s1[:], s1[:], the[:, nt:nt + 1], float(1.0 / (2 * np.pi)), ALU.mult, ALU.mult),
            waits=[tio, t3], inc="V")
        tk1 = pb.op("vector", lambda e: e.tensor_copy(pb.q_i32[:, 0:R] if False else s2[:].bitcast(I32), s1[:]), waits=[tan, T.get("s2_rd")], inc="V")
        tk2 = pb.op("vector", lambda e: e.tensor_copy(s2[:], s2[:].bitcast(I32)), waits=[tk1], inc="V")
        tk3 = pb.op("vector", lambda e: e.tensor_sub(s1[:], s1[:], s2[:]), waits=[tk2], inc="V")
        tk4 = pb.op("vector", lambda e: e.tensor_scalar(
            s1[:], s1[:], float(2 * np.pi), None, ALU.mult, ALU.bypass), waits=[tk3], inc="V")
        tab = pb.op("scalar", lambda e: e.activation(s2[:], s1[:], AF.Abs), waits=[tk4], inc="S")
        tco = pb.op("scalar", lambda e: e.activation(
            cosT[:], s2[:], AF.Sin, scale=np.float32(-1.0), bias=np.float32(np.pi / 2)),
            waits=[tab, T.get("cos_rd")], inc="S")
        tsi = pb.op("scalar", lambda e: e.activation(sinT[:], s1[:], AF.Sin),
                    waits=[tk4, T.get("sin_rd")], inc="S")
        w0 = evs + [tco, tsi]
        r1 = pb.op("vector", lambda e: e.tensor_mul(s1[:], cosT[:], br[:]), waits=w0 + [T.get("s1w")], inc="V")
        r2 = pb.op("vector", lambda e: e.tensor_mul(s2[:], sinT[:], bi[:]), waits=w0 + [T.get("s2w")], inc="V")
        r3 = pb.op("vector", lambda e: e.tensor_add(s1[:], s1[:], s2[:]), waits=[r1, r2], inc="V")
        r4 = pb.op("vector", lambda e: e.tensor_mul(s2[:], cosT[:], bi[:]), waits=[r3], inc="V")
        r5 = pb.op("vector", lambda e: e.tensor_mul(bi[:], sinT[:], br[:]), waits=[r4, T.get("bi_rd")], inc="V")
        r6 = pb.op("vector", lambda e: e.tensor_sub(s2[:], s2[:], bi[:]), waits=[r5], inc="V")
        T[("bu_rd", id(br))] = r5
        prev = [r3, r6]
        for ri, (src, dst) in enumerate(((s1, br), (s2, bi))):
            tprev = prev[ri]
            for c in range(NC):
                init = 0.0 if c == 0 else dst[:, c * 512 - 1:c * 512]
                tprev = pb.op("vector", lambda e, src=src, dst=dst, c=c, init=init, nt=nt:
                              e.tensor_tensor_scan(
                    dst[:, c * 512:(c + 1) * 512],
                    rho[:, nt:nt + 1].broadcast_to([128, 512]),
                    src[:, c * 512:(c + 1) * 512], init, ALU.mult, ALU.add),
                    waits=[tprev, t2], inc="V")
            prev[ri] = tprev
        T[("bu_rd", id(bi))] = prev[1]
        p1 = pb.op("vector", lambda e: e.tensor_mul(s1[:], cosT[:], br[:]), waits=prev, inc="V")
        p2 = pb.op("vector", lambda e: e.tensor_mul(s2[:], sinT[:], bi[:]), waits=prev, inc="V")
        p3 = pb.op("vector", lambda e: e.tensor_sub(s1[:], s1[:], s2[:]), waits=[p1, p2], inc="V")
        p4 = pb.op("vector", lambda e: e.tensor_mul(s2[:], cosT[:], bi[:]), waits=[p3], inc="V")
        p5 = pb.op("vector", lambda e: e.tensor_mul(bi[:], sinT[:], br[:]), waits=[p4], inc="V")
        T[("bu_rd", id(br))] = p5
        T["cos_rd"] = p4
        T["sin_rd"] = p5
        p6 = pb.op("vector", lambda e: e.tensor_add(s2[:], s2[:], bi[:]), waits=[p5], inc="V")
        T[("bu_rd", id(bi))] = p6
        w1 = pb.op("gpsimd", lambda e, nt=nt: e.dma_start(
            out=xsr_d[128 * nt:128 * (nt + 1), :], in_=s1[:]), waits=[p3], inc="Hs", inc_by=16)
        w2 = pb.op("gpsimd", lambda e, nt=nt: e.dma_start(
            out=xsi_d[128 * nt:128 * (nt + 1), :], in_=s2[:]), waits=[p6], inc="Hs", inc_by=16)
        T["s1w"] = w1
        T["s2w"] = w2
        T["s1_rd"] = w1
        T["s2_rd"] = w2
        T["xs_done"] = w2
    pb.pop_scope(gate=[T["xs_done"]])
    alloc_wstage(pb, 4096, f8=False, tst=None, sfx="y")
    CrT, tcr = emit_weight_T(pb, Cre, D, NN, "crt")
    CiT, tci = emit_weight_T(pb, Cim, D, NN, "cit", fold=-1.0)
    DmT, tdm = emit_weight_T(pb, Dm, D, D, "dmt")
    xmv = [pb.sb(f"xmv{i}", [128, 512], BF16) for i in range(8)]
    psM2 = [pb.ps(f"ps2y{i}", [128, 512], F32) for i in range(2)]
    psY = [pb.ps(f"psYy{i}", [128, 128], BF16) for i in range(2)]
    yofm = pb.sb("yofm2", [128, 8 * 512], BF16)
    ytk = pb.sb("ytk2", [128, D], BF16)
    xar = [pb.sb(f"xry{i}", [128, D], F32) for i in range(2)]
    for ch in range(NC):
        lds = []
        for k in range(4):
            tl = pb.op("sync", lambda e, k=k, ch=ch: e.dma_start(
                out=xmv[k][:], in_=xsr_d[128 * k:128 * (k + 1), 512 * ch:512 * (ch + 1)]),
                waits=[T["xs_done"], T.get(("xmv", k))], inc="Xq", inc_by=16)
            tl2 = pb.op("sync", lambda e, k=k, ch=ch: e.dma_start(
                out=xmv[4 + k][:], in_=xsi_d[128 * k:128 * (k + 1), 512 * ch:512 * (ch + 1)]),
                waits=[T["xs_done"], T.get(("xmv", 4 + k))], inc="Xq", inc_by=16)
            lds += [tl, tl2]
        for m in range(8):
            ps = psM2[m % 2]
            def mmy(e, m=m, ch=ch, ps=ps):
                inst = None
                for k in range(4):
                    inst = e.matmul(ps[:], CrT[:, k * D + 128 * m:k * D + 128 * (m + 1)],
                                    xmv[k][:], start=(k == 0), stop=False)
                for k in range(4):
                    inst = e.matmul(ps[:], CiT[:, k * D + 128 * m:k * D + 128 * (m + 1)],
                                    xmv[4 + k][:], start=False, stop=False)
                for k in range(8):
                    inst = e.matmul(ps[:], DmT[:, k * D + 128 * m:k * D + 128 * (m + 1)],
                                    xnfm[:, k * R + ch * 512:k * R + ch * 512 + 512],
                                    start=False, stop=(k == 7))
                return inst
            tm = pb.op("tensor", mmy, waits=lds + [tcr, tci, tdm, T["xnfm"],
                       T.get(("psM2y", m % 2))], inc="T")
            if m == 7:
                for k in range(8):
                    T[("xmv", k)] = tm
            te = pb.op("scalar", lambda e, m=m, ps=ps: e.copy(
                yofm[:, m * 512:(m + 1) * 512], ps[:]),
                waits=[tm, T.get(("yofmy_rd", m))], inc="S")
            T[("psM2y", m % 2)] = te
            T[("yofmy", m)] = te
        for ti in range(4):
            t = 4 * ch + ti
            trb = []
            for m in range(8):
                ps = psY[m % 2]
                tt = pb.op("tensor", lambda e, m=m, ti=ti, ps=ps: e.transpose(
                    ps[:], yofm[:, m * 512 + ti * 128: m * 512 + ti * 128 + 128], idb[:]),
                    waits=[T[("yofmy", m)], T.get(("psYy", m % 2))] + T["id"], inc="T")
                if ti == 3:
                    T[("yofmy_rd", m)] = tt
                ev = pb.op("vector", lambda e, m=m, ps=ps: e.tensor_copy(
                    ytk[:, 128 * m:128 * (m + 1)], ps[:]),
                    waits=[tt, T.get("ytky_rd")], inc="V")
                T[("psYy", m % 2)] = ev
                trb.append(ev)
            a = xar[ti % 2]
            tlx = pb.op("sync", lambda e, t=t, a=a: e.dma_start(out=a[:], in_=xv[t]),
                        waits=[T.get(("xay", ti % 2))], inc="Xq", inc_by=16)
            tad = pb.op("vector", lambda e, a=a: e.tensor_add(a[:], a[:], ytk[:]),
                        waits=trb + [tlx], inc="V")
            T["ytky_rd"] = tad
            tst = pb.op("scalar", lambda e, t=t, a=a: e.dma_start(out=ov[t], in_=a[:]),
                        waits=[tad], inc="St", inc_by=16)
            T[("xay", ti % 2)] = tst
    pb.wait("gpsimd", ("St", pb.cnt["St"]))
    return pb.finalize()



_PROG_CACHE = {}
_MASK_CACHE = {}


def _get_prog(name, builder):
    if name not in _PROG_CACHE:
        _PROG_CACHE[name] = builder()
    return _PROG_CACHE[name]


def _threefry2x32(k0, k1, x0, x1):
    u32 = np.uint32
    ks0, ks1 = k0.astype(u32), k1.astype(u32)
    ks2 = (ks0 ^ ks1 ^ u32(0x1BD11BDA)).astype(u32)
    x0 = (x0 + ks0).astype(u32)
    x1 = (x1 + ks1).astype(u32)
    rot = [[13, 15, 26, 6], [17, 29, 16, 24]]
    ks = [ks0, ks1, ks2]
    for i in range(5):
        for r in rot[i % 2]:
            x0 = (x0 + x1).astype(u32)
            x1 = ((x1 << u32(r)) | (x1 >> u32(32 - r))).astype(u32)
            x1 = (x1 ^ x0).astype(u32)
        x0 = (x0 + ks[(i + 1) % 3]).astype(u32)
        x1 = (x1 + ks[(i + 2) % 3] + u32(i + 1)).astype(u32)
    return x0, x1


def _dropout_mask05():
    if "m" not in _MASK_CACHE:
        u32 = np.uint32
        n = 4 * 4096 * 1024
        j = np.arange(n, dtype=u32)
        z = np.zeros(n, u32)
        o0, o1 = _threefry2x32(np.full(n, 0, u32), np.full(n, 42, u32), z, j)
        bits = (o0 ^ o1).astype(u32)
        u = ((bits >> u32(9)) | u32(0x3F800000)).view(np.float32) - np.float32(1.0)
        keep = u < np.float32(0.8)
        _MASK_CACHE["m"] = np.where(keep, np.float32(0.5 / 0.8),
                                    np.float32(0.0)).astype(np.float32).reshape(16384, 1024)
    return _MASK_CACHE["m"]


def _run_ffn(name, builder, x, per_core_ins):
    nc = _get_prog(name, builder)
    im = []
    for c in range(8):
        d = {"xin": np.ascontiguousarray(x[c * 2048:(c + 1) * 2048])}
        d.update(per_core_ins(c))
        im.append(d)
    res = run_bass_kernel_spmd(nc, im, core_ids=list(range(8)))
    return np.concatenate([res.results[c]["xout"] for c in range(8)], 0)


def _run_attn(name, builder, x, common_ins):
    nc = _get_prog(name, builder)
    im = []
    for c in range(8):
        b = c // 2
        d = {"xin": np.ascontiguousarray(x[b * 4096:(b + 1) * 4096])}
        d.update(common_ins)
        im.append(d)
    res = run_bass_kernel_spmd(nc, im, core_ids=list(range(8)))
    out = np.empty_like(x)
    for c in range(8):
        b, h = c // 2, c % 2
        out[b * 4096 + h * 2048: b * 4096 + (h + 1) * 2048] = \
            res.results[c]["xout"][h * 2048:(h + 1) * 2048]
    return out


def kernel(**P):
    x = np.ascontiguousarray(np.asarray(P["L0_attn_x"], np.float32).reshape(16384, 1024))
    mask = _dropout_mask05()

    l0f = lambda xx: _run_ffn("l0ffn", build_l0ffn, xx, lambda c: {
        "w1": P["L0_ffn_w1"], "b1": P["L0_ffn_b1"], "w2": P["L0_ffn_w2"],
        "b2": P["L0_ffn_b2"], "mask": np.ascontiguousarray(mask[c * 2048:(c + 1) * 2048])})
    l1f = lambda xx: _run_ffn("l1ffn", build_l1ffn, xx, lambda c: {
        "W_exp": P["L1_ffn_W_exp"], "b_exp": P["L1_ffn_b_exp"], "up_w": P["L1_ffn_up_weights"],
        "up_b": P["L1_ffn_up_bias"], "dn_w": P["L1_ffn_down_weights"],
        "W_con": P["L1_ffn_W_con"], "b_con": P["L1_ffn_b_con"], "dn_b": P["L1_ffn_down_bias"]})
    l2f = lambda xx: _run_ffn("l2ffn", build_l2ffn, xx, lambda c: {
        "w_gate": P["L2_ffn_w_gate"], "w_up": P["L2_ffn_w_up"], "w_down": P["L2_ffn_w_down"]})
    l3f = lambda xx: _run_ffn("l3ffn", build_l3ffn, xx, lambda c: {
        "w_gate": P["L3_ffn_w_gate"], "w_up": P["L3_ffn_w_up"], "w_down": P["L3_ffn_w_down"]})

    x = l0f(x)
    x = _run_attn("l0attn", lambda: build_causal_attn("l0"), x, {
        "Wq": P["L0_attn_Wq"], "Wk": P["L0_attn_Wk"], "Wv": P["L0_attn_Wv"],
        "Wu": P["L0_attn_Wu"], "bu": P["L0_attn_bu"]})
    x = l0f(x)
    x = l1f(x)
    x = _run_attn("l1attn", build_l1attn, x, {
        "Wq": P["L1_attn_Wq"], "bq": P["L1_attn_bq"], "Wk": P["L1_attn_Wk"],
        "bk": P["L1_attn_bk"], "Wv": P["L1_attn_Wv"], "bv": P["L1_attn_bv"],
        "Wo": P["L1_attn_Wo"], "bo": P["L1_attn_bo"]})
    x = l1f(x)
    x = l2f(x)
    x = _run_attn("l2attn", build_l2attn, x, {
        "nu_log": P["L2_attn_nu_log"], "theta_log": P["L2_attn_theta_log"],
        "gamma_log": P["L2_attn_gamma_log"], "B_re": P["L2_attn_B_re"],
        "B_im": P["L2_attn_B_im"], "C_re": P["L2_attn_C_re"], "C_im": P["L2_attn_C_im"],
        "Dmat": P["L2_attn_D"]})
    x = l2f(x)
    x = l3f(x)
    x = _run_attn("l3attn", lambda: build_causal_attn("l3"), x, {
        "w_in": P["L3_attn_w_in"], "b_in": P["L3_attn_b_in"],
        "w_out": P["L3_attn_w_out"], "b_out": P["L3_attn_b_out"]})
    x = l3f(x)
    return x.reshape(4, 4096, 1024)


# version 04
# speedup vs baseline: 1.0523x; 1.0523x over previous
import math
import numpy as np
from contextlib import ExitStack

import concourse.bass as bass
import concourse.mybir as mybir
from concourse.bass_utils import run_bass_kernel_spmd

F32 = mybir.dt.float32
BF16 = mybir.dt.bfloat16
F16 = mybir.dt.float16
F8 = mybir.dt.float8e4
I32 = mybir.dt.int32
AF = mybir.ActivationFunctionType
ALU = mybir.AluOpType
AX = mybir.AxisListType

ENGS = ("tensor", "vector", "scalar", "gpsimd", "sync")
NCORES = 8
D = 1024
EPS = 1e-5
PRUNE_B = False


class PB:

    def __init__(self):
        self.nc = bass.Bass("TRN2", target_bir_lowering=False)
        self.es = ExitStack()
        self.ops = {k: [] for k in ENGS}
        self.sem = {}
        self.cnt = {}
        self._uid = 0
        self.tag = {}
        self._owner = {}
        self._ewait = {k: {} for k in ENGS}
        self._scopes = []
        self.mem_gate = {}
        self.prefix = ""

    def uid(self):
        self._uid += 1
        return self._uid

    def semaphore(self, name):
        if name not in self.sem:
            self.sem[name] = self.es.enter_context(self.nc.semaphore(name))
            self.cnt[name] = 0
        return self.sem[name]

    def sb(self, name, shape, dtype=F32):
        es = self._scopes[-1] if self._scopes else self.es
        return es.enter_context(self.nc.sbuf_tensor(self.prefix + name, list(shape), dtype))

    def push_scope(self):
        self._scopes.append(ExitStack())

    def pop_scope(self, gate=()):
        self._scopes.pop().close()
        for t in (gate if isinstance(gate, (list, tuple)) else [gate]):
            if t is not None and t[1] > self.mem_gate.get(t[0], -1):
                self.mem_gate[t[0]] = t[1]

    def ps(self, name, shape, dtype=F32):
        es = self._scopes[-1] if self._scopes else self.es
        return es.enter_context(self.nc.psum_tensor(self.prefix + name, list(shape), dtype))

    def dram(self, name, shape, dtype=F32, kind="Internal"):
        nm = name if kind != "Internal" else self.prefix + name
        return self.nc.dram_tensor(nm, list(shape), dtype, kind=kind).ap()

    def const(self, arr, name=None):
        nm = self.prefix + (name or f"c{self.uid()}")
        return self.nc.inline_tensor(np.ascontiguousarray(arr), nm).ap()

    def _prune(self, eng, w):
        best = {}
        for s, v in w:
            if v > best.get(s, -1):
                best[s] = v
        ew = self._ewait[eng]
        out = []
        for s, v in best.items():
            if PRUNE_B and self._owner.get(s, 0) == eng and v <= self.cnt.get(s, 0):
                continue
            if ew.get(s, -1) >= v:
                continue
            out.append((s, v))
            ew[s] = v
        return out

    def op(self, eng, fn, waits=(), inc=None, inc_by=1):
        w = [x for x in waits if x is not None] + list(self.mem_gate.items())
        tag = None
        if inc is not None:
            self.semaphore(inc)
            own = self._owner.setdefault(inc, None if inc_by == 16 else eng)
            if inc_by == 16 or own != eng:
                self._owner[inc] = None
            if inc_by == 16 and self.cnt[inc] > 0:
                w.append((inc, self.cnt[inc]))
        w = self._prune(eng, w)
        if inc is not None:
            self.cnt[inc] += inc_by
            tag = (inc, self.cnt[inc])
        for s, _ in w:
            self.semaphore(s)
        sems = self.sem

        def run(e, w=tuple(w), fn=fn, tag=tag, inc_by=inc_by):
            for (s, v) in w:
                e.wait_ge(sems[s], v)
            inst = fn(e)
            if tag is not None:
                assert inst is not None
                inst.then_inc(sems[tag[0]], inc_by)

        self.ops[eng].append(run)
        return tag

    def wait(self, eng, *tags):
        for s, v in self._prune(eng, [t for t in tags if t is not None]):
            self.semaphore(s)
            self.ops[eng].append(lambda e, s=s, v=v: e.wait_ge(self.sem[s], v))

    def finalize(self):
        with self.es, self.nc.Block() as block:
            for name in ENGS:
                getattr(block, name)(lambda e, name=name: [f(e) for f in self.ops[name]])
        return self.nc



def emit_identity(pb):
    if getattr(pb, "_idf", None) is not None:
        return pb._idf, pb._idb
    eye = np.eye(128, dtype=np.float32)
    h = pb.const(eye, "ident128")
    idf = pb.sb("ident_f32", [128, 128], F32)
    idb = pb.sb("ident_bf16", [128, 128], BF16)
    t1 = pb.op("sync", lambda e: e.dma_start(out=idf[:], in_=h), inc="Wq", inc_by=16)
    t2 = pb.op("vector", lambda e: e.tensor_copy(idb[:], idf[:]), waits=[t1], inc="V")
    pb.tag["id"] = [t1, t2]
    pb._idf, pb._idb = idf, idb
    return idf, idb


def alloc_wstage(pb, elems=8192, f8=True, tst=4096, sfx=""):
    pb.w_elems = elems
    pb.w_st32 = pb.sb("w_st32" + sfx, [128, elems], F32)
    pb.w_stq = pb.sb("w_stq" + sfx, [128, elems], F8) if f8 else None
    pb.w_stbf = pb.sb("w_stbf" + sfx, [128, elems], BF16)
    pb.w_tst = pb.sb("w_tst" + sfx, [128, tst], BF16) if tst else None
    pb.tag["wstage"] = None


def emit_weight_T(pb, w_ap, O, I, name, quant=None, fold=None, dst_dtype=BF16,
                  fold_first_rows=None, wT_pre=None, fold_cols=None):
    KT = I // 128
    wT = wT_pre if wT_pre is not None else pb.sb(f"{name}T", [128, KT * O], dst_dtype)
    pre = pb.dram(f"{name}_pre", [O, I], BF16)
    ot = min(max(1, pb.w_elems // I), O // 128)
    oc = ot * 128
    n_oc = O // oc
    assert O % oc == 0
    st32, stq, stbf = pb.w_st32, pb.w_stq, pb.w_stbf
    wv = w_ap.rearrange("(a p) i -> a p i", p=128)
    prev = pb.tag.get("wstage")
    last = None
    for c in range(n_oc):
        tags = []
        for r in range(ot):
            tl = pb.op("sync", lambda e, c=c, r=r: e.dma_start(
                out=st32[:, r * I:(r + 1) * I], in_=wv[c * ot + r]),
                waits=[last, prev], inc="Wq", inc_by=16)
            tags.append(tl)
        if quant == 'e4m3':
            tq = pb.op("vector", lambda e: e.tensor_copy(
                stq[:, :ot * I], st32[:, :ot * I]), waits=tags + [prev], inc="V")
            tb = pb.op("vector", lambda e: e.tensor_copy(
                stbf[:, :ot * I], stq[:, :ot * I]), waits=[tq], inc="V")
        elif callable(quant):
            tb = quant(pb, st32[:, :ot * I], stbf[:, :ot * I], ot * I, tags + [prev])
        else:
            tb = pb.op("vector", lambda e: e.tensor_copy(
                stbf[:, :ot * I], st32[:, :ot * I]), waits=tags + [prev], inc="V")
        if fold is not None:
            tb = pb.op("vector", lambda e: e.tensor_scalar(
                stbf[:, :ot * I], stbf[:, :ot * I], float(fold), None, ALU.mult, ALU.bypass),
                waits=[tb], inc="V")
        if fold_cols is not None:
            for r in range(ot):
                tb = pb.op("vector", lambda e, r=r, c=c: e.tensor_scalar(
                    stbf[:, r * I:(r + 1) * I], stbf[:, r * I:(r + 1) * I],
                    fold_cols[0][:, c * ot + r:c * ot + r + 1], None, ALU.mult, ALU.bypass),
                    waits=[tb, fold_cols[1]], inc="V")
        if fold_first_rows is not None and c * oc < fold_first_rows[0]:
            nr = min(ot, (fold_first_rows[0] - c * oc) // 128)
            tb = pb.op("vector", lambda e, nr=nr: e.tensor_scalar(
                stbf[:, :nr * I], stbf[:, :nr * I], float(fold_first_rows[1]), None,
                ALU.mult, ALU.bypass), waits=[tb], inc="V")
        prev_w = [tb]
        for r in range(ot):
            last = pb.op("gpsimd", lambda e, c=c, r=r: e.dma_start(
                out=pre.rearrange("(a p) i -> a p i", p=128)[c * ot + r],
                in_=stbf[:, r * I:(r + 1) * I]),
                waits=prev_w, inc="Ws", inc_by=16)
        
    rdy = []
    tprev = None
    for k in range(KT):
        if dst_dtype == BF16:
            t = pb.op("sync", lambda e, k=k: e.dma_start(
                wT[:, k * O:(k + 1) * O], pre[:, 128 * k:128 * (k + 1)], transpose=True),
                waits=[last], inc="Wq", inc_by=16)
        else:
            tt = pb.op("sync", lambda e, k=k, tst=pb.w_tst: e.dma_start(
                tst[:, :O], pre[:, 128 * k:128 * (k + 1)], transpose=True),
                waits=[last, tprev], inc="Wq", inc_by=16)
            t = pb.op("vector", lambda e, k=k, tst=pb.w_tst: e.tensor_copy(
                wT[:, k * O:(k + 1) * O], tst[:, :O]), waits=[tt], inc="V")
            tprev = t
        rdy.append(t)
    pb.tag["wstage"] = rdy[-1]
    return wT, rdy[-1]


def emit_bias_cols(pb, b_ap, n, name, idf, quant=None):
    m = n // 128
    stage = pb.sb(f"{name}_stg", [m, 128], F32)
    out = pb.sb(f"{name}c", [128, m], F32)
    pst = pb.ps(f"{name}_ps", [128, m], F32)
    t1 = pb.op("sync", lambda e: e.dma_start(
        out=stage[:], in_=b_ap.rearrange("(a p) -> a p", p=128)), inc="Wq", inc_by=16)
    t2 = pb.op("tensor", lambda e: e.transpose(pst[:], stage[:], idf[0:m, 0:m]),
               waits=[t1] + pb.tag["id"], inc="T")
    if quant == 'e4m3':
        q8 = pb.sb(f"{name}q8", [128, m], F8)
        t3 = pb.op("vector", lambda e: e.tensor_copy(q8[:], pst[:]), waits=[t2], inc="V")
        t4 = pb.op("vector", lambda e: e.tensor_copy(out[:], q8[:]), waits=[t3], inc="V")
        return out, t4
    t3 = pb.op("vector", lambda e: e.tensor_copy(out[:], pst[:]), waits=[t2], inc="V")
    return out, t3


def alloc_ln(pb):
    pb.ln_bufs = (pb.sb("ln_bn6", [128, 12], F32), pb.sb("ln_mv", [128, 2], F32),
                  pb.sb("ln_rstd", [128, 1], F32), pb.sb("ln_negmr", [128, 1], F32))
    pb.tag["ln_prev"] = None


def emit_ln(pb, xa, xn_out, waits):
    bn6, mv, rstd, negmr = pb.ln_bufs
    prev = pb.tag.get("ln_prev")
    t0 = pb.op("vector", lambda e: e.bn_stats(bn6[:, 0:6], xa[:, 0:512]),
               waits=list(waits) + [prev], inc="V")
    t1 = pb.op("vector", lambda e: e.bn_stats(bn6[:, 6:12], xa[:, 512:1024]),
               waits=[t0], inc="V")
    t2 = pb.op("vector", lambda e: e.bn_aggr(mv[:], bn6[:]), waits=[t1], inc="V")
    t3 = pb.op("scalar", lambda e: e.activation(rstd[:], mv[:, 1:2], AF.Ln, bias=np.float32(EPS)),
               waits=[t2], inc="S")
    t4 = pb.op("scalar", lambda e: e.activation(rstd[:], rstd[:], AF.Exp, scale=np.float32(-0.5)),
               waits=[t3], inc="S")
    t5 = pb.op("vector", lambda e: e.tensor_scalar(
        negmr[:], mv[:, 0:1], rstd[:, 0:1], -1.0, ALU.mult, ALU.mult), waits=[t4], inc="V")
    t6 = pb.op("scalar", lambda e: e.activation(
        xn_out[:], xa[:], AF.Identity, bias=negmr[:, 0:1], scale=rstd[:, 0:1]),
        waits=[t5], inc="S")
    pb.tag["ln_prev"] = t6
    return t6



def build_l0ffn(R=2048, pb=None, io=None):
    fused = pb is not None
    if not fused:
        pb = PB()
    gi = lambda n, s, d=F32: io[n] if fused else pb.dram(n, s, d, kind="ExternalInput")
    xin = gi("xin", [R, D])
    w1 = gi("w1", [4096, D])
    b1 = gi("b1", [4096])
    w2 = gi("w2", [D, 4096])
    b2 = gi("b2", [D])
    mask = gi("mask", [R, D])
    xout = io["xout"] if fused else pb.dram("xout", [R, D], F32, kind="ExternalOutput")

    idf, idb = emit_identity(pb)
    alloc_wstage(pb)
    alloc_ln(pb)
    w1T, w1rdy = emit_weight_T(pb, w1, 4096, D, "w1", quant='e4m3', dst_dtype=F8)
    w2T, w2rdy = emit_weight_T(pb, w2, D, 4096, "w2", quant='e4m3', dst_dtype=F8)
    b1c, b1rdy = emit_bias_cols(pb, b1, 4096, "b1", idf, quant='e4m3')
    b2c, b2rdy = emit_bias_cols(pb, b2, D, "b2", idf, quant='e4m3')

    xa = [pb.sb(f"xa{i}", [128, D], F32) for i in range(2)]
    xn = pb.sb("xn", [128, D], F32)
    xq8 = pb.sb("xq8", [128, D], F8)
    xqb = pb.sb("xqb", [128, D], BF16)
    xfm = pb.sb("xfm", [128, 8 * 512], BF16)
    psT = [pb.ps(f"psT{i}", [128, 128], BF16) for i in range(2)]
    psM = [pb.ps(f"psM{i}", [128, 512], F32) for i in range(2)]
    h16 = pb.sb("h16", [128, 512], F16)
    hfull = pb.sb("hfull", [128, 32 * 512], F8)
    y16 = pb.sb("y16", [128, 512], F16)
    yfm = pb.sb("yfm", [128, 8 * 512], F32)
    psY = [pb.ps(f"psY{i}", [128, 128], F32) for i in range(2)]
    ytk = pb.sb("ytk", [128, D], F32)
    mk = pb.sb("mk", [128, D], F32)

    xv = xin.rearrange("(t p) d -> t p d", p=128)
    ov = xout.rearrange("(t p) d -> t p d", p=128)
    mkv = mask.rearrange("(t p) d -> t p d", p=128)
    T = pb.tag

    for n in range(R // 512):
        tr_tags = []
        for ti in range(4):
            t = 4 * n + ti
            a = xa[ti % 2]
            tl = pb.op("sync", lambda e, t=t, a=a: e.dma_start(out=a[:], in_=xv[t]),
                       waits=[T.get(("xa", ti % 2))], inc="Xq", inc_by=16)
            tln = emit_ln(pb, a, xn, [tl, T.get("xn_rd")])
            T[("xa", ti % 2)] = tln
            t8 = pb.op("vector", lambda e: e.tensor_copy(xq8[:], xn[:]),
                       waits=[tln, T.get("xq8_rd")], inc="V")
            T["xn_rd"] = t8
            tb = pb.op("vector", lambda e: e.tensor_copy(xqb[:], xq8[:]),
                       waits=[t8, T.get("xqb_rd")], inc="V")
            T["xq8_rd"] = tb
            for f in range(8):
                ps = psT[f % 2]
                tt = pb.op("tensor", lambda e, f=f, ps=ps: e.transpose(
                    ps[:], xqb[:, 128 * f:128 * (f + 1)], idb[:]),
                    waits=[tb, T.get(("psT", f % 2))] + T["id"], inc="T")
                ev = pb.op("vector", lambda e, f=f, ti=ti, ps=ps: e.tensor_copy(
                    xfm[:, f * 512 + ti * 128: f * 512 + ti * 128 + 128], ps[:]),
                    waits=[tt], inc="V")
                T[("psT", f % 2)] = ev
                tr_tags.append(ev)
                if f == 7:
                    T["xqb_rd"] = tt
        for m in range(32):
            ps = psM[m % 2]
            def mm(e, m=m, ps=ps):
                inst = None
                for k in range(8):
                    inst = e.matmul(ps[:], w1T[:, k * 4096 + 128 * m: k * 4096 + 128 * (m + 1)],
                                    xfm[:, k * 512:(k + 1) * 512],
                                    start=(k == 0), stop=(k == 7))
                return inst
            tm = pb.op("tensor", mm, waits=tr_tags + [w1rdy, T.get(("psM", m % 2))], inc="T")
            te = pb.op("scalar", lambda e, m=m, ps=ps: e.activation(
                h16[:], ps[:], AF.Identity, bias=b1c[:, m:m + 1]),
                waits=[tm, b1rdy, T.get("h16_rd")], inc="S")
            T[("psM", m % 2)] = te
            tr8 = pb.op("vector", lambda e, m=m: e.tensor_relu(
                hfull[:, m * 512:(m + 1) * 512], h16[:]), waits=[te], inc="V")
            T["h16_rd"] = tr8
        hread = []
        for m in range(8):
            ps = psM[m % 2]
            def mmy(e, m=m, ps=ps):
                inst = None
                for k in range(32):
                    inst = e.matmul(ps[:], w2T[:, k * D + 128 * m: k * D + 128 * (m + 1)],
                                    hfull[:, k * 512:(k + 1) * 512],
                                    start=(k == 0), stop=(k == 31))
                return inst
            tm = pb.op("tensor", mmy,
                       waits=[T["h16_rd"], w2rdy, T.get(("psM", m % 2))], inc="T")
            hread.append(tm)
            te = pb.op("scalar", lambda e, m=m, ps=ps: e.activation(
                y16[:], ps[:], AF.Identity, bias=b2c[:, m:m + 1]),
                waits=[tm, b2rdy, T.get("y16_rd")], inc="S")
            T[("psM", m % 2)] = te
            tc = pb.op("vector", lambda e, m=m: e.tensor_copy(
                yfm[:, m * 512:(m + 1) * 512], y16[:]), waits=[te, T.get(("yfm_rd", m))], inc="V")
            T["y16_rd"] = tc
            T[("yfm", m)] = tc
        for ti in range(4):
            t = 4 * n + ti
            trb = []
            for m in range(8):
                ps = psY[m % 2]
                tt = pb.op("tensor", lambda e, m=m, ti=ti, ps=ps: e.transpose(
                    ps[:], yfm[:, m * 512 + ti * 128: m * 512 + ti * 128 + 128], idf[:]),
                    waits=[T[("yfm", m)], T.get(("psY", m % 2))] + T["id"], inc="T")
                if ti == 3:
                    T[("yfm_rd", m)] = tt
                ev = pb.op("vector", lambda e, m=m, ps=ps: e.tensor_copy(
                    ytk[:, 128 * m:128 * (m + 1)], ps[:]),
                    waits=[tt, T.get("ytk_rd")], inc="V")
                T[("psY", m % 2)] = ev
                trb.append(ev)
            tlm = pb.op("sync", lambda e, t=t: e.dma_start(out=mk[:], in_=mkv[t]),
                        waits=[T.get("mk_rd")], inc="Xq", inc_by=16)
            a = xa[ti % 2]
            tlx = pb.op("sync", lambda e, t=t, a=a: e.dma_start(out=a[:], in_=xv[t]),
                        waits=[T.get(("xa", ti % 2))], inc="Xq", inc_by=16)
            tmu = pb.op("vector", lambda e: e.tensor_mul(ytk[:], ytk[:], mk[:]),
                        waits=trb + [tlm], inc="V")
            T["mk_rd"] = tmu
            tad = pb.op("vector", lambda e, a=a: e.tensor_add(a[:], a[:], ytk[:]),
                        waits=[tmu, tlx], inc="V")
            T["ytk_rd"] = tad
            tst = pb.op("scalar", lambda e, t=t, a=a: e.dma_start(out=ov[t], in_=a[:]),
                        waits=[tad], inc="St", inc_by=16)
            T[("xa", ti % 2)] = tst
    if fused:
        return [("St", pb.cnt["St"])]
    pb.wait("gpsimd", ("St", pb.cnt["St"]))
    return pb.finalize()



def emit_l1_compose(pb, Wexp, bexp, upw, upb, dnw, Wcon, idf):
    T = pb.tag
    DH = 4096
    beff = pb.sb("beff", [128, 32], F32)

    WexpT_dram = pb.dram("WexpT", [DH, D], F32)
    R1p_dram = pb.dram("R1p", [DH, 1025], F32)
    S1_dram = pb.dram("S1d", [DH, D], F32)
    A_dram = pb.dram("A_dram", [DH, 1025], F32)
    Wc_dram = pb.dram("Wc_dram", [DH, D], F32)

    pb.push_scope()
    wstg = pb.sb("wstg", [128, 1024], F32)
    tstg = pb.sb("tstg", [128, 1024], F32)
    psW = [pb.ps(f"psW{i}", [128, 128], F32) for i in range(2)]
    wev = Wexp.rearrange("(r p) c -> r p c", p=128)
    for cc in range(32):
        ld = []
        for r in range(8):
            tl = pb.op("sync", lambda e, cc=cc, r=r: e.dma_start(
                out=wstg[:, r * 128:(r + 1) * 128], in_=wev[r, :, 128 * cc:128 * (cc + 1)]),
                waits=[T.get("wstg_rd")], inc="Wq", inc_by=16)
            ld.append(tl)
        evs = []
        for r in range(8):
            ps = psW[r % 2]
            tt = pb.op("tensor", lambda e, r=r, ps=ps: e.transpose(
                ps[:], wstg[:, r * 128:(r + 1) * 128], idf[:]),
                waits=ld + [T.get(("psW", r % 2))] + T["id"], inc="T")
            ev = pb.op("vector", lambda e, r=r, ps=ps: e.tensor_copy(
                tstg[:, r * 128:(r + 1) * 128], ps[:]), waits=[tt, T.get("tstg_rd")], inc="V")
            T[("psW", r % 2)] = ev
            evs.append(ev)
            if r == 7:
                T["wstg_rd"] = tt
        tw = pb.op("gpsimd", lambda e, cc=cc: e.dma_start(
            out=WexpT_dram[128 * cc:128 * (cc + 1), :], in_=tstg[:]),
            waits=evs, inc="Ws", inc_by=16)
        T["tstg_rd"] = tw
        T["wexpT_done"] = tw

    C0 = np.zeros((64, DH), np.float32)
    for g in range(64):
        C0[:, g * 64:(g + 1) * 64] = np.eye(64, dtype=np.float32)
    D0 = np.zeros((64, DH), np.float32)
    for r in range(64):
        D0[r, r * 64:(r + 1) * 64] = 1.0
    hC0 = pb.const(C0, "C0"); hD0 = pb.const(D0, "D0")
    X = pb.sb("bfX", [64, DH], F32)
    X2 = pb.sb("bfX2", [64, DH], F32)
    TMP = pb.sb("bfTMP", [64, DH // 2], F32)
    coef = [pb.sb(f"coef{i}", [64, DH // 2], F32) for i in range(4)]
    srow = pb.sb("srow", [1, 8192], F32)
    ones1 = pb.sb("ones1", [1, 64], F32)
    psR = pb.ps("psR", [64, 512], F32)
    t_on = pb.op("vector", lambda e: e.memset(ones1[:], 1.0), inc="V")

    def load_coefs(w_ap, stage):
        tl = pb.op("sync", lambda e: e.dma_start(out=srow[:], in_=w_ap[stage].rearrange(
            "k a b -> (k a b)").rearrange("f -> 1 f") if False else
            w_ap.rearrange("s k a b -> s (k a b)")[stage:stage + 1, :]),
            waits=[T.get("srow_rd")], inc="Wq", inc_by=16)
        tags = []
        for ij in range(4):
            for ch in range(4):
                tm = pb.op("tensor", lambda e, ij=ij, ch=ch: e.matmul(
                    psR[:], ones1[:], srow.rearrange("p (k four) -> p k four", four=4)
                    [:, 512 * ch:512 * (ch + 1), ij], start=True, stop=True),
                    waits=[tl, t_on, T.get("psR_rd")], inc="T")
                ev = pb.op("vector", lambda e, ij=ij, ch=ch: e.tensor_copy(
                    coef[ij][:, 512 * ch:512 * (ch + 1)], psR[:]),
                    waits=[tm, T.get(("coef_rd", ij))], inc="V")
                T["psR_rd"] = ev
                tags.append(ev)
        T["srow_rd"] = tags[-1]
        return tags

    def apply_stage(src, dst, stride, transposed, ctags):
        nb = DH // (2 * stride)
        sA = src.rearrange("p (nb two ss) -> p nb two ss", two=2, ss=stride)
        dA = dst.rearrange("p (nb two ss) -> p nb two ss", two=2, ss=stride)
        cw = [c.rearrange("p (nb ss) -> p nb ss", ss=stride) for c in coef]
        if not transposed:
            iaa, iab, iba, ibb = 0, 1, 2, 3
        else:
            iaa, iab, iba, ibb = 0, 2, 1, 3
        ts_ = []
        w0 = list(ctags) + [T.get("bfly_prev")]
        t1 = pb.op("vector", lambda e: e.tensor_mul(dA[:, :, 0, :], sA[:, :, 0, :],
                   cw[iaa][:]), waits=w0, inc="V")
        t2 = pb.op("vector", lambda e: e.tensor_mul(TMP.rearrange(
            "p (nb ss) -> p nb ss", ss=stride)[:], sA[:, :, 1, :], cw[iab][:]),
            waits=w0 + [T.get("tmp_rd")], inc="V")
        t3 = pb.op("vector", lambda e: e.tensor_add(dA[:, :, 0, :], dA[:, :, 0, :],
                   TMP.rearrange("p (nb ss) -> p nb ss", ss=stride)[:]), waits=[t1, t2], inc="V")
        t4 = pb.op("vector", lambda e: e.tensor_mul(dA[:, :, 1, :], sA[:, :, 0, :],
                   cw[iba][:]), waits=w0, inc="V")
        t5 = pb.op("vector", lambda e: e.tensor_mul(TMP.rearrange(
            "p (nb ss) -> p nb ss", ss=stride)[:], sA[:, :, 1, :], cw[ibb][:]),
            waits=[t3, t4], inc="V")
        T["tmp_rd"] = t5
        t6 = pb.op("vector", lambda e: e.tensor_add(dA[:, :, 1, :], dA[:, :, 1, :],
                   TMP.rearrange("p (nb ss) -> p nb ss", ss=stride)[:]), waits=[t5], inc="V")
        T["bfly_prev"] = t6
        return t6

    def build_compact(dst_name, init_h, w_ap, stages, transposed):
        tl = pb.op("sync", lambda e: e.dma_start(out=X[:], in_=init_h),
                   waits=[T.get("bfly_prev")], inc="Wq", inc_by=16)
        T["bfly_prev"] = tl
        cur, nxt = X, X2
        for st in stages:
            ctags = load_coefs(w_ap, st)
            apply_stage(cur, nxt, 2 ** st, transposed, ctags)
            cur, nxt = nxt, cur
        out = pb.sb(dst_name, [64, DH], F32)
        tc = pb.op("vector", lambda e: e.tensor_copy(out[:], cur[:]),
                   waits=[T["bfly_prev"]], inc="V")
        T["bfly_prev"] = tc
        return out, tc

    C_u, tCu = build_compact("C_u", hC0, upw, range(6), False)
    D_u, tDu = build_compact("D_u", hD0, upw, range(6, 12), False)
    DtT, tDt = build_compact("DtT", hD0, dnw, range(11, 5, -1), True)
    CTd, tCt = build_compact("CTd", hC0, dnw, range(5, -1, -1), True)
    cdone = [tCu, tDu, tDt, tCt]

    pack = [pb.sb(f"pack{i}", [128, 128], F32) for i in range(2)]
    movb = [pb.sb(f"movb{i}", [128, 1025], F32) for i in range(2)]
    evb = [pb.sb(f"evb{i}", [128, 1025], F32) for i in range(2)]
    upcol = pb.sb("upcol", [128, 1], F32)
    psC = [pb.ps(f"psC{i}", [128, 512], F32) for i in range(2)]
    psB = pb.ps("psB", [128, 1], F32)
    tz = [pb.op("vector", lambda e, i=i: e.memset(pack[i][:], 0.0), inc="V") for i in range(2)]

    def pair_loop(nm, stat_src, stat_strided, mov_fn, nmov, evac_fn):
        for p2 in range(32):
            pk = pack[p2 % 2]
            def cpy(e, p2=p2, pk=pk):
                if stat_strided:
                    e.tensor_copy(pk[0:64, 0:64], stat_src[:, (2 * p2)::64])
                    return e.tensor_copy(pk[64:128, 64:128], stat_src[:, (2 * p2 + 1)::64])
                else:
                    e.tensor_copy(pk[0:64, 0:64], stat_src[:, (2 * p2) * 64:(2 * p2 + 1) * 64])
                    return e.tensor_copy(pk[64:128, 64:128],
                                         stat_src[:, (2 * p2 + 1) * 64:(2 * p2 + 2) * 64])
            tp = pb.op("vector", cpy, waits=cdone + tz + [T.get((nm, "pk", p2 % 2))], inc="V")
            mb = movb[p2 % 2]
            tm_ld = mov_fn(p2, mb)
            mms = []
            for ch in range((nmov + 511) // 512):
                n0 = 512 * ch
                n1 = min(nmov, n0 + 512)
                ps = psC[ch % 2] if n1 - n0 > 1 else psB
                tmm = pb.op("tensor", lambda e, ch=ch, n0=n0, n1=n1, ps=ps, pk=pk, mb=mb:
                            e.matmul(ps[:, 0:n1 - n0], pk[:], mb[:, n0:n1],
                                     start=True, stop=True),
                            waits=[tp, tm_ld, T.get((nm, "ps", ch))], inc="T")
                mms.append((tmm, ch, n0, n1))
            T[(nm, "pk", p2 % 2)] = mms[-1][0]
            eb = evb[p2 % 2]
            evs = []
            for (tmm, ch, n0, n1) in mms:
                ps = psC[ch % 2] if n1 - n0 > 1 else psB
                ev = pb.op("vector", lambda e, ch=ch, n0=n0, n1=n1, ps=ps, eb=eb:
                           e.tensor_copy(eb[:, n0:n1], ps[:, 0:n1 - n0]),
                           waits=[tmm, T.get((nm, "eb", p2 % 2))], inc="V")
                T[(nm, "ps", ch)] = ev
                evs.append(ev)
            tout = evac_fn(p2, eb, evs)
            T[(nm, "eb", p2 % 2)] = tout
    bev = bexp.rearrange("(a p) -> a p", p=128)
    def movR1(p2, mb):
        t1 = pb.op("sync", lambda e, p2=p2: e.dma_start(
            out=mb[:, 0:1024], in_=WexpT_dram[128 * p2:128 * (p2 + 1), :]),
            waits=[T["wexpT_done"], T.get(("R1", "mb_rd", p2 % 2))], inc="Wq", inc_by=16)
        t2 = pb.op("sync", lambda e, p2=p2: e.dma_start(
            out=mb[:, 1024:1025].rearrange("p c -> (p c)") if False else mb[:, 1024:1025],
            in_=bev[p2].unsqueeze(1)), waits=[t1], inc="Wq", inc_by=16)
        return t2
    def evR1(p2, eb, evs):
        r1v = R1p_dram.rearrange("(j g) c -> g j c", g=64)
        t1 = pb.op("gpsimd", lambda e, p2=p2, eb=eb: e.dma_start(
            out=r1v[2 * p2], in_=eb[0:64, :]), waits=evs, inc="Ws", inc_by=16)
        t2 = pb.op("gpsimd", lambda e, p2=p2, eb=eb: e.dma_start(
            out=r1v[2 * p2 + 1], in_=eb[64:128, :]), waits=evs, inc="Ws", inc_by=16)
        T[("R1", "mb_rd", p2 % 2)] = t2
        T["r1_done"] = t2
        return t2
    pair_loop("R1", C_u, False, movR1, 1025, evR1)
    ubv = upb.rearrange("(hi lo) -> lo hi", lo=64)
    def movR2(p2, mb):
        return pb.op("sync", lambda e, p2=p2, mb=mb: e.dma_start(
            out=mb[:], in_=R1p_dram[128 * p2:128 * (p2 + 1), :]),
            waits=[T["r1_done"], T.get(("R2", "mb_rd", p2 % 2))], inc="Wq", inc_by=16)
    def evR2(p2, eb, evs):
        tl0 = pb.op("sync", lambda e, p2=p2: e.dma_start(
            out=upcol[0:64, 0:1], in_=ubv[2 * p2].unsqueeze(1),
            allow_slow_non_contiguous=True),
            waits=[T.get("upcol_rd")], inc="Wq", inc_by=16)
        tl = pb.op("sync", lambda e, p2=p2: e.dma_start(
            out=upcol[64:128, 0:1], in_=ubv[2 * p2 + 1].unsqueeze(1),
            allow_slow_non_contiguous=True),
            waits=[tl0], inc="Wq", inc_by=16)
        tb = pb.op("vector", lambda e, p2=p2, eb=eb: e.tensor_add(
            beff[:, p2:p2 + 1], eb[:, 1024:1025], upcol[:]), waits=evs + [tl], inc="V")
        T["upcol_rd"] = tb
        t1 = pb.op("gpsimd", lambda e, p2=p2, eb=eb: e.dma_start(
            out=A_dram[128 * p2:128 * (p2 + 1), :], in_=eb[:]), waits=evs, inc="Ws", inc_by=16)
        T[("R2", "mb_rd", p2 % 2)] = t1
        T["a_done"] = t1
        pb.tag["beff_done"] = tb
        return t1
    pair_loop("R2", D_u, True, movR2, 1025, evR2)
    wcv = Wcon.rearrange("(hi lo) d -> lo hi d", lo=64)
    def movS1(p2, mb):
        ta = pb.op("sync", lambda e, p2=p2, mb=mb: e.dma_start(
            out=mb[0:64, 0:1024], in_=wcv[2 * p2]),
            waits=[T.get(("S1", "mb_rd", p2 % 2))], inc="Wq", inc_by=16)
        return pb.op("sync", lambda e, p2=p2, mb=mb: e.dma_start(
            out=mb[64:128, 0:1024], in_=wcv[2 * p2 + 1]),
            waits=[ta], inc="Wq", inc_by=16)
    def evS1(p2, eb, evs):
        s1v = S1_dram.rearrange("(hi lo) d -> lo hi d", lo=64)
        t1 = pb.op("gpsimd", lambda e, p2=p2, eb=eb: e.dma_start(
            out=s1v[2 * p2], in_=eb[0:64, 0:1024]), waits=evs, inc="Ws", inc_by=16)
        t2 = pb.op("gpsimd", lambda e, p2=p2, eb=eb: e.dma_start(
            out=s1v[2 * p2 + 1], in_=eb[64:128, 0:1024]), waits=evs, inc="Ws", inc_by=16)
        T[("S1", "mb_rd", p2 % 2)] = t2
        T["s1_done"] = t2
        return t2
    pair_loop("S1", DtT, True, movS1, 1024, evS1)
    def movS2(p2, mb):
        return pb.op("sync", lambda e, p2=p2, mb=mb: e.dma_start(
            out=mb[:, 0:1024], in_=S1_dram[128 * p2:128 * (p2 + 1), :]),
            waits=[T["s1_done"], T.get(("S2", "mb_rd", p2 % 2))], inc="Wq", inc_by=16)
    def evS2(p2, eb, evs):
        wv2 = Wc_dram.rearrange("(j hi) d -> hi j d", hi=64)
        t1 = pb.op("gpsimd", lambda e, p2=p2, eb=eb: e.dma_start(
            out=wv2[2 * p2], in_=eb[0:64, 0:1024]), waits=evs, inc="Ws", inc_by=16)
        t2 = pb.op("gpsimd", lambda e, p2=p2, eb=eb: e.dma_start(
            out=wv2[2 * p2 + 1], in_=eb[64:128, 0:1024]), waits=evs, inc="Ws", inc_by=16)
        T[("S2", "mb_rd", p2 % 2)] = t2
        T["wc_done"] = t2
        return t2
    pair_loop("S2", CTd, False, movS2, 1024, evS2)
    done = [T["a_done"], T["wc_done"], T["beff_done"]]
    pb.pop_scope(gate=done)
    return A_dram, Wc_dram, beff, done


def build_l1ffn(R=2048, pb=None, io=None):
    fused = pb is not None
    if not fused:
        pb = PB()
    gi = lambda n, s, d=F32: io[n] if fused else pb.dram(n, s, d, kind="ExternalInput")
    xin = gi("xin", [R, D])
    Wexp = gi("W_exp", [D, 4096])
    bexp = gi("b_exp", [4096])
    upw = gi("up_w", [12, 2048, 2, 2])
    upb = gi("up_b", [4096])
    dnw = gi("dn_w", [12, 2048, 2, 2])
    Wcon = gi("W_con", [4096, D])
    bcon = gi("b_con", [D])
    dnb = gi("dn_b", [D])
    xout = io["xout"] if fused else pb.dram("xout", [R, D], F32, kind="ExternalOutput")

    idf, idb = emit_identity(pb)
    alloc_ln(pb)
    A_dram, Wc_dram, beff, cps_done = emit_l1_compose(pb, Wexp, bexp, upw, upb, dnw, Wcon, idf)

    T = pb.tag
    AT = pb.sb("AT", [128, 8 * 4096], BF16)
    WcT = pb.sb("WcT", [128, 32 * 1024], BF16)
    psT = [pb.ps(f"psT{i}", [128, 128], BF16) for i in range(2)]
    psM = [pb.ps(f"psM{i}", [128, 512], F32) for i in range(2)]
    psY = [pb.ps(f"psY{i}", [128, 128], F32) for i in range(2)]
    psYb = [pb.ps(f"psYb{i}", [128, 128], BF16) for i in range(2)]
    psA = psY
    bout = pb.sb("bout", [128, 8], F32)
    pb.push_scope()
    astg = pb.sb("astg", [128, 1024], F32)
    for mblk in range(32):
        tl = pb.op("sync", lambda e, mblk=mblk: e.dma_start(
            out=astg[:], in_=A_dram[128 * mblk:128 * (mblk + 1), 0:1024]),
            waits=cps_done + [T.get("astg_rd")], inc="Wq", inc_by=16)
        for rr in range(8):
            ps = psA[rr % 2]
            tt = pb.op("tensor", lambda e, rr=rr, ps=ps: e.transpose(
                ps[:], astg[:, 128 * rr:128 * (rr + 1)], idf[:]),
                waits=[tl, T.get(("psA", rr % 2)), T.get(("psY", rr % 2))] + T["id"], inc="T")
            ev = pb.op("vector", lambda e, rr=rr, mblk=mblk, ps=ps: e.tensor_copy(
                AT[:, rr * 4096 + 128 * mblk: rr * 4096 + 128 * (mblk + 1)], ps[:]),
                waits=[tt], inc="V")
            T[("psA", rr % 2)] = ev
            if rr == 7:
                T["astg_rd"] = tt
            T["at_done"] = ev
    wcstg = pb.sb("wcstg", [128, 1024], F32)
    for k in range(32):
        tl = pb.op("sync", lambda e, k=k: e.dma_start(
            out=wcstg[:], in_=Wc_dram[128 * k:128 * (k + 1), :]),
            waits=cps_done + [T.get("wcstg_rd")], inc="Wq", inc_by=16)
        tc = pb.op("vector", lambda e, k=k: e.tensor_scalar(
            WcT[:, k * 1024:(k + 1) * 1024], wcstg[:], 0.5, None, ALU.mult, ALU.bypass),
            waits=[tl], inc="V")
        T["wcstg_rd"] = tc
        T["wc_res"] = tc
    pb.pop_scope() if False else None
    bstg1 = pb.sb("bstg1", [8, 128], F32)
    bstg2 = pb.sb("bstg2", [8, 128], F32)

    t1 = pb.op("sync", lambda e: e.dma_start(out=bstg1[:], in_=bcon.rearrange(
        "(a p) -> a p", p=128)), inc="Wq", inc_by=16)
    t2 = pb.op("sync", lambda e: e.dma_start(out=bstg2[:], in_=dnb.rearrange(
        "(a p) -> a p", p=128)), inc="Wq", inc_by=16)
    t3 = pb.op("vector", lambda e: e.tensor_add(bstg1[:], bstg1[:], bstg2[:]),
               waits=[t1, t2], inc="V")
    t4 = pb.op("tensor", lambda e: e.transpose(psY[0][:, 0:8], bstg1[:], idf[0:8, 0:8]),
               waits=[t3, T.get(("psA", 0))] + T["id"], inc="T")
    tbo = pb.op("vector", lambda e: e.tensor_scalar(
        bout[:], psY[0][:, 0:8], 0.5, None, ALU.mult, ALU.bypass), waits=[t4], inc="V")
    T[("psY", 0)] = tbo
    pb.pop_scope()

    xa = [pb.sb(f"xa{i}", [128, D], F32) for i in range(2)]
    xqb = pb.sb("xqb", [128, D], BF16)
    xfm = pb.sb("xfm", [128, 8 * 512], BF16)
    hfull = pb.sb("hfull", [128, 32 * 512], BF16)
    yfm = pb.sb("yfm", [128, 8 * 512], BF16)
    ytk = pb.sb("ytk", [128, D], BF16)

    xv = xin.rearrange("(t p) d -> t p d", p=128)
    ov = xout.rearrange("(t p) d -> t p d", p=128)

    for n in range(R // 512):
        tr_tags = []
        for ti in range(4):
            t = 4 * n + ti
            a = xa[ti % 2]
            tl = pb.op("sync", lambda e, t=t, a=a: e.dma_start(out=a[:], in_=xv[t]),
                       waits=[T.get(("xa", ti % 2))], inc="Xq", inc_by=16)
            tln = emit_ln_bf16(pb, a, xqb, [tl, T.get("xqb_rd")])
            T[("xa", ti % 2)] = tln
            for f in range(8):
                ps = psT[f % 2]
                tt = pb.op("tensor", lambda e, f=f, ps=ps: e.transpose(
                    ps[:], xqb[:, 128 * f:128 * (f + 1)], idb[:]),
                    waits=[tln, T.get(("psT", f % 2))] + T["id"], inc="T")
                ev = pb.op("vector", lambda e, f=f, ti=ti, ps=ps: e.tensor_copy(
                    xfm[:, f * 512 + ti * 128: f * 512 + ti * 128 + 128], ps[:]),
                    waits=[tt], inc="V")
                T[("psT", f % 2)] = ev
                tr_tags.append(ev)
                if f == 7:
                    T["xqb_rd"] = tt
        for m in range(32):
            ps = psM[m % 2]
            def mm(e, m=m, ps=ps):
                inst = None
                for k in range(8):
                    inst = e.matmul(ps[:], AT[:, k * 4096 + 128 * m: k * 4096 + 128 * (m + 1)],
                                    xfm[:, k * 512:(k + 1) * 512],
                                    start=(k == 0), stop=(k == 7))
                return inst
            tm = pb.op("tensor", mm, waits=tr_tags + [T["at_done"], T.get(("psM", m % 2))],
                       inc="T")
            te = pb.op("scalar", lambda e, m=m, ps=ps: e.activation(
                hfull[:, m * 512:(m + 1) * 512], ps[:], AF.Gelu, bias=beff[:, m:m + 1]),
                waits=[tm, cps_done[2], T.get("hf_guard")], inc="S")
            T[("psM", m % 2)] = te
            T["h_done"] = te
        for m in range(8):
            ps = psM[m % 2]
            def mmy(e, m=m, ps=ps):
                inst = None
                for k in range(32):
                    inst = e.matmul(ps[:], WcT[:, k * 1024 + 128 * m: k * 1024 + 128 * (m + 1)],
                                    hfull[:, k * 512:(k + 1) * 512],
                                    start=(k == 0), stop=(k == 31))
                return inst
            tm = pb.op("tensor", mmy, waits=[T["h_done"], T["wc_res"], T.get(("psM", m % 2))],
                       inc="T")
            if m == 7:
                T["hf_guard"] = tm
            te = pb.op("scalar", lambda e, m=m, ps=ps: e.activation(
                yfm[:, m * 512:(m + 1) * 512], ps[:], AF.Identity, bias=bout[:, m:m + 1]),
                waits=[tm, tbo, T.get(("yfm_rd", m))], inc="S")
            T[("psM", m % 2)] = te
            T[("yfm", m)] = te
        for ti in range(4):
            t = 4 * n + ti
            trb = []
            for m in range(8):
                ps = psYb[m % 2]
                tt = pb.op("tensor", lambda e, m=m, ti=ti, ps=ps: e.transpose(
                    ps[:], yfm[:, m * 512 + ti * 128: m * 512 + ti * 128 + 128], idb[:]),
                    waits=[T[("yfm", m)], T.get(("psYb", m % 2))] + T["id"], inc="T")
                if ti == 3:
                    T[("yfm_rd", m)] = tt
                ev = pb.op("vector", lambda e, m=m, ps=ps: e.tensor_copy(
                    ytk[:, 128 * m:128 * (m + 1)], ps[:]),
                    waits=[tt, T.get("ytk_rd")], inc="V")
                T[("psYb", m % 2)] = ev
                trb.append(ev)
            a = xa[ti % 2]
            tlx = pb.op("sync", lambda e, t=t, a=a: e.dma_start(out=a[:], in_=xv[t]),
                        waits=[T.get(("xa", ti % 2))], inc="Xq", inc_by=16)
            tad = pb.op("vector", lambda e, a=a: e.tensor_add(a[:], a[:], ytk[:]),
                        waits=trb + [tlx], inc="V")
            T["ytk_rd"] = tad
            tst = pb.op("scalar", lambda e, t=t, a=a: e.dma_start(out=ov[t], in_=a[:]),
                        waits=[tad], inc="St", inc_by=16)
            T[("xa", ti % 2)] = tst
    if fused:
        return [("St", pb.cnt["St"])]
    pb.wait("gpsimd", ("St", pb.cnt["St"]))
    return pb.finalize()


def emit_ln_bf16(pb, xa, out_bf, waits):
    return emit_ln(pb, xa, out_bf, waits)



def alloc_fp4(pb, W=4096, need_p=True):
    pb.q_W = W
    pb.q_i32 = pb.sb("q_i32", [128, W], I32)
    pb.q_f32 = pb.sb("q_f32", [128, W], F32)
    pb.q_p = pb.sb("q_p", [128, W], F32) if need_p else None
    pb.q_pi = pb.sb("q_pi", [128, W], F32) if need_p else None
    pb.q_bm = pb.sb("q_bm", [128, W // 16], F32)
    pb.q_sc = pb.sb("q_sc", [128, W // 16], F32)
    pb.q_si = pb.sb("q_si", [128, W // 16], F32)
    pb.q_sh = pb.sb("q_sh", [128, W // 16], F32)
    pb.tag["q_busy"] = None


def emit_fp4_rows(pb, src, dst, W, waits, out_factor=0.5):
    nb = W // 16
    i32, f32a, fp_, fpi = pb.q_i32, pb.q_f32, pb.q_p, pb.q_pi
    bm, sc, si, sh = pb.q_bm, pb.q_sc, pb.q_si, pb.q_sh
    v3 = lambda t: t[:, 0:W].rearrange("p (nb b) -> p nb b", b=16)
    vb = lambda t: t[:, 0:nb].unsqueeze(2).broadcast_to([128, nb, 16])
    w0 = [x for x in waits] + [pb.tag.get("q_busy")]
    t1 = pb.op("vector", lambda e: e.tensor_reduce(
        bm[:, 0:nb].unsqueeze(2), v3(src), AX.X, ALU.max, apply_absolute_value=True),
        waits=w0, inc="V")
    t2 = pb.op("vector", lambda e: e.tensor_scalar(
        sc[:, 0:nb], bm[:, 0:nb], float(1 / 6.0), float(1e-12), ALU.mult, ALU.max),
        waits=[t1], inc="V")
    t3 = pb.op("vector", lambda e: e.reciprocal(si[:, 0:nb], sc[:, 0:nb]), waits=[t2], inc="V")
    t4 = pb.op("vector", lambda e: e.tensor_mul(v3(f32a), v3(src), vb(si)), waits=[t3] + w0, inc="V")
    t5 = pb.op("vector", lambda e: e.tensor_scalar(
        i32[:, 0:W], f32a[:, 0:W].bitcast(I32), 0x7F800000, None, ALU.bitwise_and, ALU.bypass),
        waits=[t4], inc="V")
    t6 = pb.op("vector", lambda e: e.tensor_scalar(
        fp_[:, 0:W], i32[:, 0:W].bitcast(F32), 1.0, 4.0,
        ALU.max, ALU.min), waits=[t5], inc="V")
    t7 = pb.op("vector", lambda e: e.reciprocal(fpi[:, 0:W], fp_[:, 0:W]), waits=[t6], inc="V")
    t8 = pb.op("vector", lambda e: e.scalar_tensor_tensor(
        f32a[:, 0:W], f32a[:, 0:W], 2.0, fpi[:, 0:W], ALU.mult, ALU.mult),
        waits=[t7], inc="V")
    t9 = pb.op("vector", lambda e: e.tensor_copy(i32[:, 0:W], f32a[:, 0:W]), waits=[t8], inc="V")
    t10 = pb.op("vector", lambda e: e.tensor_copy(f32a[:, 0:W], i32[:, 0:W]), waits=[t9], inc="V")
    t11 = pb.op("vector", lambda e: e.tensor_mul(f32a[:, 0:W], f32a[:, 0:W], fp_[:, 0:W]),
                waits=[t10], inc="V")
    t12 = pb.op("vector", lambda e: e.tensor_scalar(
        sh[:, 0:nb], sc[:, 0:nb], float(out_factor), None, ALU.mult, ALU.bypass),
        waits=[t2], inc="V")
    t13 = pb.op("vector", lambda e: e.tensor_mul(v3(dst), v3(f32a), vb(sh)),
                waits=[t11, t12], inc="V")
    pb.tag["q_busy"] = t13
    return t13


def build_l3ffn(R=2048, pb=None, io=None):
    fused = pb is not None
    if not fused:
        pb = PB()
    NI = 2816
    NKT = NI // 128
    gi = lambda n, s, d=F32: io[n] if fused else pb.dram(n, s, d, kind="ExternalInput")
    xin = gi("xin", [R, D])
    wg = gi("w_gate", [NI, D])
    wu = gi("w_up", [NI, D])
    wd = gi("w_down", [D, NI])
    xout = io["xout"] if fused else pb.dram("xout", [R, D], F32, kind="ExternalOutput")
    hq_dram = pb.dram("hq_dram", [NI, R], BF16)

    idf, idb = emit_identity(pb)
    alloc_ln(pb)
    alloc_fp4(pb, 2816)
    alloc_wstage(pb, 2816, f8=False, tst=None)
    T = pb.tag
    qfn = lambda pb, s32, sbf, W, tags: emit_fp4_rows(pb, s32, sbf, W, tags, out_factor=0.5)
    qfn_half = lambda pb, s32, sbf, W, tags: emit_fp4_rows(pb, s32, sbf, W, tags, out_factor=0.25)

    chunks = [(0, 512), (512, 1024), (1024, 1536), (1536, 2048), (2048, 2560), (2560, 2816)]

    pb.push_scope()
    wgT, wgrdy = emit_weight_T(pb, wg, NI, D, "wg", quant=qfn, dst_dtype=BF16)
    wuT, wurdy = emit_weight_T(pb, wu, NI, D, "wu", quant=qfn, dst_dtype=BF16)
    xa = [pb.sb(f"xa{i}", [128, D], F32) for i in range(2)]
    xn = pb.sb("xn", [128, D], F32)
    xqb = pb.sb("xqb", [128, D], BF16)
    xfm = pb.sb("xfm", [128, 8 * 128], BF16)
    psT = [pb.ps(f"psT{i}", [128, 128], BF16) for i in range(2)]
    psM = [pb.ps(f"psM{i}", [128, 512], F32) for i in range(2)]
    psB = [pb.ps(f"psB{i}", [128, 128], BF16) for i in range(2)]
    sgtk = pb.sb("sgtk", [128, NI], F32)
    utk = pb.sb("utk", [128, NI], F32)
    hqtk = pb.sb("hqtk", [128, NI], BF16)
    hqf = pb.sb("hqf", [128, NKT * 128], BF16)
    xv = xin.rearrange("(t p) d -> t p d", p=128)
    ov = xout.rearrange("(t p) d -> t p d", p=128)
    for t in range(R // 128):
        a = xa[t % 2]
        tl = pb.op("sync", lambda e, t=t, a=a: e.dma_start(out=a[:], in_=xv[t]),
                   waits=[T.get(("xa", t % 2))], inc="Xq", inc_by=16)
        tln = emit_ln(pb, a, xn, [tl, T.get("xn_rd")])
        T[("xa", t % 2)] = tln
        tq = emit_fp4_rows(pb, xn, xqb, D, [tln, T.get("xqb_rd")], out_factor=0.5)
        T["xn_rd"] = tq
        tr_tags = []
        for f in range(8):
            ps = psT[f % 2]
            tt = pb.op("tensor", lambda e, f=f, ps=ps: e.transpose(
                ps[:], xqb[:, 128 * f:128 * (f + 1)], idb[:]),
                waits=[tq, T.get(("psT", f % 2))] + T["id"], inc="T")
            ev = pb.op("vector", lambda e, f=f, ps=ps: e.tensor_copy(
                xfm[:, f * 128:(f + 1) * 128], ps[:]),
                waits=[tt, T.get("xfm_rd")], inc="V")
            T[("psT", f % 2)] = ev
            tr_tags.append(ev)
            if f == 7:
                T["xqb_rd"] = tt
        ev_tags = []
        for ci, (c0, c1) in enumerate(chunks):
            for which, wT, wrdy, dstt in (("g", wgT, wgrdy, sgtk), ("u", wuT, wurdy, utk)):
                ps = psM[0] if which == "g" else psM[1]
                def mmg(e, c0=c0, c1=c1, ps=ps, wT=wT):
                    inst = None
                    for k in range(8):
                        inst = e.matmul(ps[:, 0:c1 - c0], xfm[:, k * 128:(k + 1) * 128],
                                        wT[:, k * NI + c0:k * NI + c1],
                                        start=(k == 0), stop=(k == 7))
                    return inst
                tm = pb.op("tensor", mmg,
                           waits=tr_tags + [wrdy, T.get(("psM", which, ci % 1))], inc="T")
                if which == "g":
                    te = pb.op("scalar", lambda e, c0=c0, c1=c1, ps=ps: e.activation(
                        sgtk[:, c0:c1], ps[:, 0:c1 - c0], AF.Silu),
                        waits=[tm, T.get("sg_rd")], inc="S")
                else:
                    te = pb.op("scalar", lambda e, c0=c0, c1=c1, ps=ps: e.copy(
                        utk[:, c0:c1], ps[:, 0:c1 - c0]),
                        waits=[tm, T.get("sg_rd")], inc="S")
                T[("psM", which, 0)] = te
                ev_tags.append(te)
        th = pb.op("vector", lambda e: e.tensor_mul(sgtk[:], sgtk[:], utk[:]),
                   waits=ev_tags, inc="V")
        thq = emit_fp4_rows(pb, sgtk, hqtk, NI, [th], out_factor=0.5)
        T["sg_rd"] = thq
        sp_tags = []
        for f in range(NKT):
            ps = psB[f % 2]
            tt = pb.op("tensor", lambda e, f=f, ps=ps: e.transpose(
                ps[:], hqtk[:, 128 * f:128 * (f + 1)], idb[:]),
                waits=[thq, T.get(("psB", f % 2))] + T["id"], inc="T")
            ev = pb.op("vector", lambda e, f=f, t=t, ps=ps: e.tensor_copy(
                hqf[:, f * 128:(f + 1) * 128], ps[:]),
                waits=[tt, T.get("hqf_rd")], inc="V")
            T[("psB", f % 2)] = ev
            sp_tags.append(ev)
        tsp = None
        for f in range(NKT):
            tsp = pb.op("gpsimd", lambda e, t=t, f=f: e.dma_start(
                out=hq_dram[128 * f:128 * (f + 1), 128 * t:128 * (t + 1)],
                in_=hqf[:, f * 128:(f + 1) * 128]),
                waits=sp_tags, inc="Hs", inc_by=16)
        T["hqf_rd"] = tsp
        T["hq_done"] = tsp
    pb.pop_scope(gate=[T["hq_done"]])

    wdT, wdrdy = emit_weight_T(pb, wd, D, NI, "wd", quant=qfn_half, dst_dtype=BF16)
    xa2 = [pb.sb(f"xb{i}", [128, D], F32) for i in range(2)]
    hld = [pb.sb(f"hld{i}", [128, 512], BF16) for i in range(2)]
    psM2 = [pb.ps(f"ps2{i}", [128, 512], F32) for i in range(2)]
    psY = [pb.ps(f"psY{i}", [128, 128], BF16) for i in range(2)]
    yfm = pb.sb("yfm", [128, 8 * 512], BF16)
    ytk = pb.sb("ytk", [128, D], BF16)
    for n in range(R // 512):
        ld_tags = []
        for m in range(8):
            ps = psM2[m % 2]
            mm_tag = None
            for k in range(NKT):
                hb = hld[k % 2]
                tl = pb.op("sync", lambda e, k=k, n=n, hb=hb: e.dma_start(
                    out=hb[:], in_=hq_dram[128 * k:128 * (k + 1), 512 * n:512 * (n + 1)]),
                    waits=[T["hq_done"], T.get(("hld", k % 2))], inc="Xq", inc_by=16)
                mm_tag = pb.op("tensor", lambda e, k=k, m=m, ps=ps, hb=hb: e.matmul(
                    ps[:], wdT[:, k * D + 128 * m:k * D + 128 * (m + 1)], hb[:],
                    start=(k == 0), stop=(k == NKT - 1)),
                    waits=[tl, wdrdy, T.get(("psM2", m % 2))], inc="T")
                T[("hld", k % 2)] = mm_tag
            te = pb.op("scalar", lambda e, m=m, ps=ps: e.copy(
                yfm[:, m * 512:(m + 1) * 512], ps[:]),
                waits=[mm_tag, T.get(("yfm_rd", m))], inc="S")
            T[("psM2", m % 2)] = te
            T[("yfm", m)] = te
        for ti in range(4):
            t = 4 * n + ti
            trb = []
            for m in range(8):
                ps = psY[m % 2]
                tt = pb.op("tensor", lambda e, m=m, ti=ti, ps=ps: e.transpose(
                    ps[:], yfm[:, m * 512 + ti * 128: m * 512 + ti * 128 + 128], idb[:]),
                    waits=[T[("yfm", m)], T.get(("psY", m % 2))] + T["id"], inc="T")
                if ti == 3:
                    T[("yfm_rd", m)] = tt
                ev = pb.op("vector", lambda e, m=m, ps=ps: e.tensor_copy(
                    ytk[:, 128 * m:128 * (m + 1)], ps[:]),
                    waits=[tt, T.get("ytk_rd")], inc="V")
                T[("psY", m % 2)] = ev
                trb.append(ev)
            a = xa2[ti % 2]
            tlx = pb.op("sync", lambda e, t=t, a=a: e.dma_start(out=a[:], in_=xv[t]),
                        waits=[T.get(("xa2", ti % 2))], inc="Xq", inc_by=16)
            tad = pb.op("vector", lambda e, a=a: e.tensor_add(a[:], a[:], ytk[:]),
                        waits=trb + [tlx], inc="V")
            T["ytk_rd"] = tad
            tst = pb.op("scalar", lambda e, t=t, a=a: e.dma_start(out=ov[t], in_=a[:]),
                        waits=[tad], inc="St", inc_by=16)
            T[("xa2", ti % 2)] = tst
    if fused:
        return [("St", pb.cnt["St"])]
    pb.wait("gpsimd", ("St", pb.cnt["St"]))
    return pb.finalize()



def emit_lpbq_rows(pb, src, dst, W, waits, fold=None):
    nb = W // 32
    i32, f32a = pb.q_i32, pb.q_f32
    bm, sc, si, sh = pb.q_bm, pb.q_sc, pb.q_si, pb.q_sh
    v3 = lambda t: t[:, 0:W].rearrange("p (nb b) -> p nb b", b=32)
    vb = lambda t: t[:, 0:nb].unsqueeze(2).broadcast_to([128, nb, 32])
    w0 = [x for x in waits] + [pb.tag.get("q_busy")]
    t1 = pb.op("vector", lambda e: e.tensor_reduce(
        bm[:, 0:nb].unsqueeze(2), v3(src), AX.X, ALU.max, apply_absolute_value=True),
        waits=w0, inc="V")
    t2 = pb.op("vector", lambda e: e.tensor_scalar(
        sc[:, 0:nb], bm[:, 0:nb], float(1 / 7.0), 1e-12, ALU.mult, ALU.max),
        waits=[t1], inc="V")
    t3 = pb.op("vector", lambda e: e.reciprocal(si[:, 0:nb], sc[:, 0:nb]), waits=[t2], inc="V")
    t4 = pb.op("vector", lambda e: e.tensor_mul(v3(f32a), v3(src), vb(si)), waits=[t3] + w0, inc="V")
    t5 = pb.op("vector", lambda e: e.tensor_copy(i32[:, 0:W], f32a[:, 0:W]), waits=[t4], inc="V")
    t6 = pb.op("vector", lambda e: e.tensor_copy(f32a[:, 0:W], i32[:, 0:W]), waits=[t5], inc="V")
    if fold is not None:
        t6b = pb.op("vector", lambda e: e.tensor_scalar(
            sh[:, 0:nb], sc[:, 0:nb], float(fold), None, ALU.mult, ALU.bypass),
            waits=[t2], inc="V")
        svec = sh
    else:
        t6b = t2
        svec = sc
    t7 = pb.op("vector", lambda e: e.tensor_mul(v3(dst), v3(f32a), vb(svec)),
               waits=[t6, t6b], inc="V")
    pb.tag["q_busy"] = t7
    return t7


def emit_global_absmax(pb, col, name, idf, gmax_out):
    T = pb.tag
    if not hasattr(pb, "gm_ps"):
        pb.gm_ps = pb.ps("gm_ps", [1, 128], F32)
        pb.gm_row = pb.sb("gm_row", [1, 128], F32)
        pb.gm_st = pb.sb("gm_st", [1, 16], F32)
    psx = pb.gm_ps
    row = pb.gm_row
    cin = pb.dram(f"gm_in_{name}", [1, 16], F32)
    cout = pb.dram(f"gm_out_{name}", [1, 16], F32)
    t1 = pb.op("tensor", lambda e: e.transpose(psx[:], col[:], idf[:]),
               waits=[T[("gmcol", name)], T.get("gm_busy")] + T["id"], inc="T")
    t2 = pb.op("vector", lambda e: e.tensor_copy(row[:], psx[:]), waits=[t1], inc="V")
    t3 = pb.op("vector", lambda e: e.tensor_reduce(
        gmax_out[:], row[:], AX.X, ALU.max), waits=[t2], inc="V")
    t3b = pb.op("vector", lambda e, gst=pb.gm_st: e.tensor_copy(
        gst[:], gmax_out[0:1, 0:1].broadcast_to([1, 16])),
        waits=[t3, T.get("gm_st_rd")], inc="V")
    t4 = pb.op("gpsimd", lambda e, gst=pb.gm_st: e.dma_start(out=cin, in_=gst[:]),
               waits=[t3b], inc="Hs", inc_by=16)
    T["gm_st_rd"] = t4
    t5 = pb.op("gpsimd", lambda e: e.collective_compute(
        "AllReduce", ALU.max, [[0, 1, 2, 3, 4, 5, 6, 7]], ins=[cin.opt()], outs=[cout.opt()]),
        waits=[t4], inc="CC")
    t6 = pb.op("gpsimd", lambda e: e.dma_start(out=gmax_out[:], in_=cout[:, 0:1]),
               waits=[t5], inc="Hs", inc_by=16)
    T["gm_busy"] = t2
    return t6


def emit_scale_cols(pb, gmax, s_col, si_col, ones1, waits, psrep):
    t1 = pb.op("vector", lambda e: e.tensor_scalar(
        gmax[0:1, 0:1], gmax[0:1, 0:1], float(1 / 32767.0), 1e-12, ALU.mult, ALU.max),
        waits=waits, inc="V")
    tsi = pb.op("vector", lambda e: e.reciprocal(gmax[0:1, 1:2], gmax[0:1, 0:1]),
                waits=[t1], inc="V")
    tm = pb.op("tensor", lambda e: e.matmul(psrep[:, 0:2], ones1[:], gmax[0:1, 0:2],
               start=True, stop=True), waits=[tsi, pb.tag.get("psrep_rd")], inc="T")
    t2 = pb.op("vector", lambda e: e.tensor_copy(s_col[:], psrep[:, 0:1]), waits=[tm], inc="V")
    t3 = pb.op("vector", lambda e: e.tensor_copy(si_col[:], psrep[:, 1:2]), waits=[tm], inc="V")
    pb.tag["psrep_rd"] = t3
    return t3


def emit_qdq4(pb, src, dst, si_col, s_col, tmp32, i32t, waits, n=512):
    waits = list(waits) + [pb.tag.get(("qdq_busy", id(tmp32)))]
    t1 = pb.op("vector", lambda e: e.tensor_scalar(
        tmp32[:, 0:n], src, si_col[:, 0:1], None, ALU.mult, ALU.bypass), waits=waits, inc="V")
    t2 = pb.op("vector", lambda e: e.tensor_copy(i32t[:, 0:n], tmp32[:, 0:n]), waits=[t1], inc="V")
    t3 = pb.op("vector", lambda e: e.tensor_copy(tmp32[:, 0:n], i32t[:, 0:n]), waits=[t2], inc="V")
    t4 = pb.op("vector", lambda e: e.tensor_scalar(
        dst, tmp32[:, 0:n], s_col[:, 0:1], None, ALU.mult, ALU.bypass), waits=[t3], inc="V")
    pb.tag[("qdq_busy", id(tmp32))] = t4
    return t4


def build_l2ffn(R=2048, stop_phase=99, pb=None, io=None):
    fused = pb is not None
    if not fused:
        pb = PB()
    NI = 3072
    NKT = NI // 128
    gi = lambda n, s, d=F32: io[n] if fused else pb.dram(n, s, d, kind="ExternalInput")
    xin = gi("xin", [R, D])
    wg = gi("w_gate", [NI, D])
    wu = gi("w_up", [NI, D])
    wd = gi("w_down", [D, NI])
    xout = io["xout"] if fused else pb.dram("xout", [R, D], F32, kind="ExternalOutput")
    xn_dram = pb.dram("xn_dram", [R, D], F32)
    xq_dram = pb.dram("xq_dram", [D, R], BF16)
    g_dram = pb.dram("g_dram", [NI, R], BF16)
    u_dram = pb.dram("u_dram", [NI, R], BF16)
    g2_dram = pb.dram("g2_dram", [NI, R], BF16)
    o_dram = pb.dram("o_dram", [NI, R], BF16)

    idf, idb = emit_identity(pb)
    alloc_ln(pb)
    alloc_fp4(pb, 3072, need_p=False)
    T = pb.tag
    ones1 = pb.sb("ones1r", [1, 128], F32)
    t_on = pb.op("vector", lambda e: e.memset(ones1[:], 1.0), inc="V")
    T["id"] = T["id"] + [t_on]
    psrep = pb.ps("psrep", [128, 2], F32)
    gx = pb.sb("gx", [1, 2], F32)
    gg = pb.sb("gg", [1, 2], F32)
    gu = pb.sb("gu", [1, 2], F32)
    g2 = pb.sb("g2", [1, 2], F32)
    go = pb.sb("go", [1, 2], F32)
    scx = pb.sb("scx", [128, 2], F32)
    scg = pb.sb("scg", [128, 2], F32)
    scu = pb.sb("scu", [128, 2], F32)
    sc2 = pb.sb("sc2", [128, 2], F32)
    sco = pb.sb("sco", [128, 2], F32)
    mxc = pb.sb("mxc", [128, 5], F32)
    mtmp = pb.sb("mtmp", [128, 1], F32)
    tmz = pb.op("vector", lambda e: e.memset(mxc[:], 0.0), inc="V")
    T["mxc_z"] = tmz
    tmp32 = pb.sb("tmp32", [128, 512], F32)
    tmp32b = pb.sb("tmp32b", [128, 512], F32)
    i32t = pb.sb("i32t", [128, 512], I32)
    xv = xin.rearrange("(t p) d -> t p d", p=128)
    xnv = xn_dram.rearrange("(t p) d -> t p d", p=128)
    ov = xout.rearrange("(t p) d -> t p d", p=128)

    pb.push_scope()
    alloc_wstage(pb, 3072, f8=False)
    wgT, wgrdy = emit_weight_T(pb, wg, NI, D, "wg",
                               quant=lambda p_, a, b, W, tg: emit_lpbq_rows(p_, a, b, W, tg))
    wuT, wurdy = emit_weight_T(pb, wu, NI, D, "wu",
                               quant=lambda p_, a, b, W, tg: emit_lpbq_rows(p_, a, b, W, tg))
    xa = [pb.sb(f"xa{i}", [128, D], F32) for i in range(2)]
    xn = pb.sb("xn", [128, D], F32)
    xqb = pb.sb("xqb", [128, D], BF16)
    xfm = pb.sb("xfm", [128, 8 * 512], BF16)
    psT = [pb.ps(f"psT{i}", [128, 128], BF16) for i in range(2)]
    psM = [pb.ps(f"psM{i}", [128, 512], F32) for i in range(2)]
    gbf = pb.sb("gbf", [128, 512], BF16)
    ubf = pb.sb("ubf", [128, 512], BF16)
    for t in range(R // 128):
        a = xa[t % 2]
        tl = pb.op("sync", lambda e, t=t, a=a: e.dma_start(out=a[:], in_=xv[t]),
                   waits=[T.get(("xa", t % 2))], inc="Xq", inc_by=16)
        tln = emit_ln(pb, a, xn, [tl, T.get("xn_rd")])
        T[("xa", t % 2)] = tln
        tsp = pb.op("gpsimd", lambda e, t=t: e.dma_start(out=xnv[t], in_=xn[:]),
                    waits=[tln], inc="Hs", inc_by=16)
        tmx = pb.op("vector", lambda e: e.tensor_reduce(
            mtmp[:], xn[:], AX.X, ALU.max, apply_absolute_value=True),
            waits=[tln, T.get("mtmp_rd")], inc="V")
        tac = pb.op("vector", lambda e: e.tensor_max(
            mxc[:, 0:1], mxc[:, 0:1], mtmp[:]), waits=[tmx, T["mxc_z"], T.get("mx0")], inc="V")
        T["mtmp_rd"] = tac
        T["mx0"] = tac
        T["xn_rd"] = tsp
    T[("gmcol", "x")] = T["mx0"]
    if stop_phase < 1:
        pb.wait("gpsimd", pb.tag["xn_rd"]); pb.pop_scope(); return pb.finalize()
    tgx = emit_global_absmax(pb, mxc[:, 0:1], "x", idf, gx[0:1, 0:1])
    tscx = emit_scale_cols(pb, gx, scx[:, 0:1], scx[:, 1:2], ones1, [tgx], psrep)
    if stop_phase < 2:
        pb.wait("gpsimd", tscx); pb.pop_scope(); return pb.finalize()
    for t in range(R // 128):
        tl = pb.op("sync", lambda e, t=t: e.dma_start(out=xn[:], in_=xnv[t]),
                   waits=[T.get("xn_rd2"), T["xn_rd"], T["mx0"]], inc="Xq", inc_by=16)
        tq = emit_qdq4(pb, xn[:], xqb[:], scx[:, 1:2], scx[:, 0:1], pb.q_f32, pb.q_i32,
                       [tl, tscx, T.get("xqb_rd")], n=1024)
        T["xn_rd2"] = tq
        for f in range(8):
            ps = psT[f % 2]
            tt = pb.op("tensor", lambda e, f=f, ps=ps: e.transpose(
                ps[:], xqb[:, 128 * f:128 * (f + 1)], idb[:]),
                waits=[tq, T.get(("psT", f % 2))] + T["id"], inc="T")
            ev = pb.op("vector", lambda e, f=f, ps=ps: e.tensor_copy(
                xfm[:, f * 128:(f + 1) * 128] if False else xfm[:, f * 512:f * 512 + 128], ps[:]),
                waits=[tt], inc="V")
            tw = pb.op("gpsimd", lambda e, f=f, t=t: e.dma_start(
                out=xq_dram[128 * f:128 * (f + 1), 128 * t:128 * (t + 1)],
                in_=xfm[:, f * 512:f * 512 + 128]), waits=[ev], inc="Hs", inc_by=16)
            T[("psT", f % 2)] = tw
            if f == 7:
                T["xqb_rd"] = tt
                T["xq_done"] = tw
    if stop_phase < 3:
        pb.wait("gpsimd", pb.tag["xq_done"]); pb.pop_scope(); return pb.finalize()
    for n in range(R // 512):
        lds = []
        for f in range(8):
            tl = pb.op("sync", lambda e, f=f, n=n: e.dma_start(
                out=xfm[:, f * 512:(f + 1) * 512],
                in_=xq_dram[128 * f:128 * (f + 1), 512 * n:512 * (n + 1)]),
                waits=[T["xq_done"], T.get("xfm_rd")], inc="Xq", inc_by=16)
            lds.append(tl)
        for m in range(NKT):
            for wi, (wT, wrdy, obf, od, mcol) in enumerate(
                    ((wgT, wgrdy, gbf, g_dram, 1), (wuT, wurdy, ubf, u_dram, 2))):
                ps = psM[wi]
                def mmg(e, m=m, ps=ps, wT=wT):
                    inst = None
                    for k in range(8):
                        inst = e.matmul(ps[:], wT[:, k * NI + 128 * m:k * NI + 128 * (m + 1)],
                                        xfm[:, k * 512:(k + 1) * 512],
                                        start=(k == 0), stop=(k == 7))
                    return inst
                tm = pb.op("tensor", mmg, waits=lds + [wrdy, T.get(("psMl", wi)),
                           T.get(("psMlr", wi))], inc="T")
                if m == NKT - 1:
                    T["xfm_rd"] = tm
                te = pb.op("scalar", lambda e, ps=ps, obf=obf: e.copy(obf[:], ps[:]),
                           waits=[tm, T.get(("obf", wi))], inc="S")
                tr = pb.op("vector", lambda e, ps=ps: e.tensor_reduce(
                    mtmp[:], ps[:], AX.X, ALU.max, apply_absolute_value=True),
                    waits=[tm, te, T.get("mtmp_rd")], inc="V")
                tr2 = pb.op("vector", lambda e, mcol=mcol: e.tensor_max(
                    mxc[:, mcol:mcol + 1], mxc[:, mcol:mcol + 1], mtmp[:]),
                    waits=[tr, T["mxc_z"], T.get(("mxa", mcol))], inc="V")
                T["mtmp_rd"] = tr2
                T[("mxa", mcol)] = tr2
                T[("psMlr", wi)] = tr
                tw = pb.op("gpsimd", lambda e, m=m, n=n, obf=obf, od=od: e.dma_start(
                    out=od[128 * m:128 * (m + 1), 512 * n:512 * (n + 1)], in_=obf[:]),
                    waits=[te, tr2], inc="Hs", inc_by=16)
                T[("obf", wi)] = tw
                T[("psMl", wi)] = te
                T["gu_done"] = tw
    T[("gmcol", "g")] = T[("mxa", 1)]
    T[("gmcol", "u")] = T[("mxa", 2)]
    if stop_phase < 4:
        pb.wait("gpsimd", pb.tag["gu_done"]); pb.pop_scope(); return pb.finalize()
    tgg = emit_global_absmax(pb, mxc[:, 1:2], "g", idf, gg[0:1, 0:1])
    tgu = emit_global_absmax(pb, mxc[:, 2:3], "u", idf, gu[0:1, 0:1])
    tscg = emit_scale_cols(pb, gg, scg[:, 0:1], scg[:, 1:2], ones1, [tgg], psrep)
    tscu = emit_scale_cols(pb, gu, scu[:, 0:1], scu[:, 1:2], ones1, [tgu], psrep)
    if stop_phase < 5:
        pb.wait("gpsimd", tscu); pb.pop_scope(); return pb.finalize()
    sigt = pb.sb("sigt", [128, 512], F32)
    for n in range(R // 512):
        for m in range(NKT):
            tl = pb.op("sync", lambda e, m=m, n=n: e.dma_start(
                out=gbf[:], in_=g_dram[128 * m:128 * (m + 1), 512 * n:512 * (n + 1)]),
                waits=[T["gu_done"], T.get("gbf_rd")], inc="Xq", inc_by=16)
            tq = emit_qdq4(pb, gbf[:], tmp32b[:], scg[:, 1:2], scg[:, 0:1], pb.q_f32,
                           pb.q_i32, [tl, tscg, T.get("t32b_rd")])
            T["gbf_rd"] = tq
            tsg = pb.op("scalar", lambda e: e.activation(sigt[:], tmp32b[:], AF.Sigmoid),
                        waits=[tq, T.get("sig_rd")], inc="S")
            t1 = pb.op("vector", lambda e, qf=pb.q_f32: e.tensor_scalar(
                qf[:, 0:512], sigt[:], 65536.0, None, ALU.mult, ALU.bypass),
                waits=[tsg, pb.tag.get("q_busy"), pb.tag.get(("qdq_busy", id(pb.q_f32)))], inc="V")
            t2 = pb.op("vector", lambda e, qf=pb.q_f32, qi=pb.q_i32: e.tensor_copy(
                qi[:, 0:512], qf[:, 0:512]), waits=[t1], inc="V")
            t3 = pb.op("vector", lambda e, qf=pb.q_f32, qi=pb.q_i32: e.tensor_copy(
                qf[:, 0:512], qi[:, 0:512]), waits=[t2], inc="V")
            t4 = pb.op("vector", lambda e, qf=pb.q_f32: e.tensor_scalar(
                sigt[:], qf[:, 0:512], 65535.0, float(1 / 65536.0), ALU.min, ALU.mult),
                waits=[t3], inc="V")
            pb.tag["q_busy"] = t4
            pb.tag[("qdq_busy", id(pb.q_f32))] = t4
            T["sig_rd"] = t4
            tg2 = pb.op("vector", lambda e: e.tensor_mul(tmp32b[:], tmp32b[:], sigt[:]),
                        waits=[t4], inc="V")
            tmr = pb.op("vector", lambda e: e.tensor_reduce(
                mtmp[:], tmp32b[:], AX.X, ALU.max, apply_absolute_value=True),
                waits=[tg2, T.get("mtmp_rd")], inc="V")
            tmx = pb.op("vector", lambda e: e.tensor_max(
                mxc[:, 3:4], mxc[:, 3:4], mtmp[:]), waits=[tmr, T.get(("mxa", 3))], inc="V")
            T["mtmp_rd"] = tmx
            T[("mxa", 3)] = tmx
            tcb = pb.op("vector", lambda e: e.tensor_copy(gbf[:], tmp32b[:]),
                        waits=[tg2, tmx if False else tmx], inc="V")
            T["t32b_rd"] = tcb
            tw = pb.op("gpsimd", lambda e, m=m, n=n: e.dma_start(
                out=g2_dram[128 * m:128 * (m + 1), 512 * n:512 * (n + 1)], in_=gbf[:]),
                waits=[tcb], inc="Hs", inc_by=16)
            T["gbf_rd"] = tw
            T["g2_done"] = tw
    T[("gmcol", "g2")] = T[("mxa", 3)]
    tg2g = emit_global_absmax(pb, mxc[:, 3:4], "g2", idf, g2[0:1, 0:1])
    tsc2 = emit_scale_cols(pb, g2, sc2[:, 0:1], sc2[:, 1:2], ones1, [tg2g], psrep)
    if stop_phase < 6:
        pb.wait("gpsimd", tsc2); pb.pop_scope(); return pb.finalize()
    for n in range(R // 512):
        for m in range(NKT):
            tl1 = pb.op("sync", lambda e, m=m, n=n: e.dma_start(
                out=gbf[:], in_=g2_dram[128 * m:128 * (m + 1), 512 * n:512 * (n + 1)]),
                waits=[T["g2_done"], T.get("gbf_rd")], inc="Xq", inc_by=16)
            tl2 = pb.op("sync", lambda e, m=m, n=n: e.dma_start(
                out=ubf[:], in_=u_dram[128 * m:128 * (m + 1), 512 * n:512 * (n + 1)]),
                waits=[T["gu_done"], T.get("ubf_rd")], inc="Xq", inc_by=16)
            tq1 = emit_qdq4(pb, gbf[:], tmp32b[:], sc2[:, 1:2], sc2[:, 0:1], pb.q_f32,
                            pb.q_i32, [tl1, tsc2, T.get("t32b_rd")])
            T["gbf_rd"] = tq1
            tq2 = emit_qdq4(pb, ubf[:], tmp32[:], scu[:, 1:2], scu[:, 0:1], pb.q_f32,
                            pb.q_i32, [tl2, tscu, T.get("t32_rd")])
            T["ubf_rd"] = tq2
            to = pb.op("vector", lambda e: e.tensor_mul(tmp32b[:], tmp32b[:], tmp32[:]),
                       waits=[tq1, tq2], inc="V")
            T["t32_rd"] = to
            tmr = pb.op("vector", lambda e: e.tensor_reduce(
                mtmp[:], tmp32b[:], AX.X, ALU.max, apply_absolute_value=True),
                waits=[to, T.get("mtmp_rd")], inc="V")
            tmx = pb.op("vector", lambda e: e.tensor_max(
                mxc[:, 4:5], mxc[:, 4:5], mtmp[:]), waits=[tmr, T.get(("mxa", 4))], inc="V")
            T["mtmp_rd"] = tmx
            T[("mxa", 4)] = tmx
            tcb = pb.op("vector", lambda e: e.tensor_copy(gbf[:], tmp32b[:]),
                        waits=[to, tmx], inc="V")
            T["t32b_rd"] = tcb
            tw = pb.op("gpsimd", lambda e, m=m, n=n: e.dma_start(
                out=o_dram[128 * m:128 * (m + 1), 512 * n:512 * (n + 1)], in_=gbf[:]),
                waits=[tcb], inc="Hs", inc_by=16)
            T["gbf_rd"] = tw
            T["o_done"] = tw
    T[("gmcol", "o")] = T[("mxa", 4)]
    tgo = emit_global_absmax(pb, mxc[:, 4:5], "o", idf, go[0:1, 0:1])
    tsco = emit_scale_cols(pb, go, sco[:, 0:1], sco[:, 1:2], ones1, [tgo], psrep)
    if stop_phase < 7:
        pb.wait("gpsimd", tsco); pb.pop_scope(); return pb.finalize()
    pb.pop_scope()
    wdT, wdrdy = None, None
    pb.push_scope()
    alloc_wstage(pb, 3072, f8=False, sfx="d")
    wdT, wdrdy = emit_weight_T(pb, wd, D, NI, "wd",
                               quant=lambda p_, a, b, W, tg: emit_lpbq_rows(p_, a, b, W, tg, fold=0.5))
    oqf = pb.sb("oqf", [128, NKT * 512], BF16)
    obf2 = pb.sb("obf2", [128, 512], BF16)
    psM2 = [pb.ps(f"ps2{i}", [128, 512], F32) for i in range(2)]
    psY = [pb.ps(f"psY{i}", [128, 128], BF16) for i in range(2)]
    yfm = pb.sb("yfm", [128, 8 * 512], BF16)
    ytk = pb.sb("ytk", [128, D], BF16)
    xa2 = [pb.sb(f"xb{i}", [128, D], F32) for i in range(2)]
    for n in range(R // 512):
        qt = []
        for k in range(NKT):
            tl = pb.op("sync", lambda e, k=k, n=n: e.dma_start(
                out=obf2[:], in_=o_dram[128 * k:128 * (k + 1), 512 * n:512 * (n + 1)]),
                waits=[T["o_done"], T.get("obf2_rd")], inc="Xq", inc_by=16)
            tq = emit_qdq4(pb, obf2[:], oqf[:, k * 512:(k + 1) * 512], sco[:, 1:2],
                           sco[:, 0:1], tmp32, i32t, [tl, tsco, T.get(("oqf_rd", k))])
            T["obf2_rd"] = tq
            qt.append(tq)
        for m in range(8):
            ps = psM2[m % 2]
            def mmy(e, m=m, ps=ps):
                inst = None
                for k in range(NKT):
                    inst = e.matmul(ps[:], wdT[:, k * D + 128 * m:k * D + 128 * (m + 1)],
                                    oqf[:, k * 512:(k + 1) * 512],
                                    start=(k == 0), stop=(k == NKT - 1))
                return inst
            tm = pb.op("tensor", mmy, waits=qt + [wdrdy, T.get(("psM2", m % 2))], inc="T")
            if m == 7:
                for k in range(NKT):
                    T[("oqf_rd", k)] = tm
            te = pb.op("scalar", lambda e, m=m, ps=ps: e.copy(
                yfm[:, m * 512:(m + 1) * 512], ps[:]),
                waits=[tm, T.get(("yfm_rd", m))], inc="S")
            T[("psM2", m % 2)] = te
            T[("yfm", m)] = te
        for ti in range(4):
            t = 4 * n + ti
            trb = []
            for m in range(8):
                ps = psY[m % 2]
                tt = pb.op("tensor", lambda e, m=m, ti=ti, ps=ps: e.transpose(
                    ps[:], yfm[:, m * 512 + ti * 128: m * 512 + ti * 128 + 128], idb[:]),
                    waits=[T[("yfm", m)], T.get(("psY", m % 2))] + T["id"], inc="T")
                if ti == 3:
                    T[("yfm_rd", m)] = tt
                ev = pb.op("vector", lambda e, m=m, ps=ps: e.tensor_copy(
                    ytk[:, 128 * m:128 * (m + 1)], ps[:]),
                    waits=[tt, T.get("ytk_rd")], inc="V")
                T[("psY", m % 2)] = ev
                trb.append(ev)
            a = xa2[ti % 2]
            tlx = pb.op("sync", lambda e, t=t, a=a: e.dma_start(out=a[:], in_=xv[t]),
                        waits=[T.get(("xa2", ti % 2))], inc="Xq", inc_by=16)
            tad = pb.op("vector", lambda e, a=a: e.tensor_add(a[:], a[:], ytk[:]),
                        waits=trb + [tlx], inc="V")
            T["ytk_rd"] = tad
            tst = pb.op("scalar", lambda e, t=t, a=a: e.dma_start(out=ov[t], in_=a[:]),
                        waits=[tad], inc="St", inc_by=16)
            T[("xa2", ti % 2)] = tst
    pb.pop_scope()
    if fused:
        return [("St", pb.cnt["St"])]
    pb.wait("gpsimd", ("St", pb.cnt["St"]))
    return pb.finalize()



def _causal_mask_table():
    m = np.zeros((4, 128, 512), np.float32)
    for d in range(4):
        for j in range(4):
            if d < j:
                m[d][:, j * 128:(j + 1) * 128] = 1.0
            elif d == j:
                tri = (np.arange(128)[:, None] <= np.arange(128)[None, :])
                m[d][:, j * 128:(j + 1) * 128] = tri.astype(np.float32)
    return m


def build_causal_attn(kind, R=4096, pb=None, io=None):
    fused = pb is not None
    if not fused:
        pb = PB()
    gi = lambda n, s, d=F32: io[n] if fused else pb.dram(n, s, d, kind="ExternalInput")
    xin = gi("xin", [R, D])
    if kind == "l0":
        Wq = gi("Wq", [64, 64])
        Wk = gi("Wk", [64, 64])
        Wv = gi("Wv", [64, 64])
        Wu = gi("Wu", [D, D])
        bu = gi("bu", [D])
    else:
        w_in = gi("w_in", [3 * D, D])
        b_in = gi("b_in", [3 * D])
        w_out = gi("w_out", [D, D])
        b_out = gi("b_out", [D])
    xout = io["xout"] if fused else pb.dram("xout", [R, D], F32, kind="ExternalOutput")
    qd = pb.dram("q_dram", [D, R], BF16)
    kd = pb.dram("k_dram", [D, R], BF16)
    vd = pb.dram("v_dram", [D, R], BF16)
    yd = pb.dram("y_dram", [D, R], BF16)

    idf, idb = emit_identity(pb)
    alloc_ln(pb)
    T = pb.tag
    NT = R // 128
    NC = R // 512
    xv = xin.rearrange("(t p) d -> t p d", p=128)
    ov = xout.rearrange("(t p) d -> t p d", p=128)

    woT = pb.sb("woT", [128, 8 * D], BF16)
    if kind == "l0":
        boc, bordy = emit_bias_cols(pb, bu, D, "bo", idf)
    else:
        boc, bordy = emit_bias_cols(pb, b_out, D, "bo", idf)

    pb.push_scope()
    alloc_wstage(pb, 4096, f8=False, tst=None)
    wo_src = Wu if kind == "l0" else w_out
    _, wordy = emit_weight_T(pb, wo_src, D, D, "wo", wT_pre=woT)
    if kind == "l3":
        wiT, wirdy = emit_weight_T(pb, w_in, 3 * D, D, "wi", fold_first_rows=(1024, 0.125))
        bic, birdy = emit_bias_cols(pb, b_in, 3 * D, "bi", idf)
        bis = pb.op("vector", lambda e: e.tensor_scalar(
            bic[:, 0:8], bic[:, 0:8], 0.125, None, ALU.mult, ALU.bypass),
            waits=[birdy], inc="V")
    else:
        s4 = float(D) ** -0.25
        packs = []
        wsm = pb.sb("wsm", [64, 64], F32)
        psw = pb.ps("psw", [64, 64], F32)
        wsc = pb.sb("wsc", [128, 64], F32)
        for nm, wap, scl in (("q", Wq, s4), ("k", Wk, s4), ("v", Wv, 1.0)):
            pk = pb.sb(f"pk{nm}", [128, 128], BF16)
            tz = pb.op("vector", lambda e, pk=pk: e.memset(pk[:], 0.0), inc="V")
            tl = pb.op("sync", lambda e, wap=wap: e.dma_start(out=wsm[:], in_=wap),
                       waits=[T.get("wsm_rd")], inc="Wq", inc_by=16)
            tt = pb.op("tensor", lambda e: e.transpose(psw[:], wsm[:], idf[0:64, 0:64]),
                       waits=[tl, T.get("psw_rd")] + T["id"], inc="T")
            T["wsm_rd"] = tt
            tsc = pb.op("vector", lambda e, scl=scl: e.tensor_scalar(
                wsc[0:64, :], psw[:], float(scl), None, ALU.mult, ALU.bypass),
                waits=[tt, T.get("wsc_rd")], inc="V")
            T["psw_rd"] = tsc
            c1 = pb.op("vector", lambda e, pk=pk: e.tensor_copy(pk[0:64, 0:64], wsc[0:64, :]),
                       waits=[tsc, tz], inc="V")
            c2 = pb.op("vector", lambda e, pk=pk: e.tensor_copy(pk[64:128, 64:128], wsc[0:64, :]),
                       waits=[tsc, tz], inc="V")
            T["wsc_rd"] = c2
            packs.append((pk, [c1, c2]))
        (wqp, wqt), (wkp, wkt), (wvp, wvt) = packs

    xa = [pb.sb(f"xa{i}", [128, D], F32) for i in range(2)]
    xqb = pb.sb("xqb", [128, D], BF16)
    xnfm = pb.sb("xnfm", [128, 8 * R], BF16)
    psT = [pb.ps(f"psT{i}", [128, 128], BF16) for i in range(2)]
    psM = [pb.ps(f"psM{i}", [128, 512], F32) for i in range(2)]
    qkvb = pb.sb("qkvb", [128, 512], BF16)
    for t in range(NT):
        a = xa[t % 2]
        tl = pb.op("sync", lambda e, t=t, a=a: e.dma_start(out=a[:], in_=xv[t]),
                   waits=[T.get(("xa", t % 2))], inc="Xq", inc_by=16)
        tln = emit_ln(pb, a, xqb, [tl, T.get("xqb_rd")])
        T[("xa", t % 2)] = tln
        for f in range(8):
            ps = psT[f % 2]
            tt = pb.op("tensor", lambda e, f=f, ps=ps: e.transpose(
                ps[:], xqb[:, 128 * f:128 * (f + 1)], idb[:]),
                waits=[tln, T.get(("psT", f % 2))] + T["id"], inc="T")
            ev = pb.op("vector", lambda e, f=f, t=t, ps=ps: e.tensor_copy(
                xnfm[:, f * R + t * 128: f * R + t * 128 + 128], ps[:]),
                waits=[tt], inc="V")
            T[("psT", f % 2)] = ev
            T["xnfm"] = ev
            if f == 7:
                T["xqb_rd"] = tt
    for hp in range(8):
        for pi_, (nm, dram_) in enumerate((("q", qd), ("k", kd), ("v", vd))):
            for ch in range(NC):
                ps = psM[ch % 2]
                if kind == "l3":
                    def mmp(e, hp=hp, pi_=pi_, ch=ch, ps=ps):
                        inst = None
                        for k in range(8):
                            col = k * 3072 + pi_ * 1024 + 128 * hp
                            inst = e.matmul(ps[:], wiT[:, col:col + 128],
                                            xnfm[:, k * R + ch * 512:k * R + ch * 512 + 512],
                                            start=(k == 0), stop=(k == 7))
                        return inst
                    tm = pb.op("tensor", mmp, waits=[T["xnfm"], wirdy,
                               T.get(("psMp", ch % 2))], inc="T")
                    te = pb.op("scalar", lambda e, hp=hp, pi_=pi_, ps=ps: e.activation(
                        qkvb[:], ps[:], AF.Identity, bias=bic[:, pi_ * 8 + hp:pi_ * 8 + hp + 1]),
                        waits=[tm, bis, T.get("qkvb_rd")], inc="S")
                else:
                    pk = (wqp, wkp, wvp)[pi_]
                    pkt = (wqt, wkt, wvt)[pi_]
                    def mmp(e, hp=hp, ch=ch, ps=ps, pk=pk):
                        return e.matmul(ps[:], pk[:],
                                        xnfm[:, hp * R + ch * 512:hp * R + ch * 512 + 512],
                                        start=True, stop=True)
                    tm = pb.op("tensor", mmp, waits=[T["xnfm"]] + pkt +
                               [T.get(("psMp", ch % 2))], inc="T")
                    te = pb.op("scalar", lambda e, ps=ps: e.copy(qkvb[:], ps[:]),
                               waits=[tm, T.get("qkvb_rd")], inc="S")
                T[("psMp", ch % 2)] = te
                tw = pb.op("gpsimd", lambda e, hp=hp, ch=ch, dram_=dram_: e.dma_start(
                    out=dram_[128 * hp:128 * (hp + 1), 512 * ch:512 * (ch + 1)], in_=qkvb[:]),
                    waits=[te], inc="Hs", inc_by=16)
                T["qkvb_rd"] = tw
                T["qkv_done"] = tw
    pb.pop_scope(gate=[T["qkv_done"]])

    pb.push_scope()
    maskh = pb.const(_causal_mask_table(), "cmask")
    mk32 = pb.sb("mk32", [128, 4 * 512], F32)
    mkb = pb.sb("mkb", [128, 4 * 512], BF16)
    tm1 = pb.op("sync", lambda e: e.dma_start(
        out=mk32[:], in_=maskh.rearrange("d p c -> p d c")), inc="Wq", inc_by=16)
    tm2 = pb.op("vector", lambda e: e.tensor_copy(mkb[:], mk32[:]), waits=[tm1], inc="V")
    ones1 = pb.sb("ones1", [1, 64], F32)
    tON = pb.op("vector", lambda e: e.memset(ones1[:], 1.0), inc="V")
    qf = pb.sb("qf", [128, R], BF16)
    kf = pb.sb("kf", [128, R], BF16)
    vf = pb.sb("vf", [128, R], BF16)
    V2 = [pb.sb(f"V2_{h}", [128, 32 * 65], BF16) for h in range(2)]
    expT = [pb.sb(f"expT{h}", [128, 512], BF16) for h in range(2)]
    yun = [pb.sb(f"yun{h}", [64, 512], F32) for h in range(2)]
    rep_sb = pb.sb("rep_sb", [64, 512], F32)
    rcol = pb.sb("rcol", [1, 512], F32)
    ybf = pb.sb("ybf", [64, 512], BF16)
    psS = [pb.ps(f"psS{h}", [128, 512], F32) for h in range(2)]
    psA = [pb.ps(f"psA{h}", [65, 512], F32) for h in range(2)]
    psV = pb.ps("psV", [128, 64], BF16)
    psR = pb.ps("psR", [64, 512], F32)
    for hp in range(8):
        tq = pb.op("sync", lambda e, hp=hp: e.dma_start(
            out=qf[:], in_=qd[128 * hp:128 * (hp + 1), :]),
            waits=[T["qkv_done"], T.get("qf_rd")], inc="Xq", inc_by=16)
        tk = pb.op("sync", lambda e, hp=hp: e.dma_start(
            out=kf[:], in_=kd[128 * hp:128 * (hp + 1), :]),
            waits=[T["qkv_done"], T.get("kf_rd")], inc="Xq", inc_by=16)
        tv = pb.op("sync", lambda e, hp=hp: e.dma_start(
            out=vf[:], in_=vd[128 * hp:128 * (hp + 1), :]),
            waits=[T["qkv_done"], T.get("vf_rd")], inc="Xq", inc_by=16)
        vrd = []
        for h in range(2):
            tz = pb.op("vector", lambda e, h=h: e.memset(V2[h][:], 1.0),
                       waits=[T.get(("V2", h))], inc="V")
            for ka in range(32):
                ttr = pb.op("tensor", lambda e, h=h, ka=ka: e.transpose(
                    psV[:], vf[64 * h:64 * h + 64, 128 * ka:128 * (ka + 1)],
                    idb[64 * h:64 * h + 64, 64 * h:64 * h + 64]),
                    waits=[tv, T.get("psV_rd")] + T["id"], inc="T")
                ev = pb.op("vector", lambda e, h=h, ka=ka: e.tensor_copy(
                    V2[h][:, 65 * ka:65 * ka + 64], psV[:]), waits=[ttr, tz], inc="V")
                T["psV_rd"] = ev
                vrd.append(ev)
                if h == 1 and ka == 31:
                    T["vf_rd"] = ttr
        for jc in range(NC):
            kmax = 4 * jc + 4
            avt = [None, None]
            for ka in range(kmax):
                for h in range(2):
                    tmm = pb.op("tensor", lambda e, h=h, ka=ka, jc=jc: e.matmul(
                        psS[h][:], kf[64 * h:64 * h + 64, 128 * ka:128 * (ka + 1)],
                        qf[64 * h:64 * h + 64, jc * 512:(jc + 1) * 512],
                        tile_position=(64 * h, 0), start=True, stop=True),
                        waits=[tq, tk, T.get(("psS", h))], inc="T")
                    tex = pb.op("scalar", lambda e, h=h: e.activation(
                        expT[h][:], psS[h][:], AF.Exp),
                        waits=[tmm, T.get(("expT", h))], inc="S")
                    T[("psS", h)] = tex
                    tready = tex
                    if ka >= 4 * jc:
                        d = ka - 4 * jc
                        tready = pb.op("vector", lambda e, h=h, d=d: e.tensor_mul(
                            expT[h][:], expT[h][:], mkb[:, d * 512:(d + 1) * 512]),
                            waits=[tex, tm2], inc="V")
                    tav = pb.op("tensor", lambda e, h=h, ka=ka, kmax=kmax: e.matmul(
                        psA[h][:], V2[h][:, 65 * ka:65 * (ka + 1)], expT[h][:],
                        start=(ka == 0), stop=(ka == kmax - 1)),
                        waits=[tready, vrd[-1]] + ([T.get(("psA", h))] if ka == 0 else []),
                        inc="T")
                    T[("expT", h)] = tav
                    avt[h] = tav
            for h in range(2):
                trc = pb.op("vector", lambda e, h=h: e.reciprocal(
                    rcol[:], psA[h][64:65, :]), waits=[avt[h], T.get("rcol_rd")], inc="V")
                trp = pb.op("tensor", lambda e: e.matmul(
                    psR[:], ones1[:], rcol[:], start=True, stop=True),
                    waits=[trc, tON, T.get("psR_rd")], inc="T")
                tyu = pb.op("scalar", lambda e, h=h: e.copy(yun[h][:], psA[h][0:64, :]),
                            waits=[trc, avt[h], T.get(("yun_rd", h))], inc="S")
                trs = pb.op("vector", lambda e: e.tensor_copy(rep_sb[:], psR[:]),
                            waits=[trp, T.get("rep_rd")], inc="V")
                T["psR_rd"] = trs
                T["rcol_rd"] = trp
                T[("psA", h)] = tyu
                tyn = pb.op("vector", lambda e, h=h: e.tensor_mul(
                    ybf[:], yun[h][:], rep_sb[:]), waits=[tyu, trs, T.get("ybf_rd")], inc="V")
                T[("yun_rd", h)] = tyn
                T["rep_rd"] = tyn
                tw = pb.op("gpsimd", lambda e, hp=hp, h=h, jc=jc: e.dma_start(
                    out=yd[128 * hp + 64 * h:128 * hp + 64 * h + 64, jc * 512:(jc + 1) * 512],
                    in_=ybf[:]), waits=[tyn], inc="Hs", inc_by=16)
                T["ybf_rd"] = tw
                T["y_done"] = tw
        T["qf_rd"] = avt[0]
        T["kf_rd"] = avt[0]
        T[("V2", 0)] = avt[0]
        T[("V2", 1)] = avt[1]
    pb.pop_scope(gate=[T["y_done"]])

    ymv = [pb.sb(f"ymv{i}", [128, 512], BF16) for i in range(8)]
    psM2 = [pb.ps(f"ps2{i}", [128, 512], F32) for i in range(2)]
    psY = [pb.ps(f"psYo{i}", [128, 128], BF16) for i in range(2)]
    yofm = pb.sb("yofm", [128, 8 * 512], BF16)
    ytk = pb.sb("ytk", [128, D], BF16)
    ypos = pb.sb("ypos", [128, D], F32)
    yneg = pb.sb("yneg", [128, D], F32)
    xar = [pb.sb(f"xar{i}", [128, D], F32) for i in range(2)]
    for ch in range(NC):
        lds = []
        for k in range(8):
            tl = pb.op("sync", lambda e, k=k, ch=ch: e.dma_start(
                out=ymv[k][:], in_=yd[128 * k:128 * (k + 1), 512 * ch:512 * (ch + 1)]),
                waits=[T["y_done"], T.get(("ymv", k))], inc="Xq", inc_by=16)
            lds.append(tl)
        for m in range(8):
            ps = psM2[m % 2]
            def mmo(e, m=m, ps=ps):
                inst = None
                for k in range(8):
                    inst = e.matmul(ps[:], woT[:, k * D + 128 * m:k * D + 128 * (m + 1)],
                                    ymv[k][:], start=(k == 0), stop=(k == 7))
                return inst
            tm = pb.op("tensor", mmo, waits=lds + [wordy, T.get(("psM2", m % 2))], inc="T")
            if m == 7:
                for k in range(8):
                    T[("ymv", k)] = tm
            te = pb.op("scalar", lambda e, m=m, ps=ps: e.activation(
                yofm[:, m * 512:(m + 1) * 512], ps[:], AF.Identity, bias=boc[:, m:m + 1]),
                waits=[tm, bordy, T.get(("yofm_rd", m))], inc="S")
            T[("psM2", m % 2)] = te
            T[("yofm", m)] = te
        for ti in range(4):
            t = 4 * ch + ti
            trb = []
            for m in range(8):
                ps = psY[m % 2]
                tt = pb.op("tensor", lambda e, m=m, ti=ti, ps=ps: e.transpose(
                    ps[:], yofm[:, m * 512 + ti * 128: m * 512 + ti * 128 + 128], idb[:]),
                    waits=[T[("yofm", m)], T.get(("psYo", m % 2))] + T["id"], inc="T")
                if ti == 3:
                    T[("yofm_rd", m)] = tt
                ev = pb.op("vector", lambda e, m=m, ps=ps: e.tensor_copy(
                    ytk[:, 128 * m:128 * (m + 1)], ps[:]),
                    waits=[tt, T.get("ytk_rd")], inc="V")
                T[("psYo", m % 2)] = ev
                trb.append(ev)
            a = xar[ti % 2]
            tlx = pb.op("sync", lambda e, t=t, a=a: e.dma_start(out=a[:], in_=xv[t]),
                        waits=[T.get(("xar", ti % 2))], inc="Xq", inc_by=16)
            if kind == "l3":
                tp = pb.op("vector", lambda e: e.tensor_scalar_max(ypos[:], ytk[:], 0.0),
                           waits=trb + [T.get("ypos_rd")], inc="V")
                tn = pb.op("vector", lambda e: e.tensor_scalar_min(yneg[:], ytk[:], 0.0),
                           waits=trb + [T.get("yneg_rd")], inc="V")
                T["ytk_rd"] = tn
                ta1 = pb.op("vector", lambda e, a=a: e.tensor_add(a[:], a[:], ypos[:]),
                            waits=[tp, tlx], inc="V")
                tad = pb.op("vector", lambda e, a=a: e.tensor_add(a[:], a[:], yneg[:]),
                            waits=[tn, ta1], inc="V")
                T["ypos_rd"] = ta1
                T["yneg_rd"] = tad
            else:
                tad = pb.op("vector", lambda e, a=a: e.tensor_add(a[:], a[:], ytk[:]),
                            waits=trb + [tlx], inc="V")
                T["ytk_rd"] = tad
            tst = pb.op("scalar", lambda e, t=t, a=a: e.dma_start(out=ov[t], in_=a[:]),
                        waits=[tad], inc="St", inc_by=16)
            T[("xar", ti % 2)] = tst
    if fused:
        return [("St", pb.cnt["St"])]
    pb.wait("gpsimd", ("St", pb.cnt["St"]))
    return pb.finalize()






def emit_ln_transpose_all(pb, xv, NT, xnfm, R, idb, xa, xqb, psT):
    T = pb.tag
    for t in range(NT):
        a = xa[t % 2]
        tl = pb.op("sync", lambda e, t=t, a=a: e.dma_start(out=a[:], in_=xv[t]),
                   waits=[T.get(("xa", t % 2))], inc="Xq", inc_by=16)
        tln = emit_ln(pb, a, xqb, [tl, T.get("xqb_rd")])
        T[("xa", t % 2)] = tln
        for f in range(8):
            ps = psT[f % 2]
            tt = pb.op("tensor", lambda e, f=f, ps=ps: e.transpose(
                ps[:], xqb[:, 128 * f:128 * (f + 1)], idb[:]),
                waits=[tln, T.get(("psT", f % 2))] + T["id"], inc="T")
            ev = pb.op("vector", lambda e, f=f, t=t, ps=ps: e.tensor_copy(
                xnfm[:, f * R + t * 128: f * R + t * 128 + 128], ps[:]),
                waits=[tt], inc="V")
            T[("psT", f % 2)] = ev
            T["xnfm"] = ev
            if f == 7:
                T["xqb_rd"] = tt
    return T["xnfm"]


def emit_outproj_residual(pb, yd, woT, wordy, boc, bordy, xv, ov, NC, idb, idf, nan_fix,
                          ydone_key="y_done"):
    T = pb.tag
    ymv = [pb.sb(f"ymv{i}", [128, 512], BF16) for i in range(8)]
    psM2 = [pb.ps(f"ps2o{i}", [128, 512], F32) for i in range(2)]
    psY = [pb.ps(f"psYo{i}", [128, 128], BF16) for i in range(2)]
    yofm = pb.sb("yofm", [128, 8 * 512], BF16)
    ytk = pb.sb("ytk_o", [128, D], BF16)
    if nan_fix:
        ypos = pb.sb("ypos", [128, D], F32)
        yneg = pb.sb("yneg", [128, D], F32)
    xar = [pb.sb(f"xar{i}", [128, D], F32) for i in range(2)]
    for ch in range(NC):
        lds = []
        for k in range(8):
            tl = pb.op("sync", lambda e, k=k, ch=ch: e.dma_start(
                out=ymv[k][:], in_=yd[128 * k:128 * (k + 1), 512 * ch:512 * (ch + 1)]),
                waits=[T[ydone_key], T.get(("ymv", k))], inc="Xq", inc_by=16)
            lds.append(tl)
        for m in range(8):
            ps = psM2[m % 2]
            def mmo(e, m=m, ps=ps):
                inst = None
                for k in range(8):
                    inst = e.matmul(ps[:], woT[:, k * D + 128 * m:k * D + 128 * (m + 1)],
                                    ymv[k][:], start=(k == 0), stop=(k == 7))
                return inst
            tm = pb.op("tensor", mmo, waits=[lds[-1], wordy, T.get(("psM2o", m % 2))], inc="T")
            if m == 7:
                for k in range(8):
                    T[("ymv", k)] = tm
            te = pb.op("scalar", lambda e, m=m, ps=ps: e.activation(
                yofm[:, m * 512:(m + 1) * 512], ps[:], AF.Identity, bias=boc[:, m:m + 1]),
                waits=[tm, bordy, T.get(("yofmo_rd", m))], inc="S")
            T[("psM2o", m % 2)] = te
            T[("yofmo", m)] = te
        for ti in range(4):
            t = 4 * ch + ti
            trb = []
            for m in range(8):
                ps = psY[m % 2]
                tt = pb.op("tensor", lambda e, m=m, ti=ti, ps=ps: e.transpose(
                    ps[:], yofm[:, m * 512 + ti * 128: m * 512 + ti * 128 + 128], idb[:]),
                    waits=[T[("yofmo", m)], T.get(("psYoo", m % 2))] + T["id"], inc="T")
                if ti == 3:
                    T[("yofmo_rd", m)] = tt
                ev = pb.op("vector", lambda e, m=m, ps=ps: e.tensor_copy(
                    ytk[:, 128 * m:128 * (m + 1)], ps[:]),
                    waits=[tt, T.get("ytko_rd")], inc="V")
                T[("psYoo", m % 2)] = ev
                trb.append(ev)
            a = xar[ti % 2]
            tlx = pb.op("sync", lambda e, t=t, a=a: e.dma_start(out=a[:], in_=xv[t]),
                        waits=[T.get(("xaro", ti % 2))], inc="Xq", inc_by=16)
            if nan_fix:
                tp = pb.op("vector", lambda e: e.tensor_scalar_max(ypos[:], ytk[:], 0.0),
                           waits=trb + [T.get("ypos_rd")], inc="V")
                tn = pb.op("vector", lambda e: e.tensor_scalar_min(yneg[:], ytk[:], 0.0),
                           waits=trb + [T.get("yneg_rd")], inc="V")
                T["ytko_rd"] = tn
                ta1 = pb.op("vector", lambda e, a=a: e.tensor_add(a[:], a[:], ypos[:]),
                            waits=[tp, tlx], inc="V")
                tad = pb.op("vector", lambda e, a=a: e.tensor_add(a[:], a[:], yneg[:]),
                            waits=[tn, ta1], inc="V")
                T["ypos_rd"] = ta1
                T["yneg_rd"] = tad
            else:
                tad = pb.op("vector", lambda e, a=a: e.tensor_add(a[:], a[:], ytk[:]),
                            waits=trb + [tlx], inc="V")
                T["ytko_rd"] = tad
            tst = pb.op("scalar", lambda e, t=t, a=a: e.dma_start(out=ov[t], in_=a[:]),
                        waits=[tad], inc="St", inc_by=16)
            T[("xaro", ti % 2)] = tst


def emit_weight_plain(pb, w_ap, O, I, name, bias_like=False):
    KT = I // 128
    wT = pb.sb(f"{name}P", [128, KT * O], BF16)
    st = pb.sb(f"{name}_pst", [128, O], F32)
    wv = w_ap.rearrange("(a p) i -> a p i", p=128) if w_ap.ndim == 2 else w_ap
    last = None
    rdy = None
    for k in range(KT):
        tl = pb.op("sync", lambda e, k=k: e.dma_start(out=st[:], in_=wv[k]),
                   waits=[last], inc="Wq", inc_by=16)
        rdy = pb.op("vector", lambda e, k=k: e.tensor_copy(wT[:, k * O:(k + 1) * O], st[:]),
                    waits=[tl], inc="V")
        last = rdy
    return wT, rdy



def build_l1attn(R=4096, pb=None, io=None):
    fused = pb is not None
    if not fused:
        pb = PB()
    gi = lambda n, s, d=F32: io[n] if fused else pb.dram(n, s, d, kind="ExternalInput")
    xin = gi("xin", [R, D])
    Wq = gi("Wq", [D, 16, 64])
    bq = gi("bq", [16, 64])
    Wk = gi("Wk", [D, 16, 64])
    bk = gi("bk", [16, 64])
    Wv = gi("Wv", [D, 16, 64])
    bv = gi("bv", [16, 64])
    Wo = gi("Wo", [16, 64, D])
    bo = gi("bo", [D])
    xout = io["xout"] if fused else pb.dram("xout", [R, D], F32, kind="ExternalOutput")
    qd = pb.dram("qf_dram", [D, R], BF16)
    kd = pb.dram("kf_dram", [D, R], BF16)
    vd = pb.dram("vf_dram", [D, R], BF16)
    yd = pb.dram("y_dram", [D, R], BF16)
    idf, idb = emit_identity(pb)
    alloc_ln(pb)
    T = pb.tag
    NT, NC = R // 128, R // 512
    xv = xin.rearrange("(t p) d -> t p d", p=128)
    ov = xout.rearrange("(t p) d -> t p d", p=128)
    woP, wordy = emit_weight_plain(pb, Wo.rearrange("h k d -> (h k) d"), D, D, "wo")
    boc, bordy = emit_bias_cols(pb, bo, D, "bo", idf)
    pb.push_scope()
    wqP, t1r = emit_weight_plain(pb, Wq.rearrange("d h k -> d (h k)"), D, D, "wq")
    wkP, t2r = emit_weight_plain(pb, Wk.rearrange("d h k -> d (h k)"), D, D, "wk")
    wvP, t3r = emit_weight_plain(pb, Wv.rearrange("d h k -> d (h k)"), D, D, "wv")
    bqc, tb1 = emit_bias_cols(pb, bq.rearrange("h k -> (h k)"), D, "bq", idf)
    bkc, tb2 = emit_bias_cols(pb, bk.rearrange("h k -> (h k)"), D, "bk", idf)
    bvc, tb3 = emit_bias_cols(pb, bv.rearrange("h k -> (h k)"), D, "bv", idf)
    xa = [pb.sb(f"xa{i}", [128, D], F32) for i in range(2)]
    xqb = pb.sb("xqb", [128, D], BF16)
    xnfm = pb.sb("xnfm", [128, 8 * R], BF16)
    psT = [pb.ps(f"psT{i}", [128, 128], BF16) for i in range(2)]
    psM = [pb.ps(f"psM{i}", [128, 512], F32) for i in range(2)]
    tmpa = pb.sb("tmpa", [128, 512], F32)
    tmpb = pb.sb("tmpb", [128, 512], F32)
    tmpc = pb.sb("tmpc", [128, 512], F32)
    qkvb = pb.sb("qkvb", [128, 512], BF16)
    emit_ln_transpose_all(pb, xv, NT, xnfm, R, idb, xa, xqb, psT)
    for hp in range(8):
        for (wP, wrdy, bcol, brdy, dram_, elu) in ((wqP, t1r, bqc, tb1, qd, True),
                                                   (wkP, t2r, bkc, tb2, kd, True),
                                                   (wvP, t3r, bvc, tb3, vd, False)):
            for ch in range(NC):
                ps = psM[ch % 2]
                def mmp(e, hp=hp, ch=ch, ps=ps, wP=wP):
                    inst = None
                    for k in range(8):
                        inst = e.matmul(ps[:], wP[:, k * D + 128 * hp:k * D + 128 * (hp + 1)],
                                        xnfm[:, k * R + ch * 512:k * R + ch * 512 + 512],
                                        start=(k == 0), stop=(k == 7))
                    return inst
                tm = pb.op("tensor", mmp, waits=[T["xnfm"], wrdy, T.get(("psMp", ch % 2))],
                           inc="T")
                te = pb.op("scalar", lambda e, hp=hp, ps=ps, bcol=bcol: e.activation(
                    tmpa[:], ps[:], AF.Identity, bias=bcol[:, hp:hp + 1]),
                    waits=[tm, brdy, T.get("tmpa_rd")], inc="S")
                T[("psMp", ch % 2)] = te
                if elu:
                    t1 = pb.op("vector", lambda e: e.tensor_scalar_min(tmpb[:], tmpa[:], 0.0),
                               waits=[te, T.get("tmpb_rd")], inc="V")
                    t2 = pb.op("scalar", lambda e: e.activation(tmpb[:], tmpb[:], AF.Exp),
                               waits=[t1], inc="S")
                    t3 = pb.op("vector", lambda e: e.tensor_relu(tmpc[:], tmpa[:]),
                               waits=[te, T.get("tmpc_rd")], inc="V")
                    T["tmpa_rd"] = t3
                    t4 = pb.op("vector", lambda e: e.tensor_add(qkvb[:], tmpb[:], tmpc[:]),
                               waits=[t2, t3, T.get("qkvb_rd")], inc="V")
                    T["tmpb_rd"] = t4
                    T["tmpc_rd"] = t4
                    tev = t4
                else:
                    tev = pb.op("vector", lambda e: e.tensor_copy(qkvb[:], tmpa[:]),
                                waits=[te, T.get("qkvb_rd")], inc="V")
                    T["tmpa_rd"] = tev
                tw = pb.op("gpsimd", lambda e, hp=hp, ch=ch, dram_=dram_: e.dma_start(
                    out=dram_[128 * hp:128 * (hp + 1), 512 * ch:512 * (ch + 1)], in_=qkvb[:]),
                    waits=[tev], inc="Hs", inc_by=16)
                T["qkvb_rd"] = tw
                T["qkv_done"] = tw
    pb.pop_scope(gate=[T["qkv_done"]])
    pb.push_scope()
    ones1 = pb.sb("ones1a", [1, 64], F32)
    tON = pb.op("vector", lambda e: e.memset(ones1[:], 1.0), inc="V")
    qf = pb.sb("qf", [128, R], BF16)
    kf = pb.sb("kf", [128, R], BF16)
    vf = pb.sb("vf", [128, R], BF16)
    kq_tk = [pb.sb(f"kqtk{h}", [128, 32 * 64], BF16) for h in range(2)]
    v2tk = [pb.sb(f"v2tk{h}", [128, 32 * 65], BF16) for h in range(2)]
    kvs = pb.sb("kvs2", [128, 65], BF16)
    rcol = pb.sb("rcol", [1, 512], F32)
    rce = pb.sb("rce", [1, 512], F32)
    rep_sb = pb.sb("rep_sb", [64, 512], F32)
    yun = pb.sb("yun", [64, 512], F32)
    ybf = pb.sb("ybf", [64, 512], BF16)
    psV = pb.ps("psV2", [128, 128], BF16)
    psKV = [pb.ps(f"psKV{h}", [64, 65], F32) for h in range(2)]
    psY2 = [pb.ps(f"psY2{h}", [65, 512], F32) for h in range(2)]
    psR = pb.ps("psR2", [64, 512], F32)
    for hp in range(8):
        tq = pb.op("sync", lambda e, hp=hp: e.dma_start(
            out=qf[:], in_=qd[128 * hp:128 * (hp + 1), :]),
            waits=[T["qkv_done"], T.get("qf_rd")], inc="Xq", inc_by=16)
        tk = pb.op("sync", lambda e, hp=hp: e.dma_start(
            out=kf[:], in_=kd[128 * hp:128 * (hp + 1), :]),
            waits=[T["qkv_done"], T.get("kf_rd")], inc="Xq", inc_by=16)
        tv = pb.op("sync", lambda e, hp=hp: e.dma_start(
            out=vf[:], in_=vd[128 * hp:128 * (hp + 1), :]),
            waits=[T["qkv_done"], T.get("vf_rd")], inc="Xq", inc_by=16)
        trd = []
        for h in range(2):
            tz = pb.op("vector", lambda e, h=h: e.memset(v2tk[h][:], 1.0),
                       waits=[T.get(("v2g", h))], inc="V")
            for ka in range(32):
                tt1 = pb.op("tensor", lambda e, h=h, ka=ka: e.transpose(
                    psV[:, 0:64], kf[64 * h:64 * h + 64, 128 * ka:128 * (ka + 1)],
                    idb[64 * h:64 * h + 64, 64 * h:64 * h + 64]),
                    waits=[tk, T.get("psV_rd")] + T["id"], inc="T")
                ev1 = pb.op("vector", lambda e, h=h, ka=ka: e.tensor_copy(
                    kq_tk[h][:, 64 * ka:64 * (ka + 1)], psV[:, 0:64]),
                    waits=[tt1, T.get(("kqg", h))], inc="V")
                T["psV_rd"] = ev1
                tt2 = pb.op("tensor", lambda e, h=h, ka=ka: e.transpose(
                    psV[:, 0:64], vf[64 * h:64 * h + 64, 128 * ka:128 * (ka + 1)],
                    idb[64 * h:64 * h + 64, 64 * h:64 * h + 64]),
                    waits=[tv, ev1] + T["id"], inc="T")
                ev2 = pb.op("vector", lambda e, h=h, ka=ka: e.tensor_copy(
                    v2tk[h][:, 65 * ka:65 * ka + 64], psV[:, 0:64]),
                    waits=[tt2, tz], inc="V")
                T["psV_rd"] = ev2
                trd.append(ev2)
            if h == 1:
                T["kf_rd"] = tt2
                T["vf_rd"] = tt2
        for h in range(2):
            def mkv(e, h=h):
                inst = None
                for ka in range(32):
                    inst = e.matmul(psKV[h][:], kq_tk[h][:, 64 * ka:64 * (ka + 1)],
                                    v2tk[h][:, 65 * ka:65 * (ka + 1)],
                                    start=(ka == 0), stop=(ka == 31))
                return inst
            tkv = pb.op("tensor", mkv, waits=[trd[-1], T.get(("psKVg", h))], inc="T")
            tkc = pb.op("vector", lambda e, h=h: e.tensor_copy(
                kvs[64 * h:64 * h + 64, :], psKV[h][:]),
                waits=[tkv, T.get(("kvs_rd", h))], inc="V")
            T[("psKVg", h)] = tkc
            T[("kvsrdy", h)] = tkc
        for jc in range(NC):
            for h in range(2):
                tmy = pb.op("tensor", lambda e, h=h, jc=jc: e.matmul(
                    psY2[h][:], kvs[64 * h:64 * h + 64, :],
                    qf[64 * h:64 * h + 64, jc * 512:(jc + 1) * 512],
                    tile_position=(64 * h, 0), start=True, stop=True),
                    waits=[tq, T[("kvsrdy", h)], T.get(("psY2g", h))], inc="T")
                teps = pb.op("vector", lambda e, h=h: e.tensor_scalar(
                    rce[:], psY2[h][64:65, :], 1e-6, None, ALU.add, ALU.bypass),
                    waits=[tmy, T.get("rce_rd")], inc="V")
                trc = pb.op("vector", lambda e: e.reciprocal(rcol[:], rce[:]),
                            waits=[teps, T.get("rcol_rd")], inc="V")
                T["rce_rd"] = trc
                trp = pb.op("tensor", lambda e: e.matmul(
                    psR[:], ones1[:], rcol[:], start=True, stop=True),
                    waits=[trc, tON, T.get("psRg")], inc="T")
                tyu = pb.op("scalar", lambda e, h=h: e.copy(yun[:], psY2[h][0:64, :]),
                            waits=[teps, T.get("yun_rd")], inc="S")
                T[("psY2g", h)] = tyu
                trs = pb.op("vector", lambda e: e.tensor_copy(rep_sb[:], psR[:]),
                            waits=[trp, T.get("rep_rd")], inc="V")
                T["psRg"] = trs
                T["rcol_rd"] = trp
                tyn = pb.op("vector", lambda e: e.tensor_mul(ybf[:], yun[:], rep_sb[:]),
                            waits=[tyu, trs, T.get("ybf_rd")], inc="V")
                T["yun_rd"] = tyn
                T["rep_rd"] = tyn
                tw = pb.op("gpsimd", lambda e, hp=hp, h=h, jc=jc: e.dma_start(
                    out=yd[128 * hp + 64 * h:128 * hp + 64 * h + 64, jc * 512:(jc + 1) * 512],
                    in_=ybf[:]), waits=[tyn], inc="Hs", inc_by=16)
                T["ybf_rd"] = tw
                T["y_done"] = tw
        T["qf_rd"] = tmy
        for h in range(2):
            T[("v2g", h)] = tmy
            T[("kqg", h)] = T[("psKVg", h)]
            T[("kvs_rd", h)] = tmy
    pb.pop_scope(gate=[T["y_done"]])
    emit_outproj_residual(pb, yd, woP, wordy, boc, bordy, xv, ov, NC, idb, idf, False)
    if fused:
        return [("St", pb.cnt["St"])]
    pb.wait("gpsimd", ("St", pb.cnt["St"]))
    return pb.finalize()



def build_l2attn(R=4096, pb=None, io=None):
    fused = pb is not None
    if not fused:
        pb = PB()
    NN = 512
    gi = lambda n, s, d=F32: io[n] if fused else pb.dram(n, s, d, kind="ExternalInput")
    xin = gi("xin", [R, D])
    nu = gi("nu_log", [NN])
    th = gi("theta_log", [NN])
    ga = gi("gamma_log", [NN])
    Bre = gi("B_re", [NN, D])
    Bim = gi("B_im", [NN, D])
    Cre = gi("C_re", [D, NN])
    Cim = gi("C_im", [D, NN])
    Dm = gi("Dmat", [D, D])
    xout = io["xout"] if fused else pb.dram("xout", [R, D], F32, kind="ExternalOutput")
    xsr_d = pb.dram("xsr_d", [NN, R], BF16)
    xsi_d = pb.dram("xsi_d", [NN, R], BF16)
    idf, idb = emit_identity(pb)
    alloc_ln(pb)
    T = pb.tag
    NT, NC = R // 128, R // 512
    xv = xin.rearrange("(t p) d -> t p d", p=128)
    ov = xout.rearrange("(t p) d -> t p d", p=128)
    nuc, tnu = emit_bias_cols(pb, nu, NN, "nu", idf)
    thc, tth = emit_bias_cols(pb, th, NN, "th", idf)
    gac, tga = emit_bias_cols(pb, ga, NN, "ga", idf)
    rho = pb.sb("rho", [128, 4], F32)
    the = pb.sb("the", [128, 4], F32)
    gam = pb.sb("gam", [128, 4], F32)
    t1 = pb.op("scalar", lambda e: e.activation(rho[:], nuc[:], AF.Exp), waits=[tnu], inc="S")
    t2 = pb.op("scalar", lambda e: e.activation(rho[:], rho[:], AF.Exp, scale=np.float32(-1.0)),
               waits=[t1], inc="S")
    t3 = pb.op("scalar", lambda e: e.activation(the[:], thc[:], AF.Exp), waits=[tth], inc="S")
    t4 = pb.op("scalar", lambda e: e.activation(gam[:], gac[:], AF.Exp), waits=[tga], inc="S")
    xnfm = pb.sb("xnfm", [128, 8 * R], BF16)
    pb.push_scope()
    alloc_wstage(pb, 2048, f8=False, tst=None)
    Bgr, tbr = emit_weight_T(pb, Bre, NN, D, "bgr", fold_cols=(gam, t4))
    Bgi, tbi = emit_weight_T(pb, Bim, NN, D, "bgi", fold_cols=(gam, t4))
    xa = [pb.sb(f"xa{i}", [128, D], F32) for i in range(2)]
    xqb = pb.sb("xqb", [128, D], BF16)
    psT = [pb.ps(f"psT{i}", [128, 128], BF16) for i in range(2)]
    psM = [pb.ps(f"psM{i}", [128, 512], F32) for i in range(2)]
    emit_ln_transpose_all(pb, xv, NT, xnfm, R, idb, xa, xqb, psT)
    br = pb.sb("br", [128, R], F32)
    bi = pb.sb("bi", [128, R], F32)
    s1 = pb.sb("s1", [128, R], F32)
    s2 = pb.sb("s2", [128, R], F32)
    cosT = pb.sb("cosT", [128, R], F32)
    sinT = pb.sb("sinT", [128, R], F32)

    for nt in range(4):
        evs = []
        for ch in range(NC):
            for ri, (wT, wrdy, dst) in enumerate(((Bgr, tbr, br), (Bgi, tbi, bi))):
                ps = psM[ri]
                def mmb(e, nt=nt, ch=ch, ps=ps, wT=wT):
                    inst = None
                    for k in range(8):
                        inst = e.matmul(ps[:], wT[:, k * NN + 128 * nt:k * NN + 128 * (nt + 1)],
                                        xnfm[:, k * R + ch * 512:k * R + ch * 512 + 512],
                                        start=(k == 0), stop=(k == 7))
                    return inst
                tm = pb.op("tensor", mmb, waits=[T["xnfm"], wrdy, T.get(("psMb", ri))], inc="T")
                te = pb.op("scalar", lambda e, ch=ch, ps=ps, dst=dst: e.copy(
                    dst[:, ch * 512:(ch + 1) * 512], ps[:]),
                    waits=[tm, T.get(("bu_rd", id(dst)))], inc="S")
                T[("psMb", ri)] = te
                evs.append(te)
        tio = pb.op("gpsimd", lambda e: e.iota(
            s1[:], [[1, R]], base=0, channel_multiplier=0,
            allow_small_or_imprecise_dtypes=True), waits=[T.get("s1_rd")], inc="G")
        tan = pb.op("vector", lambda e, nt=nt: e.tensor_scalar(
            s1[:], s1[:], the[:, nt:nt + 1], float(1.0 / (2 * np.pi)), ALU.mult, ALU.mult),
            waits=[tio, t3], inc="V")
        tk1 = pb.op("vector", lambda e: e.tensor_copy(pb.q_i32[:, 0:R] if False else s2[:].bitcast(I32), s1[:]), waits=[tan, T.get("s2_rd")], inc="V")
        tk2 = pb.op("vector", lambda e: e.tensor_copy(s2[:], s2[:].bitcast(I32)), waits=[tk1], inc="V")
        tk3 = pb.op("vector", lambda e: e.tensor_sub(s1[:], s1[:], s2[:]), waits=[tk2], inc="V")
        tk4 = pb.op("vector", lambda e: e.tensor_scalar(
            s1[:], s1[:], float(2 * np.pi), None, ALU.mult, ALU.bypass), waits=[tk3], inc="V")
        tab = pb.op("scalar", lambda e: e.activation(s2[:], s1[:], AF.Abs), waits=[tk4], inc="S")
        tco = pb.op("scalar", lambda e: e.activation(
            cosT[:], s2[:], AF.Sin, scale=np.float32(-1.0), bias=np.float32(np.pi / 2)),
            waits=[tab, T.get("cos_rd")], inc="S")
        tsi = pb.op("scalar", lambda e: e.activation(sinT[:], s1[:], AF.Sin),
                    waits=[tk4, T.get("sin_rd")], inc="S")
        w0 = evs + [tco, tsi]
        r1 = pb.op("vector", lambda e: e.tensor_mul(s1[:], cosT[:], br[:]), waits=w0 + [T.get("s1w")], inc="V")
        r2 = pb.op("vector", lambda e: e.tensor_mul(s2[:], sinT[:], bi[:]), waits=w0 + [T.get("s2w")], inc="V")
        r3 = pb.op("vector", lambda e: e.tensor_add(s1[:], s1[:], s2[:]), waits=[r1, r2], inc="V")
        r4 = pb.op("vector", lambda e: e.tensor_mul(s2[:], cosT[:], bi[:]), waits=[r3], inc="V")
        r5 = pb.op("vector", lambda e: e.tensor_mul(bi[:], sinT[:], br[:]), waits=[r4, T.get("bi_rd")], inc="V")
        r6 = pb.op("vector", lambda e: e.tensor_sub(s2[:], s2[:], bi[:]), waits=[r5], inc="V")
        T[("bu_rd", id(br))] = r5
        prev = [r3, r6]
        for ri, (src, dst) in enumerate(((s1, br), (s2, bi))):
            tprev = prev[ri]
            for c in range(NC):
                init = 0.0 if c == 0 else dst[:, c * 512 - 1:c * 512]
                tprev = pb.op("vector", lambda e, src=src, dst=dst, c=c, init=init, nt=nt:
                              e.tensor_tensor_scan(
                    dst[:, c * 512:(c + 1) * 512],
                    rho[:, nt:nt + 1].broadcast_to([128, 512]),
                    src[:, c * 512:(c + 1) * 512], init, ALU.mult, ALU.add),
                    waits=[tprev, t2], inc="V")
            prev[ri] = tprev
        T[("bu_rd", id(bi))] = prev[1]
        p1 = pb.op("vector", lambda e: e.tensor_mul(s1[:], cosT[:], br[:]), waits=prev, inc="V")
        p2 = pb.op("vector", lambda e: e.tensor_mul(s2[:], sinT[:], bi[:]), waits=prev, inc="V")
        p3 = pb.op("vector", lambda e: e.tensor_sub(s1[:], s1[:], s2[:]), waits=[p1, p2], inc="V")
        p4 = pb.op("vector", lambda e: e.tensor_mul(s2[:], cosT[:], bi[:]), waits=[p3], inc="V")
        p5 = pb.op("vector", lambda e: e.tensor_mul(bi[:], sinT[:], br[:]), waits=[p4], inc="V")
        T[("bu_rd", id(br))] = p5
        T["cos_rd"] = p4
        T["sin_rd"] = p5
        p6 = pb.op("vector", lambda e: e.tensor_add(s2[:], s2[:], bi[:]), waits=[p5], inc="V")
        T[("bu_rd", id(bi))] = p6
        w1 = pb.op("gpsimd", lambda e, nt=nt: e.dma_start(
            out=xsr_d[128 * nt:128 * (nt + 1), :], in_=s1[:]), waits=[p3], inc="Hs", inc_by=16)
        w2 = pb.op("gpsimd", lambda e, nt=nt: e.dma_start(
            out=xsi_d[128 * nt:128 * (nt + 1), :], in_=s2[:]), waits=[p6], inc="Hs", inc_by=16)
        T["s1w"] = w1
        T["s2w"] = w2
        T["s1_rd"] = w1
        T["s2_rd"] = w2
        T["xs_done"] = w2
    pb.pop_scope(gate=[T["xs_done"]])
    alloc_wstage(pb, 4096, f8=False, tst=None, sfx="y")
    CrT, tcr = emit_weight_T(pb, Cre, D, NN, "crt")
    CiT, tci = emit_weight_T(pb, Cim, D, NN, "cit", fold=-1.0)
    DmT, tdm = emit_weight_T(pb, Dm, D, D, "dmt")
    xmv = [pb.sb(f"xmv{i}", [128, 512], BF16) for i in range(8)]
    psM2 = [pb.ps(f"ps2y{i}", [128, 512], F32) for i in range(2)]
    psY = [pb.ps(f"psYy{i}", [128, 128], BF16) for i in range(2)]
    yofm = pb.sb("yofm2", [128, 8 * 512], BF16)
    ytk = pb.sb("ytk2", [128, D], BF16)
    xar = [pb.sb(f"xry{i}", [128, D], F32) for i in range(2)]
    for ch in range(NC):
        lds = []
        for k in range(4):
            tl = pb.op("sync", lambda e, k=k, ch=ch: e.dma_start(
                out=xmv[k][:], in_=xsr_d[128 * k:128 * (k + 1), 512 * ch:512 * (ch + 1)]),
                waits=[T["xs_done"], T.get(("xmv", k))], inc="Xq", inc_by=16)
            tl2 = pb.op("sync", lambda e, k=k, ch=ch: e.dma_start(
                out=xmv[4 + k][:], in_=xsi_d[128 * k:128 * (k + 1), 512 * ch:512 * (ch + 1)]),
                waits=[T["xs_done"], T.get(("xmv", 4 + k))], inc="Xq", inc_by=16)
            lds += [tl, tl2]
        for m in range(8):
            ps = psM2[m % 2]
            def mmy(e, m=m, ch=ch, ps=ps):
                inst = None
                for k in range(4):
                    inst = e.matmul(ps[:], CrT[:, k * D + 128 * m:k * D + 128 * (m + 1)],
                                    xmv[k][:], start=(k == 0), stop=False)
                for k in range(4):
                    inst = e.matmul(ps[:], CiT[:, k * D + 128 * m:k * D + 128 * (m + 1)],
                                    xmv[4 + k][:], start=False, stop=False)
                for k in range(8):
                    inst = e.matmul(ps[:], DmT[:, k * D + 128 * m:k * D + 128 * (m + 1)],
                                    xnfm[:, k * R + ch * 512:k * R + ch * 512 + 512],
                                    start=False, stop=(k == 7))
                return inst
            tm = pb.op("tensor", mmy, waits=lds + [tcr, tci, tdm, T["xnfm"],
                       T.get(("psM2y", m % 2))], inc="T")
            if m == 7:
                for k in range(8):
                    T[("xmv", k)] = tm
            te = pb.op("scalar", lambda e, m=m, ps=ps: e.copy(
                yofm[:, m * 512:(m + 1) * 512], ps[:]),
                waits=[tm, T.get(("yofmy_rd", m))], inc="S")
            T[("psM2y", m % 2)] = te
            T[("yofmy", m)] = te
        for ti in range(4):
            t = 4 * ch + ti
            trb = []
            for m in range(8):
                ps = psY[m % 2]
                tt = pb.op("tensor", lambda e, m=m, ti=ti, ps=ps: e.transpose(
                    ps[:], yofm[:, m * 512 + ti * 128: m * 512 + ti * 128 + 128], idb[:]),
                    waits=[T[("yofmy", m)], T.get(("psYy", m % 2))] + T["id"], inc="T")
                if ti == 3:
                    T[("yofmy_rd", m)] = tt
                ev = pb.op("vector", lambda e, m=m, ps=ps: e.tensor_copy(
                    ytk[:, 128 * m:128 * (m + 1)], ps[:]),
                    waits=[tt, T.get("ytky_rd")], inc="V")
                T[("psYy", m % 2)] = ev
                trb.append(ev)
            a = xar[ti % 2]
            tlx = pb.op("sync", lambda e, t=t, a=a: e.dma_start(out=a[:], in_=xv[t]),
                        waits=[T.get(("xay", ti % 2))], inc="Xq", inc_by=16)
            tad = pb.op("vector", lambda e, a=a: e.tensor_add(a[:], a[:], ytk[:]),
                        waits=trb + [tlx], inc="V")
            T["ytky_rd"] = tad
            tst = pb.op("scalar", lambda e, t=t, a=a: e.dma_start(out=ov[t], in_=a[:]),
                        waits=[tad], inc="St", inc_by=16)
            T[("xay", ti % 2)] = tst
    if fused:
        return [("St", pb.cnt["St"])]
    pb.wait("gpsimd", ("St", pb.cnt["St"]))
    return pb.finalize()




def emit_exchange(pb, x_own, exsel_sb, xfull):
    zb = pb.dram("exz", [4096, D])
    tile = [pb.sb(f"ex_t{i}", [128, D]) for i in range(2)]
    mt = [pb.sb(f"ex_m{i}", [128, D]) for i in range(2)]
    xv = x_own.rearrange("(t p) d -> t p d", p=128)
    zv = zb.rearrange("(t p) d -> t p d", p=128)
    T = pb.tag
    ts = None
    for t in range(16):
        a, m = tile[t % 2], mt[t % 2]
        tl = pb.op("sync", lambda e, t=t, a=a: e.dma_start(out=a[:], in_=xv[t]),
                   waits=[T.get(("ex_t", t % 2))], inc="Xq", inc_by=16)
        tm0 = pb.op("vector", lambda e, t=t, a=a, m=m: e.tensor_scalar(
            m[:], a[:], exsel_sb[:, t:t + 1], None, ALU.mult, ALU.bypass),
            waits=[tl, T.get(("ex_m", t % 2)), T.get("exsel")], inc="V")
        ts0 = pb.op("scalar", lambda e, t=t, m=m: e.dma_start(out=zv[t], in_=m[:]),
                    waits=[tm0], inc="St", inc_by=16)
        tm1 = pb.op("vector", lambda e, t=t, a=a, m=m: e.tensor_scalar(
            m[:], a[:], exsel_sb[:, t + 16:t + 17], None, ALU.mult, ALU.bypass),
            waits=[ts0], inc="V")
        T[("ex_t", t % 2)] = tm1
        ts = pb.op("scalar", lambda e, t=t, m=m: e.dma_start(out=zv[t + 16], in_=m[:]),
                   waits=[tm1], inc="St", inc_by=16)
        T[("ex_m", t % 2)] = ts
    tc = pb.op("gpsimd", lambda e: e.collective_compute(
        "AllReduce", ALU.add, [[0, 1], [2, 3], [4, 5], [6, 7]],
        ins=[zb.opt()], outs=[xfull.opt()]), waits=[ts], inc="CC")
    return tc


def emit_select_own(pb, xafull, sela_sb, selb_sb, x_own_out):
    xv = xafull.rearrange("(t p) d -> t p d", p=128)
    ov = x_own_out.rearrange("(t p) d -> t p d", p=128)
    a = pb.sb("so_a", [128, D])
    b = pb.sb("so_b", [128, D])
    T = pb.tag
    ts = None
    for t in range(16):
        tla = pb.op("sync", lambda e, t=t: e.dma_start(out=a[:], in_=xv[t]),
                    waits=[T.get("so_a_rd")], inc="Xq", inc_by=16)
        tlb = pb.op("sync", lambda e, t=t: e.dma_start(out=b[:], in_=xv[t + 16]),
                    waits=[T.get("so_b_rd")], inc="Xq", inc_by=16)
        tma = pb.op("vector", lambda e, t=t: e.tensor_scalar(
            a[:], a[:], sela_sb[:, t:t + 1], None, ALU.mult, ALU.bypass),
            waits=[tla, T.get("sel")], inc="V")
        tmb = pb.op("vector", lambda e, t=t: e.tensor_scalar(
            b[:], b[:], selb_sb[:, t:t + 1], None, ALU.mult, ALU.bypass),
            waits=[tlb], inc="V")
        tad = pb.op("vector", lambda e: e.tensor_add(a[:], a[:], b[:]),
                    waits=[tma, tmb], inc="V")
        T["so_b_rd"] = tad
        ts = pb.op("scalar", lambda e, t=t: e.dma_start(out=ov[t], in_=a[:]),
                   waits=[tad], inc="St", inc_by=16)
        T["so_a_rd"] = ts
    return ts


_FUSED_SPECS = {
    "l0f": {"w1": [4096, D], "b1": [4096], "w2": [D, 4096], "b2": [D]},
    "l0a": {"Wq": [64, 64], "Wk": [64, 64], "Wv": [64, 64], "Wu": [D, D], "bu": [D]},
    "l1f": {"W_exp": [D, 4096], "b_exp": [4096], "up_w": [12, 2048, 2, 2],
            "up_b": [4096], "dn_w": [12, 2048, 2, 2], "W_con": [4096, D],
            "b_con": [D], "dn_b": [D]},
    "l1a": {"Wq": [D, 16, 64], "bq": [16, 64], "Wk": [D, 16, 64], "bk": [16, 64],
            "Wv": [D, 16, 64], "bv": [16, 64], "Wo": [16, 64, D], "bo": [D]},
    "l2f": {"w_gate": [3072, D], "w_up": [3072, D], "w_down": [D, 3072]},
    "l2a": {"nu_log": [512], "theta_log": [512], "gamma_log": [512],
            "B_re": [512, D], "B_im": [512, D], "C_re": [D, 512], "C_im": [D, 512],
            "Dmat": [D, D]},
    "l3f": {"w_gate": [2816, D], "w_up": [2816, D], "w_down": [D, 2816]},
    "l3a": {"w_in": [3 * D, D], "b_in": [3 * D], "w_out": [D, D], "b_out": [D]},
}


def build_fused(layers=(0, 1, 2, 3)):
    pb = PB()
    x0 = pb.dram("x0", [2048, D], F32, kind="ExternalInput")
    xoutF = pb.dram("xout", [2048, D], F32, kind="ExternalOutput")
    mask = pb.dram("mask", [2048, D], F32, kind="ExternalInput")
    exsel_in = pb.dram("exsel", [128, 32], F32, kind="ExternalInput")
    sela_in = pb.dram("sela", [128, 16], F32, kind="ExternalInput")
    selb_in = pb.dram("selb", [128, 16], F32, kind="ExternalInput")
    W = {}
    for grp, dd in _FUSED_SPECS.items():
        if int(grp[1]) not in layers:
            continue
        W[grp] = {n: pb.dram(f"{grp}_{n}", s, F32, kind="ExternalInput")
                  for n, s in dd.items()}

    emit_identity(pb)
    exsel_sb = pb.sb("exsel_sb", [128, 32])
    sela_sb = pb.sb("sela_sb", [128, 16])
    selb_sb = pb.sb("selb_sb", [128, 16])
    t1 = pb.op("sync", lambda e: e.dma_start(out=exsel_sb[:], in_=exsel_in), inc="Xq", inc_by=16)
    t2 = pb.op("sync", lambda e: e.dma_start(out=sela_sb[:], in_=sela_in), inc="Xq", inc_by=16)
    t3 = pb.op("sync", lambda e: e.dma_start(out=selb_sb[:], in_=selb_in), inc="Xq", inc_by=16)
    pb.tag["exsel"] = t1
    pb.tag["sel"] = t3

    ffns = {0: (build_l0ffn, "l0f"), 1: (build_l1ffn, "l1f"),
            2: (build_l2ffn, "l2f"), 3: (build_l3ffn, "l3f")}
    attns = {0: (lambda **kw: build_causal_attn("l0", **kw), "l0a"),
             1: (build_l1attn, "l1a"),
             2: (build_l2attn, "l2a"),
             3: (lambda **kw: build_causal_attn("l3", **kw), "l3a")}

    x_cur = x0
    for L in layers:
        fb, fkey = ffns[L]
        ab, akey = attns[L]
        for rep in range(2):
            last = (L == max(layers) and rep == 1)
            xn = xoutF if last else pb.dram(f"xc{L}{rep}", [2048, D])
            pb.prefix = f"F{L}{rep}_"
            pb.push_scope()
            io = dict(W[fkey])
            io["xin"] = x_cur
            io["xout"] = xn
            if fkey == "l0f":
                io["mask"] = mask
            g = fb(R=2048, pb=pb, io=io)
            pb.pop_scope(gate=g)
            x_cur = xn
            if rep == 0:
                pb.prefix = f"X{L}_"
                pb.push_scope()
                xfull = pb.dram("xfull", [4096, D])
                tc = emit_exchange(pb, x_cur, exsel_sb, xfull)
                pb.pop_scope(gate=[tc])
                xa = pb.dram(f"Xa{L}_res", [4096, D])
                pb.prefix = f"A{L}_"
                pb.push_scope()
                io = dict(W[akey])
                io["xin"] = xfull
                io["xout"] = xa
                g = ab(R=4096, pb=pb, io=io)
                pb.pop_scope(gate=g)
                pb.prefix = f"S{L}_"
                pb.push_scope()
                xo = pb.dram("xo", [2048, D])
                g2 = emit_select_own(pb, xa, sela_sb, selb_sb, xo)
                pb.pop_scope(gate=[g2])
                x_cur = xo
    pb.prefix = ""
    for s in ("St", "Xq", "Wq", "Ws", "Hs", "CC"):
        if s in pb.cnt:
            pb.wait("gpsimd", (s, pb.cnt[s]))
    return pb.finalize()


_FUSED_PKEYS = {
    "l0f": {"w1": "L0_ffn_w1", "b1": "L0_ffn_b1", "w2": "L0_ffn_w2", "b2": "L0_ffn_b2"},
    "l0a": {"Wq": "L0_attn_Wq", "Wk": "L0_attn_Wk", "Wv": "L0_attn_Wv",
            "Wu": "L0_attn_Wu", "bu": "L0_attn_bu"},
    "l1f": {"W_exp": "L1_ffn_W_exp", "b_exp": "L1_ffn_b_exp", "up_w": "L1_ffn_up_weights",
            "up_b": "L1_ffn_up_bias", "dn_w": "L1_ffn_down_weights",
            "W_con": "L1_ffn_W_con", "b_con": "L1_ffn_b_con", "dn_b": "L1_ffn_down_bias"},
    "l1a": {"Wq": "L1_attn_Wq", "bq": "L1_attn_bq", "Wk": "L1_attn_Wk", "bk": "L1_attn_bk",
            "Wv": "L1_attn_Wv", "bv": "L1_attn_bv", "Wo": "L1_attn_Wo", "bo": "L1_attn_bo"},
    "l2f": {"w_gate": "L2_ffn_w_gate", "w_up": "L2_ffn_w_up", "w_down": "L2_ffn_w_down"},
    "l2a": {"nu_log": "L2_attn_nu_log", "theta_log": "L2_attn_theta_log",
            "gamma_log": "L2_attn_gamma_log", "B_re": "L2_attn_B_re", "B_im": "L2_attn_B_im",
            "C_re": "L2_attn_C_re", "C_im": "L2_attn_C_im", "Dmat": "L2_attn_D"},
    "l3f": {"w_gate": "L3_ffn_w_gate", "w_up": "L3_ffn_w_up", "w_down": "L3_ffn_w_down"},
    "l3a": {"w_in": "L3_attn_w_in", "b_in": "L3_attn_b_in",
            "w_out": "L3_attn_w_out", "b_out": "L3_attn_b_out"},
}


_PROG_CACHE = {}
_MASK_CACHE = {}


def _get_prog(name, builder):
    if name not in _PROG_CACHE:
        _PROG_CACHE[name] = builder()
    return _PROG_CACHE[name]


def _threefry2x32(k0, k1, x0, x1):
    u32 = np.uint32
    ks0, ks1 = k0.astype(u32), k1.astype(u32)
    ks2 = (ks0 ^ ks1 ^ u32(0x1BD11BDA)).astype(u32)
    x0 = (x0 + ks0).astype(u32)
    x1 = (x1 + ks1).astype(u32)
    rot = [[13, 15, 26, 6], [17, 29, 16, 24]]
    ks = [ks0, ks1, ks2]
    for i in range(5):
        for r in rot[i % 2]:
            x0 = (x0 + x1).astype(u32)
            x1 = ((x1 << u32(r)) | (x1 >> u32(32 - r))).astype(u32)
            x1 = (x1 ^ x0).astype(u32)
        x0 = (x0 + ks[(i + 1) % 3]).astype(u32)
        x1 = (x1 + ks[(i + 2) % 3] + u32(i + 1)).astype(u32)
    return x0, x1


def _dropout_mask05():
    if "m" not in _MASK_CACHE:
        u32 = np.uint32
        n = 4 * 4096 * 1024
        j = np.arange(n, dtype=u32)
        z = np.zeros(n, u32)
        o0, o1 = _threefry2x32(np.full(n, 0, u32), np.full(n, 42, u32), z, j)
        bits = (o0 ^ o1).astype(u32)
        u = ((bits >> u32(9)) | u32(0x3F800000)).view(np.float32) - np.float32(1.0)
        keep = u < np.float32(0.8)
        _MASK_CACHE["m"] = np.where(keep, np.float32(0.5 / 0.8),
                                    np.float32(0.0)).astype(np.float32).reshape(16384, 1024)
    return _MASK_CACHE["m"]


def _run_ffn(name, builder, x, per_core_ins):
    nc = _get_prog(name, builder)
    im = []
    for c in range(8):
        d = {"xin": np.ascontiguousarray(x[c * 2048:(c + 1) * 2048])}
        d.update(per_core_ins(c))
        im.append(d)
    res = run_bass_kernel_spmd(nc, im, core_ids=list(range(8)))
    return np.concatenate([res.results[c]["xout"] for c in range(8)], 0)


def _run_attn(name, builder, x, common_ins):
    nc = _get_prog(name, builder)
    im = []
    for c in range(8):
        b = c // 2
        d = {"xin": np.ascontiguousarray(x[b * 4096:(b + 1) * 4096])}
        d.update(common_ins)
        im.append(d)
    res = run_bass_kernel_spmd(nc, im, core_ids=list(range(8)))
    out = np.empty_like(x)
    for c in range(8):
        b, h = c // 2, c % 2
        out[b * 4096 + h * 2048: b * 4096 + (h + 1) * 2048] = \
            res.results[c]["xout"][h * 2048:(h + 1) * 2048]
    return out


def kernel(**P):
    x = np.ascontiguousarray(np.asarray(P["L0_attn_x"], np.float32).reshape(16384, 1024))
    mask = _dropout_mask05()
    nc = _get_prog("fused", build_fused)
    im = []
    for c in range(8):
        half = c % 2
        exsel = np.zeros((128, 32), np.float32)
        exsel[:, half * 16:(half + 1) * 16] = 1.0
        sela = np.full((128, 16), 1.0 if half == 0 else 0.0, np.float32)
        selb = np.full((128, 16), 1.0 if half == 1 else 0.0, np.float32)
        d = {"x0": np.ascontiguousarray(x[c * 2048:(c + 1) * 2048]),
             "mask": np.ascontiguousarray(mask[c * 2048:(c + 1) * 2048]),
             "exsel": exsel, "sela": sela, "selb": selb}
        for grp, mp in _FUSED_PKEYS.items():
            for n, pk in mp.items():
                d[f"{grp}_{n}"] = np.ascontiguousarray(np.asarray(P[pk], np.float32))
        im.append(d)
    res = run_bass_kernel_spmd(nc, im, core_ids=list(range(8)))
    out = np.concatenate([res.results[c]["xout"] for c in range(8)], 0)
    return out.reshape(4, 4096, 1024)


def kernel_unfused(**P):
    x = np.ascontiguousarray(np.asarray(P["L0_attn_x"], np.float32).reshape(16384, 1024))
    mask = _dropout_mask05()

    l0f = lambda xx: _run_ffn("l0ffn", build_l0ffn, xx, lambda c: {
        "w1": P["L0_ffn_w1"], "b1": P["L0_ffn_b1"], "w2": P["L0_ffn_w2"],
        "b2": P["L0_ffn_b2"], "mask": np.ascontiguousarray(mask[c * 2048:(c + 1) * 2048])})
    l1f = lambda xx: _run_ffn("l1ffn", build_l1ffn, xx, lambda c: {
        "W_exp": P["L1_ffn_W_exp"], "b_exp": P["L1_ffn_b_exp"], "up_w": P["L1_ffn_up_weights"],
        "up_b": P["L1_ffn_up_bias"], "dn_w": P["L1_ffn_down_weights"],
        "W_con": P["L1_ffn_W_con"], "b_con": P["L1_ffn_b_con"], "dn_b": P["L1_ffn_down_bias"]})
    l2f = lambda xx: _run_ffn("l2ffn", build_l2ffn, xx, lambda c: {
        "w_gate": P["L2_ffn_w_gate"], "w_up": P["L2_ffn_w_up"], "w_down": P["L2_ffn_w_down"]})
    l3f = lambda xx: _run_ffn("l3ffn", build_l3ffn, xx, lambda c: {
        "w_gate": P["L3_ffn_w_gate"], "w_up": P["L3_ffn_w_up"], "w_down": P["L3_ffn_w_down"]})

    x = l0f(x)
    x = _run_attn("l0attn", lambda: build_causal_attn("l0"), x, {
        "Wq": P["L0_attn_Wq"], "Wk": P["L0_attn_Wk"], "Wv": P["L0_attn_Wv"],
        "Wu": P["L0_attn_Wu"], "bu": P["L0_attn_bu"]})
    x = l0f(x)
    x = l1f(x)
    x = _run_attn("l1attn", build_l1attn, x, {
        "Wq": P["L1_attn_Wq"], "bq": P["L1_attn_bq"], "Wk": P["L1_attn_Wk"],
        "bk": P["L1_attn_bk"], "Wv": P["L1_attn_Wv"], "bv": P["L1_attn_bv"],
        "Wo": P["L1_attn_Wo"], "bo": P["L1_attn_bo"]})
    x = l1f(x)
    x = l2f(x)
    x = _run_attn("l2attn", build_l2attn, x, {
        "nu_log": P["L2_attn_nu_log"], "theta_log": P["L2_attn_theta_log"],
        "gamma_log": P["L2_attn_gamma_log"], "B_re": P["L2_attn_B_re"],
        "B_im": P["L2_attn_B_im"], "C_re": P["L2_attn_C_re"], "C_im": P["L2_attn_C_im"],
        "Dmat": P["L2_attn_D"]})
    x = l2f(x)
    x = l3f(x)
    x = _run_attn("l3attn", lambda: build_causal_attn("l3"), x, {
        "w_in": P["L3_attn_w_in"], "b_in": P["L3_attn_b_in"],
        "w_out": P["L3_attn_w_out"], "b_out": P["L3_attn_b_out"]})
    x = l3f(x)
    return x.reshape(4, 4096, 1024)


# version 05
# speedup vs baseline: 1.1821x; 1.1233x over previous
import math
import numpy as np
from contextlib import ExitStack

import concourse.bass as bass
import concourse.mybir as mybir
from concourse.bass_utils import run_bass_kernel_spmd

F32 = mybir.dt.float32
BF16 = mybir.dt.bfloat16
F16 = mybir.dt.float16
F8 = mybir.dt.float8e4
I32 = mybir.dt.int32
AF = mybir.ActivationFunctionType
ALU = mybir.AluOpType
AX = mybir.AxisListType

ENGS = ("tensor", "vector", "scalar", "gpsimd", "sync")
NCORES = 8
D = 1024
EPS = 1e-5
PRUNE_B = False


class PB:

    def __init__(self):
        self.nc = bass.Bass("TRN2", target_bir_lowering=False)
        self.es = ExitStack()
        self.ops = {k: [] for k in ENGS}
        self.sem = {}
        self.cnt = {}
        self._uid = 0
        self.tag = {}
        self._owner = {}
        self._ewait = {k: {} for k in ENGS}
        self._scopes = []
        self.mem_gate = {}
        self.prefix = ""

    def uid(self):
        self._uid += 1
        return self._uid

    def semaphore(self, name):
        if name not in self.sem:
            self.sem[name] = self.es.enter_context(self.nc.semaphore(name))
            self.cnt[name] = 0
        return self.sem[name]

    def sb(self, name, shape, dtype=F32):
        es = self._scopes[-1] if self._scopes else self.es
        return es.enter_context(self.nc.sbuf_tensor(self.prefix + name, list(shape), dtype))

    def push_scope(self):
        self._scopes.append(ExitStack())

    def pop_scope(self, gate=()):
        self._scopes.pop().close()
        for t in (gate if isinstance(gate, (list, tuple)) else [gate]):
            if t is not None and t[1] > self.mem_gate.get(t[0], -1):
                self.mem_gate[t[0]] = t[1]

    def ps(self, name, shape, dtype=F32):
        es = self._scopes[-1] if self._scopes else self.es
        return es.enter_context(self.nc.psum_tensor(self.prefix + name, list(shape), dtype))

    def dram(self, name, shape, dtype=F32, kind="Internal"):
        nm = name if kind != "Internal" else self.prefix + name
        return self.nc.dram_tensor(nm, list(shape), dtype, kind=kind).ap()

    def const(self, arr, name=None):
        nm = self.prefix + (name or f"c{self.uid()}")
        return self.nc.inline_tensor(np.ascontiguousarray(arr), nm).ap()

    def _prune(self, eng, w):
        best = {}
        for s, v in w:
            if v > best.get(s, -1):
                best[s] = v
        ew = self._ewait[eng]
        out = []
        for s, v in best.items():
            if PRUNE_B and self._owner.get(s, 0) == eng and v <= self.cnt.get(s, 0):
                continue
            if ew.get(s, -1) >= v:
                continue
            out.append((s, v))
            ew[s] = v
        return out

    def op(self, eng, fn, waits=(), inc=None, inc_by=1):
        w = [x for x in waits if x is not None] + list(self.mem_gate.items())
        tag = None
        if inc is not None:
            self.semaphore(inc)
            own = self._owner.setdefault(inc, None if inc_by == 16 else eng)
            if inc_by == 16 or own != eng:
                self._owner[inc] = None
            if inc_by == 16 and self.cnt[inc] > 0:
                w.append((inc, self.cnt[inc]))
        w = self._prune(eng, w)
        if inc is not None:
            self.cnt[inc] += inc_by
            tag = (inc, self.cnt[inc])
        for s, _ in w:
            self.semaphore(s)
        sems = self.sem

        def run(e, w=tuple(w), fn=fn, tag=tag, inc_by=inc_by):
            for (s, v) in w:
                e.wait_ge(sems[s], v)
            inst = fn(e)
            if tag is not None:
                assert inst is not None
                inst.then_inc(sems[tag[0]], inc_by)

        self.ops[eng].append(run)
        return tag

    def wait(self, eng, *tags):
        for s, v in self._prune(eng, [t for t in tags if t is not None]):
            self.semaphore(s)
            self.ops[eng].append(lambda e, s=s, v=v: e.wait_ge(self.sem[s], v))

    def finalize(self):
        with self.es, self.nc.Block() as block:
            for name in ENGS:
                getattr(block, name)(lambda e, name=name: [f(e) for f in self.ops[name]])
        return self.nc



def emit_identity(pb):
    if getattr(pb, "_idf", None) is not None:
        return pb._idf, pb._idb
    eye = np.eye(128, dtype=np.float32)
    h = pb.const(eye, "ident128")
    idf = pb.sb("ident_f32", [128, 128], F32)
    idb = pb.sb("ident_bf16", [128, 128], BF16)
    t1 = pb.op("sync", lambda e: e.dma_start(out=idf[:], in_=h), inc="Wq", inc_by=16)
    t2 = pb.op("vector", lambda e: e.tensor_copy(idb[:], idf[:]), waits=[t1], inc="V")
    pb.tag["id"] = [t1, t2]
    pb._idf, pb._idb = idf, idb
    return idf, idb


def alloc_wstage(pb, elems=8192, f8=True, tst=4096, sfx=""):
    pb.w_elems = elems
    pb.w_st32 = pb.sb("w_st32" + sfx, [128, elems], F32)
    pb.w_stq = pb.sb("w_stq" + sfx, [128, elems], F8) if f8 else None
    pb.w_stbf = pb.sb("w_stbf" + sfx, [128, elems], BF16)
    pb.w_tst = pb.sb("w_tst" + sfx, [128, tst], BF16) if tst else None
    pb.tag["wstage"] = None


def emit_weight_T(pb, w_ap, O, I, name, quant=None, fold=None, dst_dtype=BF16,
                  fold_first_rows=None, wT_pre=None, fold_cols=None):
    KT = I // 128
    wT = wT_pre if wT_pre is not None else pb.sb(f"{name}T", [128, KT * O], dst_dtype)
    cache = getattr(pb, "wcache", None)
    ck = ("wT", name)
    if cache is not None and ck in cache:
        pre = cache[ck]
        last = None
        skip_stage = True
    else:
        pre = pb.dram(f"{name}_pre", [O, I], BF16)
        if cache is not None:
            cache[ck] = pre
        skip_stage = False
    ot = min(max(1, pb.w_elems // I), O // 128)
    oc = ot * 128
    n_oc = O // oc
    assert O % oc == 0
    st32, stq, stbf = pb.w_st32, pb.w_stq, pb.w_stbf
    wv = w_ap.rearrange("(a p) i -> a p i", p=128)
    prev = pb.tag.get("wstage")
    last = None
    for c in range(n_oc if not skip_stage else 0):
        tags = []
        for r in range(ot):
            tl = pb.op("sync", lambda e, c=c, r=r: e.dma_start(
                out=st32[:, r * I:(r + 1) * I], in_=wv[c * ot + r]),
                waits=[last, prev], inc="Wq", inc_by=16)
            tags.append(tl)
        if quant == 'e4m3':
            tq = pb.op("vector", lambda e: e.tensor_copy(
                stq[:, :ot * I], st32[:, :ot * I]), waits=tags + [prev], inc="V")
            tb = pb.op("vector", lambda e: e.tensor_copy(
                stbf[:, :ot * I], stq[:, :ot * I]), waits=[tq], inc="V")
        elif callable(quant):
            tb = quant(pb, st32[:, :ot * I], stbf[:, :ot * I], ot * I, tags + [prev])
        else:
            tb = pb.op("vector", lambda e: e.tensor_copy(
                stbf[:, :ot * I], st32[:, :ot * I]), waits=tags + [prev], inc="V")
        if fold is not None:
            tb = pb.op("vector", lambda e: e.tensor_scalar(
                stbf[:, :ot * I], stbf[:, :ot * I], float(fold), None, ALU.mult, ALU.bypass),
                waits=[tb], inc="V")
        if fold_cols is not None:
            for r in range(ot):
                tb = pb.op("vector", lambda e, r=r, c=c: e.tensor_scalar(
                    stbf[:, r * I:(r + 1) * I], stbf[:, r * I:(r + 1) * I],
                    fold_cols[0][:, c * ot + r:c * ot + r + 1], None, ALU.mult, ALU.bypass),
                    waits=[tb, fold_cols[1]], inc="V")
        if fold_first_rows is not None and c * oc < fold_first_rows[0]:
            nr = min(ot, (fold_first_rows[0] - c * oc) // 128)
            tb = pb.op("vector", lambda e, nr=nr: e.tensor_scalar(
                stbf[:, :nr * I], stbf[:, :nr * I], float(fold_first_rows[1]), None,
                ALU.mult, ALU.bypass), waits=[tb], inc="V")
        prev_w = [tb]
        for r in range(ot):
            last = pb.op("gpsimd", lambda e, c=c, r=r: e.dma_start(
                out=pre.rearrange("(a p) i -> a p i", p=128)[c * ot + r],
                in_=stbf[:, r * I:(r + 1) * I]),
                waits=prev_w, inc="Ws", inc_by=16)
        
    rdy = []
    tprev = None
    for k in range(KT):
        if dst_dtype == BF16:
            t = pb.op("sync", lambda e, k=k: e.dma_start(
                wT[:, k * O:(k + 1) * O], pre[:, 128 * k:128 * (k + 1)], transpose=True),
                waits=[last], inc="Wq", inc_by=16)
        else:
            tt = pb.op("sync", lambda e, k=k, tst=pb.w_tst: e.dma_start(
                tst[:, :O], pre[:, 128 * k:128 * (k + 1)], transpose=True),
                waits=[last, tprev], inc="Wq", inc_by=16)
            t = pb.op("vector", lambda e, k=k, tst=pb.w_tst: e.tensor_copy(
                wT[:, k * O:(k + 1) * O], tst[:, :O]), waits=[tt], inc="V")
            tprev = t
        rdy.append(t)
    pb.tag["wstage"] = rdy[-1]
    return wT, rdy[-1]


def emit_bias_cols(pb, b_ap, n, name, idf, quant=None):
    m = n // 128
    stage = pb.sb(f"{name}_stg", [m, 128], F32)
    out = pb.sb(f"{name}c", [128, m], F32)
    pst = pb.ps(f"{name}_ps", [128, m], F32)
    t1 = pb.op("sync", lambda e: e.dma_start(
        out=stage[:], in_=b_ap.rearrange("(a p) -> a p", p=128)), inc="Wq", inc_by=16)
    t2 = pb.op("tensor", lambda e: e.transpose(pst[:], stage[:], idf[0:m, 0:m]),
               waits=[t1] + pb.tag["id"], inc="T")
    if quant == 'e4m3':
        q8 = pb.sb(f"{name}q8", [128, m], F8)
        t3 = pb.op("vector", lambda e: e.tensor_copy(q8[:], pst[:]), waits=[t2], inc="V")
        t4 = pb.op("vector", lambda e: e.tensor_copy(out[:], q8[:]), waits=[t3], inc="V")
        return out, t4
    t3 = pb.op("vector", lambda e: e.tensor_copy(out[:], pst[:]), waits=[t2], inc="V")
    return out, t3


def alloc_ln(pb):
    pb.ln_bufs = (pb.sb("ln_bn6", [128, 12], F32), pb.sb("ln_mv", [128, 2], F32),
                  pb.sb("ln_rstd", [128, 1], F32), pb.sb("ln_negmr", [128, 1], F32))
    pb.tag["ln_prev"] = None


def emit_ln(pb, xa, xn_out, waits):
    bn6, mv, rstd, negmr = pb.ln_bufs
    prev = pb.tag.get("ln_prev")
    t0 = pb.op("vector", lambda e: e.bn_stats(bn6[:, 0:6], xa[:, 0:512]),
               waits=list(waits) + [prev], inc="V")
    t1 = pb.op("vector", lambda e: e.bn_stats(bn6[:, 6:12], xa[:, 512:1024]),
               waits=[t0], inc="V")
    t2 = pb.op("vector", lambda e: e.bn_aggr(mv[:], bn6[:]), waits=[t1], inc="V")
    t3 = pb.op("scalar", lambda e: e.activation(rstd[:], mv[:, 1:2], AF.Ln, bias=np.float32(EPS)),
               waits=[t2], inc="S")
    t4 = pb.op("scalar", lambda e: e.activation(rstd[:], rstd[:], AF.Exp, scale=np.float32(-0.5)),
               waits=[t3], inc="S")
    t5 = pb.op("vector", lambda e: e.tensor_scalar(
        negmr[:], mv[:, 0:1], rstd[:, 0:1], -1.0, ALU.mult, ALU.mult), waits=[t4], inc="V")
    t6 = pb.op("scalar", lambda e: e.activation(
        xn_out[:], xa[:], AF.Identity, bias=negmr[:, 0:1], scale=rstd[:, 0:1]),
        waits=[t5], inc="S")
    pb.tag["ln_prev"] = t6
    return t6



def build_l0ffn(R=2048, pb=None, io=None):
    fused = pb is not None
    if not fused:
        pb = PB()
    gi = lambda n, s, d=F32: io[n] if fused else pb.dram(n, s, d, kind="ExternalInput")
    xin = gi("xin", [R, D])
    w1 = gi("w1", [4096, D])
    b1 = gi("b1", [4096])
    w2 = gi("w2", [D, 4096])
    b2 = gi("b2", [D])
    mask = gi("mask", [R, D])
    xout = io["xout"] if fused else pb.dram("xout", [R, D], F32, kind="ExternalOutput")

    idf, idb = emit_identity(pb)
    alloc_wstage(pb)
    alloc_ln(pb)
    w1T, w1rdy = emit_weight_T(pb, w1, 4096, D, "w1", quant='e4m3', dst_dtype=F8)
    w2T, w2rdy = emit_weight_T(pb, w2, D, 4096, "w2", quant='e4m3', dst_dtype=F8)
    b1c, b1rdy = emit_bias_cols(pb, b1, 4096, "b1", idf, quant='e4m3')
    b2c, b2rdy = emit_bias_cols(pb, b2, D, "b2", idf, quant='e4m3')

    xa = [pb.sb(f"xa{i}", [128, D], F32) for i in range(2)]
    xn = pb.sb("xn", [128, D], F32)
    xq8 = pb.sb("xq8", [128, D], F8)
    xqb = pb.sb("xqb", [128, D], BF16)
    xfm = pb.sb("xfm", [128, 8 * 512], BF16)
    psT = [pb.ps(f"psT{i}", [128, 128], BF16) for i in range(2)]
    psM = [pb.ps(f"psM{i}", [128, 512], F32) for i in range(2)]
    h16 = pb.sb("h16", [128, 512], F16)
    hfull = pb.sb("hfull", [128, 32 * 512], F8)
    y16 = pb.sb("y16", [128, 512], F16)
    yfm = pb.sb("yfm", [128, 8 * 512], F32)
    psY = [pb.ps(f"psY{i}", [128, 128], F32) for i in range(2)]
    ytk = pb.sb("ytk", [128, D], F32)
    mk = pb.sb("mk", [128, D], F32)

    xv = xin.rearrange("(t p) d -> t p d", p=128)
    ov = xout.rearrange("(t p) d -> t p d", p=128)
    mkv = mask.rearrange("(t p) d -> t p d", p=128)
    T = pb.tag

    for n in range(R // 512):
        tr_tags = []
        for ti in range(4):
            t = 4 * n + ti
            a = xa[ti % 2]
            tl = pb.op("sync", lambda e, t=t, a=a: e.dma_start(out=a[:], in_=xv[t]),
                       waits=[T.get(("xa", ti % 2))], inc="Xq", inc_by=16)
            tln = emit_ln(pb, a, xn, [tl, T.get("xn_rd")])
            T[("xa", ti % 2)] = tln
            t8 = pb.op("vector", lambda e: e.tensor_copy(xq8[:], xn[:]),
                       waits=[tln, T.get("xq8_rd")], inc="V")
            T["xn_rd"] = t8
            tb = pb.op("vector", lambda e: e.tensor_copy(xqb[:], xq8[:]),
                       waits=[t8, T.get("xqb_rd")], inc="V")
            T["xq8_rd"] = tb
            for f in range(8):
                ps = psT[f % 2]
                tt = pb.op("tensor", lambda e, f=f, ps=ps: e.transpose(
                    ps[:], xqb[:, 128 * f:128 * (f + 1)], idb[:]),
                    waits=[tb, T.get(("psT", f % 2))] + T["id"], inc="T")
                ev = pb.op("vector", lambda e, f=f, ti=ti, ps=ps: e.tensor_copy(
                    xfm[:, f * 512 + ti * 128: f * 512 + ti * 128 + 128], ps[:]),
                    waits=[tt], inc="V")
                T[("psT", f % 2)] = ev
                tr_tags.append(ev)
                if f == 7:
                    T["xqb_rd"] = tt
        for m in range(32):
            ps = psM[m % 2]
            def mm(e, m=m, ps=ps):
                inst = None
                for k in range(8):
                    inst = e.matmul(ps[:], w1T[:, k * 4096 + 128 * m: k * 4096 + 128 * (m + 1)],
                                    xfm[:, k * 512:(k + 1) * 512],
                                    start=(k == 0), stop=(k == 7))
                return inst
            tm = pb.op("tensor", mm, waits=tr_tags + [w1rdy, T.get(("psM", m % 2))], inc="T")
            te = pb.op("scalar", lambda e, m=m, ps=ps: e.activation(
                h16[:], ps[:], AF.Identity, bias=b1c[:, m:m + 1]),
                waits=[tm, b1rdy, T.get("h16_rd")], inc="S")
            T[("psM", m % 2)] = te
            tr8 = pb.op("vector", lambda e, m=m: e.tensor_relu(
                hfull[:, m * 512:(m + 1) * 512], h16[:]), waits=[te], inc="V")
            T["h16_rd"] = tr8
        hread = []
        for m in range(8):
            ps = psM[m % 2]
            def mmy(e, m=m, ps=ps):
                inst = None
                for k in range(32):
                    inst = e.matmul(ps[:], w2T[:, k * D + 128 * m: k * D + 128 * (m + 1)],
                                    hfull[:, k * 512:(k + 1) * 512],
                                    start=(k == 0), stop=(k == 31))
                return inst
            tm = pb.op("tensor", mmy,
                       waits=[T["h16_rd"], w2rdy, T.get(("psM", m % 2))], inc="T")
            hread.append(tm)
            te = pb.op("scalar", lambda e, m=m, ps=ps: e.activation(
                y16[:], ps[:], AF.Identity, bias=b2c[:, m:m + 1]),
                waits=[tm, b2rdy, T.get("y16_rd")], inc="S")
            T[("psM", m % 2)] = te
            tc = pb.op("vector", lambda e, m=m: e.tensor_copy(
                yfm[:, m * 512:(m + 1) * 512], y16[:]), waits=[te, T.get(("yfm_rd", m))], inc="V")
            T["y16_rd"] = tc
            T[("yfm", m)] = tc
        for ti in range(4):
            t = 4 * n + ti
            trb = []
            for m in range(8):
                ps = psY[m % 2]
                tt = pb.op("tensor", lambda e, m=m, ti=ti, ps=ps: e.transpose(
                    ps[:], yfm[:, m * 512 + ti * 128: m * 512 + ti * 128 + 128], idf[:]),
                    waits=[T[("yfm", m)], T.get(("psY", m % 2))] + T["id"], inc="T")
                if ti == 3:
                    T[("yfm_rd", m)] = tt
                ev = pb.op("vector", lambda e, m=m, ps=ps: e.tensor_copy(
                    ytk[:, 128 * m:128 * (m + 1)], ps[:]),
                    waits=[tt, T.get("ytk_rd")], inc="V")
                T[("psY", m % 2)] = ev
                trb.append(ev)
            tlm = pb.op("sync", lambda e, t=t: e.dma_start(out=mk[:], in_=mkv[t]),
                        waits=[T.get("mk_rd")], inc="Xq", inc_by=16)
            a = xa[ti % 2]
            tlx = pb.op("sync", lambda e, t=t, a=a: e.dma_start(out=a[:], in_=xv[t]),
                        waits=[T.get(("xa", ti % 2))], inc="Xq", inc_by=16)
            tmu = pb.op("vector", lambda e: e.tensor_mul(ytk[:], ytk[:], mk[:]),
                        waits=trb + [tlm], inc="V")
            T["mk_rd"] = tmu
            tad = pb.op("vector", lambda e, a=a: e.tensor_add(a[:], a[:], ytk[:]),
                        waits=[tmu, tlx], inc="V")
            T["ytk_rd"] = tad
            tst = pb.op("scalar", lambda e, t=t, a=a: e.dma_start(out=ov[t], in_=a[:]),
                        waits=[tad], inc="St", inc_by=16)
            T[("xa", ti % 2)] = tst
    if fused:
        return [("St", pb.cnt["St"])]
    pb.wait("gpsimd", ("St", pb.cnt["St"]))
    return pb.finalize()



def emit_l1_compose(pb, Wexp, bexp, upw, upb, dnw, Wcon, idf):
    T = pb.tag
    DH = 4096
    beff = pb.sb("beff", [128, 32], F32)
    cache = getattr(pb, "wcache", None)
    if cache is not None and "l1comp" in cache:
        A_dram, Wc_dram, beff_dram = cache["l1comp"]
        t = pb.op("sync", lambda e: e.dma_start(out=beff[:], in_=beff_dram),
                  inc="Wq", inc_by=16)
        return A_dram, Wc_dram, beff, [t, t, t]

    WexpT_dram = pb.dram("WexpT", [DH, D], F32)
    R1p_dram = pb.dram("R1p", [DH, 1025], F32)
    S1_dram = pb.dram("S1d", [DH, D], F32)
    A_dram = pb.dram("A_dram", [DH, 1025], F32)
    Wc_dram = pb.dram("Wc_dram", [DH, D], F32)

    pb.push_scope()
    wstg = pb.sb("wstg", [128, 1024], F32)
    tstg = pb.sb("tstg", [128, 1024], F32)
    psW = [pb.ps(f"psW{i}", [128, 128], F32) for i in range(2)]
    wev = Wexp.rearrange("(r p) c -> r p c", p=128)
    for cc in range(32):
        ld = []
        for r in range(8):
            tl = pb.op("sync", lambda e, cc=cc, r=r: e.dma_start(
                out=wstg[:, r * 128:(r + 1) * 128], in_=wev[r, :, 128 * cc:128 * (cc + 1)]),
                waits=[T.get("wstg_rd")], inc="Wq", inc_by=16)
            ld.append(tl)
        evs = []
        for r in range(8):
            ps = psW[r % 2]
            tt = pb.op("tensor", lambda e, r=r, ps=ps: e.transpose(
                ps[:], wstg[:, r * 128:(r + 1) * 128], idf[:]),
                waits=ld + [T.get(("psW", r % 2))] + T["id"], inc="T")
            ev = pb.op("vector", lambda e, r=r, ps=ps: e.tensor_copy(
                tstg[:, r * 128:(r + 1) * 128], ps[:]), waits=[tt, T.get("tstg_rd")], inc="V")
            T[("psW", r % 2)] = ev
            evs.append(ev)
            if r == 7:
                T["wstg_rd"] = tt
        tw = pb.op("gpsimd", lambda e, cc=cc: e.dma_start(
            out=WexpT_dram[128 * cc:128 * (cc + 1), :], in_=tstg[:]),
            waits=evs, inc="Ws", inc_by=16)
        T["tstg_rd"] = tw
        T["wexpT_done"] = tw

    C0 = np.zeros((64, DH), np.float32)
    for g in range(64):
        C0[:, g * 64:(g + 1) * 64] = np.eye(64, dtype=np.float32)
    D0 = np.zeros((64, DH), np.float32)
    for r in range(64):
        D0[r, r * 64:(r + 1) * 64] = 1.0
    hC0 = pb.const(C0, "C0"); hD0 = pb.const(D0, "D0")
    X = pb.sb("bfX", [64, DH], F32)
    X2 = pb.sb("bfX2", [64, DH], F32)
    TMP = pb.sb("bfTMP", [64, DH // 2], F32)
    coef = [pb.sb(f"coef{i}", [64, DH // 2], F32) for i in range(4)]
    srow = pb.sb("srow", [1, 8192], F32)
    ones1 = pb.sb("ones1", [1, 64], F32)
    psR = pb.ps("psR", [64, 512], F32)
    t_on = pb.op("vector", lambda e: e.memset(ones1[:], 1.0), inc="V")

    def load_coefs(w_ap, stage):
        tl = pb.op("sync", lambda e: e.dma_start(out=srow[:], in_=w_ap[stage].rearrange(
            "k a b -> (k a b)").rearrange("f -> 1 f") if False else
            w_ap.rearrange("s k a b -> s (k a b)")[stage:stage + 1, :]),
            waits=[T.get("srow_rd")], inc="Wq", inc_by=16)
        tags = []
        for ij in range(4):
            for ch in range(4):
                tm = pb.op("tensor", lambda e, ij=ij, ch=ch: e.matmul(
                    psR[:], ones1[:], srow.rearrange("p (k four) -> p k four", four=4)
                    [:, 512 * ch:512 * (ch + 1), ij], start=True, stop=True),
                    waits=[tl, t_on, T.get("psR_rd")], inc="T")
                ev = pb.op("vector", lambda e, ij=ij, ch=ch: e.tensor_copy(
                    coef[ij][:, 512 * ch:512 * (ch + 1)], psR[:]),
                    waits=[tm, T.get(("coef_rd", ij))], inc="V")
                T["psR_rd"] = ev
                tags.append(ev)
        T["srow_rd"] = tags[-1]
        return tags

    def apply_stage(src, dst, stride, transposed, ctags):
        nb = DH // (2 * stride)
        sA = src.rearrange("p (nb two ss) -> p nb two ss", two=2, ss=stride)
        dA = dst.rearrange("p (nb two ss) -> p nb two ss", two=2, ss=stride)
        cw = [c.rearrange("p (nb ss) -> p nb ss", ss=stride) for c in coef]
        if not transposed:
            iaa, iab, iba, ibb = 0, 1, 2, 3
        else:
            iaa, iab, iba, ibb = 0, 2, 1, 3
        ts_ = []
        w0 = list(ctags) + [T.get("bfly_prev")]
        t1 = pb.op("vector", lambda e: e.tensor_mul(dA[:, :, 0, :], sA[:, :, 0, :],
                   cw[iaa][:]), waits=w0, inc="V")
        t2 = pb.op("vector", lambda e: e.tensor_mul(TMP.rearrange(
            "p (nb ss) -> p nb ss", ss=stride)[:], sA[:, :, 1, :], cw[iab][:]),
            waits=w0 + [T.get("tmp_rd")], inc="V")
        t3 = pb.op("vector", lambda e: e.tensor_add(dA[:, :, 0, :], dA[:, :, 0, :],
                   TMP.rearrange("p (nb ss) -> p nb ss", ss=stride)[:]), waits=[t1, t2], inc="V")
        t4 = pb.op("vector", lambda e: e.tensor_mul(dA[:, :, 1, :], sA[:, :, 0, :],
                   cw[iba][:]), waits=w0, inc="V")
        t5 = pb.op("vector", lambda e: e.tensor_mul(TMP.rearrange(
            "p (nb ss) -> p nb ss", ss=stride)[:], sA[:, :, 1, :], cw[ibb][:]),
            waits=[t3, t4], inc="V")
        T["tmp_rd"] = t5
        t6 = pb.op("vector", lambda e: e.tensor_add(dA[:, :, 1, :], dA[:, :, 1, :],
                   TMP.rearrange("p (nb ss) -> p nb ss", ss=stride)[:]), waits=[t5], inc="V")
        T["bfly_prev"] = t6
        return t6

    def build_compact(dst_name, init_h, w_ap, stages, transposed):
        tl = pb.op("sync", lambda e: e.dma_start(out=X[:], in_=init_h),
                   waits=[T.get("bfly_prev")], inc="Wq", inc_by=16)
        T["bfly_prev"] = tl
        cur, nxt = X, X2
        for st in stages:
            ctags = load_coefs(w_ap, st)
            apply_stage(cur, nxt, 2 ** st, transposed, ctags)
            cur, nxt = nxt, cur
        out = pb.sb(dst_name, [64, DH], F32)
        tc = pb.op("vector", lambda e: e.tensor_copy(out[:], cur[:]),
                   waits=[T["bfly_prev"]], inc="V")
        T["bfly_prev"] = tc
        return out, tc

    C_u, tCu = build_compact("C_u", hC0, upw, range(6), False)
    D_u, tDu = build_compact("D_u", hD0, upw, range(6, 12), False)
    DtT, tDt = build_compact("DtT", hD0, dnw, range(11, 5, -1), True)
    CTd, tCt = build_compact("CTd", hC0, dnw, range(5, -1, -1), True)
    cdone = [tCu, tDu, tDt, tCt]

    pack = [pb.sb(f"pack{i}", [128, 128], F32) for i in range(2)]
    movb = [pb.sb(f"movb{i}", [128, 1025], F32) for i in range(2)]
    evb = [pb.sb(f"evb{i}", [128, 1025], F32) for i in range(2)]
    upcol = pb.sb("upcol", [128, 1], F32)
    psC = [pb.ps(f"psC{i}", [128, 512], F32) for i in range(2)]
    psB = pb.ps("psB", [128, 1], F32)
    tz = [pb.op("vector", lambda e, i=i: e.memset(pack[i][:], 0.0), inc="V") for i in range(2)]

    def pair_loop(nm, stat_src, stat_strided, mov_fn, nmov, evac_fn):
        for p2 in range(32):
            pk = pack[p2 % 2]
            def cpy(e, p2=p2, pk=pk):
                if stat_strided:
                    e.tensor_copy(pk[0:64, 0:64], stat_src[:, (2 * p2)::64])
                    return e.tensor_copy(pk[64:128, 64:128], stat_src[:, (2 * p2 + 1)::64])
                else:
                    e.tensor_copy(pk[0:64, 0:64], stat_src[:, (2 * p2) * 64:(2 * p2 + 1) * 64])
                    return e.tensor_copy(pk[64:128, 64:128],
                                         stat_src[:, (2 * p2 + 1) * 64:(2 * p2 + 2) * 64])
            tp = pb.op("vector", cpy, waits=cdone + tz + [T.get((nm, "pk", p2 % 2))], inc="V")
            mb = movb[p2 % 2]
            tm_ld = mov_fn(p2, mb)
            mms = []
            for ch in range((nmov + 511) // 512):
                n0 = 512 * ch
                n1 = min(nmov, n0 + 512)
                ps = psC[ch % 2] if n1 - n0 > 1 else psB
                tmm = pb.op("tensor", lambda e, ch=ch, n0=n0, n1=n1, ps=ps, pk=pk, mb=mb:
                            e.matmul(ps[:, 0:n1 - n0], pk[:], mb[:, n0:n1],
                                     start=True, stop=True),
                            waits=[tp, tm_ld, T.get((nm, "ps", ch))], inc="T")
                mms.append((tmm, ch, n0, n1))
            T[(nm, "pk", p2 % 2)] = mms[-1][0]
            eb = evb[p2 % 2]
            evs = []
            for (tmm, ch, n0, n1) in mms:
                ps = psC[ch % 2] if n1 - n0 > 1 else psB
                ev = pb.op("vector", lambda e, ch=ch, n0=n0, n1=n1, ps=ps, eb=eb:
                           e.tensor_copy(eb[:, n0:n1], ps[:, 0:n1 - n0]),
                           waits=[tmm, T.get((nm, "eb", p2 % 2))], inc="V")
                T[(nm, "ps", ch)] = ev
                evs.append(ev)
            tout = evac_fn(p2, eb, evs)
            T[(nm, "eb", p2 % 2)] = tout
    bev = bexp.rearrange("(a p) -> a p", p=128)
    def movR1(p2, mb):
        t1 = pb.op("sync", lambda e, p2=p2: e.dma_start(
            out=mb[:, 0:1024], in_=WexpT_dram[128 * p2:128 * (p2 + 1), :]),
            waits=[T["wexpT_done"], T.get(("R1", "mb_rd", p2 % 2))], inc="Wq", inc_by=16)
        t2 = pb.op("sync", lambda e, p2=p2: e.dma_start(
            out=mb[:, 1024:1025].rearrange("p c -> (p c)") if False else mb[:, 1024:1025],
            in_=bev[p2].unsqueeze(1)), waits=[t1], inc="Wq", inc_by=16)
        return t2
    def evR1(p2, eb, evs):
        r1v = R1p_dram.rearrange("(j g) c -> g j c", g=64)
        t1 = pb.op("gpsimd", lambda e, p2=p2, eb=eb: e.dma_start(
            out=r1v[2 * p2], in_=eb[0:64, :]), waits=evs, inc="Ws", inc_by=16)
        t2 = pb.op("gpsimd", lambda e, p2=p2, eb=eb: e.dma_start(
            out=r1v[2 * p2 + 1], in_=eb[64:128, :]), waits=evs, inc="Ws", inc_by=16)
        T[("R1", "mb_rd", p2 % 2)] = t2
        T["r1_done"] = t2
        return t2
    pair_loop("R1", C_u, False, movR1, 1025, evR1)
    ubv = upb.rearrange("(hi lo) -> lo hi", lo=64)
    def movR2(p2, mb):
        return pb.op("sync", lambda e, p2=p2, mb=mb: e.dma_start(
            out=mb[:], in_=R1p_dram[128 * p2:128 * (p2 + 1), :]),
            waits=[T["r1_done"], T.get(("R2", "mb_rd", p2 % 2))], inc="Wq", inc_by=16)
    def evR2(p2, eb, evs):
        tl0 = pb.op("sync", lambda e, p2=p2: e.dma_start(
            out=upcol[0:64, 0:1], in_=ubv[2 * p2].unsqueeze(1),
            allow_slow_non_contiguous=True),
            waits=[T.get("upcol_rd")], inc="Wq", inc_by=16)
        tl = pb.op("sync", lambda e, p2=p2: e.dma_start(
            out=upcol[64:128, 0:1], in_=ubv[2 * p2 + 1].unsqueeze(1),
            allow_slow_non_contiguous=True),
            waits=[tl0], inc="Wq", inc_by=16)
        tb = pb.op("vector", lambda e, p2=p2, eb=eb: e.tensor_add(
            beff[:, p2:p2 + 1], eb[:, 1024:1025], upcol[:]), waits=evs + [tl], inc="V")
        T["upcol_rd"] = tb
        t1 = pb.op("gpsimd", lambda e, p2=p2, eb=eb: e.dma_start(
            out=A_dram[128 * p2:128 * (p2 + 1), :], in_=eb[:]), waits=evs, inc="Ws", inc_by=16)
        T[("R2", "mb_rd", p2 % 2)] = t1
        T["a_done"] = t1
        pb.tag["beff_done"] = tb
        return t1
    pair_loop("R2", D_u, True, movR2, 1025, evR2)
    wcv = Wcon.rearrange("(hi lo) d -> lo hi d", lo=64)
    def movS1(p2, mb):
        ta = pb.op("sync", lambda e, p2=p2, mb=mb: e.dma_start(
            out=mb[0:64, 0:1024], in_=wcv[2 * p2]),
            waits=[T.get(("S1", "mb_rd", p2 % 2))], inc="Wq", inc_by=16)
        return pb.op("sync", lambda e, p2=p2, mb=mb: e.dma_start(
            out=mb[64:128, 0:1024], in_=wcv[2 * p2 + 1]),
            waits=[ta], inc="Wq", inc_by=16)
    def evS1(p2, eb, evs):
        s1v = S1_dram.rearrange("(hi lo) d -> lo hi d", lo=64)
        t1 = pb.op("gpsimd", lambda e, p2=p2, eb=eb: e.dma_start(
            out=s1v[2 * p2], in_=eb[0:64, 0:1024]), waits=evs, inc="Ws", inc_by=16)
        t2 = pb.op("gpsimd", lambda e, p2=p2, eb=eb: e.dma_start(
            out=s1v[2 * p2 + 1], in_=eb[64:128, 0:1024]), waits=evs, inc="Ws", inc_by=16)
        T[("S1", "mb_rd", p2 % 2)] = t2
        T["s1_done"] = t2
        return t2
    pair_loop("S1", DtT, True, movS1, 1024, evS1)
    def movS2(p2, mb):
        return pb.op("sync", lambda e, p2=p2, mb=mb: e.dma_start(
            out=mb[:, 0:1024], in_=S1_dram[128 * p2:128 * (p2 + 1), :]),
            waits=[T["s1_done"], T.get(("S2", "mb_rd", p2 % 2))], inc="Wq", inc_by=16)
    def evS2(p2, eb, evs):
        wv2 = Wc_dram.rearrange("(j hi) d -> hi j d", hi=64)
        t1 = pb.op("gpsimd", lambda e, p2=p2, eb=eb: e.dma_start(
            out=wv2[2 * p2], in_=eb[0:64, 0:1024]), waits=evs, inc="Ws", inc_by=16)
        t2 = pb.op("gpsimd", lambda e, p2=p2, eb=eb: e.dma_start(
            out=wv2[2 * p2 + 1], in_=eb[64:128, 0:1024]), waits=evs, inc="Ws", inc_by=16)
        T[("S2", "mb_rd", p2 % 2)] = t2
        T["wc_done"] = t2
        return t2
    pair_loop("S2", CTd, False, movS2, 1024, evS2)
    done = [T["a_done"], T["wc_done"], T["beff_done"]]
    if cache is not None:
        beff_dram = pb.dram("beff_d", [128, 32], F32)
        tbd = pb.op("gpsimd", lambda e: e.dma_start(out=beff_dram, in_=beff[:]),
                    waits=[T["beff_done"]], inc="Ws", inc_by=16)
        cache["l1comp"] = (A_dram, Wc_dram, beff_dram)
        done = done + [tbd]
    pb.pop_scope(gate=done)
    return A_dram, Wc_dram, beff, done


def build_l1ffn(R=2048, pb=None, io=None):
    fused = pb is not None
    if not fused:
        pb = PB()
    gi = lambda n, s, d=F32: io[n] if fused else pb.dram(n, s, d, kind="ExternalInput")
    xin = gi("xin", [R, D])
    Wexp = gi("W_exp", [D, 4096])
    bexp = gi("b_exp", [4096])
    upw = gi("up_w", [12, 2048, 2, 2])
    upb = gi("up_b", [4096])
    dnw = gi("dn_w", [12, 2048, 2, 2])
    Wcon = gi("W_con", [4096, D])
    bcon = gi("b_con", [D])
    dnb = gi("dn_b", [D])
    xout = io["xout"] if fused else pb.dram("xout", [R, D], F32, kind="ExternalOutput")

    idf, idb = emit_identity(pb)
    alloc_ln(pb)
    A_dram, Wc_dram, beff, cps_done = emit_l1_compose(pb, Wexp, bexp, upw, upb, dnw, Wcon, idf)

    T = pb.tag
    AT = pb.sb("AT", [128, 8 * 4096], BF16)
    WcT = pb.sb("WcT", [128, 32 * 1024], BF16)
    psT = [pb.ps(f"psT{i}", [128, 128], BF16) for i in range(2)]
    psM = [pb.ps(f"psM{i}", [128, 512], F32) for i in range(2)]
    psY = [pb.ps(f"psY{i}", [128, 128], F32) for i in range(2)]
    psYb = [pb.ps(f"psYb{i}", [128, 128], BF16) for i in range(2)]
    psA = psY
    bout = pb.sb("bout", [128, 8], F32)
    pb.push_scope()
    astg = pb.sb("astg", [128, 1024], F32)
    for mblk in range(32):
        tl = pb.op("sync", lambda e, mblk=mblk: e.dma_start(
            out=astg[:], in_=A_dram[128 * mblk:128 * (mblk + 1), 0:1024]),
            waits=cps_done + [T.get("astg_rd")], inc="Wq", inc_by=16)
        for rr in range(8):
            ps = psA[rr % 2]
            tt = pb.op("tensor", lambda e, rr=rr, ps=ps: e.transpose(
                ps[:], astg[:, 128 * rr:128 * (rr + 1)], idf[:]),
                waits=[tl, T.get(("psA", rr % 2)), T.get(("psY", rr % 2))] + T["id"], inc="T")
            ev = pb.op("vector", lambda e, rr=rr, mblk=mblk, ps=ps: e.tensor_copy(
                AT[:, rr * 4096 + 128 * mblk: rr * 4096 + 128 * (mblk + 1)], ps[:]),
                waits=[tt], inc="V")
            T[("psA", rr % 2)] = ev
            if rr == 7:
                T["astg_rd"] = tt
            T["at_done"] = ev
    wcstg = pb.sb("wcstg", [128, 1024], F32)
    for k in range(32):
        tl = pb.op("sync", lambda e, k=k: e.dma_start(
            out=wcstg[:], in_=Wc_dram[128 * k:128 * (k + 1), :]),
            waits=cps_done + [T.get("wcstg_rd")], inc="Wq", inc_by=16)
        tc = pb.op("vector", lambda e, k=k: e.tensor_scalar(
            WcT[:, k * 1024:(k + 1) * 1024], wcstg[:], 0.5, None, ALU.mult, ALU.bypass),
            waits=[tl], inc="V")
        T["wcstg_rd"] = tc
        T["wc_res"] = tc
    pb.pop_scope() if False else None
    bstg1 = pb.sb("bstg1", [8, 128], F32)
    bstg2 = pb.sb("bstg2", [8, 128], F32)

    t1 = pb.op("sync", lambda e: e.dma_start(out=bstg1[:], in_=bcon.rearrange(
        "(a p) -> a p", p=128)), inc="Wq", inc_by=16)
    t2 = pb.op("sync", lambda e: e.dma_start(out=bstg2[:], in_=dnb.rearrange(
        "(a p) -> a p", p=128)), inc="Wq", inc_by=16)
    t3 = pb.op("vector", lambda e: e.tensor_add(bstg1[:], bstg1[:], bstg2[:]),
               waits=[t1, t2], inc="V")
    t4 = pb.op("tensor", lambda e: e.transpose(psY[0][:, 0:8], bstg1[:], idf[0:8, 0:8]),
               waits=[t3, T.get(("psA", 0))] + T["id"], inc="T")
    tbo = pb.op("vector", lambda e: e.tensor_scalar(
        bout[:], psY[0][:, 0:8], 0.5, None, ALU.mult, ALU.bypass), waits=[t4], inc="V")
    T[("psY", 0)] = tbo
    pb.pop_scope()

    xa = [pb.sb(f"xa{i}", [128, D], F32) for i in range(2)]
    xqb = pb.sb("xqb", [128, D], BF16)
    xfm = pb.sb("xfm", [128, 8 * 512], BF16)
    hfull = pb.sb("hfull", [128, 32 * 512], BF16)
    yfm = pb.sb("yfm", [128, 8 * 512], BF16)
    ytk = pb.sb("ytk", [128, D], BF16)

    xv = xin.rearrange("(t p) d -> t p d", p=128)
    ov = xout.rearrange("(t p) d -> t p d", p=128)

    for n in range(R // 512):
        tr_tags = []
        for ti in range(4):
            t = 4 * n + ti
            a = xa[ti % 2]
            tl = pb.op("sync", lambda e, t=t, a=a: e.dma_start(out=a[:], in_=xv[t]),
                       waits=[T.get(("xa", ti % 2))], inc="Xq", inc_by=16)
            tln = emit_ln_bf16(pb, a, xqb, [tl, T.get("xqb_rd")])
            T[("xa", ti % 2)] = tln
            for f in range(8):
                ps = psT[f % 2]
                tt = pb.op("tensor", lambda e, f=f, ps=ps: e.transpose(
                    ps[:], xqb[:, 128 * f:128 * (f + 1)], idb[:]),
                    waits=[tln, T.get(("psT", f % 2))] + T["id"], inc="T")
                ev = pb.op("vector", lambda e, f=f, ti=ti, ps=ps: e.tensor_copy(
                    xfm[:, f * 512 + ti * 128: f * 512 + ti * 128 + 128], ps[:]),
                    waits=[tt], inc="V")
                T[("psT", f % 2)] = ev
                tr_tags.append(ev)
                if f == 7:
                    T["xqb_rd"] = tt
        for m in range(32):
            ps = psM[m % 2]
            def mm(e, m=m, ps=ps):
                inst = None
                for k in range(8):
                    inst = e.matmul(ps[:], AT[:, k * 4096 + 128 * m: k * 4096 + 128 * (m + 1)],
                                    xfm[:, k * 512:(k + 1) * 512],
                                    start=(k == 0), stop=(k == 7))
                return inst
            tm = pb.op("tensor", mm, waits=tr_tags + [T["at_done"], T.get(("psM", m % 2))],
                       inc="T")
            te = pb.op("scalar", lambda e, m=m, ps=ps: e.activation(
                hfull[:, m * 512:(m + 1) * 512], ps[:], AF.Gelu, bias=beff[:, m:m + 1]),
                waits=[tm, cps_done[2], T.get("hf_guard")], inc="S")
            T[("psM", m % 2)] = te
            T["h_done"] = te
        for m in range(8):
            ps = psM[m % 2]
            def mmy(e, m=m, ps=ps):
                inst = None
                for k in range(32):
                    inst = e.matmul(ps[:], WcT[:, k * 1024 + 128 * m: k * 1024 + 128 * (m + 1)],
                                    hfull[:, k * 512:(k + 1) * 512],
                                    start=(k == 0), stop=(k == 31))
                return inst
            tm = pb.op("tensor", mmy, waits=[T["h_done"], T["wc_res"], T.get(("psM", m % 2))],
                       inc="T")
            if m == 7:
                T["hf_guard"] = tm
            te = pb.op("scalar", lambda e, m=m, ps=ps: e.activation(
                yfm[:, m * 512:(m + 1) * 512], ps[:], AF.Identity, bias=bout[:, m:m + 1]),
                waits=[tm, tbo, T.get(("yfm_rd", m))], inc="S")
            T[("psM", m % 2)] = te
            T[("yfm", m)] = te
        for ti in range(4):
            t = 4 * n + ti
            trb = []
            for m in range(8):
                ps = psYb[m % 2]
                tt = pb.op("tensor", lambda e, m=m, ti=ti, ps=ps: e.transpose(
                    ps[:], yfm[:, m * 512 + ti * 128: m * 512 + ti * 128 + 128], idb[:]),
                    waits=[T[("yfm", m)], T.get(("psYb", m % 2))] + T["id"], inc="T")
                if ti == 3:
                    T[("yfm_rd", m)] = tt
                ev = pb.op("vector", lambda e, m=m, ps=ps: e.tensor_copy(
                    ytk[:, 128 * m:128 * (m + 1)], ps[:]),
                    waits=[tt, T.get("ytk_rd")], inc="V")
                T[("psYb", m % 2)] = ev
                trb.append(ev)
            a = xa[ti % 2]
            tlx = pb.op("sync", lambda e, t=t, a=a: e.dma_start(out=a[:], in_=xv[t]),
                        waits=[T.get(("xa", ti % 2))], inc="Xq", inc_by=16)
            tad = pb.op("vector", lambda e, a=a: e.tensor_add(a[:], a[:], ytk[:]),
                        waits=trb + [tlx], inc="V")
            T["ytk_rd"] = tad
            tst = pb.op("scalar", lambda e, t=t, a=a: e.dma_start(out=ov[t], in_=a[:]),
                        waits=[tad], inc="St", inc_by=16)
            T[("xa", ti % 2)] = tst
    if fused:
        return [("St", pb.cnt["St"])]
    pb.wait("gpsimd", ("St", pb.cnt["St"]))
    return pb.finalize()


def emit_ln_bf16(pb, xa, out_bf, waits):
    return emit_ln(pb, xa, out_bf, waits)



def alloc_fp4(pb, W=4096, need_p=True):
    pb.q_W = W
    pb.q_i32 = pb.sb("q_i32", [128, W], I32)
    pb.q_f32 = pb.sb("q_f32", [128, W], F32)
    pb.q_p = pb.sb("q_p", [128, W], F32) if need_p else None
    pb.q_pi = pb.sb("q_pi", [128, W], F32) if need_p else None
    pb.q_bm = pb.sb("q_bm", [128, W // 16], F32)
    pb.q_sc = pb.sb("q_sc", [128, W // 16], F32)
    pb.q_si = pb.sb("q_si", [128, W // 16], F32)
    pb.q_sh = pb.sb("q_sh", [128, W // 16], F32)
    pb.tag["q_busy"] = None


def emit_fp4_rows(pb, src, dst, W, waits, out_factor=0.5):
    nb = W // 16
    i32, f32a, fp_, fpi = pb.q_i32, pb.q_f32, pb.q_p, pb.q_pi
    bm, sc, si, sh = pb.q_bm, pb.q_sc, pb.q_si, pb.q_sh
    v3 = lambda t: t[:, 0:W].rearrange("p (nb b) -> p nb b", b=16)
    vb = lambda t: t[:, 0:nb].unsqueeze(2).broadcast_to([128, nb, 16])
    w0 = [x for x in waits] + [pb.tag.get("q_busy")]
    t1 = pb.op("vector", lambda e: e.tensor_reduce(
        bm[:, 0:nb].unsqueeze(2), v3(src), AX.X, ALU.max, apply_absolute_value=True),
        waits=w0, inc="V")
    t2 = pb.op("vector", lambda e: e.tensor_scalar(
        sc[:, 0:nb], bm[:, 0:nb], float(1 / 6.0), float(1e-12), ALU.mult, ALU.max),
        waits=[t1], inc="V")
    t3 = pb.op("vector", lambda e: e.reciprocal(si[:, 0:nb], sc[:, 0:nb]), waits=[t2], inc="V")
    t4 = pb.op("vector", lambda e: e.tensor_mul(v3(f32a), v3(src), vb(si)), waits=[t3] + w0, inc="V")
    t5 = pb.op("vector", lambda e: e.tensor_scalar(
        i32[:, 0:W], f32a[:, 0:W].bitcast(I32), 0x7F800000, None, ALU.bitwise_and, ALU.bypass),
        waits=[t4], inc="V")
    t6 = pb.op("vector", lambda e: e.tensor_scalar(
        fp_[:, 0:W], i32[:, 0:W].bitcast(F32), 1.0, 4.0,
        ALU.max, ALU.min), waits=[t5], inc="V")
    t7 = pb.op("vector", lambda e: e.reciprocal(fpi[:, 0:W], fp_[:, 0:W]), waits=[t6], inc="V")
    t8 = pb.op("vector", lambda e: e.scalar_tensor_tensor(
        f32a[:, 0:W], f32a[:, 0:W], 2.0, fpi[:, 0:W], ALU.mult, ALU.mult),
        waits=[t7], inc="V")
    t9 = pb.op("vector", lambda e: e.tensor_copy(i32[:, 0:W], f32a[:, 0:W]), waits=[t8], inc="V")
    t10 = pb.op("vector", lambda e: e.tensor_copy(f32a[:, 0:W], i32[:, 0:W]), waits=[t9], inc="V")
    t11 = pb.op("vector", lambda e: e.tensor_mul(f32a[:, 0:W], f32a[:, 0:W], fp_[:, 0:W]),
                waits=[t10], inc="V")
    t12 = pb.op("vector", lambda e: e.tensor_scalar(
        sh[:, 0:nb], sc[:, 0:nb], float(out_factor), None, ALU.mult, ALU.bypass),
        waits=[t2], inc="V")
    t13 = pb.op("vector", lambda e: e.tensor_mul(v3(dst), v3(f32a), vb(sh)),
                waits=[t11, t12], inc="V")
    pb.tag["q_busy"] = t13
    return t13


def build_l3ffn(R=2048, pb=None, io=None):
    fused = pb is not None
    if not fused:
        pb = PB()
    NI = 2816
    NKT = NI // 128
    gi = lambda n, s, d=F32: io[n] if fused else pb.dram(n, s, d, kind="ExternalInput")
    xin = gi("xin", [R, D])
    wg = gi("w_gate", [NI, D])
    wu = gi("w_up", [NI, D])
    wd = gi("w_down", [D, NI])
    xout = io["xout"] if fused else pb.dram("xout", [R, D], F32, kind="ExternalOutput")
    hq_dram = pb.dram("hq_dram", [NI, R], BF16)

    idf, idb = emit_identity(pb)
    alloc_ln(pb)
    alloc_fp4(pb, 2816)
    alloc_wstage(pb, 2816, f8=False, tst=None)
    T = pb.tag
    qfn = lambda pb, s32, sbf, W, tags: emit_fp4_rows(pb, s32, sbf, W, tags, out_factor=0.5)
    qfn_half = lambda pb, s32, sbf, W, tags: emit_fp4_rows(pb, s32, sbf, W, tags, out_factor=0.25)

    chunks = [(0, 512), (512, 1024), (1024, 1536), (1536, 2048), (2048, 2560), (2560, 2816)]

    pb.push_scope()
    wgT, wgrdy = emit_weight_T(pb, wg, NI, D, "wg", quant=qfn, dst_dtype=BF16)
    wuT, wurdy = emit_weight_T(pb, wu, NI, D, "wu", quant=qfn, dst_dtype=BF16)
    xa = [pb.sb(f"xa{i}", [128, D], F32) for i in range(2)]
    xn = pb.sb("xn", [128, D], F32)
    xqb = pb.sb("xqb", [128, D], BF16)
    xfm = pb.sb("xfm", [128, 8 * 128], BF16)
    psT = [pb.ps(f"psT{i}", [128, 128], BF16) for i in range(2)]
    psM = [pb.ps(f"psM{i}", [128, 512], F32) for i in range(2)]
    psB = [pb.ps(f"psB{i}", [128, 128], BF16) for i in range(2)]
    sgtk = pb.sb("sgtk", [128, NI], F32)
    utk = pb.sb("utk", [128, NI], F32)
    hqtk = pb.sb("hqtk", [128, NI], BF16)
    hqf = pb.sb("hqf", [128, NKT * 128], BF16)
    xv = xin.rearrange("(t p) d -> t p d", p=128)
    ov = xout.rearrange("(t p) d -> t p d", p=128)
    for t in range(R // 128):
        a = xa[t % 2]
        tl = pb.op("sync", lambda e, t=t, a=a: e.dma_start(out=a[:], in_=xv[t]),
                   waits=[T.get(("xa", t % 2))], inc="Xq", inc_by=16)
        tln = emit_ln(pb, a, xn, [tl, T.get("xn_rd")])
        T[("xa", t % 2)] = tln
        tq = emit_fp4_rows(pb, xn, xqb, D, [tln, T.get("xqb_rd")], out_factor=0.5)
        T["xn_rd"] = tq
        tr_tags = []
        for f in range(8):
            ps = psT[f % 2]
            tt = pb.op("tensor", lambda e, f=f, ps=ps: e.transpose(
                ps[:], xqb[:, 128 * f:128 * (f + 1)], idb[:]),
                waits=[tq, T.get(("psT", f % 2))] + T["id"], inc="T")
            ev = pb.op("vector", lambda e, f=f, ps=ps: e.tensor_copy(
                xfm[:, f * 128:(f + 1) * 128], ps[:]),
                waits=[tt, T.get("xfm_rd")], inc="V")
            T[("psT", f % 2)] = ev
            tr_tags.append(ev)
            if f == 7:
                T["xqb_rd"] = tt
        ev_tags = []
        for ci, (c0, c1) in enumerate(chunks):
            for which, wT, wrdy, dstt in (("g", wgT, wgrdy, sgtk), ("u", wuT, wurdy, utk)):
                ps = psM[0] if which == "g" else psM[1]
                def mmg(e, c0=c0, c1=c1, ps=ps, wT=wT):
                    inst = None
                    for k in range(8):
                        inst = e.matmul(ps[:, 0:c1 - c0], xfm[:, k * 128:(k + 1) * 128],
                                        wT[:, k * NI + c0:k * NI + c1],
                                        start=(k == 0), stop=(k == 7))
                    return inst
                tm = pb.op("tensor", mmg,
                           waits=tr_tags + [wrdy, T.get(("psM", which, ci % 1))], inc="T")
                if which == "g":
                    te = pb.op("scalar", lambda e, c0=c0, c1=c1, ps=ps: e.activation(
                        sgtk[:, c0:c1], ps[:, 0:c1 - c0], AF.Silu),
                        waits=[tm, T.get("sg_rd")], inc="S")
                else:
                    te = pb.op("scalar", lambda e, c0=c0, c1=c1, ps=ps: e.copy(
                        utk[:, c0:c1], ps[:, 0:c1 - c0]),
                        waits=[tm, T.get("sg_rd")], inc="S")
                T[("psM", which, 0)] = te
                ev_tags.append(te)
        th = pb.op("vector", lambda e: e.tensor_mul(sgtk[:], sgtk[:], utk[:]),
                   waits=ev_tags, inc="V")
        thq = emit_fp4_rows(pb, sgtk, hqtk, NI, [th], out_factor=0.5)
        T["sg_rd"] = thq
        sp_tags = []
        for f in range(NKT):
            ps = psB[f % 2]
            tt = pb.op("tensor", lambda e, f=f, ps=ps: e.transpose(
                ps[:], hqtk[:, 128 * f:128 * (f + 1)], idb[:]),
                waits=[thq, T.get(("psB", f % 2))] + T["id"], inc="T")
            ev = pb.op("vector", lambda e, f=f, t=t, ps=ps: e.tensor_copy(
                hqf[:, f * 128:(f + 1) * 128], ps[:]),
                waits=[tt, T.get("hqf_rd")], inc="V")
            T[("psB", f % 2)] = ev
            sp_tags.append(ev)
        tsp = None
        for f in range(NKT):
            tsp = pb.op("gpsimd", lambda e, t=t, f=f: e.dma_start(
                out=hq_dram[128 * f:128 * (f + 1), 128 * t:128 * (t + 1)],
                in_=hqf[:, f * 128:(f + 1) * 128]),
                waits=sp_tags, inc="Hs", inc_by=16)
        T["hqf_rd"] = tsp
        T["hq_done"] = tsp
    pb.pop_scope(gate=[T["hq_done"]])

    wdT, wdrdy = emit_weight_T(pb, wd, D, NI, "wd", quant=qfn_half, dst_dtype=BF16)
    xa2 = [pb.sb(f"xb{i}", [128, D], F32) for i in range(2)]
    hld = [pb.sb(f"hld{i}", [128, 512], BF16) for i in range(2)]
    psM2 = [pb.ps(f"ps2{i}", [128, 512], F32) for i in range(2)]
    psY = [pb.ps(f"psY{i}", [128, 128], BF16) for i in range(2)]
    yfm = pb.sb("yfm", [128, 8 * 512], BF16)
    ytk = pb.sb("ytk", [128, D], BF16)
    for n in range(R // 512):
        ld_tags = []
        for m in range(8):
            ps = psM2[m % 2]
            mm_tag = None
            for k in range(NKT):
                hb = hld[k % 2]
                tl = pb.op("sync", lambda e, k=k, n=n, hb=hb: e.dma_start(
                    out=hb[:], in_=hq_dram[128 * k:128 * (k + 1), 512 * n:512 * (n + 1)]),
                    waits=[T["hq_done"], T.get(("hld", k % 2))], inc="Xq", inc_by=16)
                mm_tag = pb.op("tensor", lambda e, k=k, m=m, ps=ps, hb=hb: e.matmul(
                    ps[:], wdT[:, k * D + 128 * m:k * D + 128 * (m + 1)], hb[:],
                    start=(k == 0), stop=(k == NKT - 1)),
                    waits=[tl, wdrdy, T.get(("psM2", m % 2))], inc="T")
                T[("hld", k % 2)] = mm_tag
            te = pb.op("scalar", lambda e, m=m, ps=ps: e.copy(
                yfm[:, m * 512:(m + 1) * 512], ps[:]),
                waits=[mm_tag, T.get(("yfm_rd", m))], inc="S")
            T[("psM2", m % 2)] = te
            T[("yfm", m)] = te
        for ti in range(4):
            t = 4 * n + ti
            trb = []
            for m in range(8):
                ps = psY[m % 2]
                tt = pb.op("tensor", lambda e, m=m, ti=ti, ps=ps: e.transpose(
                    ps[:], yfm[:, m * 512 + ti * 128: m * 512 + ti * 128 + 128], idb[:]),
                    waits=[T[("yfm", m)], T.get(("psY", m % 2))] + T["id"], inc="T")
                if ti == 3:
                    T[("yfm_rd", m)] = tt
                ev = pb.op("vector", lambda e, m=m, ps=ps: e.tensor_copy(
                    ytk[:, 128 * m:128 * (m + 1)], ps[:]),
                    waits=[tt, T.get("ytk_rd")], inc="V")
                T[("psY", m % 2)] = ev
                trb.append(ev)
            a = xa2[ti % 2]
            tlx = pb.op("sync", lambda e, t=t, a=a: e.dma_start(out=a[:], in_=xv[t]),
                        waits=[T.get(("xa2", ti % 2))], inc="Xq", inc_by=16)
            tad = pb.op("vector", lambda e, a=a: e.tensor_add(a[:], a[:], ytk[:]),
                        waits=trb + [tlx], inc="V")
            T["ytk_rd"] = tad
            tst = pb.op("scalar", lambda e, t=t, a=a: e.dma_start(out=ov[t], in_=a[:]),
                        waits=[tad], inc="St", inc_by=16)
            T[("xa2", ti % 2)] = tst
    if fused:
        return [("St", pb.cnt["St"])]
    pb.wait("gpsimd", ("St", pb.cnt["St"]))
    return pb.finalize()



def emit_lpbq_rows(pb, src, dst, W, waits, fold=None):
    nb = W // 32
    i32, f32a = pb.q_i32, pb.q_f32
    bm, sc, si, sh = pb.q_bm, pb.q_sc, pb.q_si, pb.q_sh
    v3 = lambda t: t[:, 0:W].rearrange("p (nb b) -> p nb b", b=32)
    vb = lambda t: t[:, 0:nb].unsqueeze(2).broadcast_to([128, nb, 32])
    w0 = [x for x in waits] + [pb.tag.get("q_busy")]
    t1 = pb.op("vector", lambda e: e.tensor_reduce(
        bm[:, 0:nb].unsqueeze(2), v3(src), AX.X, ALU.max, apply_absolute_value=True),
        waits=w0, inc="V")
    t2 = pb.op("vector", lambda e: e.tensor_scalar(
        sc[:, 0:nb], bm[:, 0:nb], float(1 / 7.0), 1e-12, ALU.mult, ALU.max),
        waits=[t1], inc="V")
    t3 = pb.op("vector", lambda e: e.reciprocal(si[:, 0:nb], sc[:, 0:nb]), waits=[t2], inc="V")
    t4 = pb.op("vector", lambda e: e.tensor_mul(v3(f32a), v3(src), vb(si)), waits=[t3] + w0, inc="V")
    t5 = pb.op("vector", lambda e: e.tensor_copy(i32[:, 0:W], f32a[:, 0:W]), waits=[t4], inc="V")
    t6 = pb.op("vector", lambda e: e.tensor_copy(f32a[:, 0:W], i32[:, 0:W]), waits=[t5], inc="V")
    if fold is not None:
        t6b = pb.op("vector", lambda e: e.tensor_scalar(
            sh[:, 0:nb], sc[:, 0:nb], float(fold), None, ALU.mult, ALU.bypass),
            waits=[t2], inc="V")
        svec = sh
    else:
        t6b = t2
        svec = sc
    t7 = pb.op("vector", lambda e: e.tensor_mul(v3(dst), v3(f32a), vb(svec)),
               waits=[t6, t6b], inc="V")
    pb.tag["q_busy"] = t7
    return t7


def emit_global_absmax(pb, col, name, idf, gmax_out):
    T = pb.tag
    if not hasattr(pb, "gm_ps"):
        pb.gm_ps = pb.ps("gm_ps", [1, 128], F32)
        pb.gm_row = pb.sb("gm_row", [1, 128], F32)
        pb.gm_st = pb.sb("gm_st", [1, 16], F32)
    psx = pb.gm_ps
    row = pb.gm_row
    cin = pb.dram(f"gm_in_{name}", [1, 16], F32)
    cout = pb.dram(f"gm_out_{name}", [1, 16], F32)
    t1 = pb.op("tensor", lambda e: e.transpose(psx[:], col[:], idf[:]),
               waits=[T[("gmcol", name)], T.get("gm_busy")] + T["id"], inc="T")
    t2 = pb.op("vector", lambda e: e.tensor_copy(row[:], psx[:]), waits=[t1], inc="V")
    t3 = pb.op("vector", lambda e: e.tensor_reduce(
        gmax_out[:], row[:], AX.X, ALU.max), waits=[t2], inc="V")
    t3b = pb.op("vector", lambda e, gst=pb.gm_st: e.tensor_copy(
        gst[:], gmax_out[0:1, 0:1].broadcast_to([1, 16])),
        waits=[t3, T.get("gm_st_rd")], inc="V")
    t4 = pb.op("gpsimd", lambda e, gst=pb.gm_st: e.dma_start(out=cin, in_=gst[:]),
               waits=[t3b], inc="Hs", inc_by=16)
    T["gm_st_rd"] = t4
    t5 = pb.op("gpsimd", lambda e: e.collective_compute(
        "AllReduce", ALU.max, [[0, 1, 2, 3, 4, 5, 6, 7]], ins=[cin.opt()], outs=[cout.opt()]),
        waits=[t4], inc="CC")
    t6 = pb.op("gpsimd", lambda e: e.dma_start(out=gmax_out[:], in_=cout[:, 0:1]),
               waits=[t5], inc="Hs", inc_by=16)
    T["gm_busy"] = t2
    return t6


def emit_scale_cols(pb, gmax, s_col, si_col, ones1, waits, psrep):
    t1 = pb.op("vector", lambda e: e.tensor_scalar(
        gmax[0:1, 0:1], gmax[0:1, 0:1], float(1 / 32767.0), 1e-12, ALU.mult, ALU.max),
        waits=waits, inc="V")
    tsi = pb.op("vector", lambda e: e.reciprocal(gmax[0:1, 1:2], gmax[0:1, 0:1]),
                waits=[t1], inc="V")
    tm = pb.op("tensor", lambda e: e.matmul(psrep[:, 0:2], ones1[:], gmax[0:1, 0:2],
               start=True, stop=True), waits=[tsi, pb.tag.get("psrep_rd")], inc="T")
    t2 = pb.op("vector", lambda e: e.tensor_copy(s_col[:], psrep[:, 0:1]), waits=[tm], inc="V")
    t3 = pb.op("vector", lambda e: e.tensor_copy(si_col[:], psrep[:, 1:2]), waits=[tm], inc="V")
    pb.tag["psrep_rd"] = t3
    return t3


def emit_qdq4(pb, src, dst, si_col, s_col, tmp32, i32t, waits, n=512):
    waits = list(waits) + [pb.tag.get(("qdq_busy", id(tmp32)))]
    t1 = pb.op("vector", lambda e: e.tensor_scalar(
        tmp32[:, 0:n], src, si_col[:, 0:1], None, ALU.mult, ALU.bypass), waits=waits, inc="V")
    t2 = pb.op("vector", lambda e: e.tensor_copy(i32t[:, 0:n], tmp32[:, 0:n]), waits=[t1], inc="V")
    t3 = pb.op("vector", lambda e: e.tensor_copy(tmp32[:, 0:n], i32t[:, 0:n]), waits=[t2], inc="V")
    t4 = pb.op("vector", lambda e: e.tensor_scalar(
        dst, tmp32[:, 0:n], s_col[:, 0:1], None, ALU.mult, ALU.bypass), waits=[t3], inc="V")
    pb.tag[("qdq_busy", id(tmp32))] = t4
    return t4


def build_l2ffn(R=2048, stop_phase=99, pb=None, io=None):
    fused = pb is not None
    if not fused:
        pb = PB()
    NI = 3072
    NKT = NI // 128
    gi = lambda n, s, d=F32: io[n] if fused else pb.dram(n, s, d, kind="ExternalInput")
    xin = gi("xin", [R, D])
    wg = gi("w_gate", [NI, D])
    wu = gi("w_up", [NI, D])
    wd = gi("w_down", [D, NI])
    xout = io["xout"] if fused else pb.dram("xout", [R, D], F32, kind="ExternalOutput")
    xn_dram = pb.dram("xn_dram", [R, D], F32)
    xq_dram = pb.dram("xq_dram", [D, R], BF16)
    g_dram = pb.dram("g_dram", [NI, R], BF16)
    u_dram = pb.dram("u_dram", [NI, R], BF16)
    g2_dram = pb.dram("g2_dram", [NI, R], BF16)
    o_dram = pb.dram("o_dram", [NI, R], BF16)

    idf, idb = emit_identity(pb)
    alloc_ln(pb)
    alloc_fp4(pb, 3072, need_p=False)
    T = pb.tag
    ones1 = pb.sb("ones1r", [1, 128], F32)
    t_on = pb.op("vector", lambda e: e.memset(ones1[:], 1.0), inc="V")
    T["id"] = T["id"] + [t_on]
    psrep = pb.ps("psrep", [128, 2], F32)
    gx = pb.sb("gx", [1, 2], F32)
    gg = pb.sb("gg", [1, 2], F32)
    gu = pb.sb("gu", [1, 2], F32)
    g2 = pb.sb("g2", [1, 2], F32)
    go = pb.sb("go", [1, 2], F32)
    scx = pb.sb("scx", [128, 2], F32)
    scg = pb.sb("scg", [128, 2], F32)
    scu = pb.sb("scu", [128, 2], F32)
    sc2 = pb.sb("sc2", [128, 2], F32)
    sco = pb.sb("sco", [128, 2], F32)
    mxc = pb.sb("mxc", [128, 5], F32)
    mtmp = pb.sb("mtmp", [128, 1], F32)
    tmz = pb.op("vector", lambda e: e.memset(mxc[:], 0.0), inc="V")
    T["mxc_z"] = tmz
    tmp32 = pb.sb("tmp32", [128, 512], F32)
    tmp32b = pb.sb("tmp32b", [128, 512], F32)
    i32t = pb.sb("i32t", [128, 512], I32)
    xv = xin.rearrange("(t p) d -> t p d", p=128)
    xnv = xn_dram.rearrange("(t p) d -> t p d", p=128)
    ov = xout.rearrange("(t p) d -> t p d", p=128)

    pb.push_scope()
    alloc_wstage(pb, 3072, f8=False)
    wgT, wgrdy = emit_weight_T(pb, wg, NI, D, "wg",
                               quant=lambda p_, a, b, W, tg: emit_lpbq_rows(p_, a, b, W, tg))
    wuT, wurdy = emit_weight_T(pb, wu, NI, D, "wu",
                               quant=lambda p_, a, b, W, tg: emit_lpbq_rows(p_, a, b, W, tg))
    xa = [pb.sb(f"xa{i}", [128, D], F32) for i in range(2)]
    xn = pb.sb("xn", [128, D], F32)
    xqb = pb.sb("xqb", [128, D], BF16)
    xfm = pb.sb("xfm", [128, 8 * 512], BF16)
    psT = [pb.ps(f"psT{i}", [128, 128], BF16) for i in range(2)]
    psM = [pb.ps(f"psM{i}", [128, 512], F32) for i in range(2)]
    gbf = pb.sb("gbf", [128, 512], BF16)
    ubf = pb.sb("ubf", [128, 512], BF16)
    for t in range(R // 128):
        a = xa[t % 2]
        tl = pb.op("sync", lambda e, t=t, a=a: e.dma_start(out=a[:], in_=xv[t]),
                   waits=[T.get(("xa", t % 2))], inc="Xq", inc_by=16)
        tln = emit_ln(pb, a, xn, [tl, T.get("xn_rd")])
        T[("xa", t % 2)] = tln
        tsp = pb.op("gpsimd", lambda e, t=t: e.dma_start(out=xnv[t], in_=xn[:]),
                    waits=[tln], inc="Hs", inc_by=16)
        tmx = pb.op("vector", lambda e: e.tensor_reduce(
            mtmp[:], xn[:], AX.X, ALU.max, apply_absolute_value=True),
            waits=[tln, T.get("mtmp_rd")], inc="V")
        tac = pb.op("vector", lambda e: e.tensor_max(
            mxc[:, 0:1], mxc[:, 0:1], mtmp[:]), waits=[tmx, T["mxc_z"], T.get("mx0")], inc="V")
        T["mtmp_rd"] = tac
        T["mx0"] = tac
        T["xn_rd"] = tsp
    T[("gmcol", "x")] = T["mx0"]
    if stop_phase < 1:
        pb.wait("gpsimd", pb.tag["xn_rd"]); pb.pop_scope(); return pb.finalize()
    tgx = emit_global_absmax(pb, mxc[:, 0:1], "x", idf, gx[0:1, 0:1])
    tscx = emit_scale_cols(pb, gx, scx[:, 0:1], scx[:, 1:2], ones1, [tgx], psrep)
    if stop_phase < 2:
        pb.wait("gpsimd", tscx); pb.pop_scope(); return pb.finalize()
    for t in range(R // 128):
        tl = pb.op("sync", lambda e, t=t: e.dma_start(out=xn[:], in_=xnv[t]),
                   waits=[T.get("xn_rd2"), T["xn_rd"], T["mx0"]], inc="Xq", inc_by=16)
        tq = emit_qdq4(pb, xn[:], xqb[:], scx[:, 1:2], scx[:, 0:1], pb.q_f32, pb.q_i32,
                       [tl, tscx, T.get("xqb_rd")], n=1024)
        T["xn_rd2"] = tq
        for f in range(8):
            ps = psT[f % 2]
            tt = pb.op("tensor", lambda e, f=f, ps=ps: e.transpose(
                ps[:], xqb[:, 128 * f:128 * (f + 1)], idb[:]),
                waits=[tq, T.get(("psT", f % 2))] + T["id"], inc="T")
            ev = pb.op("vector", lambda e, f=f, ps=ps: e.tensor_copy(
                xfm[:, f * 128:(f + 1) * 128] if False else xfm[:, f * 512:f * 512 + 128], ps[:]),
                waits=[tt], inc="V")
            tw = pb.op("gpsimd", lambda e, f=f, t=t: e.dma_start(
                out=xq_dram[128 * f:128 * (f + 1), 128 * t:128 * (t + 1)],
                in_=xfm[:, f * 512:f * 512 + 128]), waits=[ev], inc="Hs", inc_by=16)
            T[("psT", f % 2)] = tw
            if f == 7:
                T["xqb_rd"] = tt
                T["xq_done"] = tw
    if stop_phase < 3:
        pb.wait("gpsimd", pb.tag["xq_done"]); pb.pop_scope(); return pb.finalize()
    for n in range(R // 512):
        lds = []
        for f in range(8):
            tl = pb.op("sync", lambda e, f=f, n=n: e.dma_start(
                out=xfm[:, f * 512:(f + 1) * 512],
                in_=xq_dram[128 * f:128 * (f + 1), 512 * n:512 * (n + 1)]),
                waits=[T["xq_done"], T.get("xfm_rd")], inc="Xq", inc_by=16)
            lds.append(tl)
        for m in range(NKT):
            for wi, (wT, wrdy, obf, od, mcol) in enumerate(
                    ((wgT, wgrdy, gbf, g_dram, 1), (wuT, wurdy, ubf, u_dram, 2))):
                ps = psM[wi]
                def mmg(e, m=m, ps=ps, wT=wT):
                    inst = None
                    for k in range(8):
                        inst = e.matmul(ps[:], wT[:, k * NI + 128 * m:k * NI + 128 * (m + 1)],
                                        xfm[:, k * 512:(k + 1) * 512],
                                        start=(k == 0), stop=(k == 7))
                    return inst
                tm = pb.op("tensor", mmg, waits=lds + [wrdy, T.get(("psMl", wi)),
                           T.get(("psMlr", wi))], inc="T")
                if m == NKT - 1:
                    T["xfm_rd"] = tm
                te = pb.op("scalar", lambda e, ps=ps, obf=obf: e.copy(obf[:], ps[:]),
                           waits=[tm, T.get(("obf", wi))], inc="S")
                tr = pb.op("vector", lambda e, ps=ps: e.tensor_reduce(
                    mtmp[:], ps[:], AX.X, ALU.max, apply_absolute_value=True),
                    waits=[tm, te, T.get("mtmp_rd")], inc="V")
                tr2 = pb.op("vector", lambda e, mcol=mcol: e.tensor_max(
                    mxc[:, mcol:mcol + 1], mxc[:, mcol:mcol + 1], mtmp[:]),
                    waits=[tr, T["mxc_z"], T.get(("mxa", mcol))], inc="V")
                T["mtmp_rd"] = tr2
                T[("mxa", mcol)] = tr2
                T[("psMlr", wi)] = tr
                tw = pb.op("gpsimd", lambda e, m=m, n=n, obf=obf, od=od: e.dma_start(
                    out=od[128 * m:128 * (m + 1), 512 * n:512 * (n + 1)], in_=obf[:]),
                    waits=[te, tr2], inc="Hs", inc_by=16)
                T[("obf", wi)] = tw
                T[("psMl", wi)] = te
                T["gu_done"] = tw
    T[("gmcol", "g")] = T[("mxa", 1)]
    T[("gmcol", "u")] = T[("mxa", 2)]
    if stop_phase < 4:
        pb.wait("gpsimd", pb.tag["gu_done"]); pb.pop_scope(); return pb.finalize()
    tgg = emit_global_absmax(pb, mxc[:, 1:2], "g", idf, gg[0:1, 0:1])
    tgu = emit_global_absmax(pb, mxc[:, 2:3], "u", idf, gu[0:1, 0:1])
    tscg = emit_scale_cols(pb, gg, scg[:, 0:1], scg[:, 1:2], ones1, [tgg], psrep)
    tscu = emit_scale_cols(pb, gu, scu[:, 0:1], scu[:, 1:2], ones1, [tgu], psrep)
    if stop_phase < 5:
        pb.wait("gpsimd", tscu); pb.pop_scope(); return pb.finalize()
    sigt = pb.sb("sigt", [128, 512], F32)
    for n in range(R // 512):
        for m in range(NKT):
            tl = pb.op("sync", lambda e, m=m, n=n: e.dma_start(
                out=gbf[:], in_=g_dram[128 * m:128 * (m + 1), 512 * n:512 * (n + 1)]),
                waits=[T["gu_done"], T.get("gbf_rd")], inc="Xq", inc_by=16)
            tq = emit_qdq4(pb, gbf[:], tmp32b[:], scg[:, 1:2], scg[:, 0:1], pb.q_f32,
                           pb.q_i32, [tl, tscg, T.get("t32b_rd")])
            T["gbf_rd"] = tq
            tsg = pb.op("scalar", lambda e: e.activation(sigt[:], tmp32b[:], AF.Sigmoid),
                        waits=[tq, T.get("sig_rd")], inc="S")
            t1 = pb.op("vector", lambda e, qf=pb.q_f32: e.tensor_scalar(
                qf[:, 0:512], sigt[:], 65536.0, None, ALU.mult, ALU.bypass),
                waits=[tsg, pb.tag.get("q_busy"), pb.tag.get(("qdq_busy", id(pb.q_f32)))], inc="V")
            t2 = pb.op("vector", lambda e, qf=pb.q_f32, qi=pb.q_i32: e.tensor_copy(
                qi[:, 0:512], qf[:, 0:512]), waits=[t1], inc="V")
            t3 = pb.op("vector", lambda e, qf=pb.q_f32, qi=pb.q_i32: e.tensor_copy(
                qf[:, 0:512], qi[:, 0:512]), waits=[t2], inc="V")
            t4 = pb.op("vector", lambda e, qf=pb.q_f32: e.tensor_scalar(
                sigt[:], qf[:, 0:512], 65535.0, float(1 / 65536.0), ALU.min, ALU.mult),
                waits=[t3], inc="V")
            pb.tag["q_busy"] = t4
            pb.tag[("qdq_busy", id(pb.q_f32))] = t4
            T["sig_rd"] = t4
            tg2 = pb.op("vector", lambda e: e.tensor_mul(tmp32b[:], tmp32b[:], sigt[:]),
                        waits=[t4], inc="V")
            tmr = pb.op("vector", lambda e: e.tensor_reduce(
                mtmp[:], tmp32b[:], AX.X, ALU.max, apply_absolute_value=True),
                waits=[tg2, T.get("mtmp_rd")], inc="V")
            tmx = pb.op("vector", lambda e: e.tensor_max(
                mxc[:, 3:4], mxc[:, 3:4], mtmp[:]), waits=[tmr, T.get(("mxa", 3))], inc="V")
            T["mtmp_rd"] = tmx
            T[("mxa", 3)] = tmx
            tcb = pb.op("vector", lambda e: e.tensor_copy(gbf[:], tmp32b[:]),
                        waits=[tg2, tmx if False else tmx], inc="V")
            T["t32b_rd"] = tcb
            tw = pb.op("gpsimd", lambda e, m=m, n=n: e.dma_start(
                out=g2_dram[128 * m:128 * (m + 1), 512 * n:512 * (n + 1)], in_=gbf[:]),
                waits=[tcb], inc="Hs", inc_by=16)
            T["gbf_rd"] = tw
            T["g2_done"] = tw
    T[("gmcol", "g2")] = T[("mxa", 3)]
    tg2g = emit_global_absmax(pb, mxc[:, 3:4], "g2", idf, g2[0:1, 0:1])
    tsc2 = emit_scale_cols(pb, g2, sc2[:, 0:1], sc2[:, 1:2], ones1, [tg2g], psrep)
    if stop_phase < 6:
        pb.wait("gpsimd", tsc2); pb.pop_scope(); return pb.finalize()
    for n in range(R // 512):
        for m in range(NKT):
            tl1 = pb.op("sync", lambda e, m=m, n=n: e.dma_start(
                out=gbf[:], in_=g2_dram[128 * m:128 * (m + 1), 512 * n:512 * (n + 1)]),
                waits=[T["g2_done"], T.get("gbf_rd")], inc="Xq", inc_by=16)
            tl2 = pb.op("sync", lambda e, m=m, n=n: e.dma_start(
                out=ubf[:], in_=u_dram[128 * m:128 * (m + 1), 512 * n:512 * (n + 1)]),
                waits=[T["gu_done"], T.get("ubf_rd")], inc="Xq", inc_by=16)
            tq1 = emit_qdq4(pb, gbf[:], tmp32b[:], sc2[:, 1:2], sc2[:, 0:1], pb.q_f32,
                            pb.q_i32, [tl1, tsc2, T.get("t32b_rd")])
            T["gbf_rd"] = tq1
            tq2 = emit_qdq4(pb, ubf[:], tmp32[:], scu[:, 1:2], scu[:, 0:1], pb.q_f32,
                            pb.q_i32, [tl2, tscu, T.get("t32_rd")])
            T["ubf_rd"] = tq2
            to = pb.op("vector", lambda e: e.tensor_mul(tmp32b[:], tmp32b[:], tmp32[:]),
                       waits=[tq1, tq2], inc="V")
            T["t32_rd"] = to
            tmr = pb.op("vector", lambda e: e.tensor_reduce(
                mtmp[:], tmp32b[:], AX.X, ALU.max, apply_absolute_value=True),
                waits=[to, T.get("mtmp_rd")], inc="V")
            tmx = pb.op("vector", lambda e: e.tensor_max(
                mxc[:, 4:5], mxc[:, 4:5], mtmp[:]), waits=[tmr, T.get(("mxa", 4))], inc="V")
            T["mtmp_rd"] = tmx
            T[("mxa", 4)] = tmx
            tcb = pb.op("vector", lambda e: e.tensor_copy(gbf[:], tmp32b[:]),
                        waits=[to, tmx], inc="V")
            T["t32b_rd"] = tcb
            tw = pb.op("gpsimd", lambda e, m=m, n=n: e.dma_start(
                out=o_dram[128 * m:128 * (m + 1), 512 * n:512 * (n + 1)], in_=gbf[:]),
                waits=[tcb], inc="Hs", inc_by=16)
            T["gbf_rd"] = tw
            T["o_done"] = tw
    T[("gmcol", "o")] = T[("mxa", 4)]
    tgo = emit_global_absmax(pb, mxc[:, 4:5], "o", idf, go[0:1, 0:1])
    tsco = emit_scale_cols(pb, go, sco[:, 0:1], sco[:, 1:2], ones1, [tgo], psrep)
    if stop_phase < 7:
        pb.wait("gpsimd", tsco); pb.pop_scope(); return pb.finalize()
    pb.pop_scope()
    wdT, wdrdy = None, None
    pb.push_scope()
    alloc_wstage(pb, 3072, f8=False, sfx="d")
    wdT, wdrdy = emit_weight_T(pb, wd, D, NI, "wd",
                               quant=lambda p_, a, b, W, tg: emit_lpbq_rows(p_, a, b, W, tg, fold=0.5))
    oqf = pb.sb("oqf", [128, NKT * 512], BF16)
    obf2 = pb.sb("obf2", [128, 512], BF16)
    psM2 = [pb.ps(f"ps2{i}", [128, 512], F32) for i in range(2)]
    psY = [pb.ps(f"psY{i}", [128, 128], BF16) for i in range(2)]
    yfm = pb.sb("yfm", [128, 8 * 512], BF16)
    ytk = pb.sb("ytk", [128, D], BF16)
    xa2 = [pb.sb(f"xb{i}", [128, D], F32) for i in range(2)]
    for n in range(R // 512):
        qt = []
        for k in range(NKT):
            tl = pb.op("sync", lambda e, k=k, n=n: e.dma_start(
                out=obf2[:], in_=o_dram[128 * k:128 * (k + 1), 512 * n:512 * (n + 1)]),
                waits=[T["o_done"], T.get("obf2_rd")], inc="Xq", inc_by=16)
            tq = emit_qdq4(pb, obf2[:], oqf[:, k * 512:(k + 1) * 512], sco[:, 1:2],
                           sco[:, 0:1], tmp32, i32t, [tl, tsco, T.get(("oqf_rd", k))])
            T["obf2_rd"] = tq
            qt.append(tq)
        for m in range(8):
            ps = psM2[m % 2]
            def mmy(e, m=m, ps=ps):
                inst = None
                for k in range(NKT):
                    inst = e.matmul(ps[:], wdT[:, k * D + 128 * m:k * D + 128 * (m + 1)],
                                    oqf[:, k * 512:(k + 1) * 512],
                                    start=(k == 0), stop=(k == NKT - 1))
                return inst
            tm = pb.op("tensor", mmy, waits=qt + [wdrdy, T.get(("psM2", m % 2))], inc="T")
            if m == 7:
                for k in range(NKT):
                    T[("oqf_rd", k)] = tm
            te = pb.op("scalar", lambda e, m=m, ps=ps: e.copy(
                yfm[:, m * 512:(m + 1) * 512], ps[:]),
                waits=[tm, T.get(("yfm_rd", m))], inc="S")
            T[("psM2", m % 2)] = te
            T[("yfm", m)] = te
        for ti in range(4):
            t = 4 * n + ti
            trb = []
            for m in range(8):
                ps = psY[m % 2]
                tt = pb.op("tensor", lambda e, m=m, ti=ti, ps=ps: e.transpose(
                    ps[:], yfm[:, m * 512 + ti * 128: m * 512 + ti * 128 + 128], idb[:]),
                    waits=[T[("yfm", m)], T.get(("psY", m % 2))] + T["id"], inc="T")
                if ti == 3:
                    T[("yfm_rd", m)] = tt
                ev = pb.op("vector", lambda e, m=m, ps=ps: e.tensor_copy(
                    ytk[:, 128 * m:128 * (m + 1)], ps[:]),
                    waits=[tt, T.get("ytk_rd")], inc="V")
                T[("psY", m % 2)] = ev
                trb.append(ev)
            a = xa2[ti % 2]
            tlx = pb.op("sync", lambda e, t=t, a=a: e.dma_start(out=a[:], in_=xv[t]),
                        waits=[T.get(("xa2", ti % 2))], inc="Xq", inc_by=16)
            tad = pb.op("vector", lambda e, a=a: e.tensor_add(a[:], a[:], ytk[:]),
                        waits=trb + [tlx], inc="V")
            T["ytk_rd"] = tad
            tst = pb.op("scalar", lambda e, t=t, a=a: e.dma_start(out=ov[t], in_=a[:]),
                        waits=[tad], inc="St", inc_by=16)
            T[("xa2", ti % 2)] = tst
    pb.pop_scope()
    if fused:
        return [("St", pb.cnt["St"])]
    pb.wait("gpsimd", ("St", pb.cnt["St"]))
    return pb.finalize()



def _causal_mask_table():
    m = np.zeros((4, 128, 512), np.float32)
    for d in range(4):
        for j in range(4):
            if d < j:
                m[d][:, j * 128:(j + 1) * 128] = 1.0
            elif d == j:
                tri = (np.arange(128)[:, None] <= np.arange(128)[None, :])
                m[d][:, j * 128:(j + 1) * 128] = tri.astype(np.float32)
    return m


def build_causal_attn(kind, R=4096, pb=None, io=None):
    fused = pb is not None
    if not fused:
        pb = PB()
    gi = lambda n, s, d=F32: io[n] if fused else pb.dram(n, s, d, kind="ExternalInput")
    xin = gi("xin", [R, D])
    if kind == "l0":
        Wq = gi("Wq", [64, 64])
        Wk = gi("Wk", [64, 64])
        Wv = gi("Wv", [64, 64])
        Wu = gi("Wu", [D, D])
        bu = gi("bu", [D])
    else:
        w_in = gi("w_in", [3 * D, D])
        b_in = gi("b_in", [3 * D])
        w_out = gi("w_out", [D, D])
        b_out = gi("b_out", [D])
    xout = io["xout"] if fused else pb.dram("xout", [R, D], F32, kind="ExternalOutput")
    qd = pb.dram("q_dram", [D, R], BF16)
    kd = pb.dram("k_dram", [D, R], BF16)
    vd = pb.dram("v_dram", [D, R], BF16)
    yd = pb.dram("y_dram", [D, R], BF16)

    idf, idb = emit_identity(pb)
    alloc_ln(pb)
    T = pb.tag
    NT = R // 128
    NC = R // 512
    xv = xin.rearrange("(t p) d -> t p d", p=128)
    ov = xout.rearrange("(t p) d -> t p d", p=128)

    woT = pb.sb("woT", [128, 8 * D], BF16)
    if kind == "l0":
        boc, bordy = emit_bias_cols(pb, bu, D, "bo", idf)
    else:
        boc, bordy = emit_bias_cols(pb, b_out, D, "bo", idf)

    pb.push_scope()
    alloc_wstage(pb, 4096, f8=False, tst=None)
    wo_src = Wu if kind == "l0" else w_out
    _, wordy = emit_weight_T(pb, wo_src, D, D, "wo", wT_pre=woT)
    if kind == "l3":
        wiT, wirdy = emit_weight_T(pb, w_in, 3 * D, D, "wi", fold_first_rows=(1024, 0.125))
        bic, birdy = emit_bias_cols(pb, b_in, 3 * D, "bi", idf)
        bis = pb.op("vector", lambda e: e.tensor_scalar(
            bic[:, 0:8], bic[:, 0:8], 0.125, None, ALU.mult, ALU.bypass),
            waits=[birdy], inc="V")
    else:
        s4 = float(D) ** -0.25
        packs = []
        wsm = pb.sb("wsm", [64, 64], F32)
        psw = pb.ps("psw", [64, 64], F32)
        wsc = pb.sb("wsc", [128, 64], F32)
        for nm, wap, scl in (("q", Wq, s4), ("k", Wk, s4), ("v", Wv, 1.0)):
            pk = pb.sb(f"pk{nm}", [128, 128], BF16)
            tz = pb.op("vector", lambda e, pk=pk: e.memset(pk[:], 0.0), inc="V")
            tl = pb.op("sync", lambda e, wap=wap: e.dma_start(out=wsm[:], in_=wap),
                       waits=[T.get("wsm_rd")], inc="Wq", inc_by=16)
            tt = pb.op("tensor", lambda e: e.transpose(psw[:], wsm[:], idf[0:64, 0:64]),
                       waits=[tl, T.get("psw_rd")] + T["id"], inc="T")
            T["wsm_rd"] = tt
            tsc = pb.op("vector", lambda e, scl=scl: e.tensor_scalar(
                wsc[0:64, :], psw[:], float(scl), None, ALU.mult, ALU.bypass),
                waits=[tt, T.get("wsc_rd")], inc="V")
            T["psw_rd"] = tsc
            c1 = pb.op("vector", lambda e, pk=pk: e.tensor_copy(pk[0:64, 0:64], wsc[0:64, :]),
                       waits=[tsc, tz], inc="V")
            c2 = pb.op("vector", lambda e, pk=pk: e.tensor_copy(pk[64:128, 64:128], wsc[0:64, :]),
                       waits=[tsc, tz], inc="V")
            T["wsc_rd"] = c2
            packs.append((pk, [c1, c2]))
        (wqp, wqt), (wkp, wkt), (wvp, wvt) = packs

    xa = [pb.sb(f"xa{i}", [128, D], F32) for i in range(2)]
    xqb = pb.sb("xqb", [128, D], BF16)
    xnfm = pb.sb("xnfm", [128, 8 * R], BF16)
    psT = [pb.ps(f"psT{i}", [128, 128], BF16) for i in range(2)]
    psM = [pb.ps(f"psM{i}", [128, 512], F32) for i in range(2)]
    qkvb = pb.sb("qkvb", [128, 512], BF16)
    for t in range(NT):
        a = xa[t % 2]
        tl = pb.op("sync", lambda e, t=t, a=a: e.dma_start(out=a[:], in_=xv[t]),
                   waits=[T.get(("xa", t % 2))], inc="Xq", inc_by=16)
        tln = emit_ln(pb, a, xqb, [tl, T.get("xqb_rd")])
        T[("xa", t % 2)] = tln
        for f in range(8):
            ps = psT[f % 2]
            tt = pb.op("tensor", lambda e, f=f, ps=ps: e.transpose(
                ps[:], xqb[:, 128 * f:128 * (f + 1)], idb[:]),
                waits=[tln, T.get(("psT", f % 2))] + T["id"], inc="T")
            ev = pb.op("vector", lambda e, f=f, t=t, ps=ps: e.tensor_copy(
                xnfm[:, f * R + t * 128: f * R + t * 128 + 128], ps[:]),
                waits=[tt], inc="V")
            T[("psT", f % 2)] = ev
            T["xnfm"] = ev
            if f == 7:
                T["xqb_rd"] = tt
    for hp in range(8):
        for pi_, (nm, dram_) in enumerate((("q", qd), ("k", kd), ("v", vd))):
            for ch in range(NC):
                ps = psM[ch % 2]
                if kind == "l3":
                    def mmp(e, hp=hp, pi_=pi_, ch=ch, ps=ps):
                        inst = None
                        for k in range(8):
                            col = k * 3072 + pi_ * 1024 + 128 * hp
                            inst = e.matmul(ps[:], wiT[:, col:col + 128],
                                            xnfm[:, k * R + ch * 512:k * R + ch * 512 + 512],
                                            start=(k == 0), stop=(k == 7))
                        return inst
                    tm = pb.op("tensor", mmp, waits=[T["xnfm"], wirdy,
                               T.get(("psMp", ch % 2))], inc="T")
                    te = pb.op("scalar", lambda e, hp=hp, pi_=pi_, ps=ps: e.activation(
                        qkvb[:], ps[:], AF.Identity, bias=bic[:, pi_ * 8 + hp:pi_ * 8 + hp + 1]),
                        waits=[tm, bis, T.get("qkvb_rd")], inc="S")
                else:
                    pk = (wqp, wkp, wvp)[pi_]
                    pkt = (wqt, wkt, wvt)[pi_]
                    def mmp(e, hp=hp, ch=ch, ps=ps, pk=pk):
                        return e.matmul(ps[:], pk[:],
                                        xnfm[:, hp * R + ch * 512:hp * R + ch * 512 + 512],
                                        start=True, stop=True)
                    tm = pb.op("tensor", mmp, waits=[T["xnfm"]] + pkt +
                               [T.get(("psMp", ch % 2))], inc="T")
                    te = pb.op("scalar", lambda e, ps=ps: e.copy(qkvb[:], ps[:]),
                               waits=[tm, T.get("qkvb_rd")], inc="S")
                T[("psMp", ch % 2)] = te
                tw = pb.op("gpsimd", lambda e, hp=hp, ch=ch, dram_=dram_: e.dma_start(
                    out=dram_[128 * hp:128 * (hp + 1), 512 * ch:512 * (ch + 1)], in_=qkvb[:]),
                    waits=[te], inc="Hs", inc_by=16)
                T["qkvb_rd"] = tw
                T["qkv_done"] = tw
    pb.pop_scope(gate=[T["qkv_done"]])

    pb.push_scope()
    maskh = pb.const(_causal_mask_table(), "cmask")
    mk32 = pb.sb("mk32", [128, 4 * 512], F32)
    mkb = pb.sb("mkb", [128, 4 * 512], BF16)
    tm1 = pb.op("sync", lambda e: e.dma_start(
        out=mk32[:], in_=maskh.rearrange("d p c -> p d c")), inc="Wq", inc_by=16)
    tm2 = pb.op("vector", lambda e: e.tensor_copy(mkb[:], mk32[:]), waits=[tm1], inc="V")
    ones1 = pb.sb("ones1", [1, 64], F32)
    tON = pb.op("vector", lambda e: e.memset(ones1[:], 1.0), inc="V")
    qf = pb.sb("qf", [128, R], BF16)
    kf = pb.sb("kf", [128, R], BF16)
    vf = pb.sb("vf", [128, R], BF16)
    V2 = [pb.sb(f"V2_{h}", [128, 32 * 65], BF16) for h in range(2)]
    expT = [pb.sb(f"expT{h}", [128, 512], BF16) for h in range(2)]
    yun = [pb.sb(f"yun{h}", [64, 512], F32) for h in range(2)]
    rep_sb = pb.sb("rep_sb", [64, 512], F32)
    rcol = pb.sb("rcol", [1, 512], F32)
    ybf = pb.sb("ybf", [64, 512], BF16)
    psS = [pb.ps(f"psS{h}", [128, 512], F32) for h in range(2)]
    psA = [pb.ps(f"psA{h}", [65, 512], F32) for h in range(2)]
    psV = pb.ps("psV", [128, 64], BF16)
    psR = pb.ps("psR", [64, 512], F32)
    for hp in range(8):
        tq = pb.op("sync", lambda e, hp=hp: e.dma_start(
            out=qf[:], in_=qd[128 * hp:128 * (hp + 1), :]),
            waits=[T["qkv_done"], T.get("qf_rd")], inc="Xq", inc_by=16)
        tk = pb.op("sync", lambda e, hp=hp: e.dma_start(
            out=kf[:], in_=kd[128 * hp:128 * (hp + 1), :]),
            waits=[T["qkv_done"], T.get("kf_rd")], inc="Xq", inc_by=16)
        tv = pb.op("sync", lambda e, hp=hp: e.dma_start(
            out=vf[:], in_=vd[128 * hp:128 * (hp + 1), :]),
            waits=[T["qkv_done"], T.get("vf_rd")], inc="Xq", inc_by=16)
        vrd = []
        for h in range(2):
            tz = pb.op("vector", lambda e, h=h: e.memset(V2[h][:], 1.0),
                       waits=[T.get(("V2", h))], inc="V")
            for ka in range(32):
                ttr = pb.op("tensor", lambda e, h=h, ka=ka: e.transpose(
                    psV[:], vf[64 * h:64 * h + 64, 128 * ka:128 * (ka + 1)],
                    idb[64 * h:64 * h + 64, 64 * h:64 * h + 64]),
                    waits=[tv, T.get("psV_rd")] + T["id"], inc="T")
                ev = pb.op("vector", lambda e, h=h, ka=ka: e.tensor_copy(
                    V2[h][:, 65 * ka:65 * ka + 64], psV[:]), waits=[ttr, tz], inc="V")
                T["psV_rd"] = ev
                vrd.append(ev)
                if h == 1 and ka == 31:
                    T["vf_rd"] = ttr
        for jc in range(NC):
            kmax = 4 * jc + 4
            avt = [None, None]
            for ka in range(kmax):
                for h in range(2):
                    tmm = pb.op("tensor", lambda e, h=h, ka=ka, jc=jc: e.matmul(
                        psS[h][:], kf[64 * h:64 * h + 64, 128 * ka:128 * (ka + 1)],
                        qf[64 * h:64 * h + 64, jc * 512:(jc + 1) * 512],
                        tile_position=(64 * h, 0), start=True, stop=True),
                        waits=[tq, tk, T.get(("psS", h))], inc="T")
                    tex = pb.op("scalar", lambda e, h=h: e.activation(
                        expT[h][:], psS[h][:], AF.Exp),
                        waits=[tmm, T.get(("expT", h))], inc="S")
                    T[("psS", h)] = tex
                    tready = tex
                    if ka >= 4 * jc:
                        d = ka - 4 * jc
                        tready = pb.op("vector", lambda e, h=h, d=d: e.tensor_mul(
                            expT[h][:], expT[h][:], mkb[:, d * 512:(d + 1) * 512]),
                            waits=[tex, tm2], inc="V")
                    tav = pb.op("tensor", lambda e, h=h, ka=ka, kmax=kmax: e.matmul(
                        psA[h][:], V2[h][:, 65 * ka:65 * (ka + 1)], expT[h][:],
                        start=(ka == 0), stop=(ka == kmax - 1)),
                        waits=[tready, vrd[-1]] + ([T.get(("psA", h))] if ka == 0 else []),
                        inc="T")
                    T[("expT", h)] = tav
                    avt[h] = tav
            for h in range(2):
                trc = pb.op("vector", lambda e, h=h: e.reciprocal(
                    rcol[:], psA[h][64:65, :]), waits=[avt[h], T.get("rcol_rd")], inc="V")
                trp = pb.op("tensor", lambda e: e.matmul(
                    psR[:], ones1[:], rcol[:], start=True, stop=True),
                    waits=[trc, tON, T.get("psR_rd")], inc="T")
                tyu = pb.op("scalar", lambda e, h=h: e.copy(yun[h][:], psA[h][0:64, :]),
                            waits=[trc, avt[h], T.get(("yun_rd", h))], inc="S")
                trs = pb.op("vector", lambda e: e.tensor_copy(rep_sb[:], psR[:]),
                            waits=[trp, T.get("rep_rd")], inc="V")
                T["psR_rd"] = trs
                T["rcol_rd"] = trp
                T[("psA", h)] = tyu
                tyn = pb.op("vector", lambda e, h=h: e.tensor_mul(
                    ybf[:], yun[h][:], rep_sb[:]), waits=[tyu, trs, T.get("ybf_rd")], inc="V")
                T[("yun_rd", h)] = tyn
                T["rep_rd"] = tyn
                tw = pb.op("gpsimd", lambda e, hp=hp, h=h, jc=jc: e.dma_start(
                    out=yd[128 * hp + 64 * h:128 * hp + 64 * h + 64, jc * 512:(jc + 1) * 512],
                    in_=ybf[:]), waits=[tyn], inc="Hs", inc_by=16)
                T["ybf_rd"] = tw
                T["y_done"] = tw
        T["qf_rd"] = avt[0]
        T["kf_rd"] = avt[0]
        T[("V2", 0)] = avt[0]
        T[("V2", 1)] = avt[1]
    pb.pop_scope(gate=[T["y_done"]])

    ymv = [pb.sb(f"ymv{i}", [128, 512], BF16) for i in range(8)]
    psM2 = [pb.ps(f"ps2{i}", [128, 512], F32) for i in range(2)]
    psY = [pb.ps(f"psYo{i}", [128, 128], BF16) for i in range(2)]
    yofm = pb.sb("yofm", [128, 8 * 512], BF16)
    ytk = pb.sb("ytk", [128, D], BF16)
    ypos = pb.sb("ypos", [128, D], F32)
    yneg = pb.sb("yneg", [128, D], F32)
    xar = [pb.sb(f"xar{i}", [128, D], F32) for i in range(2)]
    for ch in range(NC):
        lds = []
        for k in range(8):
            tl = pb.op("sync", lambda e, k=k, ch=ch: e.dma_start(
                out=ymv[k][:], in_=yd[128 * k:128 * (k + 1), 512 * ch:512 * (ch + 1)]),
                waits=[T["y_done"], T.get(("ymv", k))], inc="Xq", inc_by=16)
            lds.append(tl)
        for m in range(8):
            ps = psM2[m % 2]
            def mmo(e, m=m, ps=ps):
                inst = None
                for k in range(8):
                    inst = e.matmul(ps[:], woT[:, k * D + 128 * m:k * D + 128 * (m + 1)],
                                    ymv[k][:], start=(k == 0), stop=(k == 7))
                return inst
            tm = pb.op("tensor", mmo, waits=lds + [wordy, T.get(("psM2", m % 2))], inc="T")
            if m == 7:
                for k in range(8):
                    T[("ymv", k)] = tm
            te = pb.op("scalar", lambda e, m=m, ps=ps: e.activation(
                yofm[:, m * 512:(m + 1) * 512], ps[:], AF.Identity, bias=boc[:, m:m + 1]),
                waits=[tm, bordy, T.get(("yofm_rd", m))], inc="S")
            T[("psM2", m % 2)] = te
            T[("yofm", m)] = te
        for ti in range(4):
            t = 4 * ch + ti
            trb = []
            for m in range(8):
                ps = psY[m % 2]
                tt = pb.op("tensor", lambda e, m=m, ti=ti, ps=ps: e.transpose(
                    ps[:], yofm[:, m * 512 + ti * 128: m * 512 + ti * 128 + 128], idb[:]),
                    waits=[T[("yofm", m)], T.get(("psYo", m % 2))] + T["id"], inc="T")
                if ti == 3:
                    T[("yofm_rd", m)] = tt
                ev = pb.op("vector", lambda e, m=m, ps=ps: e.tensor_copy(
                    ytk[:, 128 * m:128 * (m + 1)], ps[:]),
                    waits=[tt, T.get("ytk_rd")], inc="V")
                T[("psYo", m % 2)] = ev
                trb.append(ev)
            a = xar[ti % 2]
            tlx = pb.op("sync", lambda e, t=t, a=a: e.dma_start(out=a[:], in_=xv[t]),
                        waits=[T.get(("xar", ti % 2))], inc="Xq", inc_by=16)
            if kind == "l3":
                tp = pb.op("vector", lambda e: e.tensor_scalar_max(ypos[:], ytk[:], 0.0),
                           waits=trb + [T.get("ypos_rd")], inc="V")
                tn = pb.op("vector", lambda e: e.tensor_scalar_min(yneg[:], ytk[:], 0.0),
                           waits=trb + [T.get("yneg_rd")], inc="V")
                T["ytk_rd"] = tn
                ta1 = pb.op("vector", lambda e, a=a: e.tensor_add(a[:], a[:], ypos[:]),
                            waits=[tp, tlx], inc="V")
                tad = pb.op("vector", lambda e, a=a: e.tensor_add(a[:], a[:], yneg[:]),
                            waits=[tn, ta1], inc="V")
                T["ypos_rd"] = ta1
                T["yneg_rd"] = tad
            else:
                tad = pb.op("vector", lambda e, a=a: e.tensor_add(a[:], a[:], ytk[:]),
                            waits=trb + [tlx], inc="V")
                T["ytk_rd"] = tad
            tst = pb.op("scalar", lambda e, t=t, a=a: e.dma_start(out=ov[t], in_=a[:]),
                        waits=[tad], inc="St", inc_by=16)
            T[("xar", ti % 2)] = tst
    if fused:
        return [("St", pb.cnt["St"])]
    pb.wait("gpsimd", ("St", pb.cnt["St"]))
    return pb.finalize()






def emit_ln_transpose_all(pb, xv, NT, xnfm, R, idb, xa, xqb, psT):
    T = pb.tag
    for t in range(NT):
        a = xa[t % 2]
        tl = pb.op("sync", lambda e, t=t, a=a: e.dma_start(out=a[:], in_=xv[t]),
                   waits=[T.get(("xa", t % 2))], inc="Xq", inc_by=16)
        tln = emit_ln(pb, a, xqb, [tl, T.get("xqb_rd")])
        T[("xa", t % 2)] = tln
        for f in range(8):
            ps = psT[f % 2]
            tt = pb.op("tensor", lambda e, f=f, ps=ps: e.transpose(
                ps[:], xqb[:, 128 * f:128 * (f + 1)], idb[:]),
                waits=[tln, T.get(("psT", f % 2))] + T["id"], inc="T")
            ev = pb.op("vector", lambda e, f=f, t=t, ps=ps: e.tensor_copy(
                xnfm[:, f * R + t * 128: f * R + t * 128 + 128], ps[:]),
                waits=[tt], inc="V")
            T[("psT", f % 2)] = ev
            T["xnfm"] = ev
            if f == 7:
                T["xqb_rd"] = tt
    return T["xnfm"]


def emit_outproj_residual(pb, yd, woT, wordy, boc, bordy, xv, ov, NC, idb, idf, nan_fix,
                          ydone_key="y_done"):
    T = pb.tag
    ymv = [pb.sb(f"ymv{i}", [128, 512], BF16) for i in range(8)]
    psM2 = [pb.ps(f"ps2o{i}", [128, 512], F32) for i in range(2)]
    psY = [pb.ps(f"psYo{i}", [128, 128], BF16) for i in range(2)]
    yofm = pb.sb("yofm", [128, 8 * 512], BF16)
    ytk = pb.sb("ytk_o", [128, D], BF16)
    if nan_fix:
        ypos = pb.sb("ypos", [128, D], F32)
        yneg = pb.sb("yneg", [128, D], F32)
    xar = [pb.sb(f"xar{i}", [128, D], F32) for i in range(2)]
    for ch in range(NC):
        lds = []
        for k in range(8):
            tl = pb.op("sync", lambda e, k=k, ch=ch: e.dma_start(
                out=ymv[k][:], in_=yd[128 * k:128 * (k + 1), 512 * ch:512 * (ch + 1)]),
                waits=[T[ydone_key], T.get(("ymv", k))], inc="Xq", inc_by=16)
            lds.append(tl)
        for m in range(8):
            ps = psM2[m % 2]
            def mmo(e, m=m, ps=ps):
                inst = None
                for k in range(8):
                    inst = e.matmul(ps[:], woT[:, k * D + 128 * m:k * D + 128 * (m + 1)],
                                    ymv[k][:], start=(k == 0), stop=(k == 7))
                return inst
            tm = pb.op("tensor", mmo, waits=[lds[-1], wordy, T.get(("psM2o", m % 2))], inc="T")
            if m == 7:
                for k in range(8):
                    T[("ymv", k)] = tm
            te = pb.op("scalar", lambda e, m=m, ps=ps: e.activation(
                yofm[:, m * 512:(m + 1) * 512], ps[:], AF.Identity, bias=boc[:, m:m + 1]),
                waits=[tm, bordy, T.get(("yofmo_rd", m))], inc="S")
            T[("psM2o", m % 2)] = te
            T[("yofmo", m)] = te
        for ti in range(4):
            t = 4 * ch + ti
            trb = []
            for m in range(8):
                ps = psY[m % 2]
                tt = pb.op("tensor", lambda e, m=m, ti=ti, ps=ps: e.transpose(
                    ps[:], yofm[:, m * 512 + ti * 128: m * 512 + ti * 128 + 128], idb[:]),
                    waits=[T[("yofmo", m)], T.get(("psYoo", m % 2))] + T["id"], inc="T")
                if ti == 3:
                    T[("yofmo_rd", m)] = tt
                ev = pb.op("vector", lambda e, m=m, ps=ps: e.tensor_copy(
                    ytk[:, 128 * m:128 * (m + 1)], ps[:]),
                    waits=[tt, T.get("ytko_rd")], inc="V")
                T[("psYoo", m % 2)] = ev
                trb.append(ev)
            a = xar[ti % 2]
            tlx = pb.op("sync", lambda e, t=t, a=a: e.dma_start(out=a[:], in_=xv[t]),
                        waits=[T.get(("xaro", ti % 2))], inc="Xq", inc_by=16)
            if nan_fix:
                tp = pb.op("vector", lambda e: e.tensor_scalar_max(ypos[:], ytk[:], 0.0),
                           waits=trb + [T.get("ypos_rd")], inc="V")
                tn = pb.op("vector", lambda e: e.tensor_scalar_min(yneg[:], ytk[:], 0.0),
                           waits=trb + [T.get("yneg_rd")], inc="V")
                T["ytko_rd"] = tn
                ta1 = pb.op("vector", lambda e, a=a: e.tensor_add(a[:], a[:], ypos[:]),
                            waits=[tp, tlx], inc="V")
                tad = pb.op("vector", lambda e, a=a: e.tensor_add(a[:], a[:], yneg[:]),
                            waits=[tn, ta1], inc="V")
                T["ypos_rd"] = ta1
                T["yneg_rd"] = tad
            else:
                tad = pb.op("vector", lambda e, a=a: e.tensor_add(a[:], a[:], ytk[:]),
                            waits=trb + [tlx], inc="V")
                T["ytko_rd"] = tad
            tst = pb.op("scalar", lambda e, t=t, a=a: e.dma_start(out=ov[t], in_=a[:]),
                        waits=[tad], inc="St", inc_by=16)
            T[("xaro", ti % 2)] = tst


def emit_weight_plain(pb, w_ap, O, I, name, bias_like=False):
    KT = I // 128
    wT = pb.sb(f"{name}P", [128, KT * O], BF16)
    st = pb.sb(f"{name}_pst", [128, O], F32)
    wv = w_ap.rearrange("(a p) i -> a p i", p=128) if w_ap.ndim == 2 else w_ap
    last = None
    rdy = None
    for k in range(KT):
        tl = pb.op("sync", lambda e, k=k: e.dma_start(out=st[:], in_=wv[k]),
                   waits=[last], inc="Wq", inc_by=16)
        rdy = pb.op("vector", lambda e, k=k: e.tensor_copy(wT[:, k * O:(k + 1) * O], st[:]),
                    waits=[tl], inc="V")
        last = rdy
    return wT, rdy



def build_l1attn(R=4096, pb=None, io=None):
    fused = pb is not None
    if not fused:
        pb = PB()
    gi = lambda n, s, d=F32: io[n] if fused else pb.dram(n, s, d, kind="ExternalInput")
    xin = gi("xin", [R, D])
    Wq = gi("Wq", [D, 16, 64])
    bq = gi("bq", [16, 64])
    Wk = gi("Wk", [D, 16, 64])
    bk = gi("bk", [16, 64])
    Wv = gi("Wv", [D, 16, 64])
    bv = gi("bv", [16, 64])
    Wo = gi("Wo", [16, 64, D])
    bo = gi("bo", [D])
    xout = io["xout"] if fused else pb.dram("xout", [R, D], F32, kind="ExternalOutput")
    qd = pb.dram("qf_dram", [D, R], BF16)
    kd = pb.dram("kf_dram", [D, R], BF16)
    vd = pb.dram("vf_dram", [D, R], BF16)
    yd = pb.dram("y_dram", [D, R], BF16)
    idf, idb = emit_identity(pb)
    alloc_ln(pb)
    T = pb.tag
    NT, NC = R // 128, R // 512
    xv = xin.rearrange("(t p) d -> t p d", p=128)
    ov = xout.rearrange("(t p) d -> t p d", p=128)
    woP, wordy = emit_weight_plain(pb, Wo.rearrange("h k d -> (h k) d"), D, D, "wo")
    boc, bordy = emit_bias_cols(pb, bo, D, "bo", idf)
    pb.push_scope()
    wqP, t1r = emit_weight_plain(pb, Wq.rearrange("d h k -> d (h k)"), D, D, "wq")
    wkP, t2r = emit_weight_plain(pb, Wk.rearrange("d h k -> d (h k)"), D, D, "wk")
    wvP, t3r = emit_weight_plain(pb, Wv.rearrange("d h k -> d (h k)"), D, D, "wv")
    bqc, tb1 = emit_bias_cols(pb, bq.rearrange("h k -> (h k)"), D, "bq", idf)
    bkc, tb2 = emit_bias_cols(pb, bk.rearrange("h k -> (h k)"), D, "bk", idf)
    bvc, tb3 = emit_bias_cols(pb, bv.rearrange("h k -> (h k)"), D, "bv", idf)
    xa = [pb.sb(f"xa{i}", [128, D], F32) for i in range(2)]
    xqb = pb.sb("xqb", [128, D], BF16)
    xnfm = pb.sb("xnfm", [128, 8 * R], BF16)
    psT = [pb.ps(f"psT{i}", [128, 128], BF16) for i in range(2)]
    psM = [pb.ps(f"psM{i}", [128, 512], F32) for i in range(2)]
    tmpa = pb.sb("tmpa", [128, 512], F32)
    tmpb = pb.sb("tmpb", [128, 512], F32)
    tmpc = pb.sb("tmpc", [128, 512], F32)
    qkvb = pb.sb("qkvb", [128, 512], BF16)
    emit_ln_transpose_all(pb, xv, NT, xnfm, R, idb, xa, xqb, psT)
    for hp in range(8):
        for (wP, wrdy, bcol, brdy, dram_, elu) in ((wqP, t1r, bqc, tb1, qd, True),
                                                   (wkP, t2r, bkc, tb2, kd, True),
                                                   (wvP, t3r, bvc, tb3, vd, False)):
            for ch in range(NC):
                ps = psM[ch % 2]
                def mmp(e, hp=hp, ch=ch, ps=ps, wP=wP):
                    inst = None
                    for k in range(8):
                        inst = e.matmul(ps[:], wP[:, k * D + 128 * hp:k * D + 128 * (hp + 1)],
                                        xnfm[:, k * R + ch * 512:k * R + ch * 512 + 512],
                                        start=(k == 0), stop=(k == 7))
                    return inst
                tm = pb.op("tensor", mmp, waits=[T["xnfm"], wrdy, T.get(("psMp", ch % 2))],
                           inc="T")
                te = pb.op("scalar", lambda e, hp=hp, ps=ps, bcol=bcol: e.activation(
                    tmpa[:], ps[:], AF.Identity, bias=bcol[:, hp:hp + 1]),
                    waits=[tm, brdy, T.get("tmpa_rd")], inc="S")
                T[("psMp", ch % 2)] = te
                if elu:
                    t1 = pb.op("vector", lambda e: e.tensor_scalar_min(tmpb[:], tmpa[:], 0.0),
                               waits=[te, T.get("tmpb_rd")], inc="V")
                    t2 = pb.op("scalar", lambda e: e.activation(tmpb[:], tmpb[:], AF.Exp),
                               waits=[t1], inc="S")
                    t3 = pb.op("vector", lambda e: e.tensor_relu(tmpc[:], tmpa[:]),
                               waits=[te, T.get("tmpc_rd")], inc="V")
                    T["tmpa_rd"] = t3
                    t4 = pb.op("vector", lambda e: e.tensor_add(qkvb[:], tmpb[:], tmpc[:]),
                               waits=[t2, t3, T.get("qkvb_rd")], inc="V")
                    T["tmpb_rd"] = t4
                    T["tmpc_rd"] = t4
                    tev = t4
                else:
                    tev = pb.op("vector", lambda e: e.tensor_copy(qkvb[:], tmpa[:]),
                                waits=[te, T.get("qkvb_rd")], inc="V")
                    T["tmpa_rd"] = tev
                tw = pb.op("gpsimd", lambda e, hp=hp, ch=ch, dram_=dram_: e.dma_start(
                    out=dram_[128 * hp:128 * (hp + 1), 512 * ch:512 * (ch + 1)], in_=qkvb[:]),
                    waits=[tev], inc="Hs", inc_by=16)
                T["qkvb_rd"] = tw
                T["qkv_done"] = tw
    pb.pop_scope(gate=[T["qkv_done"]])
    pb.push_scope()
    ones1 = pb.sb("ones1a", [1, 64], F32)
    tON = pb.op("vector", lambda e: e.memset(ones1[:], 1.0), inc="V")
    qf = pb.sb("qf", [128, R], BF16)
    kf = pb.sb("kf", [128, R], BF16)
    vf = pb.sb("vf", [128, R], BF16)
    kq_tk = [pb.sb(f"kqtk{h}", [128, 32 * 64], BF16) for h in range(2)]
    v2tk = [pb.sb(f"v2tk{h}", [128, 32 * 65], BF16) for h in range(2)]
    kvs = pb.sb("kvs2", [128, 65], BF16)
    rcol = pb.sb("rcol", [1, 512], F32)
    rce = pb.sb("rce", [1, 512], F32)
    rep_sb = pb.sb("rep_sb", [64, 512], F32)
    yun = pb.sb("yun", [64, 512], F32)
    ybf = pb.sb("ybf", [64, 512], BF16)
    psV = pb.ps("psV2", [128, 128], BF16)
    psKV = [pb.ps(f"psKV{h}", [64, 65], F32) for h in range(2)]
    psY2 = [pb.ps(f"psY2{h}", [65, 512], F32) for h in range(2)]
    psR = pb.ps("psR2", [64, 512], F32)
    for hp in range(8):
        tq = pb.op("sync", lambda e, hp=hp: e.dma_start(
            out=qf[:], in_=qd[128 * hp:128 * (hp + 1), :]),
            waits=[T["qkv_done"], T.get("qf_rd")], inc="Xq", inc_by=16)
        tk = pb.op("sync", lambda e, hp=hp: e.dma_start(
            out=kf[:], in_=kd[128 * hp:128 * (hp + 1), :]),
            waits=[T["qkv_done"], T.get("kf_rd")], inc="Xq", inc_by=16)
        tv = pb.op("sync", lambda e, hp=hp: e.dma_start(
            out=vf[:], in_=vd[128 * hp:128 * (hp + 1), :]),
            waits=[T["qkv_done"], T.get("vf_rd")], inc="Xq", inc_by=16)
        trd = []
        for h in range(2):
            tz = pb.op("vector", lambda e, h=h: e.memset(v2tk[h][:], 1.0),
                       waits=[T.get(("v2g", h))], inc="V")
            for ka in range(32):
                tt1 = pb.op("tensor", lambda e, h=h, ka=ka: e.transpose(
                    psV[:, 0:64], kf[64 * h:64 * h + 64, 128 * ka:128 * (ka + 1)],
                    idb[64 * h:64 * h + 64, 64 * h:64 * h + 64]),
                    waits=[tk, T.get("psV_rd")] + T["id"], inc="T")
                ev1 = pb.op("vector", lambda e, h=h, ka=ka: e.tensor_copy(
                    kq_tk[h][:, 64 * ka:64 * (ka + 1)], psV[:, 0:64]),
                    waits=[tt1, T.get(("kqg", h))], inc="V")
                T["psV_rd"] = ev1
                tt2 = pb.op("tensor", lambda e, h=h, ka=ka: e.transpose(
                    psV[:, 0:64], vf[64 * h:64 * h + 64, 128 * ka:128 * (ka + 1)],
                    idb[64 * h:64 * h + 64, 64 * h:64 * h + 64]),
                    waits=[tv, ev1] + T["id"], inc="T")
                ev2 = pb.op("vector", lambda e, h=h, ka=ka: e.tensor_copy(
                    v2tk[h][:, 65 * ka:65 * ka + 64], psV[:, 0:64]),
                    waits=[tt2, tz], inc="V")
                T["psV_rd"] = ev2
                trd.append(ev2)
            if h == 1:
                T["kf_rd"] = tt2
                T["vf_rd"] = tt2
        for h in range(2):
            def mkv(e, h=h):
                inst = None
                for ka in range(32):
                    inst = e.matmul(psKV[h][:], kq_tk[h][:, 64 * ka:64 * (ka + 1)],
                                    v2tk[h][:, 65 * ka:65 * (ka + 1)],
                                    start=(ka == 0), stop=(ka == 31))
                return inst
            tkv = pb.op("tensor", mkv, waits=[trd[-1], T.get(("psKVg", h))], inc="T")
            tkc = pb.op("vector", lambda e, h=h: e.tensor_copy(
                kvs[64 * h:64 * h + 64, :], psKV[h][:]),
                waits=[tkv, T.get(("kvs_rd", h))], inc="V")
            T[("psKVg", h)] = tkc
            T[("kvsrdy", h)] = tkc
        for jc in range(NC):
            for h in range(2):
                tmy = pb.op("tensor", lambda e, h=h, jc=jc: e.matmul(
                    psY2[h][:], kvs[64 * h:64 * h + 64, :],
                    qf[64 * h:64 * h + 64, jc * 512:(jc + 1) * 512],
                    tile_position=(64 * h, 0), start=True, stop=True),
                    waits=[tq, T[("kvsrdy", h)], T.get(("psY2g", h))], inc="T")
                teps = pb.op("vector", lambda e, h=h: e.tensor_scalar(
                    rce[:], psY2[h][64:65, :], 1e-6, None, ALU.add, ALU.bypass),
                    waits=[tmy, T.get("rce_rd")], inc="V")
                trc = pb.op("vector", lambda e: e.reciprocal(rcol[:], rce[:]),
                            waits=[teps, T.get("rcol_rd")], inc="V")
                T["rce_rd"] = trc
                trp = pb.op("tensor", lambda e: e.matmul(
                    psR[:], ones1[:], rcol[:], start=True, stop=True),
                    waits=[trc, tON, T.get("psRg")], inc="T")
                tyu = pb.op("scalar", lambda e, h=h: e.copy(yun[:], psY2[h][0:64, :]),
                            waits=[teps, T.get("yun_rd")], inc="S")
                T[("psY2g", h)] = tyu
                trs = pb.op("vector", lambda e: e.tensor_copy(rep_sb[:], psR[:]),
                            waits=[trp, T.get("rep_rd")], inc="V")
                T["psRg"] = trs
                T["rcol_rd"] = trp
                tyn = pb.op("vector", lambda e: e.tensor_mul(ybf[:], yun[:], rep_sb[:]),
                            waits=[tyu, trs, T.get("ybf_rd")], inc="V")
                T["yun_rd"] = tyn
                T["rep_rd"] = tyn
                tw = pb.op("gpsimd", lambda e, hp=hp, h=h, jc=jc: e.dma_start(
                    out=yd[128 * hp + 64 * h:128 * hp + 64 * h + 64, jc * 512:(jc + 1) * 512],
                    in_=ybf[:]), waits=[tyn], inc="Hs", inc_by=16)
                T["ybf_rd"] = tw
                T["y_done"] = tw
        T["qf_rd"] = tmy
        for h in range(2):
            T[("v2g", h)] = tmy
            T[("kqg", h)] = T[("psKVg", h)]
            T[("kvs_rd", h)] = tmy
    pb.pop_scope(gate=[T["y_done"]])
    emit_outproj_residual(pb, yd, woP, wordy, boc, bordy, xv, ov, NC, idb, idf, False)
    if fused:
        return [("St", pb.cnt["St"])]
    pb.wait("gpsimd", ("St", pb.cnt["St"]))
    return pb.finalize()



def build_l2attn(R=4096, pb=None, io=None):
    fused = pb is not None
    if not fused:
        pb = PB()
    NN = 512
    gi = lambda n, s, d=F32: io[n] if fused else pb.dram(n, s, d, kind="ExternalInput")
    xin = gi("xin", [R, D])
    nu = gi("nu_log", [NN])
    th = gi("theta_log", [NN])
    ga = gi("gamma_log", [NN])
    Bre = gi("B_re", [NN, D])
    Bim = gi("B_im", [NN, D])
    Cre = gi("C_re", [D, NN])
    Cim = gi("C_im", [D, NN])
    Dm = gi("Dmat", [D, D])
    xout = io["xout"] if fused else pb.dram("xout", [R, D], F32, kind="ExternalOutput")
    xsr_d = pb.dram("xsr_d", [NN, R], BF16)
    xsi_d = pb.dram("xsi_d", [NN, R], BF16)
    idf, idb = emit_identity(pb)
    alloc_ln(pb)
    T = pb.tag
    NT, NC = R // 128, R // 512
    xv = xin.rearrange("(t p) d -> t p d", p=128)
    ov = xout.rearrange("(t p) d -> t p d", p=128)
    nuc, tnu = emit_bias_cols(pb, nu, NN, "nu", idf)
    thc, tth = emit_bias_cols(pb, th, NN, "th", idf)
    gac, tga = emit_bias_cols(pb, ga, NN, "ga", idf)
    rho = pb.sb("rho", [128, 4], F32)
    the = pb.sb("the", [128, 4], F32)
    gam = pb.sb("gam", [128, 4], F32)
    t1 = pb.op("scalar", lambda e: e.activation(rho[:], nuc[:], AF.Exp), waits=[tnu], inc="S")
    t2 = pb.op("scalar", lambda e: e.activation(rho[:], rho[:], AF.Exp, scale=np.float32(-1.0)),
               waits=[t1], inc="S")
    t3 = pb.op("scalar", lambda e: e.activation(the[:], thc[:], AF.Exp), waits=[tth], inc="S")
    t4 = pb.op("scalar", lambda e: e.activation(gam[:], gac[:], AF.Exp), waits=[tga], inc="S")
    xnfm = pb.sb("xnfm", [128, 8 * R], BF16)
    pb.push_scope()
    alloc_wstage(pb, 2048, f8=False, tst=None)
    Bgr, tbr = emit_weight_T(pb, Bre, NN, D, "bgr", fold_cols=(gam, t4))
    Bgi, tbi = emit_weight_T(pb, Bim, NN, D, "bgi", fold_cols=(gam, t4))
    xa = [pb.sb(f"xa{i}", [128, D], F32) for i in range(2)]
    xqb = pb.sb("xqb", [128, D], BF16)
    psT = [pb.ps(f"psT{i}", [128, 128], BF16) for i in range(2)]
    psM = [pb.ps(f"psM{i}", [128, 512], F32) for i in range(2)]
    emit_ln_transpose_all(pb, xv, NT, xnfm, R, idb, xa, xqb, psT)
    br = pb.sb("br", [128, R], F32)
    bi = pb.sb("bi", [128, R], F32)
    s1 = pb.sb("s1", [128, R], F32)
    s2 = pb.sb("s2", [128, R], F32)
    cosT = pb.sb("cosT", [128, R], F32)
    sinT = pb.sb("sinT", [128, R], F32)

    for nt in range(4):
        evs = []
        for ch in range(NC):
            for ri, (wT, wrdy, dst) in enumerate(((Bgr, tbr, br), (Bgi, tbi, bi))):
                ps = psM[ri]
                def mmb(e, nt=nt, ch=ch, ps=ps, wT=wT):
                    inst = None
                    for k in range(8):
                        inst = e.matmul(ps[:], wT[:, k * NN + 128 * nt:k * NN + 128 * (nt + 1)],
                                        xnfm[:, k * R + ch * 512:k * R + ch * 512 + 512],
                                        start=(k == 0), stop=(k == 7))
                    return inst
                tm = pb.op("tensor", mmb, waits=[T["xnfm"], wrdy, T.get(("psMb", ri))], inc="T")
                te = pb.op("scalar", lambda e, ch=ch, ps=ps, dst=dst: e.copy(
                    dst[:, ch * 512:(ch + 1) * 512], ps[:]),
                    waits=[tm, T.get(("bu_rd", id(dst)))], inc="S")
                T[("psMb", ri)] = te
                evs.append(te)
        tio = pb.op("gpsimd", lambda e: e.iota(
            s1[:], [[1, R]], base=0, channel_multiplier=0,
            allow_small_or_imprecise_dtypes=True), waits=[T.get("s1_rd")], inc="G")
        tan = pb.op("vector", lambda e, nt=nt: e.tensor_scalar(
            s1[:], s1[:], the[:, nt:nt + 1], float(1.0 / (2 * np.pi)), ALU.mult, ALU.mult),
            waits=[tio, t3], inc="V")
        tk1 = pb.op("vector", lambda e: e.tensor_copy(pb.q_i32[:, 0:R] if False else s2[:].bitcast(I32), s1[:]), waits=[tan, T.get("s2_rd")], inc="V")
        tk2 = pb.op("vector", lambda e: e.tensor_copy(s2[:], s2[:].bitcast(I32)), waits=[tk1], inc="V")
        tk3 = pb.op("vector", lambda e: e.tensor_sub(s1[:], s1[:], s2[:]), waits=[tk2], inc="V")
        tk4 = pb.op("vector", lambda e: e.tensor_scalar(
            s1[:], s1[:], float(2 * np.pi), None, ALU.mult, ALU.bypass), waits=[tk3], inc="V")
        tab = pb.op("scalar", lambda e: e.activation(s2[:], s1[:], AF.Abs), waits=[tk4], inc="S")
        tco = pb.op("scalar", lambda e: e.activation(
            cosT[:], s2[:], AF.Sin, scale=np.float32(-1.0), bias=np.float32(np.pi / 2)),
            waits=[tab, T.get("cos_rd")], inc="S")
        tsi = pb.op("scalar", lambda e: e.activation(sinT[:], s1[:], AF.Sin),
                    waits=[tk4, T.get("sin_rd")], inc="S")
        w0 = evs + [tco, tsi]
        r1 = pb.op("vector", lambda e: e.tensor_mul(s1[:], cosT[:], br[:]), waits=w0 + [T.get("s1w")], inc="V")
        r2 = pb.op("vector", lambda e: e.tensor_mul(s2[:], sinT[:], bi[:]), waits=w0 + [T.get("s2w")], inc="V")
        r3 = pb.op("vector", lambda e: e.tensor_add(s1[:], s1[:], s2[:]), waits=[r1, r2], inc="V")
        r4 = pb.op("vector", lambda e: e.tensor_mul(s2[:], cosT[:], bi[:]), waits=[r3], inc="V")
        r5 = pb.op("vector", lambda e: e.tensor_mul(bi[:], sinT[:], br[:]), waits=[r4, T.get("bi_rd")], inc="V")
        r6 = pb.op("vector", lambda e: e.tensor_sub(s2[:], s2[:], bi[:]), waits=[r5], inc="V")
        T[("bu_rd", id(br))] = r5
        prev = [r3, r6]
        for ri, (src, dst) in enumerate(((s1, br), (s2, bi))):
            tprev = prev[ri]
            for c in range(NC):
                init = 0.0 if c == 0 else dst[:, c * 512 - 1:c * 512]
                tprev = pb.op("vector", lambda e, src=src, dst=dst, c=c, init=init, nt=nt:
                              e.tensor_tensor_scan(
                    dst[:, c * 512:(c + 1) * 512],
                    rho[:, nt:nt + 1].broadcast_to([128, 512]),
                    src[:, c * 512:(c + 1) * 512], init, ALU.mult, ALU.add),
                    waits=[tprev, t2], inc="V")
            prev[ri] = tprev
        T[("bu_rd", id(bi))] = prev[1]
        p1 = pb.op("vector", lambda e: e.tensor_mul(s1[:], cosT[:], br[:]), waits=prev, inc="V")
        p2 = pb.op("vector", lambda e: e.tensor_mul(s2[:], sinT[:], bi[:]), waits=prev, inc="V")
        p3 = pb.op("vector", lambda e: e.tensor_sub(s1[:], s1[:], s2[:]), waits=[p1, p2], inc="V")
        p4 = pb.op("vector", lambda e: e.tensor_mul(s2[:], cosT[:], bi[:]), waits=[p3], inc="V")
        p5 = pb.op("vector", lambda e: e.tensor_mul(bi[:], sinT[:], br[:]), waits=[p4], inc="V")
        T[("bu_rd", id(br))] = p5
        T["cos_rd"] = p4
        T["sin_rd"] = p5
        p6 = pb.op("vector", lambda e: e.tensor_add(s2[:], s2[:], bi[:]), waits=[p5], inc="V")
        T[("bu_rd", id(bi))] = p6
        w1 = pb.op("gpsimd", lambda e, nt=nt: e.dma_start(
            out=xsr_d[128 * nt:128 * (nt + 1), :], in_=s1[:]), waits=[p3], inc="Hs", inc_by=16)
        w2 = pb.op("gpsimd", lambda e, nt=nt: e.dma_start(
            out=xsi_d[128 * nt:128 * (nt + 1), :], in_=s2[:]), waits=[p6], inc="Hs", inc_by=16)
        T["s1w"] = w1
        T["s2w"] = w2
        T["s1_rd"] = w1
        T["s2_rd"] = w2
        T["xs_done"] = w2
    pb.pop_scope(gate=[T["xs_done"]])
    alloc_wstage(pb, 4096, f8=False, tst=None, sfx="y")
    CrT, tcr = emit_weight_T(pb, Cre, D, NN, "crt")
    CiT, tci = emit_weight_T(pb, Cim, D, NN, "cit", fold=-1.0)
    DmT, tdm = emit_weight_T(pb, Dm, D, D, "dmt")
    xmv = [pb.sb(f"xmv{i}", [128, 512], BF16) for i in range(8)]
    psM2 = [pb.ps(f"ps2y{i}", [128, 512], F32) for i in range(2)]
    psY = [pb.ps(f"psYy{i}", [128, 128], BF16) for i in range(2)]
    yofm = pb.sb("yofm2", [128, 8 * 512], BF16)
    ytk = pb.sb("ytk2", [128, D], BF16)
    xar = [pb.sb(f"xry{i}", [128, D], F32) for i in range(2)]
    for ch in range(NC):
        lds = []
        for k in range(4):
            tl = pb.op("sync", lambda e, k=k, ch=ch: e.dma_start(
                out=xmv[k][:], in_=xsr_d[128 * k:128 * (k + 1), 512 * ch:512 * (ch + 1)]),
                waits=[T["xs_done"], T.get(("xmv", k))], inc="Xq", inc_by=16)
            tl2 = pb.op("sync", lambda e, k=k, ch=ch: e.dma_start(
                out=xmv[4 + k][:], in_=xsi_d[128 * k:128 * (k + 1), 512 * ch:512 * (ch + 1)]),
                waits=[T["xs_done"], T.get(("xmv", 4 + k))], inc="Xq", inc_by=16)
            lds += [tl, tl2]
        for m in range(8):
            ps = psM2[m % 2]
            def mmy(e, m=m, ch=ch, ps=ps):
                inst = None
                for k in range(4):
                    inst = e.matmul(ps[:], CrT[:, k * D + 128 * m:k * D + 128 * (m + 1)],
                                    xmv[k][:], start=(k == 0), stop=False)
                for k in range(4):
                    inst = e.matmul(ps[:], CiT[:, k * D + 128 * m:k * D + 128 * (m + 1)],
                                    xmv[4 + k][:], start=False, stop=False)
                for k in range(8):
                    inst = e.matmul(ps[:], DmT[:, k * D + 128 * m:k * D + 128 * (m + 1)],
                                    xnfm[:, k * R + ch * 512:k * R + ch * 512 + 512],
                                    start=False, stop=(k == 7))
                return inst
            tm = pb.op("tensor", mmy, waits=lds + [tcr, tci, tdm, T["xnfm"],
                       T.get(("psM2y", m % 2))], inc="T")
            if m == 7:
                for k in range(8):
                    T[("xmv", k)] = tm
            te = pb.op("scalar", lambda e, m=m, ps=ps: e.copy(
                yofm[:, m * 512:(m + 1) * 512], ps[:]),
                waits=[tm, T.get(("yofmy_rd", m))], inc="S")
            T[("psM2y", m % 2)] = te
            T[("yofmy", m)] = te
        for ti in range(4):
            t = 4 * ch + ti
            trb = []
            for m in range(8):
                ps = psY[m % 2]
                tt = pb.op("tensor", lambda e, m=m, ti=ti, ps=ps: e.transpose(
                    ps[:], yofm[:, m * 512 + ti * 128: m * 512 + ti * 128 + 128], idb[:]),
                    waits=[T[("yofmy", m)], T.get(("psYy", m % 2))] + T["id"], inc="T")
                if ti == 3:
                    T[("yofmy_rd", m)] = tt
                ev = pb.op("vector", lambda e, m=m, ps=ps: e.tensor_copy(
                    ytk[:, 128 * m:128 * (m + 1)], ps[:]),
                    waits=[tt, T.get("ytky_rd")], inc="V")
                T[("psYy", m % 2)] = ev
                trb.append(ev)
            a = xar[ti % 2]
            tlx = pb.op("sync", lambda e, t=t, a=a: e.dma_start(out=a[:], in_=xv[t]),
                        waits=[T.get(("xay", ti % 2))], inc="Xq", inc_by=16)
            tad = pb.op("vector", lambda e, a=a: e.tensor_add(a[:], a[:], ytk[:]),
                        waits=trb + [tlx], inc="V")
            T["ytky_rd"] = tad
            tst = pb.op("scalar", lambda e, t=t, a=a: e.dma_start(out=ov[t], in_=a[:]),
                        waits=[tad], inc="St", inc_by=16)
            T[("xay", ti % 2)] = tst
    if fused:
        return [("St", pb.cnt["St"])]
    pb.wait("gpsimd", ("St", pb.cnt["St"]))
    return pb.finalize()




def emit_exchange(pb, x_own, exsel_sb, xfull):
    zb = pb.dram("exz", [4096, D])
    tile = [pb.sb(f"ex_t{i}", [128, D]) for i in range(2)]
    mt = [pb.sb(f"ex_m{i}", [128, D]) for i in range(2)]
    xv = x_own.rearrange("(t p) d -> t p d", p=128)
    zv = zb.rearrange("(t p) d -> t p d", p=128)
    T = pb.tag
    ts = None
    for t in range(16):
        a, m = tile[t % 2], mt[t % 2]
        tl = pb.op("sync", lambda e, t=t, a=a: e.dma_start(out=a[:], in_=xv[t]),
                   waits=[T.get(("ex_t", t % 2))], inc="Xq", inc_by=16)
        tm0 = pb.op("vector", lambda e, t=t, a=a, m=m: e.tensor_scalar(
            m[:], a[:], exsel_sb[:, t:t + 1], None, ALU.mult, ALU.bypass),
            waits=[tl, T.get(("ex_m", t % 2)), T.get("exsel")], inc="V")
        ts0 = pb.op("scalar", lambda e, t=t, m=m: e.dma_start(out=zv[t], in_=m[:]),
                    waits=[tm0], inc="St", inc_by=16)
        tm1 = pb.op("vector", lambda e, t=t, a=a, m=m: e.tensor_scalar(
            m[:], a[:], exsel_sb[:, t + 16:t + 17], None, ALU.mult, ALU.bypass),
            waits=[ts0], inc="V")
        T[("ex_t", t % 2)] = tm1
        ts = pb.op("scalar", lambda e, t=t, m=m: e.dma_start(out=zv[t + 16], in_=m[:]),
                   waits=[tm1], inc="St", inc_by=16)
        T[("ex_m", t % 2)] = ts
    tc = pb.op("gpsimd", lambda e: e.collective_compute(
        "AllReduce", ALU.add, [[0, 1], [2, 3], [4, 5], [6, 7]],
        ins=[zb.opt()], outs=[xfull.opt()]), waits=[ts], inc="CC")
    return tc


def emit_select_own(pb, xafull, sela_sb, selb_sb, x_own_out):
    xv = xafull.rearrange("(t p) d -> t p d", p=128)
    ov = x_own_out.rearrange("(t p) d -> t p d", p=128)
    a = pb.sb("so_a", [128, D])
    b = pb.sb("so_b", [128, D])
    T = pb.tag
    ts = None
    for t in range(16):
        tla = pb.op("sync", lambda e, t=t: e.dma_start(out=a[:], in_=xv[t]),
                    waits=[T.get("so_a_rd")], inc="Xq", inc_by=16)
        tlb = pb.op("sync", lambda e, t=t: e.dma_start(out=b[:], in_=xv[t + 16]),
                    waits=[T.get("so_b_rd")], inc="Xq", inc_by=16)
        tma = pb.op("vector", lambda e, t=t: e.tensor_scalar(
            a[:], a[:], sela_sb[:, t:t + 1], None, ALU.mult, ALU.bypass),
            waits=[tla, T.get("sel")], inc="V")
        tmb = pb.op("vector", lambda e, t=t: e.tensor_scalar(
            b[:], b[:], selb_sb[:, t:t + 1], None, ALU.mult, ALU.bypass),
            waits=[tlb], inc="V")
        tad = pb.op("vector", lambda e: e.tensor_add(a[:], a[:], b[:]),
                    waits=[tma, tmb], inc="V")
        T["so_b_rd"] = tad
        ts = pb.op("scalar", lambda e, t=t: e.dma_start(out=ov[t], in_=a[:]),
                   waits=[tad], inc="St", inc_by=16)
        T["so_a_rd"] = ts
    return ts


_FUSED_SPECS = {
    "l0f": {"w1": [4096, D], "b1": [4096], "w2": [D, 4096], "b2": [D]},
    "l0a": {"Wq": [64, 64], "Wk": [64, 64], "Wv": [64, 64], "Wu": [D, D], "bu": [D]},
    "l1f": {"W_exp": [D, 4096], "b_exp": [4096], "up_w": [12, 2048, 2, 2],
            "up_b": [4096], "dn_w": [12, 2048, 2, 2], "W_con": [4096, D],
            "b_con": [D], "dn_b": [D]},
    "l1a": {"Wq": [D, 16, 64], "bq": [16, 64], "Wk": [D, 16, 64], "bk": [16, 64],
            "Wv": [D, 16, 64], "bv": [16, 64], "Wo": [16, 64, D], "bo": [D]},
    "l2f": {"w_gate": [3072, D], "w_up": [3072, D], "w_down": [D, 3072]},
    "l2a": {"nu_log": [512], "theta_log": [512], "gamma_log": [512],
            "B_re": [512, D], "B_im": [512, D], "C_re": [D, 512], "C_im": [D, 512],
            "Dmat": [D, D]},
    "l3f": {"w_gate": [2816, D], "w_up": [2816, D], "w_down": [D, 2816]},
    "l3a": {"w_in": [3 * D, D], "b_in": [3 * D], "w_out": [D, D], "b_out": [D]},
}


def build_fused(layers=(0, 1, 2, 3)):
    pb = PB()
    x0 = pb.dram("x0", [2048, D], F32, kind="ExternalInput")
    xoutF = pb.dram("xout", [2048, D], F32, kind="ExternalOutput")
    mask = pb.dram("mask", [2048, D], F32, kind="ExternalInput")
    exsel_in = pb.dram("exsel", [128, 32], F32, kind="ExternalInput")
    sela_in = pb.dram("sela", [128, 16], F32, kind="ExternalInput")
    selb_in = pb.dram("selb", [128, 16], F32, kind="ExternalInput")
    W = {}
    for grp, dd in _FUSED_SPECS.items():
        if int(grp[1]) not in layers:
            continue
        W[grp] = {n: pb.dram(f"{grp}_{n}", s, F32, kind="ExternalInput")
                  for n, s in dd.items()}

    emit_identity(pb)
    exsel_sb = pb.sb("exsel_sb", [128, 32])
    sela_sb = pb.sb("sela_sb", [128, 16])
    selb_sb = pb.sb("selb_sb", [128, 16])
    t1 = pb.op("sync", lambda e: e.dma_start(out=exsel_sb[:], in_=exsel_in), inc="Xq", inc_by=16)
    t2 = pb.op("sync", lambda e: e.dma_start(out=sela_sb[:], in_=sela_in), inc="Xq", inc_by=16)
    t3 = pb.op("sync", lambda e: e.dma_start(out=selb_sb[:], in_=selb_in), inc="Xq", inc_by=16)
    pb.tag["exsel"] = t1
    pb.tag["sel"] = t3

    ffns = {0: (build_l0ffn, "l0f"), 1: (build_l1ffn, "l1f"),
            2: (build_l2ffn, "l2f"), 3: (build_l3ffn, "l3f")}
    attns = {0: (lambda **kw: build_causal_attn("l0", **kw), "l0a"),
             1: (build_l1attn, "l1a"),
             2: (build_l2attn, "l2a"),
             3: (lambda **kw: build_causal_attn("l3", **kw), "l3a")}

    x_cur = x0
    for L in layers:
        fb, fkey = ffns[L]
        ab, akey = attns[L]
        lcache = {}
        for rep in range(2):
            last = (L == max(layers) and rep == 1)
            xn = xoutF if last else pb.dram(f"xc{L}{rep}", [2048, D])
            pb.prefix = f"F{L}{rep}_"
            pb.wcache = lcache
            pb.push_scope()
            io = dict(W[fkey])
            io["xin"] = x_cur
            io["xout"] = xn
            if fkey == "l0f":
                io["mask"] = mask
            g = fb(R=2048, pb=pb, io=io)
            pb.pop_scope(gate=g)
            x_cur = xn
            if rep == 0:
                pb.prefix = f"X{L}_"
                pb.push_scope()
                xfull = pb.dram("xfull", [4096, D])
                tc = emit_exchange(pb, x_cur, exsel_sb, xfull)
                pb.pop_scope(gate=[tc])
                xa = pb.dram(f"Xa{L}_res", [4096, D])
                pb.prefix = f"A{L}_"
                pb.wcache = None
                pb.push_scope()
                io = dict(W[akey])
                io["xin"] = xfull
                io["xout"] = xa
                g = ab(R=4096, pb=pb, io=io)
                pb.pop_scope(gate=g)
                pb.prefix = f"S{L}_"
                pb.push_scope()
                xo = pb.dram("xo", [2048, D])
                g2 = emit_select_own(pb, xa, sela_sb, selb_sb, xo)
                pb.pop_scope(gate=[g2])
                x_cur = xo
    pb.prefix = ""
    for s in ("St", "Xq", "Wq", "Ws", "Hs", "CC"):
        if s in pb.cnt:
            pb.wait("gpsimd", (s, pb.cnt[s]))
    return pb.finalize()


_FUSED_PKEYS = {
    "l0f": {"w1": "L0_ffn_w1", "b1": "L0_ffn_b1", "w2": "L0_ffn_w2", "b2": "L0_ffn_b2"},
    "l0a": {"Wq": "L0_attn_Wq", "Wk": "L0_attn_Wk", "Wv": "L0_attn_Wv",
            "Wu": "L0_attn_Wu", "bu": "L0_attn_bu"},
    "l1f": {"W_exp": "L1_ffn_W_exp", "b_exp": "L1_ffn_b_exp", "up_w": "L1_ffn_up_weights",
            "up_b": "L1_ffn_up_bias", "dn_w": "L1_ffn_down_weights",
            "W_con": "L1_ffn_W_con", "b_con": "L1_ffn_b_con", "dn_b": "L1_ffn_down_bias"},
    "l1a": {"Wq": "L1_attn_Wq", "bq": "L1_attn_bq", "Wk": "L1_attn_Wk", "bk": "L1_attn_bk",
            "Wv": "L1_attn_Wv", "bv": "L1_attn_bv", "Wo": "L1_attn_Wo", "bo": "L1_attn_bo"},
    "l2f": {"w_gate": "L2_ffn_w_gate", "w_up": "L2_ffn_w_up", "w_down": "L2_ffn_w_down"},
    "l2a": {"nu_log": "L2_attn_nu_log", "theta_log": "L2_attn_theta_log",
            "gamma_log": "L2_attn_gamma_log", "B_re": "L2_attn_B_re", "B_im": "L2_attn_B_im",
            "C_re": "L2_attn_C_re", "C_im": "L2_attn_C_im", "Dmat": "L2_attn_D"},
    "l3f": {"w_gate": "L3_ffn_w_gate", "w_up": "L3_ffn_w_up", "w_down": "L3_ffn_w_down"},
    "l3a": {"w_in": "L3_attn_w_in", "b_in": "L3_attn_b_in",
            "w_out": "L3_attn_w_out", "b_out": "L3_attn_b_out"},
}


_PROG_CACHE = {}
_MASK_CACHE = {}


def _get_prog(name, builder):
    if name not in _PROG_CACHE:
        _PROG_CACHE[name] = builder()
    return _PROG_CACHE[name]


def _threefry2x32(k0, k1, x0, x1):
    u32 = np.uint32
    ks0, ks1 = k0.astype(u32), k1.astype(u32)
    ks2 = (ks0 ^ ks1 ^ u32(0x1BD11BDA)).astype(u32)
    x0 = (x0 + ks0).astype(u32)
    x1 = (x1 + ks1).astype(u32)
    rot = [[13, 15, 26, 6], [17, 29, 16, 24]]
    ks = [ks0, ks1, ks2]
    for i in range(5):
        for r in rot[i % 2]:
            x0 = (x0 + x1).astype(u32)
            x1 = ((x1 << u32(r)) | (x1 >> u32(32 - r))).astype(u32)
            x1 = (x1 ^ x0).astype(u32)
        x0 = (x0 + ks[(i + 1) % 3]).astype(u32)
        x1 = (x1 + ks[(i + 2) % 3] + u32(i + 1)).astype(u32)
    return x0, x1


def _dropout_mask05():
    if "m" not in _MASK_CACHE:
        u32 = np.uint32
        n = 4 * 4096 * 1024
        j = np.arange(n, dtype=u32)
        z = np.zeros(n, u32)
        o0, o1 = _threefry2x32(np.full(n, 0, u32), np.full(n, 42, u32), z, j)
        bits = (o0 ^ o1).astype(u32)
        u = ((bits >> u32(9)) | u32(0x3F800000)).view(np.float32) - np.float32(1.0)
        keep = u < np.float32(0.8)
        _MASK_CACHE["m"] = np.where(keep, np.float32(0.5 / 0.8),
                                    np.float32(0.0)).astype(np.float32).reshape(16384, 1024)
    return _MASK_CACHE["m"]


def _run_ffn(name, builder, x, per_core_ins):
    nc = _get_prog(name, builder)
    im = []
    for c in range(8):
        d = {"xin": np.ascontiguousarray(x[c * 2048:(c + 1) * 2048])}
        d.update(per_core_ins(c))
        im.append(d)
    res = run_bass_kernel_spmd(nc, im, core_ids=list(range(8)))
    return np.concatenate([res.results[c]["xout"] for c in range(8)], 0)


def _run_attn(name, builder, x, common_ins):
    nc = _get_prog(name, builder)
    im = []
    for c in range(8):
        b = c // 2
        d = {"xin": np.ascontiguousarray(x[b * 4096:(b + 1) * 4096])}
        d.update(common_ins)
        im.append(d)
    res = run_bass_kernel_spmd(nc, im, core_ids=list(range(8)))
    out = np.empty_like(x)
    for c in range(8):
        b, h = c // 2, c % 2
        out[b * 4096 + h * 2048: b * 4096 + (h + 1) * 2048] = \
            res.results[c]["xout"][h * 2048:(h + 1) * 2048]
    return out


def kernel(**P):
    x = np.ascontiguousarray(np.asarray(P["L0_attn_x"], np.float32).reshape(16384, 1024))
    mask = _dropout_mask05()
    nc = _get_prog("fused", build_fused)
    im = []
    for c in range(8):
        half = c % 2
        exsel = np.zeros((128, 32), np.float32)
        exsel[:, half * 16:(half + 1) * 16] = 1.0
        sela = np.full((128, 16), 1.0 if half == 0 else 0.0, np.float32)
        selb = np.full((128, 16), 1.0 if half == 1 else 0.0, np.float32)
        d = {"x0": np.ascontiguousarray(x[c * 2048:(c + 1) * 2048]),
             "mask": np.ascontiguousarray(mask[c * 2048:(c + 1) * 2048]),
             "exsel": exsel, "sela": sela, "selb": selb}
        for grp, mp in _FUSED_PKEYS.items():
            for n, pk in mp.items():
                d[f"{grp}_{n}"] = np.ascontiguousarray(np.asarray(P[pk], np.float32))
        im.append(d)
    res = run_bass_kernel_spmd(nc, im, core_ids=list(range(8)))
    out = np.concatenate([res.results[c]["xout"] for c in range(8)], 0)
    return out.reshape(4, 4096, 1024)


def kernel_unfused(**P):
    x = np.ascontiguousarray(np.asarray(P["L0_attn_x"], np.float32).reshape(16384, 1024))
    mask = _dropout_mask05()

    l0f = lambda xx: _run_ffn("l0ffn", build_l0ffn, xx, lambda c: {
        "w1": P["L0_ffn_w1"], "b1": P["L0_ffn_b1"], "w2": P["L0_ffn_w2"],
        "b2": P["L0_ffn_b2"], "mask": np.ascontiguousarray(mask[c * 2048:(c + 1) * 2048])})
    l1f = lambda xx: _run_ffn("l1ffn", build_l1ffn, xx, lambda c: {
        "W_exp": P["L1_ffn_W_exp"], "b_exp": P["L1_ffn_b_exp"], "up_w": P["L1_ffn_up_weights"],
        "up_b": P["L1_ffn_up_bias"], "dn_w": P["L1_ffn_down_weights"],
        "W_con": P["L1_ffn_W_con"], "b_con": P["L1_ffn_b_con"], "dn_b": P["L1_ffn_down_bias"]})
    l2f = lambda xx: _run_ffn("l2ffn", build_l2ffn, xx, lambda c: {
        "w_gate": P["L2_ffn_w_gate"], "w_up": P["L2_ffn_w_up"], "w_down": P["L2_ffn_w_down"]})
    l3f = lambda xx: _run_ffn("l3ffn", build_l3ffn, xx, lambda c: {
        "w_gate": P["L3_ffn_w_gate"], "w_up": P["L3_ffn_w_up"], "w_down": P["L3_ffn_w_down"]})

    x = l0f(x)
    x = _run_attn("l0attn", lambda: build_causal_attn("l0"), x, {
        "Wq": P["L0_attn_Wq"], "Wk": P["L0_attn_Wk"], "Wv": P["L0_attn_Wv"],
        "Wu": P["L0_attn_Wu"], "bu": P["L0_attn_bu"]})
    x = l0f(x)
    x = l1f(x)
    x = _run_attn("l1attn", build_l1attn, x, {
        "Wq": P["L1_attn_Wq"], "bq": P["L1_attn_bq"], "Wk": P["L1_attn_Wk"],
        "bk": P["L1_attn_bk"], "Wv": P["L1_attn_Wv"], "bv": P["L1_attn_bv"],
        "Wo": P["L1_attn_Wo"], "bo": P["L1_attn_bo"]})
    x = l1f(x)
    x = l2f(x)
    x = _run_attn("l2attn", build_l2attn, x, {
        "nu_log": P["L2_attn_nu_log"], "theta_log": P["L2_attn_theta_log"],
        "gamma_log": P["L2_attn_gamma_log"], "B_re": P["L2_attn_B_re"],
        "B_im": P["L2_attn_B_im"], "C_re": P["L2_attn_C_re"], "C_im": P["L2_attn_C_im"],
        "Dmat": P["L2_attn_D"]})
    x = l2f(x)
    x = l3f(x)
    x = _run_attn("l3attn", lambda: build_causal_attn("l3"), x, {
        "w_in": P["L3_attn_w_in"], "b_in": P["L3_attn_b_in"],
        "w_out": P["L3_attn_w_out"], "b_out": P["L3_attn_b_out"]})
    x = l3f(x)
    return x.reshape(4, 4096, 1024)
